# version 26
# baseline (speedup 1.0000x reference)
import numpy as np
from collections import defaultdict
from contextlib import ExitStack

import concourse.bass as bass
import concourse.mybir as mybir
from concourse.bass_utils import run_bass_kernel_spmd

F32 = mybir.dt.float32
BF16 = mybir.dt.bfloat16
I32 = mybir.dt.int32
AF = mybir.ActivationFunctionType
OP = mybir.AluOpType
AX = mybir.AxisListType

D = 1024
NF = 4096
NT = 2048
H = 16
HK = 8
M = 266
MCH = [(0, 128), (128, 128), (256, 10)]
EPS = 1e-4
DN = float(64.0 ** -0.25)
LNEPS = float(np.log(EPS))
NE = 8
RK = 64
KC = 8


def build_nc(npairs=4, limit=None):
    nc = bass.Bass("TRN2", target_bir_lowering=False)

    d_xT = nc.dram_tensor("xT", [D, NF], F32, kind="ExternalInput")
    d_wq = nc.dram_tensor("wq", [D, D], F32, kind="ExternalInput")
    d_wk = nc.dram_tensor("wk", [D, 512], F32, kind="ExternalInput")
    d_wv = nc.dram_tensor("wv", [D, 512], F32, kind="ExternalInput")
    d_wo = nc.dram_tensor("wo", [D, D], F32, kind="ExternalInput")
    d_bo = nc.dram_tensor("bo_cols", [128, 8], F32, kind="ExternalInput")
    d_projT = nc.dram_tensor("projT2", [128, M], F32, kind="ExternalInput")
    d_bwT = nc.dram_tensor("base_wT", [D, D], F32, kind="ExternalInput")
    d_bb = nc.dram_tensor("baseb_cols", [128, 8], F32, kind="ExternalInput")
    d_smT = nc.dram_tensor("sigmaT_ec", [128, 64], F32, kind="ExternalInput")
    d_muT = nc.dram_tensor("muT_ec", [128, 64], F32, kind="ExternalInput")
    d_dw = nc.dram_tensor("down_w", [NE, D, RK], F32, kind="ExternalInput")
    d_db = nc.dram_tensor("down_b", [NE, RK], F32, kind="ExternalInput")
    d_uwT = nc.dram_tensor("up_wT", [NE, D, RK], F32, kind="ExternalInput")
    d_ubT = nc.dram_tensor("up_bT", [D, NE], F32, kind="ExternalInput")
    d_cwT = nc.dram_tensor("conv_wT", [NE, D, D], F32, kind="ExternalInput")
    d_cb = nc.dram_tensor("conv_b", [NE, D], F32, kind="ExternalInput")
    d_ones = nc.dram_tensor("c_ones", [128, 128], F32, kind="ExternalInput")
    d_lnones = nc.dram_tensor("c_lnones", [128, 128], F32, kind="ExternalInput")
    d_ident = nc.dram_tensor("c_ident", [128, 128], F32, kind="ExternalInput")
    d_selq = nc.dram_tensor("c_selq", [D, H], F32, kind="ExternalInput")
    d_selk = nc.dram_tensor("c_selk", [512, HK], F32, kind="ExternalInput")
    d_iota = nc.dram_tensor("c_iota", [1, 8], F32, kind="ExternalInput")

    d_out = nc.dram_tensor("outT", [D, NT], F32, kind="ExternalOutput")
    ag1_in = nc.dram_tensor("ag1_in", [65, HK * M], BF16)
    ag1_out = nc.dram_tensor("ag1_out", [130, HK * M], BF16)
    ag2_in = nc.dram_tensor("ag2_in", [128, 16], F32)
    ag2_out = nc.dram_tensor("ag2_out", [256, 16], F32)
    ag3_in = nc.dram_tensor("ag3_in", [1, 8], F32)
    ag3_out = nc.dram_tensor("ag3_out", [2, 8], F32)
    groups = [[2 * i, 2 * i + 1] for i in range(npairs)]

    with ExitStack() as cx:
        en = cx.enter_context
        G16 = en(nc.sbuf_tensor("G16", [128, 81032], BF16))
        G32 = en(nc.sbuf_tensor("G32", [128, 3328], F32))
        Wa = en(nc.sbuf_tensor("W16a", [128, 8192], BF16))
        Wb = en(nc.sbuf_tensor("W16b", [128, 8192], BF16))
        c_ones = en(nc.sbuf_tensor("cb_ones", [128, 128], BF16))
        c_lnones = en(nc.sbuf_tensor("cb_lnones", [128, 128], BF16))
        c_idb = en(nc.sbuf_tensor("cb_id", [128, 128], BF16))
        c_idf = en(nc.sbuf_tensor("cf_id", [128, 128], F32))
        c_projb = en(nc.sbuf_tensor("cb_proj", [128, M], BF16))
        c_selq = en(nc.sbuf_tensor("cb_selq", [128, 128], BF16))
        c_selk = en(nc.sbuf_tensor("cb_selk", [128, 32], BF16))
        c_bo = en(nc.sbuf_tensor("cf_bo", [128, 8], F32))
        c_bb = en(nc.sbuf_tensor("cf_bb", [128, 8], F32))
        c_smT = en(nc.sbuf_tensor("cf_smT", [128, 64], F32))
        c_muT = en(nc.sbuf_tensor("cf_muT", [128, 64], F32))
        c_iota = en(nc.sbuf_tensor("cf_iota", [1, 8], F32))
        t_sc = en(nc.sbuf_tensor("t_sc", [128, 16], F32))
        t_scb = en(nc.sbuf_tensor("t_scb", [1, 16], BF16))
        t_loss = en(nc.sbuf_tensor("t_loss", [1, 32], F32))
        t_idx = en(nc.sbuf_tensor("t_idx", [1, 2], I32))
        t_bdrow = en(nc.sbuf_tensor("t_bdrow", [1, RK], BF16))
        pA = en(nc.psum_tensor("pA", [128, 2048], F32))
        pB = en(nc.psum_tensor("pB", [128, 2048], F32))
        pBb = pB.bitcast(BF16)
        pAb = pA.bitcast(BF16)
        G16f = G16.bitcast(F32)

        CH = {}
        for nm in ("T", "V", "S", "G", "qd", "qg", "qs", "cc",
                   "gC", "gW", "gW2", "gW3", "gX", "gX2", "gX3", "gX4",
                   "gE", "gE2", "gE3", "sO2", "sO3", "sQ", "sK", "sL",
                   "sAG1", "sAG2", "sST", "sO0", "sO1", "sC"):
            CH[nm] = [en(nc.semaphore("ch_" + nm)), 0]
        PROG = defaultdict(list)
        WAITED = defaultdict(dict)

        def op(eng, waits, build, ch=None, k=1):
            w2 = []
            for w in (waits or []):
                if w is None:
                    continue
                chn, v = w
                if v > WAITED[eng].get(chn, 0):
                    WAITED[eng][chn] = v
                    w2.append((chn, v))
            evout = None
            if ch is not None:
                c = CH[ch]
                c[1] += k
                evout = (ch, c[1])

            def thunk(engine, w2=w2, build=build, ch=ch, k=k):
                for chn, v in w2:
                    engine.wait_ge(CH[chn][0], v)
                ins = build(engine)
                if ch is not None and ins is not None:
                    ins.then_inc(CH[ch][0], k)
            PROG[eng].append(thunk)
            return evout

        QCH = {"sync": "qd", "gpsimd": "qg", "scalar": "qs"}

        def lowp(build):
            def b2(e, build=build):
                with nc.allow_low_precision(reason="bf16 operand rounding is intended"):
                    return build(e)
            return b2

        def dma(eng, waits, out, in_, stream=None):
            ch = stream or QCH[eng]
            pre = CH[ch][1]
            return op(eng, (waits or []) + ([(ch, pre)] if pre else []),
                      lambda e, o=out, i=in_: e.dma_start(out=o, in_=i),
                      ch=ch, k=16)

        def g16(off, parts, width):
            return G16[0:parts, off:off + width]

        def g16p(p0, p1, off, width):
            return G16[p0:p1, off:off + width]

        def g32(off, parts, width):
            return G32[0:parts, off:off + width]

        RS = 0
        RX = 19592
        RQ = 52360
        RM = 76936
        t_bcrow = g32(2280, 1, 1024)
        t_bdf = g32(1024, 1, RK)

        evc = {}
        evc["ones"] = dma("gpsimd", [], c_ones[:, :], d_ones[:, :], stream="gC")
        evc["lnones"] = dma("gpsimd", [], c_lnones[:, :], d_lnones[:, :], stream="gC")
        evc["idb"] = dma("gpsimd", [], c_idb[:, :], d_ident[:, :], stream="gC")
        evc["idf"] = dma("sync", [], c_idf[:, :], d_ident[:, :], stream="sC")
        evc["projb"] = dma("gpsimd", [], c_projb[:, :], d_projT[:, :], stream="gC")
        evc["selq"] = dma("gpsimd", [], c_selq[:, :],
                          bass.AP(d_selq, 0, [[H, 128], [128 * H, KC], [1, H]]), stream="gC")
        evc["selk"] = dma("gpsimd", [], c_selk[:, :],
                          bass.AP(d_selk, 0, [[HK, 128], [128 * HK, 4], [1, HK]]), stream="gC")
        evc["bo"] = dma("sync", [], c_bo[:, :], d_bo[:, :], stream="sC")
        evc["bb"] = dma("sync", [], c_bb[:, :], d_bb[:, :], stream="sC")
        evc["smT"] = dma("sync", [], c_smT[:, :], d_smT[:, :], stream="sC")
        evc["muT"] = dma("sync", [], c_muT[:, :], d_muT[:, :], stream="sC")
        evc["iota"] = dma("sync", [], c_iota[:, :], d_iota[:, :], stream="sC")

        wq_t = [Wa[:, j * 1024:(j + 1) * 1024] for j in range(KC)]
        wk_t = [Wb[:, j * 512:(j + 1) * 512] for j in range(KC)]
        wv_t = [Wb[:, 4096 + j * 512:4096 + (j + 1) * 512] for j in range(KC)]
        ev_wq = [dma("gpsimd", [], wq_t[j], d_wq[j * 128:(j + 1) * 128, :], stream="gW") for j in range(KC)]
        ev_wk = [dma("gpsimd", [], wk_t[j], d_wk[j * 128:(j + 1) * 128, :], stream="gW2") for j in range(KC)]
        ev_wv = [dma("gpsimd", [], wv_t[j], d_wv[j * 128:(j + 1) * 128, :], stream="gW3") for j in range(KC)]

        ZB = 16384
        ZC = 32768
        ZD = 49152
        xb_s = [g16(s * 8192, 128, 8192) for s in range(2)]
        xsq_s = [g16(ZD + s * 1024, 128, 1024) for s in range(2)]

        def xln_ap(j, c0, w):
            if c0 < 2048:
                return g16(ZC + j * 2048 + c0, 128, w)
            return g16(ZB + j * 2048 + (c0 - 2048), 128, w)

        m_b2 = [g16(RM + s_ * 1024, 128, 1024) for s_ in range(2)]
        msq_b2 = [g16(RM + 2048 + s_ * 1024, 128, 1024) for s_ in range(2)]
        r_rbp2 = [g16p(0, 1, RM + 2048 + s_ * 1024, 1024) for s_ in range(2)]
        tmp1_2 = [g32(s_ * 1024, 128, 1024) for s_ in range(2)]

        rT32p = g32(3296, 128, 16)
        r_rbp = g16p(0, 1, RM + 1024, 1024)
        ev_rT_p1 = None
        ev_stgf = [[], []]
        mr_free = [None, None]
        ev_xln = [[None] * 4 for _ in range(KC)]
        slot_free = [None, None]
        xsq_free = [None, None]
        mr_free = [None, None]
        ev_mb_last = None
        ev_t1_last = None
        for blk in range(4):
            s = blk % 2
            tb = blk * 1024
            stg0 = 8192 if blk % 2 == 0 else (RQ + 8192) // 2
            evxr = [dma("sync", list(ev_stgf[blk % 2]),
                        G16f[0:128, stg0 + j * 1024:stg0 + (j + 1) * 1024],
                        d_xT[j * 128:(j + 1) * 128, tb:tb + 1024],
                        stream=["sQ", "sK", "sO2", "sO3"][j % 4]) for j in range(KC)]
            evx = []
            for j in range(KC):
                src = G16f[0:128, stg0 + j * 1024:stg0 + (j + 1) * 1024]
                dst = xb_s[s][:, j * 1024:(j + 1) * 1024]
                if j % 2 == 0:
                    evx.append(op("scalar", [evxr[j], slot_free[s]],
                                  lambda e, src=src, dst=dst: e.activation(
                                      dst, src, AF.Copy), ch="S"))
                else:
                    evx.append(op("vector", [evxr[j], slot_free[s]],
                                  lambda e, src=src, dst=dst: e.tensor_copy(dst, src), ch="V"))
            ev_stgf[blk % 2] = [evx[-2], evx[-1]]
            pm = pA[:, s * 1024:(s + 1) * 1024]
            pq = pB[:, s * 1024:(s + 1) * 1024]
            m_b = m_b2[s]
            msq_b = msq_b2[s]
            r_rbp = r_rbp2[s]
            tmp1 = tmp1_2[s]
            ev_mm = None
            for j in range(KC):
                for g2 in range(2):
                    ev_mm = op("tensor", [evx[j], evc["lnones"]],
                               lambda e, pm=pm, s=s, j=j, g2=g2: e.matmul(
                                   pm[:, g2 * 512:(g2 + 1) * 512],
                                   c_lnones[:, :],
                                   xb_s[s][:, j * 1024 + g2 * 512:j * 1024 + (g2 + 1) * 512],
                                   start=(j == 0), stop=(j == KC - 1)), ch="T")
            ev_mb = op("vector", [ev_mm, mr_free[s]], lambda e, pm=pm, m_b=m_b: e.tensor_copy(m_b[:, :], pm), ch="V")
            ev_mb_last = ev_mb
            ev_smm = None
            for j in range(KC):
                sj = j % 2
                ev_sq = op("scalar", [evx[j], xsq_free[sj]],
                           lambda e, s=s, j=j, sj=sj: e.activation(
                               xsq_s[sj][:, :], xb_s[s][:, j * 1024:(j + 1) * 1024],
                               AF.Square), ch="S")
                for g2 in range(2):
                    ev_smm = op("tensor", [ev_sq, ev_mm if j == 0 else None],
                                lambda e, pq=pq, j=j, sj=sj, g2=g2: e.matmul(
                                    pq[:, g2 * 512:(g2 + 1) * 512],
                                    c_lnones[:, :],
                                    xsq_s[sj][:, g2 * 512:(g2 + 1) * 512],
                                    start=(j == 0), stop=(j == KC - 1)), ch="T")
                xsq_free[sj] = ev_smm
            ev_t1 = op("vector", [ev_smm], lambda e, pq=pq, tmp1=tmp1: e.tensor_copy(tmp1, pq), ch="V")
            ev_t1_last = ev_t1
            ev_m2 = op("scalar", [ev_mb, mr_free[s]], lambda e, msq_b=msq_b, m_b=m_b: e.activation(msq_b[:, :], m_b[:, :], AF.Square), ch="S")
            ev_var = op("vector", [ev_t1, ev_m2], lambda e, tmp1=tmp1, msq_b=msq_b: e.tensor_sub(tmp1, tmp1, msq_b[:, :]), ch="V")
            ev_sd = op("scalar", [ev_var], lambda e, tmp1=tmp1: e.activation(tmp1, tmp1, AF.Sqrt, bias=np.float32(1e-5)), ch="S")
            ev_vT = [op("tensor", [ev_sd, evc["idf"], ev_mb],
                        lambda e, cnk=cnk, s=s: e.transpose(
                            pA[0:128, s * 1024 + cnk:s * 1024 + cnk + 1],
                            G32[0:1, s * 1024 + cnk * 128:s * 1024 + (cnk + 1) * 128],
                            c_idf[0:1, 0:1]), ch="T") for cnk in range(8)]
            ev_rc = op("vector", ev_vT, lowp(lambda e, s=s: e.reciprocal(
                rT32p[0:128, s * 8:s * 8 + 8], pA[0:128, s * 1024:s * 1024 + 8])), ch="V")
            ev_rT = [op("tensor", [ev_rc, evc["idf"]],
                        lambda e, cnk=cnk, s=s: e.transpose(
                            pA[0:1, s * 1024 + cnk * 128:s * 1024 + (cnk + 1) * 128],
                            rT32p[0:128, s * 8 + cnk:s * 8 + cnk + 1],
                            c_idf[:, :]), ch="T") for cnk in range(8)]
            ev_re = op("vector", ev_rT + [ev_var], lambda e, s=s, r_rbp=r_rbp: e.tensor_copy(
                r_rbp[0:1, :], pA[0:1, s * 1024:s * 1024 + 1024]), ch="V")
            ev_bcr = [op("tensor", [ev_re, evc["ones"]],
                         lambda e, g=g, s=s, r_rbp=r_rbp: e.matmul(
                             pA[0:128, s * 1024 + g * 512:s * 1024 + (g + 1) * 512],
                             c_ones[0:1, 0:128],
                             r_rbp[0:1, g * 512:(g + 1) * 512],
                             start=True, stop=True), ch="T") for g in range(2)]
            ev_r = ev_bcr[-1]
            last_x = None
            for j in range(KC):
                sj = j % 2
                ev_t = op("vector", [ev_mb, evx[j], xsq_free[sj]],
                          lambda e, s=s, j=j, sj=sj, m_b=m_b: e.tensor_sub(
                              xsq_s[sj][:, :], xb_s[s][:, j * 1024:(j + 1) * 1024], m_b[:, :]),
                          ch="V")
                ev_x = op("vector", [ev_t, ev_r],
                          lambda e, j=j, tb=tb, sj=sj, s=s: e.tensor_mul(
                              xln_ap(j, tb, 1024), xsq_s[sj][:, :],
                              pA[:, s * 1024:(s + 1) * 1024]), ch="V")
                ev_xln[j][blk] = ev_x
                xsq_free[sj] = ev_x
                last_x = ev_x
            slot_free[s] = last_x
            mr_free[s] = last_x

        kpark = [g16(och * 4096, 128, 4096) for och in range(4)]
        ev_kpark = [None] * 4
        kh_free = {0: None, 1: None}
        for och in range(4):
            ev_ev = None
            for half in range(2):
                ph = pA if half == 0 else pB
                ev_mm = None
                for g in range(4):
                    for j in range(KC):
                        w = [ev_wk[j], ev_xln[j][(half * 2048 + g * 512) // 1024]]
                        if g == 0 and j == 0:
                            w.append(kh_free[half])
                        if och == 0 and g == 0 and j == 0:
                            w += [mr_free[0], mr_free[1], slot_free[0], slot_free[1], ev_t1_last]
                        ev_mm = op("tensor", w,
                                   lambda e, ph=ph, half=half, g=g, j=j, och=och: e.matmul(
                                       ph[:, g * 512:(g + 1) * 512],
                                       wk_t[j][:, och * 128:(och + 1) * 128],
                                       xln_ap(j, half * 2048 + g * 512, 512),
                                       start=(j == 0), stop=(j == KC - 1)), ch="T")
                ev_ev = op("scalar", [ev_mm, slot_free[och // 2] if half == 0 else None],
                           lambda e, ph=ph, och=och, half=half: e.activation(
                               kpark[och][:, half * 2048:(half + 1) * 2048], ph[:, :],
                               AF.Copy, scale=np.float32(DN)), ch="S")
                kh_free[half] = ev_ev
            ev_kpark[och] = ev_ev

        def v_t(cnk):
            return g16(RQ + 8192 + cnk * 512, 128, 512)

        ev_v = [None] * 32
        for blk in range(8):
            ph = pA if blk % 2 == 0 else pB
            evl = []
            for ci in range(4):
                cnk = blk * 4 + ci
                ev_mm = None
                for j in range(KC):
                    w = [ev_wv[j]]
                    if ci == 0 and j == 0:
                        if blk == 0:
                            w += [kh_free[0], kh_free[1]]
                        elif blk == 1:
                            w += [ev_v[3]]
                        else:
                            w += [ev_v[blk * 4 - 5]]
                    ev_mm = op("tensor", w,
                               lambda e, ph=ph, ci=ci, cnk=cnk, j=j: e.matmul(
                                   ph[:, ci * 512:(ci + 1) * 512],
                                   xln_ap(j, cnk * 128, 128),
                                   wv_t[j][:, :],
                                   start=(j == 0), stop=(j == KC - 1)), ch="T")
                evl.append(ev_mm)
            for ci in range(4):
                cnk = blk * 4 + ci
                ev_v[cnk] = op("scalar", [evl[ci]],
                               lambda e, ph=ph, ci=ci, cnk=cnk: e.activation(
                                   v_t(cnk)[:, :], ph[:, ci * 512:(ci + 1) * 512],
                                   AF.Copy), ch="S")

        qpark = [g16(ZB + pj * 2048, 128, 2048) for pj in range(KC)]
        ev_qpark = [None] * KC
        ph_free = {0: None, 1: None}
        ev_qmm_last = None
        for och in range(KC):
            ph = pA if och % 2 == 0 else pB
            ev_mm = None
            for g in range(4):
                for j in range(KC):
                    w = [ev_wq[j]]
                    if g == 0 and j == 0:
                        w.append(ph_free[och % 2])
                    if och == 0 and g == 0 and j == 0:
                        w += [ev_v[27], ev_v[31]]
                    if och == 1 and g == 0 and j == 0:
                        w += [ev_v[31]]
                    ev_mm = op("tensor", w,
                               lambda e, ph=ph, g=g, j=j, och=och: e.matmul(
                                   ph[:, g * 512:(g + 1) * 512],
                                   wq_t[j][:, och * 128:(och + 1) * 128],
                                   xln_ap(j, g * 512, 512),
                                   start=(j == 0), stop=(j == KC - 1)), ch="T")
            ev_qc = op("scalar", [ev_mm],
                       lambda e, ph=ph, och=och: e.activation(
                           qpark[och][:, :], ph[:, :], AF.Copy, scale=np.float32(DN)), ch="S")
            ev_qpark[och] = ev_qc
            ph_free[och % 2] = ev_qc
            ev_qmm_last = ev_mm

        dsc = [g16(RM + s_ * 2048, 128, 2048) for s_ in range(2)]
        ev_dq = None
        ev_dsq = [None, None]
        for pj in range(KC):
            sj = pj % 2
            w = [ev_qpark[pj], ev_dsq[sj]]
            if pj == 0:
                w += [slot_free[0], slot_free[1], mr_free[0], mr_free[1]]
            ev_sq2 = op("vector", w,
                        lambda e, pj=pj, sj=sj: e.tensor_mul(
                            dsc[sj][:, :], qpark[pj][:, :], qpark[pj][:, :]), ch="V")
            for g in range(4):
                ev_dq = op("tensor", [ev_sq2, evc["selq"]] +
                           ([ph_free[0], ph_free[1]] if (pj == 0 and g == 0) else []),
                           lambda e, pj=pj, g=g, sj=sj: e.matmul(
                               pA[0:16, g * 512:(g + 1) * 512],
                               c_selq[:, pj * 16:(pj + 1) * 16],
                               dsc[sj][:, g * 512:(g + 1) * 512],
                               start=(pj == 0), stop=(pj == KC - 1)), ch="T")
            ev_dsq[sj] = ev_dq
        dqrows = g32(1024, 16, 2048)
        ev_dqr = op("vector", [ev_dq], lambda e: e.tensor_copy(dqrows, pA[0:16, 0:2048]), ch="V")
        diagTq = g32(0, 128, 256)
        ev_tq_e = None
        for cnk in range(16):
            evt = op("tensor", [ev_dqr, evc["idf"], ev_tq_e],
                     lambda e, cnk=cnk: e.transpose(
                         pB[0:128, 1536:1552],
                         dqrows[0:16, cnk * 128:(cnk + 1) * 128],
                         c_idf[0:16, 0:16]), ch="T")
            ev_tq_e = op("vector", [evt], lambda e, cnk=cnk: e.tensor_copy(
                diagTq[:, cnk * 16:(cnk + 1) * 16], pB[0:128, 1536:1552]), ch="V")
        expdqe = g32(256, 128, 256)
        ev_expdq = op("scalar", [ev_tq_e], lambda e: e.activation(
            expdqe, diagTq, AF.Exp, bias=np.float32(LNEPS)), ch="S")

        ev_dk = None
        ev_dks = [None, None]
        for j in range(4):
            for half in range(2):
                sj = half
                w = [ev_kpark[j], ev_dks[sj]]
                if j == 0 and half == 0:
                    w.append(ev_dq)
                ev_sqk = op("vector", w,
                            lambda e, j=j, half=half, sj=sj: e.tensor_mul(
                                dsc[sj][:, :], kpark[j][:, half * 2048:(half + 1) * 2048],
                                kpark[j][:, half * 2048:(half + 1) * 2048]), ch="V")
                for gi in range(4):
                    g = half * 4 + gi
                    w2 = [ev_sqk, evc["selk"]]
                    if j == 0 and g == 0:
                        w2.append(ev_dqr)
                    if j == 0 and g == 4:
                        w2.append(ev_tq_e)
                    ev_dk = op("tensor", w2,
                               lambda e, j=j, g=g, gi=gi, sj=sj: e.matmul(
                                   (pA if g < 4 else pB)[0:8, (g % 4) * 512:(g % 4) * 512 + 512],
                                   c_selk[:, j * 8:(j + 1) * 8],
                                   dsc[sj][:, gi * 512:(gi + 1) * 512],
                                   start=(j == 0), stop=(j == 3)), ch="T")
                ev_dks[sj] = ev_dk
        dkrows = g32(1024, 8, 2048)
        diagTk = g32(512, 128, 256)
        ev_tk_e = None
        for half in range(2):
            ev_dkr = op("vector", [ev_dk] + ([ev_dqr] if half == 0 else [ev_tk_e]),
                        lambda e, half=half: e.tensor_copy(
                            dkrows, (pA if half == 0 else pB)[0:8, 0:2048]), ch="V")
            for c2 in range(16):
                cnk = half * 16 + c2
                evt = op("tensor", [ev_dkr, evc["idf"], ev_tk_e],
                         lambda e, c2=c2: e.transpose(
                             pA[0:128, 1536:1544],
                             dkrows[0:8, c2 * 128:(c2 + 1) * 128],
                             c_idf[0:8, 0:8]), ch="T")
                ev_tk_e = op("vector", [evt], lambda e, cnk=cnk: e.tensor_copy(
                    diagTk[:, cnk * 8:(cnk + 1) * 8], pA[0:128, 1536:1544]), ch="V")
        expndk = g32(768, 128, 256)
        ev_expdk = op("scalar", [ev_tk_e], lambda e: e.activation(
            expndk, diagTk, AF.Exp, scale=np.float32(-1.0)), ch="S")

        wo_t = [Wa[:, j * 1024:(j + 1) * 1024] for j in range(KC)]
        ev_wo = [dma("gpsimd", [ev_qmm_last], wo_t[j], d_wo[j * 128:(j + 1) * 128, :], stream="gW2")
                 for j in range(KC)]

        etok_s = [g16(ZC + s_ * 8512, 128, 8512) for s_ in range(2)]
        vts = g16(ZD + 640, 128, 2080)
        ctx_sbb = g16p(0, 65, RM, 2128)
        maxr2 = [g32(3152, 128, 34), g32(3186, 128, 34)]
        recB = g32(3220, 128, 1)
        epscol = g32(3222, 65, 1)
        skT = g32(3224, 128, 32)
        ev_ctx_ev = []
        ev_ctx_last = None
        ev_skT_free = [None]
        batches = [list(range(b, min(b + 3, 32))) for b in range(0, 32, 3)]
        ev_scan_of = {}
        etok_free = [None, None]
        ev_exp_of = {}

        ev_tr_of = {}
        ev_mx_of = {}
        ev_rbe_of = {}
        ev_ep_of = {}
        ev_ce_of = {}

        def ks_dash(hh):
            sl = hh % 2
            kch = hh // 2
            kb = (hh % 2) * 64
            ev_exp_all = []
            for bi, bat in enumerate(batches):
                ph = pA if bi % 2 == 0 else pB
                evm = []
                for ii, cnk in enumerate(bat):
                    w = [ev_kpark[kch], evc["projb"]]
                    if bi >= 2:
                        w.append(ev_exp_all[bi - 2])
                    if hh == 0 and bi <= 1:
                        w.append(ev_v[31])
                    if hh >= 1 and bi <= 1 and ii == 0:
                        w += ev_exp_of[hh - 1][-2:]
                    evm.append(op("tensor", w,
                                  lambda e, ph=ph, ii=ii, cnk=cnk, kch=kch, kb=kb: e.matmul(
                                      ph[0:128, ii * 512:ii * 512 + M],
                                      kpark[kch][kb:kb + 64, cnk * 128:(cnk + 1) * 128],
                                      c_projb[kb:kb + 64, :],
                                      start=True, stop=True), ch="T"))
                nb = len(bat)
                ev_exp = op("scalar", evm + [etok_free[sl] if bi == 0 else None],
                            lambda e, ph=ph, nb=nb, c0=bat[0], sl=sl: e.activation(
                                etok_s[sl][:, c0 * 266:(c0 + nb) * 266].rearrange(
                                    "p (b m) -> p b m", m=266),
                                bass.AP(ph, 0, [[2048, 128], [512, nb], [1, M]]),
                                AF.Exp), ch="S")
                ev_exp_all.append(ev_exp)
            ev_exp_of[hh] = ev_exp_all

        def ks_scan(hh):
            mx = maxr2[hh % 2]
            w = [ev_exp_of[hh][-1]]
            if hh >= 2:
                w.append(ev_tr_of[hh - 2])
            ev_red = op("vector", w,
                        lambda e, sl=hh % 2, mx=mx: e.tensor_reduce(
                            mx[:, 0:32],
                            etok_s[sl][:, :].rearrange("p (c m) -> p c m", m=266),
                            AX.X, OP.max), ch="V")
            ev_red2 = op("vector", [ev_red], lambda e, mx=mx: e.tensor_reduce(
                mx[:, 32:33], mx[:, 0:32], AX.X, OP.max), ch="V")
            ev_scan_of[hh] = ev_red2

        def ks_tailA(hh):
            mx = maxr2[hh % 2]
            w = [ev_scan_of[hh], evc["idf"]]
            if hh >= 2:
                w.append(ev_mx_of[hh - 2])
            ev_tr = op("tensor", w,
                       lambda e, mx=mx: e.transpose(pB[0:1, 1536:1664], mx[:, 32:33],
                                                    c_idf[:, :]), ch="T")
            ev_tr_of[hh] = ev_tr
            ev_mx = op("vector", [ev_tr], lambda e: e.tensor_reduce(
                t_sc[0:1, 0:1], pB[0:1, 1536:1664], AX.X, OP.max), ch="V")
            ev_mx_of[hh] = ev_mx
            ev_rc = op("vector", [ev_mx], lowp(lambda e: e.reciprocal(
                t_scb[0:1, 0:1], t_sc[0:1, 0:1])), ch="V")
            w = [ev_rc, evc["ones"]]
            if hh >= 2:
                w.append(ev_rbe_of[hh - 2])
            ev_rb = op("tensor", w,
                       lambda e: e.matmul(pB[0:128, 1700:1701],
                                          c_ones[0:1, 0:128], t_scb[0:1, 0:1],
                                          start=True, stop=True), ch="T")
            ev_rbe = op("vector", [ev_rb], lambda e: e.tensor_copy(recB, pB[0:128, 1700:1701]), ch="V")
            ev_rbe_of[hh] = ev_rbe
            ev_sk = op("vector", [ev_rbe, ev_expdk, ev_skT_free[0]],
                       lambda e, hh=hh: e.tensor_scalar_mul(
                           skT, G32[0:128, 768 + hh:1024:8], recB[:, 0:1]), ch="V")
            return dict(ev_sk=ev_sk, ev_rb=ev_rb)

        def ks_tailB(hh, stA):
            sl = hh % 2
            ev_vt1 = op("vector", [stA["ev_sk"]],
                        lambda e, hh=hh: e.tensor_tensor(
                            vts[:, :].rearrange("p (c w) -> p c w", w=65)[:, :, 0:64],
                            bass.AP(G16, RQ + 8192 + hh * 64, [[81032, 128], [512, 32], [1, 64]]),
                            skT.rearrange("p (c o) -> p c o", o=1).broadcast_to((128, 32, 64)),
                            OP.mult), ch="V")
            ev_vt2 = op("vector", [stA["ev_sk"]], lambda e: e.tensor_copy(
                vts[:, :].rearrange("p (c w) -> p c w", w=65)[:, :, 64:65],
                skT.rearrange("p (c o) -> p c o", o=1)), ch="V")
            ev_skT_free[0] = ev_vt2
            ev_cs = None
            for cnk in range(32):
                w = [ev_v[cnk], evc["ones"]]
                if cnk == 0:
                    w.append(stA["ev_rb"])
                    if hh >= 2:
                        w.append(ev_ep_of[hh - 2])
                ev_cs = op("tensor", w,
                           lambda e, cnk=cnk, hh=hh: e.matmul(
                               pB[0:64, 1792:1793],
                               v_t(cnk)[:, hh * 64:hh * 64 + 64],
                               c_ones[:, 0:1],
                               start=(cnk == 0), stop=(cnk == 31)), ch="T")
            ev_ep1 = op("vector", [ev_cs], lambda e: e.tensor_scalar_mul(
                epscol[0:64, 0:1], pB[0:64, 1792:1793], EPS), ch="V")
            ev_ep2 = op("vector", [ev_ep1], lambda e: e.memset(
                epscol[64:65, 0:1], EPS * float(NF)), ch="V")
            ev_ep_of[hh] = ev_ep1
            return dict(ev_vt1=ev_vt1, ev_vt2=ev_vt2, ev_ep2=ev_ep2, **stA)

        def ks_tailC(hh, stB):
            sl = hh % 2
            ev_exp_all = ev_exp_of[hh]
            ev_ctx = None
            for cnk in range(32):
                w = [stB["ev_vt1"], stB["ev_vt2"],
                     ev_exp_all[min(cnk // 3, len(ev_exp_all) - 1)]]
                if cnk == 0 and ev_ctx_ev:
                    w.append(ev_ctx_ev[-1])
                ev_ctx = op("tensor", w,
                            lambda e, cnk=cnk, sl=sl: e.matmul(
                                pA[0:65, 1536:1536 + M],
                                vts[:, cnk * 65:(cnk + 1) * 65],
                                etok_s[sl][:, cnk * 266:(cnk + 1) * 266],
                                start=(cnk == 0), stop=(cnk == 31)), ch="T")
            etok_free[sl] = ev_ctx
            ev_ce = op("vector", [ev_ctx, stB["ev_ep2"]],
                       lambda e, hh=hh: e.tensor_scalar(
                           ctx_sbb[:, hh * M:(hh + 1) * M], pA[0:65, 1536:1536 + M],
                           epscol[0:65, 0:1], None, OP.add), ch="V")
            ev_ctx_ev.append(ev_ce)
            return ev_ctx

        ev_ctx_last = None
        stA = {}
        stB = {}
        for hh in range(HK):
            if hh >= 1:
                stA[hh - 1] = ks_tailA(hh - 1)
            ks_dash(hh)
            if hh >= 1:
                stB[hh - 1] = ks_tailB(hh - 1, stA[hh - 1])
            ks_scan(hh)
            if hh >= 1:
                ev_ctx_last = ks_tailC(hh - 1, stB[hh - 1])
        stA[HK - 1] = ks_tailA(HK - 1)
        stB[HK - 1] = ks_tailB(HK - 1, stA[HK - 1])
        ev_ctx_last = ks_tailC(HK - 1, stB[HK - 1])

        eq0 = [g16(s_ * 6144, 128, 2048) for s_ in range(2)]
        eq1 = [g16(s_ * 6144 + 2048, 128, 2048) for s_ in range(2)]
        eq2 = [g16p(0, 16, s_ * 6144 + 4096, 2048) for s_ in range(2)]
        stage = g16(ZD + 640, 128, 2048)
        merged = g16(ZC, 128, 16384)
        u_row = [g16p(0, 1, 12288 + s_ * 2048, 2048) for s_ in range(2)]
        u_tr = g32(3168, 16, 128)
        maxEq_p = [g32(3296, 128, 16), g32(1024, 128, 16)]
        u_cols_p = [g32(3312, 128, 16), g32(1040, 128, 16)]
        rbT_b = g16p(0, 128, RQ + 14336, 16)
        scanb = g16(RQ + 8192, 128, 2048)
        num_sb = [g16p(0, 65, RQ + 10240 + s * 2048, 2048) for s in range(2)]

        ev_eq_free = [None, None]
        ev_u_free = [None, None]
        ev_stage_free = None
        ev_scan_prev = None
        ev_uc_pp = [None, None]
        ev_ut_pp = [None, None]
        ev_ur_prev = None
        ev_rT_prev = None
        ev_bc_prev = None
        ev_num_free = [None, None]
        ev_par_free = [ev_ctx_ev[-1], ev_ctx_ev[-1]]
        ev_merged = []
        st_front = {}

        def qs_dash(h):
            nonlocal ev_scan_prev
            pj = h // 2
            sl = pj % 2
            es = h % 2
            qb = es * 64
            P = pA if es == 0 else pB
            Pb = pAb if es == 0 else pBb

            def dmm(m0, mw, g, w):
                return op("tensor", w,
                          lambda e, g=g, m0=m0, mw=mw, qb=qb, pj=pj, P=P: e.matmul(
                              P[0:mw, g * 512:(g + 1) * 512],
                              c_projb[qb:qb + 64, m0:m0 + mw],
                              qpark[pj][qb:qb + 64, g * 512:(g + 1) * 512],
                              start=True, stop=True), ch="T")
            w0 = [ev_qpark[pj], evc["projb"], ev_par_free[es]]
            ev_c2a = [dmm(256, 10, g, w0) for g in (0, 1)]
            ev_c2b = [dmm(256, 10, g, w0) for g in (2, 3)]
            ev_e2a = op("scalar", ev_c2a + [ev_eq_free[es]],
                        lambda e, es=es, P=P: e.activation(
                            eq2[es][0:10, 0:1024], P[0:10, 0:1024], AF.Exp), ch="S")
            ev_e2b = op("scalar", ev_c2b + [ev_eq_free[es]],
                        lambda e, es=es, P=P: e.activation(
                            eq2[es][0:10, 1024:2048], P[0:10, 1024:2048], AF.Exp), ch="S")
            ev_c0a = [dmm(0, 128, g, [ev_e2a]) for g in (0, 1)]
            ev_c0b = [dmm(0, 128, g, [ev_e2b]) for g in (2, 3)]
            ev_e0a = op("scalar", ev_c0a + [ev_eq_free[es]],
                        lambda e, es=es, P=P: e.activation(
                            eq0[es][:, 0:1024], P[:, 0:1024], AF.Exp), ch="S")
            ev_e0b = op("scalar", ev_c0b + [ev_eq_free[es]],
                        lambda e, es=es, P=P: e.activation(
                            eq0[es][:, 1024:2048], P[:, 1024:2048], AF.Exp), ch="S")
            st = dict(es=es, sl=sl, pj=pj, qb=qb, P=P, Pb=Pb, dmm=dmm,
                      ev_e2a=ev_e2a, ev_e2b=ev_e2b, ev_e0a=ev_e0a, ev_e0b=ev_e0b)
            return st

        def qs_dash2(st, h):
            es = st["es"]; P = st["P"]; dmm = st["dmm"]
            ev_c1a = [dmm(128, 128, g, [st["ev_e0a"]]) for g in (0, 1)]
            ev_c1b = [dmm(128, 128, g, [st["ev_e0b"]]) for g in (2, 3)]
            ev_e1a = op("scalar", ev_c1a + [ev_eq_free[es]],
                        lambda e, es=es, P=P: e.activation(
                            eq1[es][:, 0:1024], P[:, 0:1024], AF.Exp), ch="S")
            ev_e1b = op("scalar", ev_c1b + [ev_eq_free[es]],
                        lambda e, es=es, P=P: e.activation(
                            eq1[es][:, 1024:2048], P[:, 1024:2048], AF.Exp), ch="S")
            st["ev_e1a"] = ev_e1a
            st["ev_e1b"] = ev_e1b
            st["ev_e0"] = st["ev_e0b"]
            st["ev_e1"] = ev_e1b
            st["ev_e2"] = st["ev_e2b"]
            return st

        def qs_scan(st, h):
            nonlocal ev_scan_prev
            es = st["es"]; P = st["P"]; Pb = st["Pb"]
            ev_cm = []
            for half, (ee0, ee1, ee2) in enumerate(
                    [(st["ev_e0a"], st["ev_e1a"], st["ev_e2a"]),
                     (st["ev_e0b"], st["ev_e1b"], st["ev_e2b"])]):
                c0 = half * 1024
                w_cm = [ee0, ee1]
                if ev_scan_prev is not None:
                    w_cm.append(ev_scan_prev)
                ev_cm1 = op("vector", w_cm, lambda e, es=es, c0=c0: e.tensor_max(
                    scanb[:, c0:c0 + 1024], eq0[es][:, c0:c0 + 1024],
                    eq1[es][:, c0:c0 + 1024]), ch="V")
                ev_cm2 = op("vector", [ev_cm1, ee2], lambda e, es=es, c0=c0: e.tensor_max(
                    scanb[0:10, c0:c0 + 1024], scanb[0:10, c0:c0 + 1024],
                    eq2[es][0:10, c0:c0 + 1024]), ch="V")
                ev_cm.append(ev_cm2)
            ev_sct = []
            ev_scr = []
            for cnk in range(16):
                w = [ev_cm[cnk // 8], evc["idb"], st["ev_e1b"]]
                if cnk == 8:
                    w.append(ev_scr[0])
                ev_sct.append(op("tensor", w,
                                 lambda e, cnk=cnk, Pb=Pb: e.transpose(
                                     Pb[:, (cnk % 8) * 128:(cnk % 8) * 128 + 128],
                                     scanb[:, cnk * 128:(cnk + 1) * 128],
                                     c_idb[:, :]), ch="T"))
                if cnk % 8 == 7:
                    wr = ev_sct[-8:]
                    if cnk == 7:
                        wr = wr + [ev_uc_pp[es]]
                    ev_scr.append(op("vector", wr,
                                     lambda e, cnk=cnk, Pb=Pb, es=es: e.tensor_reduce(
                                         maxEq_p[es][:, cnk - 7:cnk + 1],
                                         Pb[:, 0:1024].rearrange("p (c f) -> p c f", f=128),
                                         AX.X, OP.max), ch="V"))
            ev_scan_prev = ev_sct[-1]
            ev_uc = op("vector", [ev_scr[-1], ev_expdq, ev_ut_pp[es]],
                       lambda e, h=h, es=es: e.tensor_mul(
                           u_cols_p[es], G32[0:128, 256 + h:512:16], maxEq_p[es]), ch="V")
            ev_uc_pp[es] = ev_uc
            st.update(ev_cm1=ev_cm[1], ev_scr=ev_scr, ev_uc=ev_uc)
            return st

        def qs_ur(st, h):
            nonlocal ev_ur_prev
            es = st["es"]
            P = st["P"]
            ev_ut = op("tensor", [st["ev_uc"], evc["idf"], st["ev_e1b"]],
                       lambda e, P=P, es=es: e.transpose(
                           P[0:16, 1024:1152], u_cols_p[es], c_idf[:, :]), ch="T")
            ev_ut_pp[es] = ev_ut
            ev_ue = op("vector", [ev_ut, ev_ur_prev], lambda e, P=P: e.tensor_copy(
                u_tr, P[0:16, 1024:1152]), ch="V")
            st["ev_ue"] = ev_ue
            ev_ur = dma("gpsimd", [ev_ue, ev_u_free[es]],
                        u_row[es][0:1, :].rearrange("o (c f) -> o c f", c=16), u_tr,
                        stream="gE")
            ev_ur_prev = ev_ur
            st["ev_ur"] = ev_ur
            return st

        def qs_am01(st, h):
            es = st["es"]; P = st["P"]
            ev_am = []
            for g in (1, 2, 3, 0):
                hv = "a" if g < 2 else "b"
                for mi in (0, 1):
                    m0, mw = MCH[mi]
                    src = ctxL[0:mw, h * 130 + mi * 65:h * 130 + mi * 65 + 65]
                    esrc = (eq0[es] if mi == 0 else eq1[es])[0:mw, g * 512:(g + 1) * 512]
                    w = [ev_ctxL[h][mi], st["ev_e0" + hv] if mi == 0 else st["ev_e1" + hv],
                         st["ev_e1" + hv]]
                    if h <= 1 and g == 1 and mi == 0:
                        w.append(ev_ctxL[H - 1][2])
                    if mi == 0:
                        if g == 0:
                            w = w + st["ev_scr"]
                        elif g == 2:
                            w.append(st["ev_ue"])
                    ev_am.append(op("tensor", w,
                                    lambda e, g=g, src=src, esrc=esrc, mi=mi, P=P: e.matmul(
                                        P[0:65, g * 512:(g + 1) * 512],
                                        src, esrc,
                                        start=(mi == 0), stop=False), ch="T"))
            st["ev_am01"] = ev_am
            return st

        def qs_am2(st, h):
            es = st["es"]; P = st["P"]
            ev_am = []
            for g in (1, 2, 3, 0):
                ev_am.append(op("tensor", [ev_ctxL[h][2], st["ev_e2"]],
                                lambda e, g=g, h=h, es=es, P=P: e.matmul(
                                    P[0:65, g * 512:(g + 1) * 512],
                                    ctxL2[0:10, h * 65:(h + 1) * 65],
                                    eq2[es][0:10, g * 512:(g + 1) * 512],
                                    start=False, stop=False), ch="T"))
                ev_am.append(op("tensor", [st["ev_ur"], ev_B_done],
                                lambda e, g=g, h=h, es=es, P=P: e.matmul(
                                    P[0:65, g * 512:(g + 1) * 512],
                                    B_rows[0:1, h * 65:(h + 1) * 65],
                                    u_row[es][0:1, g * 512:(g + 1) * 512],
                                    start=False, stop=True), ch="T"))
            ev_am = st["ev_am01"] + ev_am
            ev_u_free[es] = ev_am[-1]
            ev_eq_free[es] = ev_am[-1]
            st["ev_am"] = ev_am
            return st

        def qs_nev(st, h):
            es = st["es"]; P = st["P"]
            st["ev_nev"] = op("scalar", st["ev_am"] + [ev_num_free[es]],
                              lambda e, es=es, P=P: e.activation(
                                  num_sb[es][0:65, :], P[0:65, :], AF.Copy), ch="S")
            return st

        def qs_dT(st, h):
            es = st["es"]; Pb = st["Pb"]
            st["ev_dT"] = [op("tensor", [st["ev_nev"], evc["idb"]],
                              lambda e, cnk=cnk, es=es, Pb=Pb: e.transpose(
                                  Pb[0:128, 2048 + 2 * cnk:2048 + 2 * cnk + 1],
                                  num_sb[es][64:65, cnk * 128:(cnk + 1) * 128],
                                  c_idb[64:65, 64:65]), ch="T") for cnk in range(16)]
            return st

        def qs_rcp(st, h):
            nonlocal ev_rT_prev
            Pb = st["Pb"]
            st["ev_rcp"] = op("vector", st["ev_dT"] + [ev_rT_prev],
                              lowp(lambda e, Pb=Pb: e.reciprocal(
                                  rbT_b[0:128, 0:16], Pb[0:128, 2048:2080:2])), ch="V")
            ev_rT_prev = st["ev_rcp"]
            return st

        def qs_bc(st, h):
            P = st["P"]
            st["ev_bc"] = [op("tensor", [st["ev_rcp"], evc["idb"]],
                              lambda e, cnk=cnk, P=P: e.matmul(
                                  P[0:64, cnk * 128:(cnk + 1) * 128],
                                  bass.AP(G16, RQ + 14336 + cnk, [[81032, 128], [0, 64]]),
                                  c_idb[:, :],
                                  start=True, stop=True), ch="T") for cnk in range(16)]
            return st

        def qs_mul(st, h):
            nonlocal ev_stage_free
            es = st["es"]; P = st["P"]; pj = st["pj"]
            if es == 0:
                dsth = merged[0:64, pj * 2048:(pj + 1) * 2048]
                wex = []
            else:
                dsth = stage[0:64, 0:2048]
                wex = [ev_stage_free]
            ev_m = op("vector", st["ev_bc"] + wex + [st["ev_nev"]],
                      lambda e, dsth=dsth, es=es, P=P: e.tensor_mul(
                          dsth, num_sb[es][0:64, 0:2048], P[0:64, 0:2048]), ch="V")
            ev_merged.append(ev_m)
            ev_par_free[es] = ev_m
            ev_num_free[es] = ev_m
            out_ev = ev_m
            if es == 1:
                evd = dma("sync", [ev_m], merged[64:128, pj * 2048:(pj + 1) * 2048],
                          stage[0:64, 0:2048], stream="sST")
                ev_merged.append(evd)
                ev_stage_free = evd
                out_ev = evd
            return out_ev

        def qs_front_all(h):
            st = qs_dash(h)
            qs_dash2(st, h)
            qs_scan(st, h)
            qs_ur(st, h)
            st_front[h] = st
            return st

        prev_psum = None
        ev_ag1s = dma("sync", ev_ctx_ev, ag1_in[:, :], ctx_sbb, stream="sAG1")
        ev_cc1 = op("gpsimd", [ev_ag1s], lambda e: e.collective_compute(
            "AllGather", OP.bypass, replica_groups=groups,
            ins=[ag1_in.ap()], outs=[ag1_out.ap()]), ch="cc")
        qs_front_all(0)
        qs_front_all(1)
        ctxTb = g16p(0, 65, RQ, 4256)
        ev_ctb = [dma("sync", [ev_cc1, ev_ctx_last], ctxTb[:, g * 2128:(g + 1) * 2128],
                      ag1_out[g * 65:(g + 1) * 65, :], stream="sL") for g in range(2)]

        Bcols = g32(3152, 65, 16)
        ev_B = op("vector", ev_ctb, lambda e: e.tensor_reduce(
            Bcols, ctxTb[:, :].rearrange("p (h m) -> p h m", m=M),
            AX.X, OP.add), ch="V")
        B_rows = g16p(0, 1, RQ + 14352, 1040)
        ev_Brt = []
        ev_Br = []
        for h in range(H):
            w = [ev_B, evc["idf"], ev_ctx_last]
            if h % 4 == 0 and ev_Br:
                w.append(ev_Br[-1])
            evt = op("tensor", w,
                     lambda e, h=h: e.transpose(
                         pB[0:1, 1536 + (h % 4) * 65:1536 + (h % 4) * 65 + 65],
                         Bcols[0:65, h:h + 1], c_idf[0:65, 0:65]), ch="T")
            ev_Brt.append(evt)
            if h % 4 == 3:
                ev_Br.append(op("vector", ev_Brt[-4:],
                                lambda e, h=h: e.tensor_copy(
                                    B_rows[0:1, (h - 3) * 65:(h + 1) * 65],
                                    pB[0:1, 1536:1536 + 260]), ch="V"))
        ev_B_done = ev_Br[-1]

        ctxL = g16(RM, 128, 2080)
        ctxL2 = g16p(0, 16, RM + 2080, 1040)
        ev_ctxL = {}
        prev_t = ev_Br[-1]
        for h in range(H):
            g = h // 8
            hh = h % 8
            evs = []
            for mi, (m0, mw) in enumerate(MCH):
                evt = op("tensor", [ev_ctb[g], evc["idb"], prev_t],
                         lambda e, g=g, hh=hh, m0=m0, mw=mw: e.transpose(
                             pBb[0:mw, 3072:3072 + 65],
                             ctxTb[0:65, g * 2128 + hh * M + m0:g * 2128 + hh * M + m0 + mw],
                             c_idb[0:65, 0:65]), ch="T")
                if mi < 2:
                    dst = ctxL[0:mw, h * 130 + mi * 65:h * 130 + mi * 65 + 65]
                else:
                    dst = ctxL2[0:10, h * 65:(h + 1) * 65]
                eve = op("vector", [evt], lambda e, dst=dst, mw=mw: e.tensor_copy(
                    dst, pBb[0:mw, 3072:3072 + 65]), ch="V")
                evs.append(eve)
                prev_t = eve
            ev_ctxL[h] = evs

        qs_am01(st_front[0], 0)
        qs_am01(st_front[1], 1)
        qs_am2(st_front[0], 0)
        qs_nev(st_front[0], 0)
        qs_dT(st_front[0], 0)
        qs_rcp(st_front[0], 0)
        qs_bc(st_front[0], 0)
        prev_psum = qs_mul(st_front[0], 0)
        for h in range(2, H):
            st = qs_dash(h)
            qs_dash2(st, h)
            qs_am2(st_front[h - 1], h - 1)
            qs_scan(st, h)
            qs_nev(st_front[h - 1], h - 1)
            qs_dT(st_front[h - 1], h - 1)
            qs_ur(st, h)
            qs_rcp(st_front[h - 1], h - 1)
            qs_bc(st_front[h - 1], h - 1)
            qs_am01(st, h)
            prev_psum = qs_mul(st_front[h - 1], h - 1)
            st_front[h] = st
        st = st_front[H - 1]
        qs_am2(st, H - 1)
        qs_nev(st, H - 1)
        qs_dT(st, H - 1)
        qs_rcp(st, H - 1)
        qs_bc(st, H - 1)
        prev_psum = qs_mul(st, H - 1)

        bw_t = [Wb[:, j * 1024:(j + 1) * 1024] for j in range(KC)]
        ev_bw = [dma("gpsimd", [ev_ctx_last], bw_t[j], d_bwT[j * 128:(j + 1) * 128, :], stream="gW3")
                 for j in range(KC)]

        attn = g16(RQ + 8192, 128, 16384)
        x1 = g16(ZB, 128, 16384)
        xf = g16(ZC, 128, 16384)
        ffo = g16(0, 128, 16384)
        h_ext = g16p(0, 65, RQ + 6144, 2048)
        M_ext = g16p(0, 65, ZD, 1024)
        xbh_s = [g16(RQ + s * 2048, 128, 2048) for s in range(2)]
        ln_m = g16(RM, 128, 2048)
        ln_r = g16(RM + 2048, 128, 2048)
        ln_t = g16(ZD + 1024, 128, 2048)
        ftmp = g32(0, 128, 2048)
        m12 = g32(2048, 128, 16)
        m12b = g32(2064, 128, 16)
        ivT = g32(2080, 128, 64)
        muiv = g32(2144, 128, 64)
        mui2 = g32(2208, 128, 64)
        csum = g32(2272, 128, 8)
        sq2 = g16(RQ + 4096, 128, 2048)
        rT32 = g32(3296, 128, 16)
        r_rb = g16p(0, 1, RM + 2048, 2048)
        ivb = g16p(0, 128, ZD + 3072, 64)
        muivb = g16p(0, 128, ZD + 3136, 64)
        csb = g16p(0, 128, ZD + 3200, 8)
        _of = (RQ + 8192) // 2
        outst = [G16f[0:128, _of + s * 2048:_of + (s + 1) * 2048] for s in range(2)]
        t_ffw = G16[0:128, RQ + 16384:RQ + 17536]
        t_ones512 = G16[0:1, RQ + 17536:RQ + 18048]

        mslot = [(pA, 0), (pA, 32), (pA, 64), (pB, 0)]
        ev_attn = [None] * KC
        ev_womm = None
        wo_bfree = {0: [None] * 3, 1: [None] * 3}
        ev_mrow = [None] * 4
        for och in range(KC):
            ph = pA if och % 2 == 0 else pB
            for g in range(4):
                gb = g % 3
                ev_mm = None
                for j in range(KC):
                    w = [ev_wo[j]]
                    if j == 0:
                        w.append(wo_bfree[och % 2][gb])
                    if och == 0 and g == 0 and j == 0:
                        w += ev_merged + [prev_psum]
                    ev_mm = op("tensor", w,
                               lambda e, ph=ph, g=g, gb=gb, j=j, och=och: e.matmul(
                                   ph[:, gb * 512:(gb + 1) * 512],
                                   wo_t[j][:, och * 128:(och + 1) * 128],
                                   merged[:, j * 2048 + g * 512:j * 2048 + (g + 1) * 512],
                                   start=(j == 0), stop=(j == KC - 1)), ch="T")
                ev_ev = op("scalar", [ev_mm, evc["bo"]],
                           lambda e, ph=ph, g=g, gb=gb, och=och: e.activation(
                               attn[:, och * 2048 + g * 512:och * 2048 + (g + 1) * 512],
                               ph[:, gb * 512:(gb + 1) * 512],
                               AF.Identity, bias=c_bo[:, och:och + 1]), ch="S")
                wo_bfree[och % 2][gb] = ev_ev
                w2 = [ev_ev, evc["lnones"]]
                if och == 0:
                    w2.append(prev_psum)
                mp, mb = mslot[g]
                ev_mrow[g] = op("tensor", w2,
                                lambda e, g=g, och=och, mp=mp, mb=mb: e.matmul(
                                    mp[mb:mb + 1, 1536:2048],
                                    c_lnones[:, 0:1],
                                    attn[:, och * 2048 + g * 512:och * 2048 + (g + 1) * 512],
                                    start=(och == 0), stop=(och == KC - 1)), ch="T")
            ev_womm = ev_mrow[3]
            ev_attn[och] = ev_ev
        m_rb = g16p(0, 1, ZD, 2048)
        ev_mre = []
        for g in range(4):
            mp, mb = mslot[g]
            ev_mre.append(op("vector", list(ev_mrow),
                             lambda e, g=g, mp=mp, mb=mb: e.tensor_copy(
                                 m_rb[0:1, g * 512:(g + 1) * 512],
                                 mp[mb:mb + 1, 1536:2048]), ch="V"))
        ev_mbc = [op("tensor", ev_mre + [evc["ones"]],
                     lambda e, g=g: e.matmul(
                         pA[:, g * 512:(g + 1) * 512],
                         c_ones[0:1, 0:128],
                         m_rb[0:1, g * 512:(g + 1) * 512],
                         start=True, stop=True), ch="T") for g in range(4)]
        ev_m2b = ev_mbc[-1]

        def ln_stats(src_fn, ev_src, sq_slot, ev_sq_pre, ev_psA_pre, ev_psB_pre):
            ev_mm = None
            for j in range(KC):
                for g in range(4):
                    w = [ev_src[j], evc["lnones"]]
                    if j == 0 and g == 0:
                        w += ev_psA_pre
                    ev_mm = op("tensor", w,
                               lambda e, j=j, g=g: e.matmul(
                                   pA[:, g * 512:(g + 1) * 512],
                                   c_lnones[:, :], src_fn(j)[:, g * 512:(g + 1) * 512],
                                   start=(j == 0), stop=(j == KC - 1)), ch="T")
            ev_mb = op("vector", [ev_mm], lambda e: e.tensor_copy(ln_m[:, :], pA[:, 0:2048]), ch="V")
            ev_smm = None
            prev = list(ev_sq_pre)
            for j in range(KC):
                ev_sq = op("scalar", [ev_src[j]] + prev,
                           lambda e, j=j: e.activation(sq_slot[:, :], src_fn(j)[:, :], AF.Square),
                           ch="S")
                for g in range(4):
                    w = [ev_sq]
                    if j == 0 and g == 0:
                        w += ev_psB_pre
                    ev_smm = op("tensor", w,
                                lambda e, j=j, g=g: e.matmul(
                                    pB[:, g * 512:(g + 1) * 512],
                                    c_lnones[:, :], sq_slot[:, g * 512:(g + 1) * 512],
                                    start=(j == 0), stop=(j == KC - 1)), ch="T")
                prev = [ev_smm]
            ev_t1 = op("vector", [ev_smm, ev_mb], lambda e: e.tensor_copy(ftmp, pB[:, 0:2048]), ch="V")
            ev_m2 = op("scalar", [ev_mb] + prev, lambda e: e.activation(ln_t[:, :], ln_m[:, :], AF.Square), ch="S")
            ev_var = op("vector", [ev_t1, ev_m2], lambda e: e.tensor_sub(ftmp, ftmp, ln_t[:, :]), ch="V")
            ev_sd = op("scalar", [ev_var], lambda e: e.activation(ftmp, ftmp, AF.Sqrt, bias=np.float32(1e-5)), ch="S")
            ev_vT = [op("tensor", [ev_sd, evc["idf"], ev_t1],
                       lambda e, cnk=cnk: e.transpose(
                           pB[0:128, 1536 + cnk:1536 + cnk + 1],
                           G32[0:1, cnk * 128:(cnk + 1) * 128],
                           c_idf[0:1, 0:1]), ch="T") for cnk in range(16)]
            ev_rc = op("vector", ev_vT, lowp(lambda e: e.reciprocal(
                rT32[0:128, 0:16], pB[0:128, 1536:1552])), ch="V")
            ev_rT = [op("tensor", [ev_rc, evc["idf"]],
                       lambda e, cnk=cnk: e.transpose(
                           pB[0:1, cnk * 128:(cnk + 1) * 128],
                           rT32[0:128, cnk:cnk + 1],
                           c_idf[:, :]), ch="T") for cnk in range(16)]
            ev_re = op("vector", ev_rT, lambda e: e.tensor_copy(r_rb[0:1, :], pB[0:1, 0:2048]), ch="V")
            ev_bcr = [op("tensor", [ev_re, evc["ones"]],
                        lambda e, g=g: e.matmul(
                            pB[0:128, g * 512:(g + 1) * 512],
                            c_ones[0:1, 0:128],
                            r_rb[0:1, g * 512:(g + 1) * 512],
                            start=True, stop=True), ch="T") for g in range(4)]
            ev_r = ev_bcr[-1]
            return ev_mb, ev_r, ev_smm, ev_t1, ev_sd

        xbh_ev = [dma("gpsimd", [ev_stage_free], xbh_s[0][:, :], d_xT[0:128, 0:2048], stream="gE2"),
                  dma("gpsimd", [ev_stage_free], xbh_s[1][:, :], d_xT[128:256, 0:2048], stream="gE3")]
        ev_smm = None
        prev_sq = [ev_ctxL[H - 1][2]]
        for j in range(KC):
            ev_sq = op("scalar", [ev_attn[j]] + prev_sq,
                       lambda e, j=j: e.activation(
                           sq2[:, :], attn[:, j * 2048:(j + 1) * 2048], AF.Square), ch="S")
            for g in range(4):
                w = [ev_sq]
                if j == 0 and g == 0:
                    w += [ev_attn[KC - 1], ev_mrow[3]]
                ev_smm = op("tensor", w,
                            lambda e, j=j, g=g: e.matmul(
                                pB[:, g * 512:(g + 1) * 512],
                                c_lnones[:, :], sq2[:, g * 512:(g + 1) * 512],
                                start=(j == 0), stop=(j == KC - 1)), ch="T")
            prev_sq = [ev_smm]
        ev_ss2 = ev_smm
        ev_t1_2 = op("vector", [ev_smm, ev_mbc[-1]], lambda e: e.tensor_copy(ftmp, pB[:, 0:2048]), ch="V")
        ev_m2s = op("scalar", [ev_mbc[-1]] + prev_sq, lambda e: e.activation(
            ln_t[:, :], pA[:, 0:2048], AF.Square), ch="S")
        ev_var2 = op("vector", [ev_t1_2, ev_m2s], lambda e: e.tensor_sub(ftmp, ftmp, ln_t[:, :]), ch="V")
        ev_sd2 = op("scalar", [ev_var2], lambda e: e.activation(ftmp, ftmp, AF.Sqrt, bias=np.float32(1e-5)), ch="S")
        ev_vT2 = [op("tensor", [ev_sd2, evc["idf"], ev_t1_2],
                     lambda e, cnk=cnk: e.transpose(
                         pB[0:128, 1536 + cnk:1536 + cnk + 1],
                         G32[0:1, cnk * 128:(cnk + 1) * 128],
                         c_idf[0:1, 0:1]), ch="T") for cnk in range(16)]
        ev_rc2 = op("vector", ev_vT2, lowp(lambda e: e.reciprocal(
            rT32[0:128, 0:16], pB[0:128, 1536:1552])), ch="V")
        ev_rT2 = [op("tensor", [ev_rc2, evc["idf"]],
                     lambda e, cnk=cnk: e.transpose(
                         pB[0:1, cnk * 128:(cnk + 1) * 128],
                         rT32[0:128, cnk:cnk + 1],
                         c_idf[:, :]), ch="T") for cnk in range(16)]
        ev_re2 = op("vector", ev_rT2, lambda e: e.tensor_copy(r_rb[0:1, :], pB[0:1, 0:2048]), ch="V")
        ev_r2 = [op("tensor", [ev_re2, evc["ones"]],
                    lambda e, g=g: e.matmul(
                        pB[0:128, g * 512:(g + 1) * 512],
                        c_ones[0:1, 0:128],
                        r_rb[0:1, g * 512:(g + 1) * 512],
                        start=True, stop=True), ch="T") for g in range(4)][-1]
        ev_x1 = [None] * KC
        ev_lnt_prev = ev_r2
        for j in range(KC):
            sj = j % 2
            ev_t = op("vector", [ev_attn[j], ev_m2b, ev_lnt_prev, ev_womm],
                      lambda e, j=j: e.tensor_sub(
                          ln_t[:, :], attn[:, j * 2048:(j + 1) * 2048], pA[:, 0:2048]), ch="V")
            ev_t2 = op("vector", [ev_t, ev_r2],
                       lambda e: e.tensor_mul(ln_t[:, :], ln_t[:, :], pB[:, 0:2048]), ch="V")
            ev_x1[j] = op("vector", [ev_t2, xbh_ev[sj]],
                          lambda e, j=j, sj=sj: e.tensor_add(
                              x1[:, j * 2048:(j + 1) * 2048], ln_t[:, :], xbh_s[sj][:, :]),
                          ch="V")
            ev_lnt_prev = ev_x1[j]
            if j + 2 < KC:
                xbh_ev[sj] = dma("gpsimd", [ev_x1[j]], xbh_s[sj][:, :],
                                 d_xT[(j + 2) * 128:(j + 3) * 128, 0:2048],
                                 stream=["gE2", "gE3"][sj])

        ev_m3b, ev_r3, ev_ss3, ev_t1_3, ev_sd3 = ln_stats(
            lambda j: x1[:, j * 2048:(j + 1) * 2048], ev_x1, sq2,
            [ev_ss2, ev_sd2, ev_x1[-1]], [ev_m2b, ev_t1_2, ev_x1[-1]], [ev_t1_2, ev_x1[-1]])
        ev_xf = [None] * KC
        ev_m2c = [None] * KC
        ev_lnt_prev = ev_r3
        for j in range(KC):
            ev_t = op("vector", [ev_x1[j], ev_m3b, ev_ss3, ev_lnt_prev],
                      lambda e, j=j: e.tensor_sub(
                          ln_t[:, :], x1[:, j * 2048:(j + 1) * 2048], ln_m[:, :]), ch="V")
            ev_xf[j] = op("vector", [ev_t, ev_r3, ev_ctxL[H - 1][2], ev_merged[-1]],
                          lambda e, j=j: e.scalar_tensor_tensor(
                              xf[:, j * 2048:(j + 1) * 2048], ln_t[:, :], 1.0, pB[:, 0:2048],
                              OP.mult, OP.mult,
                              accum_out=m12[:, j:j + 1]), ch="V")
            ev_lnt_prev = ev_xf[j]
            ev_m2c[j] = op("scalar", [ev_xf[j]] + ([ev_ss3, ev_sd3] if j == 0 else [ev_m2c[j - 1]]),
                           lambda e, j=j: e.activation(
                               sq2[:, :], xf[:, j * 2048:(j + 1) * 2048], AF.Square,
                               accum_out=m12[:, 8 + j:8 + j + 1]), ch="S")

        ev_mscl = op("vector", ev_xf + ev_m2c, lambda e: e.tensor_scalar_mul(m12, m12, 1.0 / NF), ch="V")
        mtb = m12b.tensor.bitcast(BF16)
        mtb_ap = lambda c0, c1: bass.AP(mtb, 2 * 2064 + c0, [[2 * 3328, 128], [1, c1 - c0]])
        ev_mcb = op("vector", [ev_mscl], lambda e: e.tensor_copy(mtb_ap(0, 16), m12), ch="V")
        ev_iv0 = op("scalar", [evc["smT"], ev_m2c[-1]], lambda e: e.activation(ivT, c_smT[:, :], AF.Square), ch="S")
        ev_iv = op("vector", [ev_iv0], lambda e: e.reciprocal(ivT, ivT), ch="V")
        ev_mv = op("vector", [ev_iv, evc["muT"]], lambda e: e.tensor_mul(muiv, c_muT[:, :], ivT), ch="V")
        ev_m2v = op("vector", [ev_mv], lambda e: e.tensor_mul(mui2, c_muT[:, :], muiv), ch="V")
        ev_cs8 = op("vector", [ev_m2v], lambda e: e.tensor_reduce(
            csum, mui2.rearrange("p (e c) -> p e c", c=8), AX.X, OP.add), ch="V")
        ev_ivb = op("vector", [ev_iv, ev_ctxL[H - 1][2]], lambda e: e.tensor_copy(ivb, ivT), ch="V")
        ev_muivb = op("vector", [ev_mv], lambda e: e.tensor_copy(muivb, muiv), ch="V")
        ev_csb = op("vector", [ev_cs8], lambda e: e.tensor_copy(csb, csum), ch="V")
        ev_lA = None
        ev_lB = None
        for c in range(8):
            w = [ev_mcb, ev_ivb]
            if c == 0:
                w += [ev_t1_3, ev_xf[-1]]
            ev_lA = op("tensor", w, lambda e, c=c: e.matmul(
                pB[0:1, 1536:1544], mtb_ap(8 + c, 9 + c), ivb[:, c:64:8],
                start=(c == 0), stop=(c == 7)), ch="T")
        for c in range(8):
            ev_lB = op("tensor", [ev_mcb, ev_muivb], lambda e, c=c: e.matmul(
                pB[0:1, 1544:1552], mtb_ap(c, c + 1), muivb[:, c:64:8],
                start=(c == 0), stop=(c == 7)), ch="T")
        ev_lC = op("tensor", [ev_csb, evc["ones"]], lambda e: e.matmul(
            pB[0:1, 1552:1560], c_ones[:, 0:1], csb[:, 0:8],
            start=True, stop=True), ch="T")
        ev_l0 = op("vector", [ev_lA, ev_lB, ev_lC], lambda e: e.tensor_copy(
            t_loss[0:1, 8:32], pB[0:1, 1536:1560]), ch="V")
        ev_l1 = op("vector", [ev_l0], lambda e: e.scalar_tensor_tensor(
            t_loss[0:1, 0:8], t_loss[0:1, 16:24], -2.0, t_loss[0:1, 8:16],
            OP.mult, OP.add), ch="V")
        ev_ag3s = dma("sync", [ev_l1], ag3_in[:, :], t_loss[0:1, 0:8], stream="sAG2")
        ev_cc3 = op("gpsimd", [ev_ag3s], lambda e: e.collective_compute(
            "AllGather", OP.bypass, replica_groups=groups,
            ins=[ag3_in.ap()], outs=[ag3_out.ap()]), ch="cc")
        ev_l3l = dma("gpsimd", [ev_cc3], t_loss[0:1, 8:24],
                     bass.AP(ag3_out, 0, [[16, 1], [1, 16]]), stream="gE")
        ev_l2a = op("vector", [ev_l3l], lambda e: e.tensor_add(
            t_loss[0:1, 0:8], t_loss[0:1, 8:16], t_loss[0:1, 16:24]), ch="V")
        ev_l2 = op("vector", [ev_l2a], lambda e: e.tensor_add(
            t_loss[0:1, 0:8], t_loss[0:1, 0:8], t_loss[0:1, 24:32]), ch="V")
        ev_lmin = op("vector", [ev_l2], lambda e: e.tensor_reduce(
            t_sc[0:1, 0:1], t_loss[0:1, 0:8], AX.X, OP.min), ch="V")
        ev_msk = op("vector", [ev_lmin], lambda e: e.tensor_tensor(
            t_sc[0:1, 2:10], t_loss[0:1, 0:8],
            t_sc[0:1, 0:1].broadcast_to((1, 8)), OP.is_equal), ch="V")
        ev_mio = op("vector", [ev_msk, evc["iota"]], lambda e: e.tensor_mul(
            t_sc[0:1, 2:10], t_sc[0:1, 2:10], c_iota[0:1, :]), ch="V")
        ev_idxf = op("vector", [ev_mio], lambda e: e.tensor_reduce(
            t_sc[0:1, 1:2], t_sc[0:1, 2:10], AX.X, OP.add), ch="V")
        ev_idx = op("vector", [ev_idxf], lambda e: e.tensor_copy(
            t_idx[0:1, 0:1], t_sc[0:1, 1:2]), ch="V")

        wc_t = [Wa[:, j * 1024:(j + 1) * 1024] for j in range(KC)]
        idx_cell = {}

        def _load_idx(e):
            idx_cell["v"] = e.value_load(t_idx[0:1, 0:1])
            return None
        op("gpsimd", [ev_idx, ev_womm], _load_idx)
        ev_w_dw = op("gpsimd", [("gE", CH["gE"][1])], lambda e: e.dma_start(
            out=bass.AP(G16, RQ + 16384, [[81032, 128], [64, 8], [1, 64]]),
            in_=bass.AP(d_dw, idx_cell["v"] * (D * RK), [[RK, 128], [128 * RK, 8], [1, RK]])),
            ch="gE", k=16)
        ev_w_db = op("gpsimd", [("gE2", CH["gE2"][1])], lambda e: e.dma_start(
            out=t_bdf, in_=bass.AP(d_db, idx_cell["v"] * RK, [[RK, 1], [1, RK]])),
            ch="gE2", k=16)
        ev_w_uw = op("gpsimd", [("gE3", CH["gE3"][1])], lambda e: e.dma_start(
            out=bass.AP(G16, RQ + 16384 + 512, [[81032, 128], [65, 8], [1, 64]]),
            in_=bass.AP(d_uwT, idx_cell["v"] * (D * RK), [[RK, 128], [128 * RK, 8], [1, RK]])),
            ch="gE3", k=16)
        ev_w_ub = op("gpsimd", [("gE2", CH["gE2"][1])], lambda e: e.dma_start(
            out=bass.AP(G16, RQ + 16384 + 1032, [[81032, 128], [1, 8]]),
            in_=bass.AP(d_ubT, idx_cell["v"], [[NE, 128], [128 * NE, 8]]),
            allow_slow_non_contiguous=True), ch="gE2", k=16)
        ev_w_cb = op("gpsimd", [("gE3", CH["gE3"][1])], lambda e: e.dma_start(
            out=t_bcrow, in_=bass.AP(d_cb, idx_cell["v"] * D, [[D, 1], [1, D]])),
            ch="gE3", k=16)
        ev_w_cw1 = op("gpsimd", [("gE", CH["gE"][1])], lambda e: e.dma_start(
            out=bass.AP(Wa, 0, [[8192, 128], [1024, 4], [1, 1024]]),
            in_=bass.AP(d_cwT, idx_cell["v"] * (D * D), [[D, 128], [128 * D, 4], [1, D]])),
            ch="gE", k=16)
        ev_w_cw2 = op("gpsimd", [("gE2", CH["gE2"][1])], lambda e: e.dma_start(
            out=bass.AP(Wa, 4096, [[8192, 128], [1024, 4], [1, 1024]]),
            in_=bass.AP(d_cwT, idx_cell["v"] * (D * D) + 4 * 128 * D, [[D, 128], [128 * D, 4], [1, D]])),
            ch="gE2", k=16)
        ev_ffw = [ev_w_cw1, ev_w_cw2]
        ev_bdc = op("vector", [ev_w_db], lambda e: e.tensor_copy(t_bdrow[0:1, :], t_bdf), ch="V")
        ev_bu = op("vector", [ev_w_ub], lambda e: e.tensor_copy(
            bass.AP(G16, RQ + 16384 + 512 + 64, [[81032, 128], [65, 8]]),
            bass.AP(G16, RQ + 16384 + 1032, [[81032, 128], [1, 8]])), ch="V")

        ev_base = [None] * KC
        bank_free = {0: [None] * 3, 1: [None] * 3}
        for och in range(KC):
            ph = pA if och % 2 == 0 else pB
            ev_ev = None
            for g in range(4):
                gb = g % 3
                ev_mm = None
                for j in range(KC):
                    w = [ev_bw[j], ev_xf[j]]
                    if j == 0:
                        w.append(bank_free[och % 2][gb])
                    if och == 1 and g == 0 and j == 0:
                        w.append(ev_xf[-1])
                    if och == 0 and g == 0 and j == 0:
                        w += [ev_m3b, ev_t1_3]
                    ev_mm = op("tensor", w,
                               lambda e, ph=ph, g=g, gb=gb, j=j, och=och: e.matmul(
                                   ph[:, gb * 512:(gb + 1) * 512],
                                   bw_t[j][:, och * 128:(och + 1) * 128],
                                   xf[:, j * 2048 + g * 512:j * 2048 + (g + 1) * 512],
                                   start=(j == 0), stop=(j == KC - 1)), ch="T")
                ev_ev = op("scalar", [ev_mm],
                           lambda e, ph=ph, g=g, gb=gb, och=och: e.activation(
                               ffo[:, och * 2048 + g * 512:och * 2048 + (g + 1) * 512],
                               ph[:, gb * 512:(gb + 1) * 512], AF.Copy), ch="S")
                bank_free[och % 2][gb] = ev_ev
            ev_base[och] = ev_ev

        ev_hones = op("vector", [ev_stage_free, ev_eq_free[0], ev_eq_free[1]],
                      lambda e: e.memset(h_ext[64:65, :], 1.0), ch="V")
        ev_or = op("vector", [ev_hones], lambda e: e.memset(t_ones512[0:1, :], 1.0), ch="V")
        ev_hmm = []
        for g in range(4):
            evh = None
            for j in range(KC):
                w = [ev_w_dw, ev_xf[j]]
                if g == 0 and j == 0:
                    w += [ev_lA, ev_m3b, ev_base[KC - 2]]
                evh = op("tensor", w,
                         lambda e, g=g, j=j: e.matmul(
                             pA[0:64, g * 512:(g + 1) * 512],
                             t_ffw[:, j * 64:(j + 1) * 64],
                             xf[:, j * 2048 + g * 512:j * 2048 + (g + 1) * 512],
                             start=(j == 0), stop=False), ch="T")
            evh = op("tensor", [ev_w_db, ev_or, ev_bdc], lambda e, g=g: e.matmul(
                pA[0:64, g * 512:(g + 1) * 512],
                t_bdrow[0:1, :], t_ones512[0:1, :],
                start=False, stop=True), ch="T")
            ev_hmm.append(evh)
        ev_hext = op("scalar", ev_hmm + [ev_m2c[-1]], lambda e: e.activation(
            h_ext[0:64, :], pA[0:64, 0:2048], AF.Relu), ch="S")

        ev_Mmm = None
        for j in range(KC):
            for g2 in range(2):
                w = [ev_w_uw, ev_w_cw1, ev_w_cw2, ev_bu]
                if j == 0 and g2 == 0:
                    w += [ev_l2, ev_lC, ev_l0, ev_base[KC - 1]]
                ev_Mmm = op("tensor", w,
                            lambda e, j=j, g2=g2: e.matmul(
                                pB[0:65, g2 * 512:(g2 + 1) * 512],
                                t_ffw[:, 512 + j * 65:512 + (j + 1) * 65],
                                wc_t[j][:, g2 * 512:(g2 + 1) * 512],
                                start=(j == 0), stop=(j == KC - 1)), ch="T")
        ev_Me1 = op("vector", [ev_Mmm, ev_hones], lambda e: e.tensor_copy(
            M_ext[0:64, :], pB[0:64, 0:1024]), ch="V")
        ev_Me2 = op("vector", [ev_Mmm, ev_w_cb], lambda e: e.scalar_tensor_tensor(
            M_ext[64:65, :], pB[64:65, 0:1024], 1.0, t_bcrow[0:1, :],
            OP.mult, OP.add), ch="V")

        ev_ffo = [None] * KC
        yb_free = {0: None, 1: None}
        for och in range(KC):
            ph = pA if och % 2 == 0 else pB
            ev_mm = None
            for g in range(4):
                w = [ev_hext, ev_Me1, ev_Me2]
                if g == 0:
                    w.append(yb_free[och % 2])
                if och == 0 and g == 0:
                    w += [ev_lA, ev_lB, ev_lC, ev_l0]
                ev_mm = op("tensor", w,
                           lambda e, ph=ph, g=g, och=och: e.matmul(
                               ph[:, g * 512:(g + 1) * 512],
                               M_ext[0:65, och * 128:(och + 1) * 128],
                               h_ext[0:65, g * 512:(g + 1) * 512],
                               start=(g >= 0), stop=True), ch="T")
            ev_ffo[och] = op("vector", [ev_mm, ev_base[och], evc["bb"]],
                             lambda e, ph=ph, och=och: e.scalar_tensor_tensor(
                                 ffo[:, och * 2048:(och + 1) * 2048], ph[:, :],
                                 c_bb[:, och:och + 1],
                                 ffo[:, och * 2048:(och + 1) * 2048],
                                 OP.add, OP.add), ch="V")
            yb_free[och % 2] = ev_ffo[och]

        sqf = g16(ZC, 128, 2048)
        ev_m4b, ev_r4, ev_ss4, ev_t1_4, ev_sd4 = ln_stats(
            lambda j: ffo[:, j * 2048:(j + 1) * 2048], ev_ffo, sqf,
            [ev_ffo[-1], ev_ss3, ev_sd3], [ev_ffo[KC - 2], ev_ffo[KC - 1]],
            [ev_ffo[KC - 2], ev_ffo[KC - 1]])
        ev_out = []
        ost_free = [None, None]
        ev_lnt_prev = ev_r4
        for j in range(KC):
            sj = j % 2
            ev_t = op("vector", [ev_ffo[j], ev_m4b, ev_ss4, ev_lnt_prev],
                      lambda e, j=j: e.tensor_sub(
                          ln_t[:, :], ffo[:, j * 2048:(j + 1) * 2048], ln_m[:, :]), ch="V")
            ev_t2 = op("vector", [ev_t, ev_r4],
                       lambda e: e.tensor_mul(ln_t[:, :], ln_t[:, :], pB[:, 0:2048]), ch="V")
            ev_o = op("vector", [ev_t2, ev_x1[j], ost_free[sj]],
                      lambda e, j=j, sj=sj: e.tensor_add(
                          outst[sj], ln_t[:, :], x1[:, j * 2048:(j + 1) * 2048]), ch="V")
            ev_lnt_prev = ev_o
            ev_d = dma("sync", [ev_o], d_out[j * 128:(j + 1) * 128, :], outst[sj],
                       stream=["sO0", "sO1", "sO2", "sO3"][j % 4])
            ost_free[sj] = ev_d
            ev_out.append(ev_d)
        op("sync", ev_out + [(nm, CH[nm][1]) for nm in
                             ("sQ", "sK", "sL", "sAG1", "sAG2", "sST",
                              "sO0", "sO1", "sO2", "sO3", "sC", "qd")],
           lambda e: None)
        op("gpsimd", [(nm, CH[nm][1]) for nm in
                      ("gC", "gW", "gW2", "gW3", "gX", "gX2", "gX3", "gX4",
                       "gE", "gE2", "gE3", "qg", "cc")],
           lambda e: None)
        op("scalar", [("qs", CH["qs"][1])], lambda e: None)

        import kernel as _K
        _K._PROG_SIZES = {k: len(v) for k, v in PROG.items()}
        if limit is not None:
            tgt, frac = limit
            for k in PROG:
                if tgt == "all" or k == tgt:
                    n = max(1, int(len(PROG[k]) * frac))
                    PROG[k] = PROG[k][:n]
        with nc.Block() as block:
            @block.sync
            def _(eng):
                for fn in PROG["sync"]:
                    fn(eng)

            @block.gpsimd
            def _(eng):
                for fn in PROG["gpsimd"]:
                    fn(eng)

            @block.tensor
            def _(eng):
                for fn in PROG["tensor"]:
                    fn(eng)

            @block.vector
            def _(eng):
                for fn in PROG["vector"]:
                    fn(eng)

            @block.scalar
            def _(eng):
                for fn in PROG["scalar"]:
                    fn(eng)

    return nc


def _consts():
    c = {}
    c["c_ones"] = np.ones((128, 128), np.float32)
    c["c_lnones"] = np.full((128, 128), 1.0 / 1024.0, np.float32)
    c["c_ident"] = np.eye(128, dtype=np.float32)
    selq = np.zeros((D, H), np.float32)
    selq[np.arange(D), np.arange(D) // 64] = 0.5
    c["c_selq"] = selq
    selk = np.zeros((512, HK), np.float32)
    selk[np.arange(512), np.arange(512) // 64] = 0.5
    c["c_selk"] = selk
    c["c_iota"] = np.arange(8, dtype=np.float32)[None, :].copy()
    return c


def _ec_layout(a):
    return np.ascontiguousarray(a.reshape(8, 8, 128).transpose(2, 0, 1).reshape(128, 64))


def kernel(**inputs):
    x = np.asarray(inputs["L0_attn_x"], np.float32)
    Wq = np.asarray(inputs["L0_attn_Wq"], np.float32)
    Wk = np.asarray(inputs["L0_attn_Wk"], np.float32)
    Wv = np.asarray(inputs["L0_attn_Wv"], np.float32)
    Wo = np.asarray(inputs["L0_attn_Wo"], np.float32)
    bo = np.asarray(inputs["L0_attn_bo"], np.float32)
    proj = np.asarray(inputs["L0_attn_proj"], np.float32)
    bW = np.asarray(inputs["L0_ffn_base_W"], np.float32)
    bb = np.asarray(inputs["L0_ffn_base_b"], np.float32)
    dmu = np.asarray(inputs["L0_ffn_disc_mu"], np.float32)
    dsg = np.asarray(inputs["L0_ffn_disc_sigma"], np.float32)
    dW = np.asarray(inputs["L0_ffn_down_W"], np.float32)
    db = np.asarray(inputs["L0_ffn_down_b"], np.float32)
    uW = np.asarray(inputs["L0_ffn_up_W"], np.float32)
    ub = np.asarray(inputs["L0_ffn_up_b"], np.float32)
    cW = np.asarray(inputs["L0_ffn_conv_W"], np.float32)
    cb = np.asarray(inputs["L0_ffn_conv_b"], np.float32)

    cst = _consts()
    projT2 = np.ascontiguousarray(np.vstack([proj.T, proj.T]))
    base_wT = np.ascontiguousarray(bW.T)
    up_wT = np.ascontiguousarray(uW.transpose(0, 2, 1))
    conv_wT = np.ascontiguousarray(cW.transpose(0, 2, 1))
    up_bT = np.ascontiguousarray(ub.T)
    bo_cols = np.ascontiguousarray(bo.reshape(8, 128).T)
    bb_cols = np.ascontiguousarray(bb.reshape(8, 128).T)
    smT = _ec_layout(dsg)
    muT = _ec_layout(dmu)

    nc = build_nc(4)
    in_maps = []
    for c in range(8):
        b, p = c // 2, c % 2
        xb = x[b]
        rolled = np.concatenate([xb[p * 2048:(p + 1) * 2048], xb[(1 - p) * 2048:(2 - p) * 2048]], 0)
        in_maps.append({
            "xT": np.ascontiguousarray(rolled.T),
            "wq": Wq,
            "wk": np.ascontiguousarray(Wk[:, p * 512:(p + 1) * 512]),
            "wv": np.ascontiguousarray(Wv[:, p * 512:(p + 1) * 512]),
            "wo": Wo,
            "bo_cols": bo_cols,
            "projT2": projT2,
            "base_wT": base_wT,
            "baseb_cols": bb_cols,
            "sigmaT_ec": smT,
            "muT_ec": muT,
            "down_w": dW,
            "down_b": db,
            "up_wT": up_wT,
            "up_bT": up_bT,
            "conv_wT": conv_wT,
            "conv_b": cb,
            "c_iota": cst["c_iota"],
            "c_ones": cst["c_ones"],
            "c_lnones": cst["c_lnones"],
            "c_ident": cst["c_ident"],
            "c_selq": cst["c_selq"],
            "c_selk": cst["c_selk"],
        })
    res = run_bass_kernel_spmd(nc, in_maps, core_ids=list(range(8)))
    out = np.empty((4, 4096, 1024), np.float32)
    for c in range(8):
        b, p = c // 2, c % 2
        out[b, p * 2048:(p + 1) * 2048] = np.ascontiguousarray(res.results[c]["outT"].T)
    return out


# version 27
# speedup vs baseline: 1.0229x; 1.0229x over previous
import numpy as np
from collections import defaultdict
from contextlib import ExitStack

import concourse.bass as bass
import concourse.mybir as mybir
from concourse.bass_utils import run_bass_kernel_spmd

F32 = mybir.dt.float32
BF16 = mybir.dt.bfloat16
I32 = mybir.dt.int32
AF = mybir.ActivationFunctionType
OP = mybir.AluOpType
AX = mybir.AxisListType

D = 1024
NF = 4096
NT = 2048
H = 16
HK = 8
M = 266
MCH = [(0, 128), (128, 128), (256, 10)]
EPS = 1e-4
DN = float(64.0 ** -0.25)
LNEPS = float(np.log(EPS))
NE = 8
RK = 64
KC = 8


def build_nc(npairs=4, limit=None):
    nc = bass.Bass("TRN2", target_bir_lowering=False)

    d_xT = nc.dram_tensor("xT", [D, NF], F32, kind="ExternalInput")
    d_wq = nc.dram_tensor("wq", [D, D], F32, kind="ExternalInput")
    d_wk = nc.dram_tensor("wk", [D, 512], F32, kind="ExternalInput")
    d_wv = nc.dram_tensor("wv", [D, 512], F32, kind="ExternalInput")
    d_wo = nc.dram_tensor("wo", [D, D], F32, kind="ExternalInput")
    d_bo = nc.dram_tensor("bo_cols", [128, 8], F32, kind="ExternalInput")
    d_projT = nc.dram_tensor("projT2", [128, M], F32, kind="ExternalInput")
    d_bwT = nc.dram_tensor("base_wT", [D, D], F32, kind="ExternalInput")
    d_bb = nc.dram_tensor("baseb_cols", [128, 8], F32, kind="ExternalInput")
    d_smT = nc.dram_tensor("sigmaT_ec", [128, 64], F32, kind="ExternalInput")
    d_muT = nc.dram_tensor("muT_ec", [128, 64], F32, kind="ExternalInput")
    d_dw = nc.dram_tensor("down_w", [NE, D, RK], F32, kind="ExternalInput")
    d_db = nc.dram_tensor("down_b", [NE, RK], F32, kind="ExternalInput")
    d_uwT = nc.dram_tensor("up_wT", [NE, D, RK], F32, kind="ExternalInput")
    d_ubT = nc.dram_tensor("up_bT", [D, NE], F32, kind="ExternalInput")
    d_cwT = nc.dram_tensor("conv_wT", [NE, D, D], F32, kind="ExternalInput")
    d_cb = nc.dram_tensor("conv_b", [NE, D], F32, kind="ExternalInput")
    d_ones = nc.dram_tensor("c_ones", [128, 128], F32, kind="ExternalInput")
    d_lnones = nc.dram_tensor("c_lnones", [128, 128], F32, kind="ExternalInput")
    d_ident = nc.dram_tensor("c_ident", [128, 128], F32, kind="ExternalInput")
    d_selq = nc.dram_tensor("c_selq", [D, H], F32, kind="ExternalInput")
    d_selk = nc.dram_tensor("c_selk", [512, HK], F32, kind="ExternalInput")
    d_iota = nc.dram_tensor("c_iota", [1, 8], F32, kind="ExternalInput")

    d_out = nc.dram_tensor("outT", [D, NT], F32, kind="ExternalOutput")
    ag1_in = nc.dram_tensor("ag1_in", [65, HK * M], BF16)
    ag1_out = nc.dram_tensor("ag1_out", [130, HK * M], BF16)
    ag2_in = nc.dram_tensor("ag2_in", [128, 16], F32)
    ag2_out = nc.dram_tensor("ag2_out", [256, 16], F32)
    ag3_in = nc.dram_tensor("ag3_in", [1, 8], F32)
    ag3_out = nc.dram_tensor("ag3_out", [2, 8], F32)
    groups = [[2 * i, 2 * i + 1] for i in range(npairs)]

    with ExitStack() as cx:
        en = cx.enter_context
        G16 = en(nc.sbuf_tensor("G16", [128, 81032], BF16))
        G32 = en(nc.sbuf_tensor("G32", [128, 3328], F32))
        Wa = en(nc.sbuf_tensor("W16a", [128, 8192], BF16))
        Wb = en(nc.sbuf_tensor("W16b", [128, 8192], BF16))
        c_ones = en(nc.sbuf_tensor("cb_ones", [128, 128], BF16))
        c_lnones = en(nc.sbuf_tensor("cb_lnones", [128, 128], BF16))
        c_idb = en(nc.sbuf_tensor("cb_id", [128, 128], BF16))
        c_idf = en(nc.sbuf_tensor("cf_id", [128, 128], F32))
        c_projb = en(nc.sbuf_tensor("cb_proj", [128, M], BF16))
        c_selq = en(nc.sbuf_tensor("cb_selq", [128, 128], BF16))
        c_selk = en(nc.sbuf_tensor("cb_selk", [128, 32], BF16))
        c_bo = en(nc.sbuf_tensor("cf_bo", [128, 8], F32))
        c_bb = en(nc.sbuf_tensor("cf_bb", [128, 8], F32))
        c_smT = en(nc.sbuf_tensor("cf_smT", [128, 64], F32))
        c_muT = en(nc.sbuf_tensor("cf_muT", [128, 64], F32))
        c_iota = en(nc.sbuf_tensor("cf_iota", [1, 8], F32))
        t_sc = en(nc.sbuf_tensor("t_sc", [128, 16], F32))
        t_scb = en(nc.sbuf_tensor("t_scb", [1, 16], BF16))
        t_loss = en(nc.sbuf_tensor("t_loss", [1, 32], F32))
        t_idx = en(nc.sbuf_tensor("t_idx", [1, 2], I32))
        t_bdrow = en(nc.sbuf_tensor("t_bdrow", [1, RK], BF16))
        pA = en(nc.psum_tensor("pA", [128, 2048], F32))
        pB = en(nc.psum_tensor("pB", [128, 2048], F32))
        pBb = pB.bitcast(BF16)
        pAb = pA.bitcast(BF16)
        G16f = G16.bitcast(F32)

        CH = {}
        for nm in ("T", "V", "S", "G", "qd", "qg", "qs", "cc",
                   "gC", "gW", "gW2", "gW3", "gX", "gX2", "gX3", "gX4",
                   "gE", "gE2", "gE3", "sO2", "sO3", "sQ", "sK", "sL",
                   "sAG1", "sAG2", "sST", "sO0", "sO1", "sC"):
            CH[nm] = [en(nc.semaphore("ch_" + nm)), 0]
        PROG = defaultdict(list)
        WAITED = defaultdict(dict)

        def op(eng, waits, build, ch=None, k=1):
            w2 = []
            for w in (waits or []):
                if w is None:
                    continue
                chn, v = w
                if v > WAITED[eng].get(chn, 0):
                    WAITED[eng][chn] = v
                    w2.append((chn, v))
            evout = None
            if ch is not None:
                c = CH[ch]
                c[1] += k
                evout = (ch, c[1])

            def thunk(engine, w2=w2, build=build, ch=ch, k=k):
                for chn, v in w2:
                    engine.wait_ge(CH[chn][0], v)
                ins = build(engine)
                if ch is not None and ins is not None:
                    ins.then_inc(CH[ch][0], k)
            PROG[eng].append(thunk)
            return evout

        QCH = {"sync": "qd", "gpsimd": "qg", "scalar": "qs"}

        def lowp(build):
            def b2(e, build=build):
                with nc.allow_low_precision(reason="bf16 operand rounding is intended"):
                    return build(e)
            return b2

        def dma(eng, waits, out, in_, stream=None):
            ch = stream or QCH[eng]
            pre = CH[ch][1]
            return op(eng, (waits or []) + ([(ch, pre)] if pre else []),
                      lambda e, o=out, i=in_: e.dma_start(out=o, in_=i),
                      ch=ch, k=16)

        def g16(off, parts, width):
            return G16[0:parts, off:off + width]

        def g16p(p0, p1, off, width):
            return G16[p0:p1, off:off + width]

        def g32(off, parts, width):
            return G32[0:parts, off:off + width]

        RS = 0
        RX = 19592
        RQ = 52360
        RM = 76936
        t_bcrow = g32(2280, 1, 1024)
        t_bdf = g32(1024, 1, RK)

        evc = {}
        evc["ones"] = dma("gpsimd", [], c_ones[:, :], d_ones[:, :], stream="gC")
        evc["lnones"] = dma("gpsimd", [], c_lnones[:, :], d_lnones[:, :], stream="gC")
        evc["idb"] = dma("gpsimd", [], c_idb[:, :], d_ident[:, :], stream="gC")
        evc["idf"] = dma("sync", [], c_idf[:, :], d_ident[:, :], stream="sC")
        evc["projb"] = dma("gpsimd", [], c_projb[:, :], d_projT[:, :], stream="gC")
        evc["selq"] = dma("gpsimd", [], c_selq[:, :],
                          bass.AP(d_selq, 0, [[H, 128], [128 * H, KC], [1, H]]), stream="gC")
        evc["selk"] = dma("gpsimd", [], c_selk[:, :],
                          bass.AP(d_selk, 0, [[HK, 128], [128 * HK, 4], [1, HK]]), stream="gC")
        evc["bo"] = dma("sync", [], c_bo[:, :], d_bo[:, :], stream="sC")
        evc["bb"] = dma("sync", [], c_bb[:, :], d_bb[:, :], stream="sC")
        evc["smT"] = dma("sync", [], c_smT[:, :], d_smT[:, :], stream="sC")
        evc["muT"] = dma("sync", [], c_muT[:, :], d_muT[:, :], stream="sC")
        evc["iota"] = dma("sync", [], c_iota[:, :], d_iota[:, :], stream="sC")

        wq_t = [Wa[:, j * 1024:(j + 1) * 1024] for j in range(KC)]
        wk_t = [Wb[:, j * 512:(j + 1) * 512] for j in range(KC)]
        wv_t = [Wb[:, 4096 + j * 512:4096 + (j + 1) * 512] for j in range(KC)]
        ev_wq = [dma("gpsimd", [], wq_t[j], d_wq[j * 128:(j + 1) * 128, :], stream="gW") for j in range(KC)]
        ev_wk = [dma("gpsimd", [], wk_t[j], d_wk[j * 128:(j + 1) * 128, :], stream="gW2") for j in range(KC)]
        ev_wv = [dma("gpsimd", [], wv_t[j], d_wv[j * 128:(j + 1) * 128, :], stream="gW3") for j in range(KC)]

        ZB = 16384
        ZC = 32768
        ZD = 49152
        xb_s = [g16(s * 8192, 128, 8192) for s in range(2)]
        xsq_s = [g16(ZD + s * 1024, 128, 1024) for s in range(2)]

        def xln_ap(j, c0, w):
            if c0 < 2048:
                return g16(ZC + j * 2048 + c0, 128, w)
            return g16(ZB + j * 2048 + (c0 - 2048), 128, w)

        m_b2 = [g16(RM + s_ * 1024, 128, 1024) for s_ in range(2)]
        msq_b2 = [g16(RM + 2048 + s_ * 1024, 128, 1024) for s_ in range(2)]
        r_rbp2 = [g16p(0, 1, RM + 2048 + s_ * 1024, 1024) for s_ in range(2)]
        tmp1_2 = [g32(s_ * 1024, 128, 1024) for s_ in range(2)]

        rT32p = g32(3296, 128, 16)
        r_rbp = g16p(0, 1, RM + 1024, 1024)
        ev_rT_p1 = None
        ev_stgf = [[], []]
        mr_free = [None, None]
        ev_xln = [[None] * 4 for _ in range(KC)]
        slot_free = [None, None]
        xsq_free = [None, None]
        sub_s = [g16(RQ + s_ * 1024, 128, 1024) for s_ in range(2)]
        sub_free = [None, None]
        mr_free = [None, None]
        ev_mb_last = None
        ev_t1_last = None
        for blk in range(4):
            s = blk % 2
            tb = blk * 1024
            stg0 = 8192 if blk % 2 == 0 else (RQ + 8192) // 2
            evxr = [dma("sync", list(ev_stgf[blk % 2]),
                        G16f[0:128, stg0 + j * 1024:stg0 + (j + 1) * 1024],
                        d_xT[j * 128:(j + 1) * 128, tb:tb + 1024],
                        stream=["sQ", "sK", "sO2", "sO3"][j % 4]) for j in range(KC)]
            evx = []
            for j in range(KC):
                src = G16f[0:128, stg0 + j * 1024:stg0 + (j + 1) * 1024]
                dst = xb_s[s][:, j * 1024:(j + 1) * 1024]
                if j % 2 == 0:
                    evx.append(op("scalar", [evxr[j], slot_free[s]],
                                  lambda e, src=src, dst=dst: e.activation(
                                      dst, src, AF.Copy), ch="S"))
                else:
                    evx.append(op("vector", [evxr[j], slot_free[s]],
                                  lambda e, src=src, dst=dst: e.tensor_copy(dst, src), ch="V"))
            ev_stgf[blk % 2] = [evx[-2], evx[-1]]
            pm = pA[:, s * 1024:(s + 1) * 1024]
            pq = pB[:, s * 1024:(s + 1) * 1024]
            m_b = m_b2[s]
            msq_b = msq_b2[s]
            r_rbp = r_rbp2[s]
            tmp1 = tmp1_2[s]
            ev_mm = None
            for j in range(KC):
                for g2 in range(2):
                    ev_mm = op("tensor", [evx[j], evc["lnones"]],
                               lambda e, pm=pm, s=s, j=j, g2=g2: e.matmul(
                                   pm[:, g2 * 512:(g2 + 1) * 512],
                                   c_lnones[:, :],
                                   xb_s[s][:, j * 1024 + g2 * 512:j * 1024 + (g2 + 1) * 512],
                                   start=(j == 0), stop=(j == KC - 1)), ch="T")
            ev_mb = op("vector", [ev_mm, mr_free[s]], lambda e, pm=pm, m_b=m_b: e.tensor_copy(m_b[:, :], pm), ch="V")
            ev_mb_last = ev_mb
            ev_smm = None
            for j in range(KC):
                sj = j % 2
                ev_sq = op("scalar", [evx[j], xsq_free[sj]],
                           lambda e, s=s, j=j, sj=sj: e.activation(
                               xsq_s[sj][:, :], xb_s[s][:, j * 1024:(j + 1) * 1024],
                               AF.Square), ch="S")
                for g2 in range(2):
                    ev_smm = op("tensor", [ev_sq, ev_mm if j == 0 else None],
                                lambda e, pq=pq, j=j, sj=sj, g2=g2: e.matmul(
                                    pq[:, g2 * 512:(g2 + 1) * 512],
                                    c_lnones[:, :],
                                    xsq_s[sj][:, g2 * 512:(g2 + 1) * 512],
                                    start=(j == 0), stop=(j == KC - 1)), ch="T")
                xsq_free[sj] = ev_smm
            ev_t1 = op("vector", [ev_smm], lambda e, pq=pq, tmp1=tmp1: e.tensor_copy(tmp1, pq), ch="V")
            ev_t1_last = ev_t1
            ev_m2 = op("scalar", [ev_mb, mr_free[s]], lambda e, msq_b=msq_b, m_b=m_b: e.activation(msq_b[:, :], m_b[:, :], AF.Square), ch="S")
            ev_var = op("vector", [ev_t1, ev_m2], lambda e, tmp1=tmp1, msq_b=msq_b: e.tensor_sub(tmp1, tmp1, msq_b[:, :]), ch="V")
            ev_sd = op("scalar", [ev_var], lambda e, tmp1=tmp1: e.activation(tmp1, tmp1, AF.Sqrt, bias=np.float32(1e-5)), ch="S")
            ev_vT = [op("tensor", [ev_sd, evc["idf"], ev_mb],
                        lambda e, cnk=cnk, s=s: e.transpose(
                            pA[0:128, s * 1024 + cnk:s * 1024 + cnk + 1],
                            G32[0:1, s * 1024 + cnk * 128:s * 1024 + (cnk + 1) * 128],
                            c_idf[0:1, 0:1]), ch="T") for cnk in range(8)]
            ev_rc = op("vector", ev_vT, lowp(lambda e, s=s: e.reciprocal(
                rT32p[0:128, s * 8:s * 8 + 8], pA[0:128, s * 1024:s * 1024 + 8])), ch="V")
            ev_rT = [op("tensor", [ev_rc, evc["idf"]],
                        lambda e, cnk=cnk, s=s: e.transpose(
                            pA[0:1, s * 1024 + cnk * 128:s * 1024 + (cnk + 1) * 128],
                            rT32p[0:128, s * 8 + cnk:s * 8 + cnk + 1],
                            c_idf[:, :]), ch="T") for cnk in range(8)]
            ev_re = op("vector", ev_rT + [ev_var], lambda e, s=s, r_rbp=r_rbp: e.tensor_copy(
                r_rbp[0:1, :], pA[0:1, s * 1024:s * 1024 + 1024]), ch="V")
            ev_bcr = [op("tensor", [ev_re, evc["ones"]],
                         lambda e, g=g, s=s, r_rbp=r_rbp: e.matmul(
                             pA[0:128, s * 1024 + g * 512:s * 1024 + (g + 1) * 512],
                             c_ones[0:1, 0:128],
                             r_rbp[0:1, g * 512:(g + 1) * 512],
                             start=True, stop=True), ch="T") for g in range(2)]
            ev_r = ev_bcr[-1]
            last_x = None
            for j in range(KC):
                sj = j % 2
                ev_t = op("vector", [ev_mb, evx[j], sub_free[sj]],
                          lambda e, s=s, j=j, sj=sj, m_b=m_b: e.tensor_sub(
                              sub_s[sj][:, :], xb_s[s][:, j * 1024:(j + 1) * 1024], m_b[:, :]),
                          ch="V")
                ev_x = op("vector", [ev_t, ev_r],
                          lambda e, j=j, tb=tb, sj=sj, s=s: e.tensor_mul(
                              xln_ap(j, tb, 1024), sub_s[sj][:, :],
                              pA[:, s * 1024:(s + 1) * 1024]), ch="V")
                ev_xln[j][blk] = ev_x
                sub_free[sj] = ev_x
                last_x = ev_x
            slot_free[s] = last_x
            mr_free[s] = last_x

        kpark = [g16(och * 4096, 128, 4096) for och in range(4)]
        ev_kpark = [None] * 4
        kh_free = {0: None, 1: None}
        for och in range(4):
            ev_ev = None
            for half in range(2):
                ph = pA if half == 0 else pB
                ev_mm = None
                for g in range(4):
                    for j in range(KC):
                        w = [ev_wk[j], ev_xln[j][(half * 2048 + g * 512) // 1024]]
                        if g == 0 and j == 0:
                            w.append(kh_free[half])
                        if och == 0 and g == 0 and j == 0:
                            w += [mr_free[0], mr_free[1], slot_free[0], slot_free[1], ev_t1_last]
                        ev_mm = op("tensor", w,
                                   lambda e, ph=ph, half=half, g=g, j=j, och=och: e.matmul(
                                       ph[:, g * 512:(g + 1) * 512],
                                       wk_t[j][:, och * 128:(och + 1) * 128],
                                       xln_ap(j, half * 2048 + g * 512, 512),
                                       start=(j == 0), stop=(j == KC - 1)), ch="T")
                ev_ev = op("scalar", [ev_mm, slot_free[och // 2] if half == 0 else None],
                           lambda e, ph=ph, och=och, half=half: e.activation(
                               kpark[och][:, half * 2048:(half + 1) * 2048], ph[:, :],
                               AF.Copy, scale=np.float32(DN)), ch="S")
                kh_free[half] = ev_ev
            ev_kpark[och] = ev_ev

        def v_t(cnk):
            return g16(RQ + 8192 + cnk * 512, 128, 512)

        ev_v = [None] * 32
        for blk in range(8):
            ph = pA if blk % 2 == 0 else pB
            evl = []
            for ci in range(4):
                cnk = blk * 4 + ci
                ev_mm = None
                for j in range(KC):
                    w = [ev_wv[j]]
                    if ci == 0 and j == 0:
                        if blk == 0:
                            w += [kh_free[0], kh_free[1]]
                        elif blk == 1:
                            w += [ev_v[3]]
                        else:
                            w += [ev_v[blk * 4 - 5]]
                    ev_mm = op("tensor", w,
                               lambda e, ph=ph, ci=ci, cnk=cnk, j=j: e.matmul(
                                   ph[:, ci * 512:(ci + 1) * 512],
                                   xln_ap(j, cnk * 128, 128),
                                   wv_t[j][:, :],
                                   start=(j == 0), stop=(j == KC - 1)), ch="T")
                evl.append(ev_mm)
            for ci in range(4):
                cnk = blk * 4 + ci
                ev_v[cnk] = op("scalar", [evl[ci]],
                               lambda e, ph=ph, ci=ci, cnk=cnk: e.activation(
                                   v_t(cnk)[:, :], ph[:, ci * 512:(ci + 1) * 512],
                                   AF.Copy), ch="S")

        qpark = [g16(ZB + pj * 2048, 128, 2048) for pj in range(KC)]
        ev_qpark = [None] * KC
        ph_free = {0: None, 1: None}
        ev_qmm_last = None
        for och in range(KC):
            ph = pA if och % 2 == 0 else pB
            ev_mm = None
            for g in range(4):
                for j in range(KC):
                    w = [ev_wq[j]]
                    if g == 0 and j == 0:
                        w.append(ph_free[och % 2])
                    if och == 0 and g == 0 and j == 0:
                        w += [ev_v[27], ev_v[31]]
                    if och == 1 and g == 0 and j == 0:
                        w += [ev_v[31]]
                    ev_mm = op("tensor", w,
                               lambda e, ph=ph, g=g, j=j, och=och: e.matmul(
                                   ph[:, g * 512:(g + 1) * 512],
                                   wq_t[j][:, och * 128:(och + 1) * 128],
                                   xln_ap(j, g * 512, 512),
                                   start=(j == 0), stop=(j == KC - 1)), ch="T")
            ev_qc = op("scalar", [ev_mm],
                       lambda e, ph=ph, och=och: e.activation(
                           qpark[och][:, :], ph[:, :], AF.Copy, scale=np.float32(DN)), ch="S")
            ev_qpark[och] = ev_qc
            ph_free[och % 2] = ev_qc
            ev_qmm_last = ev_mm

        dsc = [g16(RM + s_ * 2048, 128, 2048) for s_ in range(2)]
        ev_dq = None
        ev_dsq = [None, None]
        for pj in range(KC):
            sj = pj % 2
            w = [ev_qpark[pj], ev_dsq[sj]]
            if pj == 0:
                w += [slot_free[0], slot_free[1], mr_free[0], mr_free[1]]
            ev_sq2 = op("vector", w,
                        lambda e, pj=pj, sj=sj: e.tensor_mul(
                            dsc[sj][:, :], qpark[pj][:, :], qpark[pj][:, :]), ch="V")
            for g in range(4):
                ev_dq = op("tensor", [ev_sq2, evc["selq"]] +
                           ([ph_free[0], ph_free[1]] if (pj == 0 and g == 0) else []),
                           lambda e, pj=pj, g=g, sj=sj: e.matmul(
                               pA[0:16, g * 512:(g + 1) * 512],
                               c_selq[:, pj * 16:(pj + 1) * 16],
                               dsc[sj][:, g * 512:(g + 1) * 512],
                               start=(pj == 0), stop=(pj == KC - 1)), ch="T")
            ev_dsq[sj] = ev_dq
        dqrows = g32(1024, 16, 2048)
        ev_dqr = op("vector", [ev_dq], lambda e: e.tensor_copy(dqrows, pA[0:16, 0:2048]), ch="V")
        diagTq = g32(0, 128, 256)
        ev_tq_e = None
        for cnk in range(16):
            evt = op("tensor", [ev_dqr, evc["idf"], ev_tq_e],
                     lambda e, cnk=cnk: e.transpose(
                         pB[0:128, 1536:1552],
                         dqrows[0:16, cnk * 128:(cnk + 1) * 128],
                         c_idf[0:16, 0:16]), ch="T")
            ev_tq_e = op("vector", [evt], lambda e, cnk=cnk: e.tensor_copy(
                diagTq[:, cnk * 16:(cnk + 1) * 16], pB[0:128, 1536:1552]), ch="V")
        expdqe = g32(256, 128, 256)
        ev_expdq = op("scalar", [ev_tq_e], lambda e: e.activation(
            expdqe, diagTq, AF.Exp, bias=np.float32(LNEPS)), ch="S")

        ev_dk = None
        ev_dks = [None, None]
        for j in range(4):
            for half in range(2):
                sj = half
                w = [ev_kpark[j], ev_dks[sj]]
                if j == 0 and half == 0:
                    w.append(ev_dq)
                ev_sqk = op("vector", w,
                            lambda e, j=j, half=half, sj=sj: e.tensor_mul(
                                dsc[sj][:, :], kpark[j][:, half * 2048:(half + 1) * 2048],
                                kpark[j][:, half * 2048:(half + 1) * 2048]), ch="V")
                for gi in range(4):
                    g = half * 4 + gi
                    w2 = [ev_sqk, evc["selk"]]
                    if j == 0 and g == 0:
                        w2.append(ev_dqr)
                    if j == 0 and g == 4:
                        w2.append(ev_tq_e)
                    ev_dk = op("tensor", w2,
                               lambda e, j=j, g=g, gi=gi, sj=sj: e.matmul(
                                   (pA if g < 4 else pB)[0:8, (g % 4) * 512:(g % 4) * 512 + 512],
                                   c_selk[:, j * 8:(j + 1) * 8],
                                   dsc[sj][:, gi * 512:(gi + 1) * 512],
                                   start=(j == 0), stop=(j == 3)), ch="T")
                ev_dks[sj] = ev_dk
        dkrows = g32(1024, 8, 2048)
        diagTk = g32(512, 128, 256)
        ev_tk_e = None
        for half in range(2):
            ev_dkr = op("vector", [ev_dk] + ([ev_dqr] if half == 0 else [ev_tk_e]),
                        lambda e, half=half: e.tensor_copy(
                            dkrows, (pA if half == 0 else pB)[0:8, 0:2048]), ch="V")
            for c2 in range(16):
                cnk = half * 16 + c2
                evt = op("tensor", [ev_dkr, evc["idf"], ev_tk_e],
                         lambda e, c2=c2: e.transpose(
                             pA[0:128, 1536:1544],
                             dkrows[0:8, c2 * 128:(c2 + 1) * 128],
                             c_idf[0:8, 0:8]), ch="T")
                ev_tk_e = op("vector", [evt], lambda e, cnk=cnk: e.tensor_copy(
                    diagTk[:, cnk * 8:(cnk + 1) * 8], pA[0:128, 1536:1544]), ch="V")
        expndk = g32(768, 128, 256)
        ev_expdk = op("scalar", [ev_tk_e], lambda e: e.activation(
            expndk, diagTk, AF.Exp, scale=np.float32(-1.0)), ch="S")

        wo_t = [Wa[:, j * 1024:(j + 1) * 1024] for j in range(KC)]
        ev_wo = [dma("gpsimd", [ev_qmm_last], wo_t[j], d_wo[j * 128:(j + 1) * 128, :], stream="gW2")
                 for j in range(KC)]

        etok_s = [g16(ZC + s_ * 8512, 128, 8512) for s_ in range(2)]
        vts = g16(ZD + 640, 128, 2080)
        ctx_sbb = g16p(0, 65, RM, 2128)
        maxr2 = [g32(3152, 128, 34), g32(3186, 128, 34)]
        recB = g32(3220, 128, 1)
        epscol = g32(3222, 65, 1)
        skT = g32(3224, 128, 32)
        ev_ctx_ev = []
        ev_ctx_last = None
        ev_skT_free = [None]
        batches = [list(range(b, min(b + 3, 32))) for b in range(0, 32, 3)]
        ev_scan_of = {}
        etok_free = [None, None]
        ev_exp_of = {}

        ev_tr_of = {}
        ev_mx_of = {}
        ev_rbe_of = {}
        ev_ep_of = {}
        ev_ce_of = {}

        def ks_dash(hh):
            sl = hh % 2
            kch = hh // 2
            kb = (hh % 2) * 64
            ev_exp_all = []
            for bi, bat in enumerate(batches):
                ph = pA if bi % 2 == 0 else pB
                evm = []
                for ii, cnk in enumerate(bat):
                    w = [ev_kpark[kch], evc["projb"]]
                    if bi >= 2:
                        w.append(ev_exp_all[bi - 2])
                    if hh == 0 and bi <= 1:
                        w.append(ev_v[31])
                    if hh >= 1 and bi <= 1 and ii == 0:
                        w += ev_exp_of[hh - 1][-2:]
                    evm.append(op("tensor", w,
                                  lambda e, ph=ph, ii=ii, cnk=cnk, kch=kch, kb=kb: e.matmul(
                                      ph[0:128, ii * 512:ii * 512 + M],
                                      kpark[kch][kb:kb + 64, cnk * 128:(cnk + 1) * 128],
                                      c_projb[kb:kb + 64, :],
                                      start=True, stop=True), ch="T"))
                nb = len(bat)
                ev_exp = op("scalar", evm + [etok_free[sl] if bi == 0 else None],
                            lambda e, ph=ph, nb=nb, c0=bat[0], sl=sl: e.activation(
                                etok_s[sl][:, c0 * 266:(c0 + nb) * 266].rearrange(
                                    "p (b m) -> p b m", m=266),
                                bass.AP(ph, 0, [[2048, 128], [512, nb], [1, M]]),
                                AF.Exp), ch="S")
                ev_exp_all.append(ev_exp)
            ev_exp_of[hh] = ev_exp_all

        def ks_scan(hh):
            mx = maxr2[hh % 2]
            w = [ev_exp_of[hh][-1]]
            if hh >= 2:
                w.append(ev_tr_of[hh - 2])
            ev_red = op("vector", w,
                        lambda e, sl=hh % 2, mx=mx: e.tensor_reduce(
                            mx[:, 0:32],
                            etok_s[sl][:, :].rearrange("p (c m) -> p c m", m=266),
                            AX.X, OP.max), ch="V")
            ev_red2 = op("vector", [ev_red], lambda e, mx=mx: e.tensor_reduce(
                mx[:, 32:33], mx[:, 0:32], AX.X, OP.max), ch="V")
            ev_scan_of[hh] = ev_red2

        def ks_tailA(hh):
            mx = maxr2[hh % 2]
            w = [ev_scan_of[hh], evc["idf"]]
            if hh >= 2:
                w.append(ev_mx_of[hh - 2])
            ev_tr = op("tensor", w,
                       lambda e, mx=mx: e.transpose(pB[0:1, 1536:1664], mx[:, 32:33],
                                                    c_idf[:, :]), ch="T")
            ev_tr_of[hh] = ev_tr
            ev_mx = op("vector", [ev_tr], lambda e: e.tensor_reduce(
                t_sc[0:1, 0:1], pB[0:1, 1536:1664], AX.X, OP.max), ch="V")
            ev_mx_of[hh] = ev_mx
            ev_rc = op("vector", [ev_mx], lowp(lambda e: e.reciprocal(
                t_scb[0:1, 0:1], t_sc[0:1, 0:1])), ch="V")
            w = [ev_rc, evc["ones"]]
            if hh >= 2:
                w.append(ev_rbe_of[hh - 2])
            ev_rb = op("tensor", w,
                       lambda e: e.matmul(pB[0:128, 1700:1701],
                                          c_ones[0:1, 0:128], t_scb[0:1, 0:1],
                                          start=True, stop=True), ch="T")
            ev_rbe = op("vector", [ev_rb], lambda e: e.tensor_copy(recB, pB[0:128, 1700:1701]), ch="V")
            ev_rbe_of[hh] = ev_rbe
            ev_sk = op("vector", [ev_rbe, ev_expdk, ev_skT_free[0]],
                       lambda e, hh=hh: e.tensor_scalar_mul(
                           skT, G32[0:128, 768 + hh:1024:8], recB[:, 0:1]), ch="V")
            return dict(ev_sk=ev_sk, ev_rb=ev_rb)

        def ks_tailB(hh, stA):
            sl = hh % 2
            ev_vt1 = op("vector", [stA["ev_sk"]],
                        lambda e, hh=hh: e.tensor_tensor(
                            vts[:, :].rearrange("p (c w) -> p c w", w=65)[:, :, 0:64],
                            bass.AP(G16, RQ + 8192 + hh * 64, [[81032, 128], [512, 32], [1, 64]]),
                            skT.rearrange("p (c o) -> p c o", o=1).broadcast_to((128, 32, 64)),
                            OP.mult), ch="V")
            ev_vt2 = op("vector", [stA["ev_sk"]], lambda e: e.tensor_copy(
                vts[:, :].rearrange("p (c w) -> p c w", w=65)[:, :, 64:65],
                skT.rearrange("p (c o) -> p c o", o=1)), ch="V")
            ev_skT_free[0] = ev_vt2
            ev_cs = None
            for cnk in range(32):
                w = [ev_v[cnk], evc["ones"]]
                if cnk == 0:
                    w.append(stA["ev_rb"])
                    if hh >= 2:
                        w.append(ev_ep_of[hh - 2])
                ev_cs = op("tensor", w,
                           lambda e, cnk=cnk, hh=hh: e.matmul(
                               pB[0:64, 1792:1793],
                               v_t(cnk)[:, hh * 64:hh * 64 + 64],
                               c_ones[:, 0:1],
                               start=(cnk == 0), stop=(cnk == 31)), ch="T")
            ev_ep1 = op("vector", [ev_cs], lambda e: e.tensor_scalar_mul(
                epscol[0:64, 0:1], pB[0:64, 1792:1793], EPS), ch="V")
            ev_ep2 = op("vector", [ev_ep1], lambda e: e.memset(
                epscol[64:65, 0:1], EPS * float(NF)), ch="V")
            ev_ep_of[hh] = ev_ep1
            return dict(ev_vt1=ev_vt1, ev_vt2=ev_vt2, ev_ep2=ev_ep2, **stA)

        def ks_tailC(hh, stB):
            sl = hh % 2
            ev_exp_all = ev_exp_of[hh]
            ev_ctx = None
            for cnk in range(32):
                w = [stB["ev_vt1"], stB["ev_vt2"],
                     ev_exp_all[min(cnk // 3, len(ev_exp_all) - 1)]]
                if cnk == 0 and ev_ctx_ev:
                    w.append(ev_ctx_ev[-1])
                ev_ctx = op("tensor", w,
                            lambda e, cnk=cnk, sl=sl: e.matmul(
                                pA[0:65, 1536:1536 + M],
                                vts[:, cnk * 65:(cnk + 1) * 65],
                                etok_s[sl][:, cnk * 266:(cnk + 1) * 266],
                                start=(cnk == 0), stop=(cnk == 31)), ch="T")
            etok_free[sl] = ev_ctx
            ev_ce = op("vector", [ev_ctx, stB["ev_ep2"]],
                       lambda e, hh=hh: e.tensor_scalar(
                           ctx_sbb[:, hh * M:(hh + 1) * M], pA[0:65, 1536:1536 + M],
                           epscol[0:65, 0:1], None, OP.add), ch="V")
            ev_ctx_ev.append(ev_ce)
            return ev_ctx

        ev_ctx_last = None
        stA = {}
        stB = {}
        for hh in range(HK):
            if hh >= 1:
                stA[hh - 1] = ks_tailA(hh - 1)
            ks_dash(hh)
            if hh >= 1:
                stB[hh - 1] = ks_tailB(hh - 1, stA[hh - 1])
            ks_scan(hh)
            if hh >= 1:
                ev_ctx_last = ks_tailC(hh - 1, stB[hh - 1])
        stA[HK - 1] = ks_tailA(HK - 1)
        stB[HK - 1] = ks_tailB(HK - 1, stA[HK - 1])
        ev_ctx_last = ks_tailC(HK - 1, stB[HK - 1])

        eq0 = [g16(s_ * 6144, 128, 2048) for s_ in range(2)]
        eq1 = [g16(s_ * 6144 + 2048, 128, 2048) for s_ in range(2)]
        eq2 = [g16p(0, 16, s_ * 6144 + 4096, 2048) for s_ in range(2)]
        stage = g16(ZD + 640, 128, 2048)
        merged = g16(ZC, 128, 16384)
        u_row = [g16p(0, 1, 12288 + s_ * 2048, 2048) for s_ in range(2)]
        u_tr = g32(3168, 16, 128)
        maxEq_p = [g32(3296, 128, 16), g32(1024, 128, 16)]
        u_cols_p = [g32(3312, 128, 16), g32(1040, 128, 16)]
        rbT_b = g16p(0, 128, RQ + 14336, 16)
        scanb = g16(RQ + 8192, 128, 2048)
        num_sb = [g16p(0, 65, RQ + 10240 + s * 2048, 2048) for s in range(2)]

        ev_eq_free = [None, None]
        ev_u_free = [None, None]
        ev_stage_free = None
        ev_scan_prev = None
        ev_uc_pp = [None, None]
        ev_ut_pp = [None, None]
        ev_ur_prev = None
        ev_rT_prev = None
        ev_bc_prev = None
        ev_num_free = [None, None]
        ev_par_free = [ev_ctx_ev[-1], ev_ctx_ev[-1]]
        ev_merged = []
        st_front = {}

        def qs_dash(h):
            nonlocal ev_scan_prev
            pj = h // 2
            sl = pj % 2
            es = h % 2
            qb = es * 64
            P = pA if es == 0 else pB
            Pb = pAb if es == 0 else pBb

            def dmm(m0, mw, g, w):
                return op("tensor", w,
                          lambda e, g=g, m0=m0, mw=mw, qb=qb, pj=pj, P=P: e.matmul(
                              P[0:mw, g * 512:(g + 1) * 512],
                              c_projb[qb:qb + 64, m0:m0 + mw],
                              qpark[pj][qb:qb + 64, g * 512:(g + 1) * 512],
                              start=True, stop=True), ch="T")
            w0 = [ev_qpark[pj], evc["projb"], ev_par_free[es]]
            ev_c2a = [dmm(256, 10, g, w0) for g in (0, 1)]
            ev_c2b = [dmm(256, 10, g, w0) for g in (2, 3)]
            ev_e2a = op("scalar", ev_c2a + [ev_eq_free[es]],
                        lambda e, es=es, P=P: e.activation(
                            eq2[es][0:10, 0:1024], P[0:10, 0:1024], AF.Exp), ch="S")
            ev_e2b = op("scalar", ev_c2b + [ev_eq_free[es]],
                        lambda e, es=es, P=P: e.activation(
                            eq2[es][0:10, 1024:2048], P[0:10, 1024:2048], AF.Exp), ch="S")
            ev_c0a = [dmm(0, 128, g, [ev_e2a]) for g in (0, 1)]
            ev_c0b = [dmm(0, 128, g, [ev_e2b]) for g in (2, 3)]
            ev_e0a = op("scalar", ev_c0a + [ev_eq_free[es]],
                        lambda e, es=es, P=P: e.activation(
                            eq0[es][:, 0:1024], P[:, 0:1024], AF.Exp), ch="S")
            ev_e0b = op("scalar", ev_c0b + [ev_eq_free[es]],
                        lambda e, es=es, P=P: e.activation(
                            eq0[es][:, 1024:2048], P[:, 1024:2048], AF.Exp), ch="S")
            st = dict(es=es, sl=sl, pj=pj, qb=qb, P=P, Pb=Pb, dmm=dmm,
                      ev_e2a=ev_e2a, ev_e2b=ev_e2b, ev_e0a=ev_e0a, ev_e0b=ev_e0b)
            return st

        def qs_dash2(st, h):
            es = st["es"]; P = st["P"]; dmm = st["dmm"]
            ev_c1a = [dmm(128, 128, g, [st["ev_e0a"]]) for g in (0, 1)]
            ev_c1b = [dmm(128, 128, g, [st["ev_e0b"]]) for g in (2, 3)]
            ev_e1a = op("scalar", ev_c1a + [ev_eq_free[es]],
                        lambda e, es=es, P=P: e.activation(
                            eq1[es][:, 0:1024], P[:, 0:1024], AF.Exp), ch="S")
            ev_e1b = op("scalar", ev_c1b + [ev_eq_free[es]],
                        lambda e, es=es, P=P: e.activation(
                            eq1[es][:, 1024:2048], P[:, 1024:2048], AF.Exp), ch="S")
            st["ev_e1a"] = ev_e1a
            st["ev_e1b"] = ev_e1b
            st["ev_e0"] = st["ev_e0b"]
            st["ev_e1"] = ev_e1b
            st["ev_e2"] = st["ev_e2b"]
            return st

        def qs_scan(st, h):
            nonlocal ev_scan_prev
            es = st["es"]; P = st["P"]; Pb = st["Pb"]
            ev_cm = []
            for half, (ee0, ee1, ee2) in enumerate(
                    [(st["ev_e0a"], st["ev_e1a"], st["ev_e2a"]),
                     (st["ev_e0b"], st["ev_e1b"], st["ev_e2b"])]):
                c0 = half * 1024
                w_cm = [ee0, ee1]
                if ev_scan_prev is not None:
                    w_cm.append(ev_scan_prev)
                ev_cm1 = op("vector", w_cm, lambda e, es=es, c0=c0: e.tensor_max(
                    scanb[:, c0:c0 + 1024], eq0[es][:, c0:c0 + 1024],
                    eq1[es][:, c0:c0 + 1024]), ch="V")
                ev_cm2 = op("vector", [ev_cm1, ee2], lambda e, es=es, c0=c0: e.tensor_max(
                    scanb[0:10, c0:c0 + 1024], scanb[0:10, c0:c0 + 1024],
                    eq2[es][0:10, c0:c0 + 1024]), ch="V")
                ev_cm.append(ev_cm2)
            ev_sct = []
            ev_scr = []
            for cnk in range(16):
                w = [ev_cm[cnk // 8], evc["idb"], st["ev_e1b"]]
                if cnk == 8:
                    w.append(ev_scr[0])
                ev_sct.append(op("tensor", w,
                                 lambda e, cnk=cnk, Pb=Pb: e.transpose(
                                     Pb[:, (cnk % 8) * 128:(cnk % 8) * 128 + 128],
                                     scanb[:, cnk * 128:(cnk + 1) * 128],
                                     c_idb[:, :]), ch="T"))
                if cnk % 8 == 7:
                    wr = ev_sct[-8:]
                    if cnk == 7:
                        wr = wr + [ev_uc_pp[es]]
                    ev_scr.append(op("vector", wr,
                                     lambda e, cnk=cnk, Pb=Pb, es=es: e.tensor_reduce(
                                         maxEq_p[es][:, cnk - 7:cnk + 1],
                                         Pb[:, 0:1024].rearrange("p (c f) -> p c f", f=128),
                                         AX.X, OP.max), ch="V"))
            ev_scan_prev = ev_sct[-1]
            ev_uc = op("vector", [ev_scr[-1], ev_expdq, ev_ut_pp[es]],
                       lambda e, h=h, es=es: e.tensor_mul(
                           u_cols_p[es], G32[0:128, 256 + h:512:16], maxEq_p[es]), ch="V")
            ev_uc_pp[es] = ev_uc
            st.update(ev_cm1=ev_cm[1], ev_scr=ev_scr, ev_uc=ev_uc)
            return st

        def qs_ur(st, h):
            nonlocal ev_ur_prev
            es = st["es"]
            P = st["P"]
            ev_ut = op("tensor", [st["ev_uc"], evc["idf"], st["ev_e1b"]],
                       lambda e, P=P, es=es: e.transpose(
                           P[0:16, 1024:1152], u_cols_p[es], c_idf[:, :]), ch="T")
            ev_ut_pp[es] = ev_ut
            ev_ue = op("vector", [ev_ut, ev_ur_prev], lambda e, P=P: e.tensor_copy(
                u_tr, P[0:16, 1024:1152]), ch="V")
            st["ev_ue"] = ev_ue
            ev_ur = dma("gpsimd", [ev_ue, ev_u_free[es]],
                        u_row[es][0:1, :].rearrange("o (c f) -> o c f", c=16), u_tr,
                        stream="gE")
            ev_ur_prev = ev_ur
            st["ev_ur"] = ev_ur
            return st

        def qs_am01(st, h):
            es = st["es"]; P = st["P"]
            ev_am = []
            for g in (1, 2, 3, 0):
                hv = "a" if g < 2 else "b"
                for mi in (0, 1):
                    m0, mw = MCH[mi]
                    src = ctxL[0:mw, h * 130 + mi * 65:h * 130 + mi * 65 + 65]
                    esrc = (eq0[es] if mi == 0 else eq1[es])[0:mw, g * 512:(g + 1) * 512]
                    w = [ev_ctxL[h][mi], st["ev_e0" + hv] if mi == 0 else st["ev_e1" + hv],
                         st["ev_e1" + hv]]
                    if h <= 1 and g == 1 and mi == 0:
                        w.append(ev_ctxL[H - 1][2])
                    if mi == 0:
                        if g == 0:
                            w = w + st["ev_scr"]
                        elif g == 2:
                            w.append(st["ev_ue"])
                    ev_am.append(op("tensor", w,
                                    lambda e, g=g, src=src, esrc=esrc, mi=mi, P=P: e.matmul(
                                        P[0:65, g * 512:(g + 1) * 512],
                                        src, esrc,
                                        start=(mi == 0), stop=False), ch="T"))
            st["ev_am01"] = ev_am
            return st

        def qs_am2(st, h):
            es = st["es"]; P = st["P"]
            ev_am = []
            for g in (1, 2, 3, 0):
                ev_am.append(op("tensor", [ev_ctxL[h][2], st["ev_e2"]],
                                lambda e, g=g, h=h, es=es, P=P: e.matmul(
                                    P[0:65, g * 512:(g + 1) * 512],
                                    ctxL2[0:10, h * 65:(h + 1) * 65],
                                    eq2[es][0:10, g * 512:(g + 1) * 512],
                                    start=False, stop=False), ch="T"))
                ev_am.append(op("tensor", [st["ev_ur"], ev_B_done],
                                lambda e, g=g, h=h, es=es, P=P: e.matmul(
                                    P[0:65, g * 512:(g + 1) * 512],
                                    B_rows[0:1, h * 65:(h + 1) * 65],
                                    u_row[es][0:1, g * 512:(g + 1) * 512],
                                    start=False, stop=True), ch="T"))
            ev_am = st["ev_am01"] + ev_am
            ev_u_free[es] = ev_am[-1]
            ev_eq_free[es] = ev_am[-1]
            st["ev_am"] = ev_am
            return st

        def qs_nev(st, h):
            es = st["es"]; P = st["P"]
            st["ev_nev"] = op("scalar", st["ev_am"] + [ev_num_free[es]],
                              lambda e, es=es, P=P: e.activation(
                                  num_sb[es][0:65, :], P[0:65, :], AF.Copy), ch="S")
            return st

        def qs_dT(st, h):
            es = st["es"]; Pb = st["Pb"]
            st["ev_dT"] = [op("tensor", [st["ev_nev"], evc["idb"]],
                              lambda e, cnk=cnk, es=es, Pb=Pb: e.transpose(
                                  Pb[0:128, 2048 + 2 * cnk:2048 + 2 * cnk + 1],
                                  num_sb[es][64:65, cnk * 128:(cnk + 1) * 128],
                                  c_idb[64:65, 64:65]), ch="T") for cnk in range(16)]
            return st

        def qs_rcp(st, h):
            nonlocal ev_rT_prev
            Pb = st["Pb"]
            st["ev_rcp"] = op("vector", st["ev_dT"] + [ev_rT_prev],
                              lowp(lambda e, Pb=Pb: e.reciprocal(
                                  rbT_b[0:128, 0:16], Pb[0:128, 2048:2080:2])), ch="V")
            ev_rT_prev = st["ev_rcp"]
            return st

        def qs_bc(st, h):
            P = st["P"]
            st["ev_bc"] = [op("tensor", [st["ev_rcp"], evc["idb"]],
                              lambda e, cnk=cnk, P=P: e.matmul(
                                  P[0:64, cnk * 128:(cnk + 1) * 128],
                                  bass.AP(G16, RQ + 14336 + cnk, [[81032, 128], [0, 64]]),
                                  c_idb[:, :],
                                  start=True, stop=True), ch="T") for cnk in range(16)]
            return st

        def qs_mul(st, h):
            nonlocal ev_stage_free
            es = st["es"]; P = st["P"]; pj = st["pj"]
            if es == 0:
                dsth = merged[0:64, pj * 2048:(pj + 1) * 2048]
                wex = []
            else:
                dsth = stage[0:64, 0:2048]
                wex = [ev_stage_free]
            ev_m = op("vector", st["ev_bc"] + wex + [st["ev_nev"]],
                      lambda e, dsth=dsth, es=es, P=P: e.tensor_mul(
                          dsth, num_sb[es][0:64, 0:2048], P[0:64, 0:2048]), ch="V")
            ev_merged.append(ev_m)
            ev_par_free[es] = ev_m
            ev_num_free[es] = ev_m
            out_ev = ev_m
            if es == 1:
                evd = dma("sync", [ev_m], merged[64:128, pj * 2048:(pj + 1) * 2048],
                          stage[0:64, 0:2048], stream="sST")
                ev_merged.append(evd)
                ev_stage_free = evd
                out_ev = evd
            return out_ev

        def qs_front_all(h):
            st = qs_dash(h)
            qs_dash2(st, h)
            qs_scan(st, h)
            qs_ur(st, h)
            st_front[h] = st
            return st

        prev_psum = None
        ev_ag1s = dma("sync", ev_ctx_ev, ag1_in[:, :], ctx_sbb, stream="sAG1")
        ev_cc1 = op("gpsimd", [ev_ag1s], lambda e: e.collective_compute(
            "AllGather", OP.bypass, replica_groups=groups,
            ins=[ag1_in.ap()], outs=[ag1_out.ap()]), ch="cc")
        qs_front_all(0)
        qs_front_all(1)
        ctxTb = g16p(0, 65, RQ, 4256)
        ev_ctb = [dma("sync", [ev_cc1, ev_ctx_last], ctxTb[:, g * 2128:(g + 1) * 2128],
                      ag1_out[g * 65:(g + 1) * 65, :], stream="sL") for g in range(2)]

        Bcols = g32(3152, 65, 16)
        ev_B = op("vector", ev_ctb, lambda e: e.tensor_reduce(
            Bcols, ctxTb[:, :].rearrange("p (h m) -> p h m", m=M),
            AX.X, OP.add), ch="V")
        B_rows = g16p(0, 1, RQ + 14352, 1040)
        ev_Brt = []
        ev_Br = []
        for h in range(H):
            w = [ev_B, evc["idf"], ev_ctx_last]
            if h % 4 == 0 and ev_Br:
                w.append(ev_Br[-1])
            evt = op("tensor", w,
                     lambda e, h=h: e.transpose(
                         pB[0:1, 1536 + (h % 4) * 65:1536 + (h % 4) * 65 + 65],
                         Bcols[0:65, h:h + 1], c_idf[0:65, 0:65]), ch="T")
            ev_Brt.append(evt)
            if h % 4 == 3:
                ev_Br.append(op("vector", ev_Brt[-4:],
                                lambda e, h=h: e.tensor_copy(
                                    B_rows[0:1, (h - 3) * 65:(h + 1) * 65],
                                    pB[0:1, 1536:1536 + 260]), ch="V"))
        ev_B_done = ev_Br[-1]

        ctxL = g16(RM, 128, 2080)
        ctxL2 = g16p(0, 16, RM + 2080, 1040)
        ev_ctxL = {}
        prev_t = ev_Br[-1]
        for h in range(H):
            g = h // 8
            hh = h % 8
            evs = []
            for mi, (m0, mw) in enumerate(MCH):
                evt = op("tensor", [ev_ctb[g], evc["idb"], prev_t],
                         lambda e, g=g, hh=hh, m0=m0, mw=mw: e.transpose(
                             pBb[0:mw, 3072:3072 + 65],
                             ctxTb[0:65, g * 2128 + hh * M + m0:g * 2128 + hh * M + m0 + mw],
                             c_idb[0:65, 0:65]), ch="T")
                if mi < 2:
                    dst = ctxL[0:mw, h * 130 + mi * 65:h * 130 + mi * 65 + 65]
                else:
                    dst = ctxL2[0:10, h * 65:(h + 1) * 65]
                eve = op("vector", [evt], lambda e, dst=dst, mw=mw: e.tensor_copy(
                    dst, pBb[0:mw, 3072:3072 + 65]), ch="V")
                evs.append(eve)
                prev_t = eve
            ev_ctxL[h] = evs

        qs_am01(st_front[0], 0)
        qs_am01(st_front[1], 1)
        qs_am2(st_front[0], 0)
        qs_nev(st_front[0], 0)
        qs_dT(st_front[0], 0)
        qs_rcp(st_front[0], 0)
        qs_bc(st_front[0], 0)
        prev_psum = qs_mul(st_front[0], 0)
        for h in range(2, H):
            st = qs_dash(h)
            qs_dash2(st, h)
            qs_am2(st_front[h - 1], h - 1)
            qs_scan(st, h)
            qs_nev(st_front[h - 1], h - 1)
            qs_dT(st_front[h - 1], h - 1)
            qs_ur(st, h)
            qs_rcp(st_front[h - 1], h - 1)
            qs_bc(st_front[h - 1], h - 1)
            qs_am01(st, h)
            prev_psum = qs_mul(st_front[h - 1], h - 1)
            st_front[h] = st
        st = st_front[H - 1]
        qs_am2(st, H - 1)
        qs_nev(st, H - 1)
        qs_dT(st, H - 1)
        qs_rcp(st, H - 1)
        qs_bc(st, H - 1)
        prev_psum = qs_mul(st, H - 1)

        bw_t = [Wb[:, j * 1024:(j + 1) * 1024] for j in range(KC)]
        ev_bw = [dma("gpsimd", [ev_ctx_last], bw_t[j], d_bwT[j * 128:(j + 1) * 128, :], stream="gW3")
                 for j in range(KC)]

        attn = g16(RQ + 8192, 128, 16384)
        x1 = g16(ZB, 128, 16384)
        xf = g16(ZC, 128, 16384)
        ffo = g16(0, 128, 16384)
        h_ext = g16p(0, 65, RQ + 6144, 2048)
        M_ext = g16p(0, 65, ZD, 1024)
        xbh_s = [g16(RQ + s * 2048, 128, 2048) for s in range(2)]
        ln_m = g16(RM, 128, 2048)
        ln_r = g16(RM + 2048, 128, 2048)
        ln_t = g16(ZD + 1024, 128, 2048)
        ftmp = g32(0, 128, 2048)
        m12 = g32(2048, 128, 16)
        m12b = g32(2064, 128, 16)
        ivT = g32(2080, 128, 64)
        muiv = g32(2144, 128, 64)
        mui2 = g32(2208, 128, 64)
        csum = g32(2272, 128, 8)
        sq2 = g16(RQ + 4096, 128, 2048)
        rT32 = g32(3296, 128, 16)
        r_rb = g16p(0, 1, RM + 2048, 2048)
        ivb = g16p(0, 128, ZD + 3072, 64)
        muivb = g16p(0, 128, ZD + 3136, 64)
        csb = g16p(0, 128, ZD + 3200, 8)
        _of = (RQ + 8192) // 2
        outst = [G16f[0:128, _of + s * 2048:_of + (s + 1) * 2048] for s in range(2)]
        t_ffw = G16[0:128, RQ + 16384:RQ + 17536]
        t_ones512 = G16[0:1, RQ + 17536:RQ + 18048]

        ev_attn = [None] * KC
        ev_womm = None
        wo_free = {0: None, 1: None}
        for och in range(KC):
            ph = pA if och % 2 == 0 else pB
            ev_mm = None
            for g in range(4):
                for j in range(KC):
                    w = [ev_wo[j]]
                    if g == 0 and j == 0:
                        w.append(wo_free[och % 2])
                    if och == 0 and g == 0 and j == 0:
                        w += ev_merged + [prev_psum]
                    ev_mm = op("tensor", w,
                               lambda e, ph=ph, g=g, j=j, och=och: e.matmul(
                                   ph[:, g * 512:(g + 1) * 512],
                                   wo_t[j][:, och * 128:(och + 1) * 128],
                                   merged[:, j * 2048 + g * 512:j * 2048 + (g + 1) * 512],
                                   start=(j == 0), stop=(j == KC - 1)), ch="T")
            ev_womm = ev_mm
            ev_attn[och] = op("scalar", [ev_mm, evc["bo"]],
                              lambda e, ph=ph, och=och: e.activation(
                                  attn[:, och * 2048:(och + 1) * 2048], ph[:, :],
                                  AF.Identity, bias=c_bo[:, och:och + 1]), ch="S")
            wo_free[och % 2] = ev_attn[och]

        def ln_stats(src_fn, ev_src, sq_slot, ev_sq_pre, ev_psA_pre, ev_psB_pre):
            ev_mm = None
            for j in range(KC):
                for g in range(4):
                    w = [ev_src[j], evc["lnones"]]
                    if j == 0 and g == 0:
                        w += ev_psA_pre
                    ev_mm = op("tensor", w,
                               lambda e, j=j, g=g: e.matmul(
                                   pA[:, g * 512:(g + 1) * 512],
                                   c_lnones[:, :], src_fn(j)[:, g * 512:(g + 1) * 512],
                                   start=(j == 0), stop=(j == KC - 1)), ch="T")
            ev_mb = op("vector", [ev_mm], lambda e: e.tensor_copy(ln_m[:, :], pA[:, 0:2048]), ch="V")
            ev_smm = None
            prev = list(ev_sq_pre)
            for j in range(KC):
                ev_sq = op("scalar", [ev_src[j]] + prev,
                           lambda e, j=j: e.activation(sq_slot[:, :], src_fn(j)[:, :], AF.Square),
                           ch="S")
                for g in range(4):
                    w = [ev_sq]
                    if j == 0 and g == 0:
                        w += ev_psB_pre
                    ev_smm = op("tensor", w,
                                lambda e, j=j, g=g: e.matmul(
                                    pB[:, g * 512:(g + 1) * 512],
                                    c_lnones[:, :], sq_slot[:, g * 512:(g + 1) * 512],
                                    start=(j == 0), stop=(j == KC - 1)), ch="T")
                prev = [ev_smm]
            ev_t1 = op("vector", [ev_smm, ev_mb], lambda e: e.tensor_copy(ftmp, pB[:, 0:2048]), ch="V")
            ev_m2 = op("scalar", [ev_mb] + prev, lambda e: e.activation(ln_t[:, :], ln_m[:, :], AF.Square), ch="S")
            ev_var = op("vector", [ev_t1, ev_m2], lambda e: e.tensor_sub(ftmp, ftmp, ln_t[:, :]), ch="V")
            ev_sd = op("scalar", [ev_var], lambda e: e.activation(ftmp, ftmp, AF.Sqrt, bias=np.float32(1e-5)), ch="S")
            ev_vT = [op("tensor", [ev_sd, evc["idf"], ev_t1],
                       lambda e, cnk=cnk: e.transpose(
                           pB[0:128, 1536 + cnk:1536 + cnk + 1],
                           G32[0:1, cnk * 128:(cnk + 1) * 128],
                           c_idf[0:1, 0:1]), ch="T") for cnk in range(16)]
            ev_rc = op("vector", ev_vT, lowp(lambda e: e.reciprocal(
                rT32[0:128, 0:16], pB[0:128, 1536:1552])), ch="V")
            ev_rT = [op("tensor", [ev_rc, evc["idf"]],
                       lambda e, cnk=cnk: e.transpose(
                           pB[0:1, cnk * 128:(cnk + 1) * 128],
                           rT32[0:128, cnk:cnk + 1],
                           c_idf[:, :]), ch="T") for cnk in range(16)]
            ev_re = op("vector", ev_rT, lambda e: e.tensor_copy(r_rb[0:1, :], pB[0:1, 0:2048]), ch="V")
            ev_bcr = [op("tensor", [ev_re, evc["ones"]],
                        lambda e, g=g: e.matmul(
                            pB[0:128, g * 512:(g + 1) * 512],
                            c_ones[0:1, 0:128],
                            r_rb[0:1, g * 512:(g + 1) * 512],
                            start=True, stop=True), ch="T") for g in range(4)]
            ev_r = ev_bcr[-1]
            return ev_mb, ev_r, ev_smm, ev_t1, ev_sd

        xbh_ev = [dma("gpsimd", [ev_stage_free], xbh_s[0][:, :], d_xT[0:128, 0:2048], stream="gE2"),
                  dma("gpsimd", [ev_stage_free], xbh_s[1][:, :], d_xT[128:256, 0:2048], stream="gE3")]
        ev_m2b, ev_r2, ev_ss2, ev_t1_2, ev_sd2 = ln_stats(
            lambda j: attn[:, j * 2048:(j + 1) * 2048], ev_attn, sq2,
            [ev_ctxL[H - 1][2]], [ev_attn[KC - 2], ev_attn[KC - 1]],
            [ev_attn[KC - 2], ev_attn[KC - 1]])
        ev_x1 = [None] * KC
        ev_lnt_prev = ev_r2
        for j in range(KC):
            sj = j % 2
            ev_t = op("vector", [ev_attn[j], ev_m2b, ev_lnt_prev, ev_womm],
                      lambda e, j=j: e.tensor_sub(
                          ln_t[:, :], attn[:, j * 2048:(j + 1) * 2048], ln_m[:, :]), ch="V")
            ev_t2 = op("vector", [ev_t, ev_r2],
                       lambda e: e.tensor_mul(ln_t[:, :], ln_t[:, :], pB[:, 0:2048]), ch="V")
            ev_x1[j] = op("vector", [ev_t2, xbh_ev[sj]],
                          lambda e, j=j, sj=sj: e.tensor_add(
                              x1[:, j * 2048:(j + 1) * 2048], ln_t[:, :], xbh_s[sj][:, :]),
                          ch="V")
            ev_lnt_prev = ev_x1[j]
            if j + 2 < KC:
                xbh_ev[sj] = dma("gpsimd", [ev_x1[j]], xbh_s[sj][:, :],
                                 d_xT[(j + 2) * 128:(j + 3) * 128, 0:2048],
                                 stream=["gE2", "gE3"][sj])

        ev_m3b, ev_r3, ev_ss3, ev_t1_3, ev_sd3 = ln_stats(
            lambda j: x1[:, j * 2048:(j + 1) * 2048], ev_x1, sq2,
            [ev_ss2, ev_sd2, ev_x1[-1]], [ev_m2b, ev_t1_2], [ev_t1_2, ev_x1[-1]])
        ev_xf = [None] * KC
        ev_m2c = [None] * KC
        ev_lnt_prev = ev_r3
        for j in range(KC):
            ev_t = op("vector", [ev_x1[j], ev_m3b, ev_ss3, ev_lnt_prev],
                      lambda e, j=j: e.tensor_sub(
                          ln_t[:, :], x1[:, j * 2048:(j + 1) * 2048], ln_m[:, :]), ch="V")
            ev_xf[j] = op("vector", [ev_t, ev_r3, ev_ctxL[H - 1][2], ev_merged[-1]],
                          lambda e, j=j: e.scalar_tensor_tensor(
                              xf[:, j * 2048:(j + 1) * 2048], ln_t[:, :], 1.0, pB[:, 0:2048],
                              OP.mult, OP.mult,
                              accum_out=m12[:, j:j + 1]), ch="V")
            ev_lnt_prev = ev_xf[j]
            ev_m2c[j] = op("scalar", [ev_xf[j]] + ([ev_ss3, ev_sd3] if j == 0 else [ev_m2c[j - 1]]),
                           lambda e, j=j: e.activation(
                               sq2[:, :], xf[:, j * 2048:(j + 1) * 2048], AF.Square,
                               accum_out=m12[:, 8 + j:8 + j + 1]), ch="S")

        ev_mscl = op("vector", ev_xf + ev_m2c, lambda e: e.tensor_scalar_mul(m12, m12, 1.0 / NF), ch="V")
        mtb = m12b.tensor.bitcast(BF16)
        mtb_ap = lambda c0, c1: bass.AP(mtb, 2 * 2064 + c0, [[2 * 3328, 128], [1, c1 - c0]])
        ev_mcb = op("vector", [ev_mscl], lambda e: e.tensor_copy(mtb_ap(0, 16), m12), ch="V")
        ev_iv0 = op("scalar", [evc["smT"], ev_m2c[-1]], lambda e: e.activation(ivT, c_smT[:, :], AF.Square), ch="S")
        ev_iv = op("vector", [ev_iv0], lambda e: e.reciprocal(ivT, ivT), ch="V")
        ev_mv = op("vector", [ev_iv, evc["muT"]], lambda e: e.tensor_mul(muiv, c_muT[:, :], ivT), ch="V")
        ev_m2v = op("vector", [ev_mv], lambda e: e.tensor_mul(mui2, c_muT[:, :], muiv), ch="V")
        ev_cs8 = op("vector", [ev_m2v], lambda e: e.tensor_reduce(
            csum, mui2.rearrange("p (e c) -> p e c", c=8), AX.X, OP.add), ch="V")
        ev_ivb = op("vector", [ev_iv, ev_ctxL[H - 1][2]], lambda e: e.tensor_copy(ivb, ivT), ch="V")
        ev_muivb = op("vector", [ev_mv], lambda e: e.tensor_copy(muivb, muiv), ch="V")
        ev_csb = op("vector", [ev_cs8], lambda e: e.tensor_copy(csb, csum), ch="V")
        ev_lA = None
        ev_lB = None
        for c in range(8):
            w = [ev_mcb, ev_ivb]
            if c == 0:
                w += [ev_t1_3, ev_xf[-1]]
            ev_lA = op("tensor", w, lambda e, c=c: e.matmul(
                pB[0:1, 1536:1544], mtb_ap(8 + c, 9 + c), ivb[:, c:64:8],
                start=(c == 0), stop=(c == 7)), ch="T")
        for c in range(8):
            ev_lB = op("tensor", [ev_mcb, ev_muivb], lambda e, c=c: e.matmul(
                pB[0:1, 1544:1552], mtb_ap(c, c + 1), muivb[:, c:64:8],
                start=(c == 0), stop=(c == 7)), ch="T")
        ev_lC = op("tensor", [ev_csb, evc["ones"]], lambda e: e.matmul(
            pB[0:1, 1552:1560], c_ones[:, 0:1], csb[:, 0:8],
            start=True, stop=True), ch="T")
        ev_l0 = op("vector", [ev_lA, ev_lB, ev_lC], lambda e: e.tensor_copy(
            t_loss[0:1, 8:32], pB[0:1, 1536:1560]), ch="V")
        ev_l1 = op("vector", [ev_l0], lambda e: e.scalar_tensor_tensor(
            t_loss[0:1, 0:8], t_loss[0:1, 16:24], -2.0, t_loss[0:1, 8:16],
            OP.mult, OP.add), ch="V")
        ev_ag3s = dma("sync", [ev_l1], ag3_in[:, :], t_loss[0:1, 0:8], stream="sAG2")
        ev_cc3 = op("gpsimd", [ev_ag3s], lambda e: e.collective_compute(
            "AllGather", OP.bypass, replica_groups=groups,
            ins=[ag3_in.ap()], outs=[ag3_out.ap()]), ch="cc")
        ev_l3l = dma("gpsimd", [ev_cc3], t_loss[0:1, 8:24],
                     bass.AP(ag3_out, 0, [[16, 1], [1, 16]]), stream="gE")
        ev_l2a = op("vector", [ev_l3l], lambda e: e.tensor_add(
            t_loss[0:1, 0:8], t_loss[0:1, 8:16], t_loss[0:1, 16:24]), ch="V")
        ev_l2 = op("vector", [ev_l2a], lambda e: e.tensor_add(
            t_loss[0:1, 0:8], t_loss[0:1, 0:8], t_loss[0:1, 24:32]), ch="V")
        ev_lmin = op("vector", [ev_l2], lambda e: e.tensor_reduce(
            t_sc[0:1, 0:1], t_loss[0:1, 0:8], AX.X, OP.min), ch="V")
        ev_msk = op("vector", [ev_lmin], lambda e: e.tensor_tensor(
            t_sc[0:1, 2:10], t_loss[0:1, 0:8],
            t_sc[0:1, 0:1].broadcast_to((1, 8)), OP.is_equal), ch="V")
        ev_mio = op("vector", [ev_msk, evc["iota"]], lambda e: e.tensor_mul(
            t_sc[0:1, 2:10], t_sc[0:1, 2:10], c_iota[0:1, :]), ch="V")
        ev_idxf = op("vector", [ev_mio], lambda e: e.tensor_reduce(
            t_sc[0:1, 1:2], t_sc[0:1, 2:10], AX.X, OP.add), ch="V")
        ev_idx = op("vector", [ev_idxf], lambda e: e.tensor_copy(
            t_idx[0:1, 0:1], t_sc[0:1, 1:2]), ch="V")

        wc_t = [Wa[:, j * 1024:(j + 1) * 1024] for j in range(KC)]
        idx_cell = {}

        def _load_idx(e):
            idx_cell["v"] = e.value_load(t_idx[0:1, 0:1])
            return None
        op("gpsimd", [ev_idx, ev_womm], _load_idx)
        ev_w_dw = op("gpsimd", [("gE", CH["gE"][1])], lambda e: e.dma_start(
            out=bass.AP(G16, RQ + 16384, [[81032, 128], [64, 8], [1, 64]]),
            in_=bass.AP(d_dw, idx_cell["v"] * (D * RK), [[RK, 128], [128 * RK, 8], [1, RK]])),
            ch="gE", k=16)
        ev_w_db = op("gpsimd", [("gE2", CH["gE2"][1])], lambda e: e.dma_start(
            out=t_bdf, in_=bass.AP(d_db, idx_cell["v"] * RK, [[RK, 1], [1, RK]])),
            ch="gE2", k=16)
        ev_w_uw = op("gpsimd", [("gE3", CH["gE3"][1])], lambda e: e.dma_start(
            out=bass.AP(G16, RQ + 16384 + 512, [[81032, 128], [65, 8], [1, 64]]),
            in_=bass.AP(d_uwT, idx_cell["v"] * (D * RK), [[RK, 128], [128 * RK, 8], [1, RK]])),
            ch="gE3", k=16)
        ev_w_ub = op("gpsimd", [("gE2", CH["gE2"][1])], lambda e: e.dma_start(
            out=bass.AP(G16, RQ + 16384 + 1032, [[81032, 128], [1, 8]]),
            in_=bass.AP(d_ubT, idx_cell["v"], [[NE, 128], [128 * NE, 8]]),
            allow_slow_non_contiguous=True), ch="gE2", k=16)
        ev_w_cb = op("gpsimd", [("gE3", CH["gE3"][1])], lambda e: e.dma_start(
            out=t_bcrow, in_=bass.AP(d_cb, idx_cell["v"] * D, [[D, 1], [1, D]])),
            ch="gE3", k=16)
        ev_w_cw1 = op("gpsimd", [("gE", CH["gE"][1])], lambda e: e.dma_start(
            out=bass.AP(Wa, 0, [[8192, 128], [1024, 4], [1, 1024]]),
            in_=bass.AP(d_cwT, idx_cell["v"] * (D * D), [[D, 128], [128 * D, 4], [1, D]])),
            ch="gE", k=16)
        ev_w_cw2 = op("gpsimd", [("gE2", CH["gE2"][1])], lambda e: e.dma_start(
            out=bass.AP(Wa, 4096, [[8192, 128], [1024, 4], [1, 1024]]),
            in_=bass.AP(d_cwT, idx_cell["v"] * (D * D) + 4 * 128 * D, [[D, 128], [128 * D, 4], [1, D]])),
            ch="gE2", k=16)
        ev_ffw = [ev_w_cw1, ev_w_cw2]
        ev_bdc = op("vector", [ev_w_db], lambda e: e.tensor_copy(t_bdrow[0:1, :], t_bdf), ch="V")
        ev_bu = op("vector", [ev_w_ub], lambda e: e.tensor_copy(
            bass.AP(G16, RQ + 16384 + 512 + 64, [[81032, 128], [65, 8]]),
            bass.AP(G16, RQ + 16384 + 1032, [[81032, 128], [1, 8]])), ch="V")

        ev_base = [None] * KC
        bank_free = {0: [None] * 3, 1: [None] * 3}
        for och in range(KC):
            ph = pA if och % 2 == 0 else pB
            ev_ev = None
            for g in range(4):
                gb = g % 3
                ev_mm = None
                for j in range(KC):
                    w = [ev_bw[j], ev_xf[j]]
                    if j == 0:
                        w.append(bank_free[och % 2][gb])
                    if och == 1 and g == 0 and j == 0:
                        w.append(ev_xf[-1])
                    if och == 0 and g == 0 and j == 0:
                        w += [ev_m3b, ev_t1_3]
                    ev_mm = op("tensor", w,
                               lambda e, ph=ph, g=g, gb=gb, j=j, och=och: e.matmul(
                                   ph[:, gb * 512:(gb + 1) * 512],
                                   bw_t[j][:, och * 128:(och + 1) * 128],
                                   xf[:, j * 2048 + g * 512:j * 2048 + (g + 1) * 512],
                                   start=(j == 0), stop=(j == KC - 1)), ch="T")
                ev_ev = op("scalar", [ev_mm],
                           lambda e, ph=ph, g=g, gb=gb, och=och: e.activation(
                               ffo[:, och * 2048 + g * 512:och * 2048 + (g + 1) * 512],
                               ph[:, gb * 512:(gb + 1) * 512], AF.Copy), ch="S")
                bank_free[och % 2][gb] = ev_ev
            ev_base[och] = ev_ev

        ev_hones = op("vector", [ev_stage_free, ev_eq_free[0], ev_eq_free[1]],
                      lambda e: e.memset(h_ext[64:65, :], 1.0), ch="V")
        ev_or = op("vector", [ev_hones], lambda e: e.memset(t_ones512[0:1, :], 1.0), ch="V")
        ev_hmm = []
        for g in range(4):
            evh = None
            for j in range(KC):
                w = [ev_w_dw, ev_xf[j]]
                if g == 0 and j == 0:
                    w += [ev_lA, ev_m3b, ev_base[KC - 2]]
                evh = op("tensor", w,
                         lambda e, g=g, j=j: e.matmul(
                             pA[0:64, g * 512:(g + 1) * 512],
                             t_ffw[:, j * 64:(j + 1) * 64],
                             xf[:, j * 2048 + g * 512:j * 2048 + (g + 1) * 512],
                             start=(j == 0), stop=False), ch="T")
            evh = op("tensor", [ev_w_db, ev_or, ev_bdc], lambda e, g=g: e.matmul(
                pA[0:64, g * 512:(g + 1) * 512],
                t_bdrow[0:1, :], t_ones512[0:1, :],
                start=False, stop=True), ch="T")
            ev_hmm.append(evh)
        ev_hext = op("scalar", ev_hmm + [ev_m2c[-1]], lambda e: e.activation(
            h_ext[0:64, :], pA[0:64, 0:2048], AF.Relu), ch="S")

        ev_Mmm = None
        for j in range(KC):
            for g2 in range(2):
                w = [ev_w_uw, ev_w_cw1, ev_w_cw2, ev_bu]
                if j == 0 and g2 == 0:
                    w += [ev_l2, ev_lC, ev_l0, ev_base[KC - 1]]
                ev_Mmm = op("tensor", w,
                            lambda e, j=j, g2=g2: e.matmul(
                                pB[0:65, g2 * 512:(g2 + 1) * 512],
                                t_ffw[:, 512 + j * 65:512 + (j + 1) * 65],
                                wc_t[j][:, g2 * 512:(g2 + 1) * 512],
                                start=(j == 0), stop=(j == KC - 1)), ch="T")
        ev_Me1 = op("vector", [ev_Mmm, ev_hones], lambda e: e.tensor_copy(
            M_ext[0:64, :], pB[0:64, 0:1024]), ch="V")
        ev_Me2 = op("vector", [ev_Mmm, ev_w_cb], lambda e: e.scalar_tensor_tensor(
            M_ext[64:65, :], pB[64:65, 0:1024], 1.0, t_bcrow[0:1, :],
            OP.mult, OP.add), ch="V")

        ev_ffo = [None] * KC
        yb_free = {0: None, 1: None}
        for och in range(KC):
            ph = pA if och % 2 == 0 else pB
            ev_mm = None
            for g in range(4):
                w = [ev_hext, ev_Me1, ev_Me2]
                if g == 0:
                    w.append(yb_free[och % 2])
                if och == 0 and g == 0:
                    w += [ev_lA, ev_lB, ev_lC, ev_l0]
                ev_mm = op("tensor", w,
                           lambda e, ph=ph, g=g, och=och: e.matmul(
                               ph[:, g * 512:(g + 1) * 512],
                               M_ext[0:65, och * 128:(och + 1) * 128],
                               h_ext[0:65, g * 512:(g + 1) * 512],
                               start=(g >= 0), stop=True), ch="T")
            ev_ffo[och] = op("vector", [ev_mm, ev_base[och], evc["bb"]],
                             lambda e, ph=ph, och=och: e.scalar_tensor_tensor(
                                 ffo[:, och * 2048:(och + 1) * 2048], ph[:, :],
                                 c_bb[:, och:och + 1],
                                 ffo[:, och * 2048:(och + 1) * 2048],
                                 OP.add, OP.add), ch="V")
            yb_free[och % 2] = ev_ffo[och]

        sqf = g16(ZC, 128, 2048)
        ev_m4b, ev_r4, ev_ss4, ev_t1_4, ev_sd4 = ln_stats(
            lambda j: ffo[:, j * 2048:(j + 1) * 2048], ev_ffo, sqf,
            [ev_ffo[-1], ev_ss3, ev_sd3], [ev_ffo[KC - 2], ev_ffo[KC - 1]],
            [ev_ffo[KC - 2], ev_ffo[KC - 1]])
        ev_out = []
        ost_free = [None, None]
        ev_lnt_prev = ev_r4
        for j in range(KC):
            sj = j % 2
            ev_t = op("vector", [ev_ffo[j], ev_m4b, ev_ss4, ev_lnt_prev],
                      lambda e, j=j: e.tensor_sub(
                          ln_t[:, :], ffo[:, j * 2048:(j + 1) * 2048], ln_m[:, :]), ch="V")
            ev_t2 = op("vector", [ev_t, ev_r4],
                       lambda e: e.tensor_mul(ln_t[:, :], ln_t[:, :], pB[:, 0:2048]), ch="V")
            ev_o = op("vector", [ev_t2, ev_x1[j], ost_free[sj]],
                      lambda e, j=j, sj=sj: e.tensor_add(
                          outst[sj], ln_t[:, :], x1[:, j * 2048:(j + 1) * 2048]), ch="V")
            ev_lnt_prev = ev_o
            ev_d = dma("sync", [ev_o], d_out[j * 128:(j + 1) * 128, :], outst[sj],
                       stream=["sO0", "sO1", "sO2", "sO3"][j % 4])
            ost_free[sj] = ev_d
            ev_out.append(ev_d)
        op("sync", ev_out + [(nm, CH[nm][1]) for nm in
                             ("sQ", "sK", "sL", "sAG1", "sAG2", "sST",
                              "sO0", "sO1", "sO2", "sO3", "sC", "qd")],
           lambda e: None)
        op("gpsimd", [(nm, CH[nm][1]) for nm in
                      ("gC", "gW", "gW2", "gW3", "gX", "gX2", "gX3", "gX4",
                       "gE", "gE2", "gE3", "qg", "cc")],
           lambda e: None)
        op("scalar", [("qs", CH["qs"][1])], lambda e: None)

        import kernel as _K
        _K._PROG_SIZES = {k: len(v) for k, v in PROG.items()}
        if limit is not None:
            tgt, frac = limit
            for k in PROG:
                if tgt == "all" or k == tgt:
                    n = max(1, int(len(PROG[k]) * frac))
                    PROG[k] = PROG[k][:n]
        with nc.Block() as block:
            @block.sync
            def _(eng):
                for fn in PROG["sync"]:
                    fn(eng)

            @block.gpsimd
            def _(eng):
                for fn in PROG["gpsimd"]:
                    fn(eng)

            @block.tensor
            def _(eng):
                for fn in PROG["tensor"]:
                    fn(eng)

            @block.vector
            def _(eng):
                for fn in PROG["vector"]:
                    fn(eng)

            @block.scalar
            def _(eng):
                for fn in PROG["scalar"]:
                    fn(eng)

    return nc


def _consts():
    c = {}
    c["c_ones"] = np.ones((128, 128), np.float32)
    c["c_lnones"] = np.full((128, 128), 1.0 / 1024.0, np.float32)
    c["c_ident"] = np.eye(128, dtype=np.float32)
    selq = np.zeros((D, H), np.float32)
    selq[np.arange(D), np.arange(D) // 64] = 0.5
    c["c_selq"] = selq
    selk = np.zeros((512, HK), np.float32)
    selk[np.arange(512), np.arange(512) // 64] = 0.5
    c["c_selk"] = selk
    c["c_iota"] = np.arange(8, dtype=np.float32)[None, :].copy()
    return c


def _ec_layout(a):
    return np.ascontiguousarray(a.reshape(8, 8, 128).transpose(2, 0, 1).reshape(128, 64))


def kernel(**inputs):
    x = np.asarray(inputs["L0_attn_x"], np.float32)
    Wq = np.asarray(inputs["L0_attn_Wq"], np.float32)
    Wk = np.asarray(inputs["L0_attn_Wk"], np.float32)
    Wv = np.asarray(inputs["L0_attn_Wv"], np.float32)
    Wo = np.asarray(inputs["L0_attn_Wo"], np.float32)
    bo = np.asarray(inputs["L0_attn_bo"], np.float32)
    proj = np.asarray(inputs["L0_attn_proj"], np.float32)
    bW = np.asarray(inputs["L0_ffn_base_W"], np.float32)
    bb = np.asarray(inputs["L0_ffn_base_b"], np.float32)
    dmu = np.asarray(inputs["L0_ffn_disc_mu"], np.float32)
    dsg = np.asarray(inputs["L0_ffn_disc_sigma"], np.float32)
    dW = np.asarray(inputs["L0_ffn_down_W"], np.float32)
    db = np.asarray(inputs["L0_ffn_down_b"], np.float32)
    uW = np.asarray(inputs["L0_ffn_up_W"], np.float32)
    ub = np.asarray(inputs["L0_ffn_up_b"], np.float32)
    cW = np.asarray(inputs["L0_ffn_conv_W"], np.float32)
    cb = np.asarray(inputs["L0_ffn_conv_b"], np.float32)

    cst = _consts()
    projT2 = np.ascontiguousarray(np.vstack([proj.T, proj.T]))
    base_wT = np.ascontiguousarray(bW.T)
    up_wT = np.ascontiguousarray(uW.transpose(0, 2, 1))
    conv_wT = np.ascontiguousarray(cW.transpose(0, 2, 1))
    up_bT = np.ascontiguousarray(ub.T)
    bo_cols = np.ascontiguousarray(bo.reshape(8, 128).T)
    bb_cols = np.ascontiguousarray(bb.reshape(8, 128).T)
    smT = _ec_layout(dsg)
    muT = _ec_layout(dmu)

    nc = build_nc(4)
    in_maps = []
    for c in range(8):
        b, p = c // 2, c % 2
        xb = x[b]
        rolled = np.concatenate([xb[p * 2048:(p + 1) * 2048], xb[(1 - p) * 2048:(2 - p) * 2048]], 0)
        in_maps.append({
            "xT": np.ascontiguousarray(rolled.T),
            "wq": Wq,
            "wk": np.ascontiguousarray(Wk[:, p * 512:(p + 1) * 512]),
            "wv": np.ascontiguousarray(Wv[:, p * 512:(p + 1) * 512]),
            "wo": Wo,
            "bo_cols": bo_cols,
            "projT2": projT2,
            "base_wT": base_wT,
            "baseb_cols": bb_cols,
            "sigmaT_ec": smT,
            "muT_ec": muT,
            "down_w": dW,
            "down_b": db,
            "up_wT": up_wT,
            "up_bT": up_bT,
            "conv_wT": conv_wT,
            "conv_b": cb,
            "c_iota": cst["c_iota"],
            "c_ones": cst["c_ones"],
            "c_lnones": cst["c_lnones"],
            "c_ident": cst["c_ident"],
            "c_selq": cst["c_selq"],
            "c_selk": cst["c_selk"],
        })
    res = run_bass_kernel_spmd(nc, in_maps, core_ids=list(range(8)))
    out = np.empty((4, 4096, 1024), np.float32)
    for c in range(8):
        b, p = c // 2, c % 2
        out[b, p * 2048:(p + 1) * 2048] = np.ascontiguousarray(res.results[c]["outT"].T)
    return out


# version 28
# speedup vs baseline: 1.0478x; 1.0242x over previous
import numpy as np
from collections import defaultdict
from contextlib import ExitStack

import concourse.bass as bass
import concourse.mybir as mybir
from concourse.bass_utils import run_bass_kernel_spmd

F32 = mybir.dt.float32
BF16 = mybir.dt.bfloat16
I32 = mybir.dt.int32
AF = mybir.ActivationFunctionType
OP = mybir.AluOpType
AX = mybir.AxisListType

D = 1024
NF = 4096
NT = 2048
H = 16
HK = 8
M = 266
MCH = [(0, 128), (128, 128), (256, 10)]
EPS = 1e-4
DN = float(64.0 ** -0.25)
LNEPS = float(np.log(EPS))
NE = 8
RK = 64
KC = 8


def build_nc(npairs=4, limit=None):
    nc = bass.Bass("TRN2", target_bir_lowering=False)

    d_xT = nc.dram_tensor("xT", [D, NF], F32, kind="ExternalInput")
    d_wq = nc.dram_tensor("wq", [D, D], F32, kind="ExternalInput")
    d_wk = nc.dram_tensor("wk", [D, 512], F32, kind="ExternalInput")
    d_wv = nc.dram_tensor("wv", [D, 512], F32, kind="ExternalInput")
    d_wo = nc.dram_tensor("wo", [D, D], F32, kind="ExternalInput")
    d_bo = nc.dram_tensor("bo_cols", [128, 8], F32, kind="ExternalInput")
    d_projT = nc.dram_tensor("projT2", [128, M], F32, kind="ExternalInput")
    d_bwT = nc.dram_tensor("base_wT", [D, D], F32, kind="ExternalInput")
    d_bb = nc.dram_tensor("baseb_cols", [128, 8], F32, kind="ExternalInput")
    d_smT = nc.dram_tensor("sigmaT_ec", [128, 64], F32, kind="ExternalInput")
    d_muT = nc.dram_tensor("muT_ec", [128, 64], F32, kind="ExternalInput")
    d_dw = nc.dram_tensor("down_w", [NE, D, RK], F32, kind="ExternalInput")
    d_db = nc.dram_tensor("down_b", [NE, RK], F32, kind="ExternalInput")
    d_uwT = nc.dram_tensor("up_wT", [NE, D, RK], F32, kind="ExternalInput")
    d_ubT = nc.dram_tensor("up_bT", [D, NE], F32, kind="ExternalInput")
    d_cwT = nc.dram_tensor("conv_wT", [NE, D, D], F32, kind="ExternalInput")
    d_cb = nc.dram_tensor("conv_b", [NE, D], F32, kind="ExternalInput")
    d_ones = nc.dram_tensor("c_ones", [128, 128], F32, kind="ExternalInput")
    d_lnones = nc.dram_tensor("c_lnones", [128, 128], F32, kind="ExternalInput")
    d_ident = nc.dram_tensor("c_ident", [128, 128], F32, kind="ExternalInput")
    d_selq = nc.dram_tensor("c_selq", [D, H], F32, kind="ExternalInput")
    d_selk = nc.dram_tensor("c_selk", [512, HK], F32, kind="ExternalInput")
    d_iota = nc.dram_tensor("c_iota", [1, 8], F32, kind="ExternalInput")

    d_out = nc.dram_tensor("outT", [D, NT], F32, kind="ExternalOutput")
    ag1_in = nc.dram_tensor("ag1_in", [65, HK * M], BF16)
    ag1_out = nc.dram_tensor("ag1_out", [130, HK * M], BF16)
    ag2_in = nc.dram_tensor("ag2_in", [128, 16], F32)
    ag2_out = nc.dram_tensor("ag2_out", [256, 16], F32)
    ag3_in = nc.dram_tensor("ag3_in", [1, 8], F32)
    ag3_out = nc.dram_tensor("ag3_out", [2, 8], F32)
    groups = [[2 * i, 2 * i + 1] for i in range(npairs)]

    with ExitStack() as cx:
        en = cx.enter_context
        G16 = en(nc.sbuf_tensor("G16", [128, 81032], BF16))
        G32 = en(nc.sbuf_tensor("G32", [128, 3328], F32))
        Wa = en(nc.sbuf_tensor("W16a", [128, 8192], BF16))
        Wb = en(nc.sbuf_tensor("W16b", [128, 8192], BF16))
        c_ones = en(nc.sbuf_tensor("cb_ones", [128, 128], BF16))
        c_lnones = en(nc.sbuf_tensor("cb_lnones", [128, 128], BF16))
        c_idb = en(nc.sbuf_tensor("cb_id", [128, 128], BF16))
        c_idf = en(nc.sbuf_tensor("cf_id", [128, 128], F32))
        c_projb = en(nc.sbuf_tensor("cb_proj", [128, M], BF16))
        c_selq = en(nc.sbuf_tensor("cb_selq", [128, 128], BF16))
        c_selk = en(nc.sbuf_tensor("cb_selk", [128, 32], BF16))
        c_bo = en(nc.sbuf_tensor("cf_bo", [128, 8], F32))
        c_bb = en(nc.sbuf_tensor("cf_bb", [128, 8], F32))
        c_smT = en(nc.sbuf_tensor("cf_smT", [128, 64], F32))
        c_muT = en(nc.sbuf_tensor("cf_muT", [128, 64], F32))
        c_iota = en(nc.sbuf_tensor("cf_iota", [1, 8], F32))
        t_sc = en(nc.sbuf_tensor("t_sc", [128, 16], F32))
        t_scb = en(nc.sbuf_tensor("t_scb", [1, 16], BF16))
        t_loss = en(nc.sbuf_tensor("t_loss", [1, 32], F32))
        t_idx = en(nc.sbuf_tensor("t_idx", [1, 2], I32))
        t_bdrow = en(nc.sbuf_tensor("t_bdrow", [1, RK], BF16))
        pA = en(nc.psum_tensor("pA", [128, 2048], F32))
        pB = en(nc.psum_tensor("pB", [128, 2048], F32))
        pBb = pB.bitcast(BF16)
        pAb = pA.bitcast(BF16)
        G16f = G16.bitcast(F32)

        CH = {}
        for nm in ("T", "V", "S", "G", "qd", "qg", "qs", "cc",
                   "gC", "gW", "gW2", "gW3", "gX", "gX2", "gX3", "gX4",
                   "gE", "gE2", "gE3", "sO2", "sO3", "sQ", "sK", "sL",
                   "sAG1", "sAG2", "sST", "sO0", "sO1", "sC"):
            CH[nm] = [en(nc.semaphore("ch_" + nm)), 0]
        PROG = defaultdict(list)
        WAITED = defaultdict(dict)

        def op(eng, waits, build, ch=None, k=1):
            w2 = []
            for w in (waits or []):
                if w is None:
                    continue
                chn, v = w
                if v > WAITED[eng].get(chn, 0):
                    WAITED[eng][chn] = v
                    w2.append((chn, v))
            evout = None
            if ch is not None:
                c = CH[ch]
                c[1] += k
                evout = (ch, c[1])

            def thunk(engine, w2=w2, build=build, ch=ch, k=k):
                for chn, v in w2:
                    engine.wait_ge(CH[chn][0], v)
                ins = build(engine)
                if ch is not None and ins is not None:
                    ins.then_inc(CH[ch][0], k)
            PROG[eng].append(thunk)
            return evout

        QCH = {"sync": "qd", "gpsimd": "qg", "scalar": "qs"}

        def lowp(build):
            def b2(e, build=build):
                with nc.allow_low_precision(reason="bf16 operand rounding is intended"):
                    return build(e)
            return b2

        def dma(eng, waits, out, in_, stream=None):
            ch = stream or QCH[eng]
            pre = CH[ch][1]
            return op(eng, (waits or []) + ([(ch, pre)] if pre else []),
                      lambda e, o=out, i=in_: e.dma_start(out=o, in_=i),
                      ch=ch, k=16)

        def g16(off, parts, width):
            return G16[0:parts, off:off + width]

        def g16p(p0, p1, off, width):
            return G16[p0:p1, off:off + width]

        def g32(off, parts, width):
            return G32[0:parts, off:off + width]

        RS = 0
        RX = 19592
        RQ = 52360
        RM = 76936
        t_bcrow = g32(2280, 1, 1024)
        t_bdf = g32(1024, 1, RK)

        evc = {}
        evc["ones"] = dma("gpsimd", [], c_ones[:, :], d_ones[:, :], stream="gC")
        evc["lnones"] = dma("gpsimd", [], c_lnones[:, :], d_lnones[:, :], stream="gC")
        evc["idb"] = dma("gpsimd", [], c_idb[:, :], d_ident[:, :], stream="gC")
        evc["idf"] = dma("sync", [], c_idf[:, :], d_ident[:, :], stream="sC")
        evc["projb"] = dma("gpsimd", [], c_projb[:, :], d_projT[:, :], stream="gC")
        evc["selq"] = dma("gpsimd", [], c_selq[:, :],
                          bass.AP(d_selq, 0, [[H, 128], [128 * H, KC], [1, H]]), stream="gC")
        evc["selk"] = dma("gpsimd", [], c_selk[:, :],
                          bass.AP(d_selk, 0, [[HK, 128], [128 * HK, 4], [1, HK]]), stream="gC")
        evc["bo"] = dma("sync", [], c_bo[:, :], d_bo[:, :], stream="sC")
        evc["bb"] = dma("sync", [], c_bb[:, :], d_bb[:, :], stream="sC")
        evc["smT"] = dma("sync", [], c_smT[:, :], d_smT[:, :], stream="sC")
        evc["muT"] = dma("sync", [], c_muT[:, :], d_muT[:, :], stream="sC")
        evc["iota"] = dma("sync", [], c_iota[:, :], d_iota[:, :], stream="sC")

        wq_t = [Wa[:, j * 1024:(j + 1) * 1024] for j in range(KC)]
        wk_t = [Wb[:, j * 512:(j + 1) * 512] for j in range(KC)]
        wv_t = [Wb[:, 4096 + j * 512:4096 + (j + 1) * 512] for j in range(KC)]
        ev_wq = [dma("gpsimd", [], wq_t[j], d_wq[j * 128:(j + 1) * 128, :], stream="gW") for j in range(KC)]
        ev_wk = [dma("gpsimd", [], wk_t[j], d_wk[j * 128:(j + 1) * 128, :], stream="gW2") for j in range(KC)]
        ev_wv = [dma("gpsimd", [], wv_t[j], d_wv[j * 128:(j + 1) * 128, :], stream="gW3") for j in range(KC)]

        ZB = 16384
        ZC = 32768
        ZD = 49152
        xb_s = [g16(s * 8192, 128, 8192) for s in range(2)]
        xsq_s = [g16(ZD + s * 1024, 128, 1024) for s in range(2)]

        def xln_ap(j, c0, w):
            if c0 < 2048:
                return g16(ZC + j * 2048 + c0, 128, w)
            return g16(ZB + j * 2048 + (c0 - 2048), 128, w)

        m_b2 = [g16(RM + s_ * 1024, 128, 1024) for s_ in range(2)]
        msq_b2 = [g16(RM + 2048 + s_ * 1024, 128, 1024) for s_ in range(2)]
        r_rbp2 = [g16p(0, 1, RM + 2048 + s_ * 1024, 1024) for s_ in range(2)]
        tmp1_2 = [g32(s_ * 1024, 128, 1024) for s_ in range(2)]

        rT32p = g32(3296, 128, 16)
        r_rbp = g16p(0, 1, RM + 1024, 1024)
        ev_rT_p1 = None
        ev_stgf = [[], []]
        mr_free = [None, None]
        ev_xln = [[None] * 4 for _ in range(KC)]
        slot_free = [None, None]
        xsq_free = [None, None]
        mr_free = [None, None]
        ev_mb_last = None
        ev_t1_last = None
        for blk in range(4):
            s = blk % 2
            tb = blk * 1024
            stg0 = 8192 if blk % 2 == 0 else (RQ + 8192) // 2
            evxr = [dma("sync", list(ev_stgf[blk % 2]),
                        G16f[0:128, stg0 + j * 1024:stg0 + (j + 1) * 1024],
                        d_xT[j * 128:(j + 1) * 128, tb:tb + 1024],
                        stream=["sQ", "sK", "sO2", "sO3"][j % 4]) for j in range(KC)]
            evx = []
            for j in range(KC):
                src = G16f[0:128, stg0 + j * 1024:stg0 + (j + 1) * 1024]
                dst = xb_s[s][:, j * 1024:(j + 1) * 1024]
                if j % 2 == 0:
                    evx.append(op("scalar", [evxr[j], slot_free[s]],
                                  lambda e, src=src, dst=dst: e.activation(
                                      dst, src, AF.Copy), ch="S"))
                else:
                    evx.append(op("vector", [evxr[j], slot_free[s]],
                                  lambda e, src=src, dst=dst: e.tensor_copy(dst, src), ch="V"))
            ev_stgf[blk % 2] = [evx[-2], evx[-1]]
            pm = pA[:, s * 1024:(s + 1) * 1024]
            pq = pB[:, s * 1024:(s + 1) * 1024]
            m_b = m_b2[s]
            msq_b = msq_b2[s]
            r_rbp = r_rbp2[s]
            tmp1 = tmp1_2[s]
            ev_mm = None
            for j in range(KC):
                for g2 in range(2):
                    ev_mm = op("tensor", [evx[j], evc["lnones"]],
                               lambda e, pm=pm, s=s, j=j, g2=g2: e.matmul(
                                   pm[:, g2 * 512:(g2 + 1) * 512],
                                   c_lnones[:, :],
                                   xb_s[s][:, j * 1024 + g2 * 512:j * 1024 + (g2 + 1) * 512],
                                   start=(j == 0), stop=(j == KC - 1)), ch="T")
            ev_mb = op("vector", [ev_mm, mr_free[s]], lambda e, pm=pm, m_b=m_b: e.tensor_copy(m_b[:, :], pm), ch="V")
            ev_mb_last = ev_mb
            ev_smm = None
            for j in range(KC):
                sj = j % 2
                ev_sq = op("scalar", [evx[j], xsq_free[sj]],
                           lambda e, s=s, j=j, sj=sj: e.activation(
                               xsq_s[sj][:, :], xb_s[s][:, j * 1024:(j + 1) * 1024],
                               AF.Square), ch="S")
                for g2 in range(2):
                    ev_smm = op("tensor", [ev_sq, ev_mm if j == 0 else None],
                                lambda e, pq=pq, j=j, sj=sj, g2=g2: e.matmul(
                                    pq[:, g2 * 512:(g2 + 1) * 512],
                                    c_lnones[:, :],
                                    xsq_s[sj][:, g2 * 512:(g2 + 1) * 512],
                                    start=(j == 0), stop=(j == KC - 1)), ch="T")
                xsq_free[sj] = ev_smm
            ev_t1 = op("vector", [ev_smm], lambda e, pq=pq, tmp1=tmp1: e.tensor_copy(tmp1, pq), ch="V")
            ev_t1_last = ev_t1
            ev_m2 = op("scalar", [ev_mb, mr_free[s]], lambda e, msq_b=msq_b, m_b=m_b: e.activation(msq_b[:, :], m_b[:, :], AF.Square), ch="S")
            ev_var = op("vector", [ev_t1, ev_m2], lambda e, tmp1=tmp1, msq_b=msq_b: e.tensor_sub(tmp1, tmp1, msq_b[:, :]), ch="V")
            ev_sd = op("scalar", [ev_var], lambda e, tmp1=tmp1: e.activation(tmp1, tmp1, AF.Sqrt, bias=np.float32(1e-5)), ch="S")
            ev_vT = [op("tensor", [ev_sd, evc["idf"], ev_mb],
                        lambda e, cnk=cnk, s=s: e.transpose(
                            pA[0:128, s * 1024 + cnk:s * 1024 + cnk + 1],
                            G32[0:1, s * 1024 + cnk * 128:s * 1024 + (cnk + 1) * 128],
                            c_idf[0:1, 0:1]), ch="T") for cnk in range(8)]
            ev_rc = op("vector", ev_vT, lowp(lambda e, s=s: e.reciprocal(
                rT32p[0:128, s * 8:s * 8 + 8], pA[0:128, s * 1024:s * 1024 + 8])), ch="V")
            ev_rT = [op("tensor", [ev_rc, evc["idf"]],
                        lambda e, cnk=cnk, s=s: e.transpose(
                            pA[0:1, s * 1024 + cnk * 128:s * 1024 + (cnk + 1) * 128],
                            rT32p[0:128, s * 8 + cnk:s * 8 + cnk + 1],
                            c_idf[:, :]), ch="T") for cnk in range(8)]
            ev_re = op("vector", ev_rT + [ev_var], lambda e, s=s, r_rbp=r_rbp: e.tensor_copy(
                r_rbp[0:1, :], pA[0:1, s * 1024:s * 1024 + 1024]), ch="V")
            ev_bcr = [op("tensor", [ev_re, evc["ones"]],
                         lambda e, g=g, s=s, r_rbp=r_rbp: e.matmul(
                             pA[0:128, s * 1024 + g * 512:s * 1024 + (g + 1) * 512],
                             c_ones[0:1, 0:128],
                             r_rbp[0:1, g * 512:(g + 1) * 512],
                             start=True, stop=True), ch="T") for g in range(2)]
            ev_r = ev_bcr[-1]
            last_x = None
            for j in range(KC):
                sj = j % 2
                ev_t = op("vector", [ev_mb, evx[j], xsq_free[sj]],
                          lambda e, s=s, j=j, sj=sj, m_b=m_b: e.tensor_sub(
                              xsq_s[sj][:, :], xb_s[s][:, j * 1024:(j + 1) * 1024], m_b[:, :]),
                          ch="V")
                ev_x = op("vector", [ev_t, ev_r],
                          lambda e, j=j, tb=tb, sj=sj, s=s: e.tensor_mul(
                              xln_ap(j, tb, 1024), xsq_s[sj][:, :],
                              pA[:, s * 1024:(s + 1) * 1024]), ch="V")
                ev_xln[j][blk] = ev_x
                xsq_free[sj] = ev_x
                last_x = ev_x
            slot_free[s] = last_x
            mr_free[s] = last_x

        kpark = [g16(och * 4096, 128, 4096) for och in range(4)]
        ev_kpark = [None] * 4
        kh_free = {0: None, 1: None}
        for och in range(4):
            ev_ev = None
            for half in range(2):
                ph = pA if half == 0 else pB
                ev_mm = None
                for g in range(4):
                    for j in range(KC):
                        w = [ev_wk[j], ev_xln[j][(half * 2048 + g * 512) // 1024]]
                        if g == 0 and j == 0:
                            w.append(kh_free[half])
                        if och == 0 and g == 0 and j == 0:
                            w += [mr_free[0], mr_free[1], slot_free[0], slot_free[1], ev_t1_last]
                        ev_mm = op("tensor", w,
                                   lambda e, ph=ph, half=half, g=g, j=j, och=och: e.matmul(
                                       ph[:, g * 512:(g + 1) * 512],
                                       wk_t[j][:, och * 128:(och + 1) * 128],
                                       xln_ap(j, half * 2048 + g * 512, 512),
                                       start=(j == 0), stop=(j == KC - 1)), ch="T")
                ev_ev = op("scalar", [ev_mm, slot_free[och // 2] if half == 0 else None],
                           lambda e, ph=ph, och=och, half=half: e.activation(
                               kpark[och][:, half * 2048:(half + 1) * 2048], ph[:, :],
                               AF.Copy, scale=np.float32(DN)), ch="S")
                kh_free[half] = ev_ev
            ev_kpark[och] = ev_ev

        def v_t(cnk):
            return g16(RQ + 8192 + cnk * 512, 128, 512)

        ev_v = [None] * 32
        for blk in range(8):
            ph = pA if blk % 2 == 0 else pB
            evl = []
            for ci in range(4):
                cnk = blk * 4 + ci
                ev_mm = None
                for j in range(KC):
                    w = [ev_wv[j]]
                    if ci == 0 and j == 0:
                        if blk == 0:
                            w += [kh_free[0], kh_free[1]]
                        elif blk == 1:
                            w += [ev_v[3]]
                        else:
                            w += [ev_v[blk * 4 - 5]]
                    ev_mm = op("tensor", w,
                               lambda e, ph=ph, ci=ci, cnk=cnk, j=j: e.matmul(
                                   ph[:, ci * 512:(ci + 1) * 512],
                                   xln_ap(j, cnk * 128, 128),
                                   wv_t[j][:, :],
                                   start=(j == 0), stop=(j == KC - 1)), ch="T")
                evl.append(ev_mm)
            for ci in range(4):
                cnk = blk * 4 + ci
                ev_v[cnk] = op("scalar", [evl[ci]],
                               lambda e, ph=ph, ci=ci, cnk=cnk: e.activation(
                                   v_t(cnk)[:, :], ph[:, ci * 512:(ci + 1) * 512],
                                   AF.Copy), ch="S")

        qpark = [g16(ZB + pj * 2048, 128, 2048) for pj in range(KC)]
        ev_qpark = [None] * KC
        ph_free = {0: None, 1: None}
        ev_qmm_last = None
        for och in range(KC):
            ph = pA if och % 2 == 0 else pB
            ev_mm = None
            for g in range(4):
                for j in range(KC):
                    w = [ev_wq[j]]
                    if g == 0 and j == 0:
                        w.append(ph_free[och % 2])
                    if och == 0 and g == 0 and j == 0:
                        w += [ev_v[27], ev_v[31]]
                    if och == 1 and g == 0 and j == 0:
                        w += [ev_v[31]]
                    ev_mm = op("tensor", w,
                               lambda e, ph=ph, g=g, j=j, och=och: e.matmul(
                                   ph[:, g * 512:(g + 1) * 512],
                                   wq_t[j][:, och * 128:(och + 1) * 128],
                                   xln_ap(j, g * 512, 512),
                                   start=(j == 0), stop=(j == KC - 1)), ch="T")
            ev_qc = op("scalar", [ev_mm],
                       lambda e, ph=ph, och=och: e.activation(
                           qpark[och][:, :], ph[:, :], AF.Copy, scale=np.float32(DN)), ch="S")
            ev_qpark[och] = ev_qc
            ph_free[och % 2] = ev_qc
            ev_qmm_last = ev_mm

        dsc = [g16(RM + s_ * 2048, 128, 2048) for s_ in range(2)]
        ev_dq = None
        ev_dsq = [None, None]
        for pj in range(KC):
            sj = pj % 2
            w = [ev_qpark[pj], ev_dsq[sj]]
            if pj == 0:
                w += [slot_free[0], slot_free[1], mr_free[0], mr_free[1]]
            ev_sq2 = op("vector", w,
                        lambda e, pj=pj, sj=sj: e.tensor_mul(
                            dsc[sj][:, :], qpark[pj][:, :], qpark[pj][:, :]), ch="V")
            for g in range(4):
                ev_dq = op("tensor", [ev_sq2, evc["selq"]] +
                           ([ph_free[0], ph_free[1]] if (pj == 0 and g == 0) else []),
                           lambda e, pj=pj, g=g, sj=sj: e.matmul(
                               pA[0:16, g * 512:(g + 1) * 512],
                               c_selq[:, pj * 16:(pj + 1) * 16],
                               dsc[sj][:, g * 512:(g + 1) * 512],
                               start=(pj == 0), stop=(pj == KC - 1)), ch="T")
            ev_dsq[sj] = ev_dq
        dqrows = g32(1024, 16, 2048)
        ev_dqr = op("vector", [ev_dq], lambda e: e.tensor_copy(dqrows, pA[0:16, 0:2048]), ch="V")
        diagTq = g32(0, 128, 256)
        ev_tq_e = None
        for cnk in range(16):
            evt = op("tensor", [ev_dqr, evc["idf"], ev_tq_e],
                     lambda e, cnk=cnk: e.transpose(
                         pB[0:128, 1536:1552],
                         dqrows[0:16, cnk * 128:(cnk + 1) * 128],
                         c_idf[0:16, 0:16]), ch="T")
            ev_tq_e = op("vector", [evt], lambda e, cnk=cnk: e.tensor_copy(
                diagTq[:, cnk * 16:(cnk + 1) * 16], pB[0:128, 1536:1552]), ch="V")
        expdqe = g32(256, 128, 256)
        ev_expdq = op("scalar", [ev_tq_e], lambda e: e.activation(
            expdqe, diagTq, AF.Exp, bias=np.float32(LNEPS)), ch="S")

        ev_dk = None
        ev_dks = [None, None]
        for j in range(4):
            for half in range(2):
                sj = half
                w = [ev_kpark[j], ev_dks[sj]]
                if j == 0 and half == 0:
                    w.append(ev_dq)
                ev_sqk = op("vector", w,
                            lambda e, j=j, half=half, sj=sj: e.tensor_mul(
                                dsc[sj][:, :], kpark[j][:, half * 2048:(half + 1) * 2048],
                                kpark[j][:, half * 2048:(half + 1) * 2048]), ch="V")
                for gi in range(4):
                    g = half * 4 + gi
                    w2 = [ev_sqk, evc["selk"]]
                    if j == 0 and g == 0:
                        w2.append(ev_dqr)
                    if j == 0 and g == 4:
                        w2.append(ev_tq_e)
                    ev_dk = op("tensor", w2,
                               lambda e, j=j, g=g, gi=gi, sj=sj: e.matmul(
                                   (pA if g < 4 else pB)[0:8, (g % 4) * 512:(g % 4) * 512 + 512],
                                   c_selk[:, j * 8:(j + 1) * 8],
                                   dsc[sj][:, gi * 512:(gi + 1) * 512],
                                   start=(j == 0), stop=(j == 3)), ch="T")
                ev_dks[sj] = ev_dk
        dkrows = g32(1024, 8, 2048)
        diagTk = g32(512, 128, 256)
        ev_tk_e = None
        for half in range(2):
            ev_dkr = op("vector", [ev_dk] + ([ev_dqr] if half == 0 else [ev_tk_e]),
                        lambda e, half=half: e.tensor_copy(
                            dkrows, (pA if half == 0 else pB)[0:8, 0:2048]), ch="V")
            for c2 in range(16):
                cnk = half * 16 + c2
                evt = op("tensor", [ev_dkr, evc["idf"], ev_tk_e],
                         lambda e, c2=c2: e.transpose(
                             pA[0:128, 1536:1544],
                             dkrows[0:8, c2 * 128:(c2 + 1) * 128],
                             c_idf[0:8, 0:8]), ch="T")
                ev_tk_e = op("vector", [evt], lambda e, cnk=cnk: e.tensor_copy(
                    diagTk[:, cnk * 8:(cnk + 1) * 8], pA[0:128, 1536:1544]), ch="V")
        expndk = g32(768, 128, 256)
        ev_expdk = op("scalar", [ev_tk_e], lambda e: e.activation(
            expndk, diagTk, AF.Exp, scale=np.float32(-1.0)), ch="S")

        wo_t = [Wa[:, j * 1024:(j + 1) * 1024] for j in range(KC)]
        ev_wo = [dma("gpsimd", [ev_qmm_last], wo_t[j], d_wo[j * 128:(j + 1) * 128, :], stream="gW2")
                 for j in range(KC)]

        etok_s = [g16(ZC + s_ * 8512, 128, 8512) for s_ in range(2)]
        vts = g16(ZD + 640, 128, 2080)
        ctx_sbb = g16p(0, 65, RM, 2128)
        maxr2 = [g32(3152, 128, 34), g32(3186, 128, 34)]
        recB = g32(3220, 128, 1)
        epscol = g32(3222, 65, 1)
        skT = g32(3224, 128, 32)
        ev_ctx_ev = []
        ev_ctx_last = None
        ev_skT_free = [None]
        batches = [list(range(b, min(b + 3, 32))) for b in range(0, 32, 3)]
        ev_scan_of = {}
        etok_free = [None, None]
        ev_exp_of = {}

        ev_tr_of = {}
        ev_mx_of = {}
        ev_rbe_of = {}
        ev_ep_of = {}
        ev_ce_of = {}

        def ks_dash(hh):
            sl = hh % 2
            kch = hh // 2
            kb = (hh % 2) * 64
            ev_exp_all = []
            for bi, bat in enumerate(batches):
                ph = pA if bi % 2 == 0 else pB
                evm = []
                for ii, cnk in enumerate(bat):
                    w = [ev_kpark[kch], evc["projb"]]
                    if bi >= 2:
                        w.append(ev_exp_all[bi - 2])
                    if hh == 0 and bi <= 1:
                        w.append(ev_v[31])
                    if hh >= 1 and bi <= 1 and ii == 0:
                        w += ev_exp_of[hh - 1][-2:]
                    evm.append(op("tensor", w,
                                  lambda e, ph=ph, ii=ii, cnk=cnk, kch=kch, kb=kb: e.matmul(
                                      ph[0:128, ii * 512:ii * 512 + M],
                                      kpark[kch][kb:kb + 64, cnk * 128:(cnk + 1) * 128],
                                      c_projb[kb:kb + 64, :],
                                      start=True, stop=True), ch="T"))
                nb = len(bat)
                ev_exp = op("scalar", evm + [etok_free[sl] if bi == 0 else None],
                            lambda e, ph=ph, nb=nb, c0=bat[0], sl=sl: e.activation(
                                etok_s[sl][:, c0 * 266:(c0 + nb) * 266].rearrange(
                                    "p (b m) -> p b m", m=266),
                                bass.AP(ph, 0, [[2048, 128], [512, nb], [1, M]]),
                                AF.Exp), ch="S")
                ev_exp_all.append(ev_exp)
            ev_exp_of[hh] = ev_exp_all

        def ks_scan(hh):
            mx = maxr2[hh % 2]
            w = [ev_exp_of[hh][-1]]
            if hh >= 2:
                w.append(ev_tr_of[hh - 2])
            ev_red = op("vector", w,
                        lambda e, sl=hh % 2, mx=mx: e.tensor_reduce(
                            mx[:, 0:32],
                            etok_s[sl][:, :].rearrange("p (c m) -> p c m", m=266),
                            AX.X, OP.max), ch="V")
            ev_red2 = op("vector", [ev_red], lambda e, mx=mx: e.tensor_reduce(
                mx[:, 32:33], mx[:, 0:32], AX.X, OP.max), ch="V")
            ev_scan_of[hh] = ev_red2

        def ks_tailA(hh):
            mx = maxr2[hh % 2]
            w = [ev_scan_of[hh], evc["idf"]]
            if hh >= 2:
                w.append(ev_mx_of[hh - 2])
            ev_tr = op("tensor", w,
                       lambda e, mx=mx: e.transpose(pB[0:1, 1536:1664], mx[:, 32:33],
                                                    c_idf[:, :]), ch="T")
            ev_tr_of[hh] = ev_tr
            ev_mx = op("vector", [ev_tr], lambda e: e.tensor_reduce(
                t_sc[0:1, 0:1], pB[0:1, 1536:1664], AX.X, OP.max), ch="V")
            ev_mx_of[hh] = ev_mx
            ev_rc = op("vector", [ev_mx], lowp(lambda e: e.reciprocal(
                t_scb[0:1, 0:1], t_sc[0:1, 0:1])), ch="V")
            w = [ev_rc, evc["ones"]]
            if hh >= 2:
                w.append(ev_rbe_of[hh - 2])
            ev_rb = op("tensor", w,
                       lambda e: e.matmul(pB[0:128, 1700:1701],
                                          c_ones[0:1, 0:128], t_scb[0:1, 0:1],
                                          start=True, stop=True), ch="T")
            ev_rbe = op("vector", [ev_rb], lambda e: e.tensor_copy(recB, pB[0:128, 1700:1701]), ch="V")
            ev_rbe_of[hh] = ev_rbe
            ev_sk = op("vector", [ev_rbe, ev_expdk, ev_skT_free[0]],
                       lambda e, hh=hh: e.tensor_scalar_mul(
                           skT, G32[0:128, 768 + hh:1024:8], recB[:, 0:1]), ch="V")
            return dict(ev_sk=ev_sk, ev_rb=ev_rb)

        def ks_tailB(hh, stA):
            sl = hh % 2
            ev_vt1 = op("vector", [stA["ev_sk"]],
                        lambda e, hh=hh: e.tensor_tensor(
                            vts[:, :].rearrange("p (c w) -> p c w", w=65)[:, :, 0:64],
                            bass.AP(G16, RQ + 8192 + hh * 64, [[81032, 128], [512, 32], [1, 64]]),
                            skT.rearrange("p (c o) -> p c o", o=1).broadcast_to((128, 32, 64)),
                            OP.mult), ch="V")
            ev_vt2 = op("vector", [stA["ev_sk"]], lambda e: e.tensor_copy(
                vts[:, :].rearrange("p (c w) -> p c w", w=65)[:, :, 64:65],
                skT.rearrange("p (c o) -> p c o", o=1)), ch="V")
            ev_skT_free[0] = ev_vt2
            ev_cs = None
            for cnk in range(32):
                w = [ev_v[cnk], evc["ones"]]
                if cnk == 0:
                    w.append(stA["ev_rb"])
                    if hh >= 2:
                        w.append(ev_ep_of[hh - 2])
                ev_cs = op("tensor", w,
                           lambda e, cnk=cnk, hh=hh: e.matmul(
                               pB[0:64, 1792:1793],
                               v_t(cnk)[:, hh * 64:hh * 64 + 64],
                               c_ones[:, 0:1],
                               start=(cnk == 0), stop=(cnk == 31)), ch="T")
            ev_ep1 = op("vector", [ev_cs], lambda e: e.tensor_scalar_mul(
                epscol[0:64, 0:1], pB[0:64, 1792:1793], EPS), ch="V")
            ev_ep2 = op("vector", [ev_ep1], lambda e: e.memset(
                epscol[64:65, 0:1], EPS * float(NF)), ch="V")
            ev_ep_of[hh] = ev_ep1
            return dict(ev_vt1=ev_vt1, ev_vt2=ev_vt2, ev_ep2=ev_ep2, **stA)

        def ks_tailC(hh, stB):
            sl = hh % 2
            ev_exp_all = ev_exp_of[hh]
            ev_ctx = None
            for cnk in range(32):
                w = [stB["ev_vt1"], stB["ev_vt2"],
                     ev_exp_all[min(cnk // 3, len(ev_exp_all) - 1)]]
                if cnk == 0 and ev_ctx_ev:
                    w.append(ev_ctx_ev[-1])
                ev_ctx = op("tensor", w,
                            lambda e, cnk=cnk, sl=sl: e.matmul(
                                pA[0:65, 1536:1536 + M],
                                vts[:, cnk * 65:(cnk + 1) * 65],
                                etok_s[sl][:, cnk * 266:(cnk + 1) * 266],
                                start=(cnk == 0), stop=(cnk == 31)), ch="T")
            etok_free[sl] = ev_ctx
            ev_ce = op("vector", [ev_ctx, stB["ev_ep2"]],
                       lambda e, hh=hh: e.tensor_scalar(
                           ctx_sbb[:, hh * M:(hh + 1) * M], pA[0:65, 1536:1536 + M],
                           epscol[0:65, 0:1], None, OP.add), ch="V")
            ev_ctx_ev.append(ev_ce)
            return ev_ctx

        ev_ctx_last = None
        stA = {}
        stB = {}
        for hh in range(HK):
            if hh >= 1:
                stA[hh - 1] = ks_tailA(hh - 1)
            ks_dash(hh)
            if hh >= 1:
                stB[hh - 1] = ks_tailB(hh - 1, stA[hh - 1])
            ks_scan(hh)
            if hh >= 1:
                ev_ctx_last = ks_tailC(hh - 1, stB[hh - 1])
        stA[HK - 1] = ks_tailA(HK - 1)
        stB[HK - 1] = ks_tailB(HK - 1, stA[HK - 1])
        ev_ctx_last = ks_tailC(HK - 1, stB[HK - 1])

        eq0 = [g16(s_ * 6144, 128, 2048) for s_ in range(2)]
        eq1 = [g16(s_ * 6144 + 2048, 128, 2048) for s_ in range(2)]
        eq2 = [g16p(0, 16, s_ * 6144 + 4096, 2048) for s_ in range(2)]
        stage = g16(ZD + 640, 128, 2048)
        merged = g16(ZC, 128, 16384)
        u_row = [g16p(0, 1, 12288 + s_ * 2048, 2048) for s_ in range(2)]
        u_tr = g32(3168, 16, 128)
        maxEq_p = [g32(3296, 128, 16), g32(1024, 128, 16)]
        u_cols_p = [g32(3312, 128, 16), g32(1040, 128, 16)]
        rbT_b = g16p(0, 128, RQ + 14336, 16)
        scanb = g16(RQ + 8192, 128, 2048)
        num_sb = [g16p(0, 65, RQ + 10240 + s * 2048, 2048) for s in range(2)]

        ev_eq_free = [None, None]
        ev_u_free = [None, None]
        ev_stage_free = None
        ev_scan_prev = None
        ev_uc_pp = [None, None]
        ev_ut_pp = [None, None]
        ev_ur_prev = None
        ev_rT_prev = None
        ev_bc_prev = None
        ev_num_free = [None, None]
        ev_par_free = [ev_ctx_ev[-1], ev_ctx_ev[-1]]
        ev_merged = []
        st_front = {}

        def qs_dash(h):
            nonlocal ev_scan_prev
            pj = h // 2
            sl = pj % 2
            es = h % 2
            qb = es * 64
            P = pA if es == 0 else pB
            Pb = pAb if es == 0 else pBb

            def dmm(m0, mw, g, w):
                return op("tensor", w,
                          lambda e, g=g, m0=m0, mw=mw, qb=qb, pj=pj, P=P: e.matmul(
                              P[0:mw, g * 512:(g + 1) * 512],
                              c_projb[qb:qb + 64, m0:m0 + mw],
                              qpark[pj][qb:qb + 64, g * 512:(g + 1) * 512],
                              start=True, stop=True), ch="T")
            w0 = [ev_qpark[pj], evc["projb"], ev_par_free[es]]
            ev_c2a = [dmm(256, 10, g, w0) for g in (0, 1)]
            ev_c2b = [dmm(256, 10, g, w0) for g in (2, 3)]
            ev_e2a = op("scalar", ev_c2a + [ev_eq_free[es]],
                        lambda e, es=es, P=P: e.activation(
                            eq2[es][0:10, 0:1024], P[0:10, 0:1024], AF.Exp), ch="S")
            ev_e2b = op("scalar", ev_c2b + [ev_eq_free[es]],
                        lambda e, es=es, P=P: e.activation(
                            eq2[es][0:10, 1024:2048], P[0:10, 1024:2048], AF.Exp), ch="S")
            ev_c0a = [dmm(0, 128, g, [ev_e2a]) for g in (0, 1)]
            ev_c0b = [dmm(0, 128, g, [ev_e2b]) for g in (2, 3)]
            ev_e0a = op("scalar", ev_c0a + [ev_eq_free[es]],
                        lambda e, es=es, P=P: e.activation(
                            eq0[es][:, 0:1024], P[:, 0:1024], AF.Exp), ch="S")
            ev_e0b = op("scalar", ev_c0b + [ev_eq_free[es]],
                        lambda e, es=es, P=P: e.activation(
                            eq0[es][:, 1024:2048], P[:, 1024:2048], AF.Exp), ch="S")
            st = dict(es=es, sl=sl, pj=pj, qb=qb, P=P, Pb=Pb, dmm=dmm,
                      ev_e2a=ev_e2a, ev_e2b=ev_e2b, ev_e0a=ev_e0a, ev_e0b=ev_e0b)
            return st

        def qs_dash2(st, h):
            es = st["es"]; P = st["P"]; dmm = st["dmm"]
            ev_c1a = [dmm(128, 128, g, [st["ev_e0a"]]) for g in (0, 1)]
            ev_c1b = [dmm(128, 128, g, [st["ev_e0b"]]) for g in (2, 3)]
            ev_e1a = op("scalar", ev_c1a + [ev_eq_free[es]],
                        lambda e, es=es, P=P: e.activation(
                            eq1[es][:, 0:1024], P[:, 0:1024], AF.Exp), ch="S")
            ev_e1b = op("scalar", ev_c1b + [ev_eq_free[es]],
                        lambda e, es=es, P=P: e.activation(
                            eq1[es][:, 1024:2048], P[:, 1024:2048], AF.Exp), ch="S")
            st["ev_e1a"] = ev_e1a
            st["ev_e1b"] = ev_e1b
            st["ev_e0"] = st["ev_e0b"]
            st["ev_e1"] = ev_e1b
            st["ev_e2"] = st["ev_e2b"]
            return st

        def qs_scan(st, h):
            nonlocal ev_scan_prev
            es = st["es"]; P = st["P"]; Pb = st["Pb"]
            ev_cm = []
            for half, (ee0, ee1, ee2) in enumerate(
                    [(st["ev_e0a"], st["ev_e1a"], st["ev_e2a"]),
                     (st["ev_e0b"], st["ev_e1b"], st["ev_e2b"])]):
                c0 = half * 1024
                w_cm = [ee0, ee1]
                if ev_scan_prev is not None:
                    w_cm.append(ev_scan_prev)
                ev_cm1 = op("vector", w_cm, lambda e, es=es, c0=c0: e.tensor_max(
                    scanb[:, c0:c0 + 1024], eq0[es][:, c0:c0 + 1024],
                    eq1[es][:, c0:c0 + 1024]), ch="V")
                ev_cm2 = op("vector", [ev_cm1, ee2], lambda e, es=es, c0=c0: e.tensor_max(
                    scanb[0:10, c0:c0 + 1024], scanb[0:10, c0:c0 + 1024],
                    eq2[es][0:10, c0:c0 + 1024]), ch="V")
                ev_cm.append(ev_cm2)
            ev_sct = []
            ev_scr = []
            for cnk in range(16):
                w = [ev_cm[cnk // 8], evc["idb"], st["ev_e1b"]]
                if cnk == 8:
                    w.append(ev_scr[0])
                ev_sct.append(op("tensor", w,
                                 lambda e, cnk=cnk, Pb=Pb: e.transpose(
                                     Pb[:, (cnk % 8) * 128:(cnk % 8) * 128 + 128],
                                     scanb[:, cnk * 128:(cnk + 1) * 128],
                                     c_idb[:, :]), ch="T"))
                if cnk % 8 == 7:
                    wr = ev_sct[-8:]
                    if cnk == 7:
                        wr = wr + [ev_uc_pp[es]]
                    ev_scr.append(op("vector", wr,
                                     lambda e, cnk=cnk, Pb=Pb, es=es: e.tensor_reduce(
                                         maxEq_p[es][:, cnk - 7:cnk + 1],
                                         Pb[:, 0:1024].rearrange("p (c f) -> p c f", f=128),
                                         AX.X, OP.max), ch="V"))
            ev_scan_prev = ev_sct[-1]
            ev_uc = op("vector", [ev_scr[-1], ev_expdq, ev_ut_pp[es]],
                       lambda e, h=h, es=es: e.tensor_mul(
                           u_cols_p[es], G32[0:128, 256 + h:512:16], maxEq_p[es]), ch="V")
            ev_uc_pp[es] = ev_uc
            st.update(ev_cm1=ev_cm[1], ev_scr=ev_scr, ev_uc=ev_uc)
            return st

        def qs_ur(st, h):
            nonlocal ev_ur_prev
            es = st["es"]
            P = st["P"]
            ev_ut = op("tensor", [st["ev_uc"], evc["idf"], st["ev_e1b"]],
                       lambda e, P=P, es=es: e.transpose(
                           P[0:16, 1024:1152], u_cols_p[es], c_idf[:, :]), ch="T")
            ev_ut_pp[es] = ev_ut
            ev_ue = op("vector", [ev_ut, ev_ur_prev], lambda e, P=P: e.tensor_copy(
                u_tr, P[0:16, 1024:1152]), ch="V")
            st["ev_ue"] = ev_ue
            ev_ur = dma("gpsimd", [ev_ue, ev_u_free[es]],
                        u_row[es][0:1, :].rearrange("o (c f) -> o c f", c=16), u_tr,
                        stream="gE")
            ev_ur_prev = ev_ur
            st["ev_ur"] = ev_ur
            return st

        def qs_am01(st, h):
            es = st["es"]; P = st["P"]
            ev_am = []
            for g in (1, 2, 3, 0):
                hv = "a" if g < 2 else "b"
                for mi in (0, 1):
                    m0, mw = MCH[mi]
                    src = ctxL[0:mw, h * 130 + mi * 65:h * 130 + mi * 65 + 65]
                    esrc = (eq0[es] if mi == 0 else eq1[es])[0:mw, g * 512:(g + 1) * 512]
                    w = [ev_ctxL[h][mi], st["ev_e0" + hv] if mi == 0 else st["ev_e1" + hv],
                         st["ev_e1" + hv]]
                    if h <= 1 and g == 1 and mi == 0:
                        w.append(ev_ctxL[H - 1][2])
                    if mi == 0:
                        if g == 0:
                            w = w + st["ev_scr"]
                        elif g == 2:
                            w.append(st["ev_ue"])
                    ev_am.append(op("tensor", w,
                                    lambda e, g=g, src=src, esrc=esrc, mi=mi, P=P: e.matmul(
                                        P[0:65, g * 512:(g + 1) * 512],
                                        src, esrc,
                                        start=(mi == 0), stop=False), ch="T"))
            st["ev_am01"] = ev_am
            return st

        def qs_am2(st, h):
            es = st["es"]; P = st["P"]
            ev_am = []
            for g in (1, 2, 3, 0):
                ev_am.append(op("tensor", [ev_ctxL[h][2], st["ev_e2"]],
                                lambda e, g=g, h=h, es=es, P=P: e.matmul(
                                    P[0:65, g * 512:(g + 1) * 512],
                                    ctxL2[0:10, h * 65:(h + 1) * 65],
                                    eq2[es][0:10, g * 512:(g + 1) * 512],
                                    start=False, stop=False), ch="T"))
                ev_am.append(op("tensor", [st["ev_ur"], ev_B_done],
                                lambda e, g=g, h=h, es=es, P=P: e.matmul(
                                    P[0:65, g * 512:(g + 1) * 512],
                                    B_rows[0:1, h * 65:(h + 1) * 65],
                                    u_row[es][0:1, g * 512:(g + 1) * 512],
                                    start=False, stop=True), ch="T"))
            ev_am = st["ev_am01"] + ev_am
            ev_u_free[es] = ev_am[-1]
            ev_eq_free[es] = ev_am[-1]
            st["ev_am"] = ev_am
            return st

        def qs_nev(st, h):
            es = st["es"]; P = st["P"]
            st["ev_nev"] = op("scalar", st["ev_am"] + [ev_num_free[es]],
                              lambda e, es=es, P=P: e.activation(
                                  num_sb[es][0:65, :], P[0:65, :], AF.Copy), ch="S")
            return st

        def qs_dT(st, h):
            es = st["es"]; Pb = st["Pb"]
            st["ev_dT"] = [op("tensor", [st["ev_nev"], evc["idb"]],
                              lambda e, cnk=cnk, es=es, Pb=Pb: e.transpose(
                                  Pb[0:128, 2048 + 2 * cnk:2048 + 2 * cnk + 1],
                                  num_sb[es][64:65, cnk * 128:(cnk + 1) * 128],
                                  c_idb[64:65, 64:65]), ch="T") for cnk in range(16)]
            return st

        def qs_rcp(st, h):
            nonlocal ev_rT_prev
            Pb = st["Pb"]
            st["ev_rcp"] = op("vector", st["ev_dT"] + [ev_rT_prev],
                              lowp(lambda e, Pb=Pb: e.reciprocal(
                                  rbT_b[0:128, 0:16], Pb[0:128, 2048:2080:2])), ch="V")
            ev_rT_prev = st["ev_rcp"]
            return st

        def qs_bc(st, h):
            P = st["P"]
            st["ev_bc"] = [op("tensor", [st["ev_rcp"], evc["idb"]],
                              lambda e, cnk=cnk, P=P: e.matmul(
                                  P[0:64, cnk * 128:(cnk + 1) * 128],
                                  bass.AP(G16, RQ + 14336 + cnk, [[81032, 128], [0, 64]]),
                                  c_idb[:, :],
                                  start=True, stop=True), ch="T") for cnk in range(16)]
            return st

        def qs_mul(st, h):
            nonlocal ev_stage_free
            es = st["es"]; P = st["P"]; pj = st["pj"]
            if es == 0:
                dsth = merged[0:64, pj * 2048:(pj + 1) * 2048]
                wex = []
            else:
                dsth = stage[0:64, 0:2048]
                wex = [ev_stage_free]
            ev_m = op("vector", st["ev_bc"] + wex + [st["ev_nev"]],
                      lambda e, dsth=dsth, es=es, P=P: e.tensor_mul(
                          dsth, num_sb[es][0:64, 0:2048], P[0:64, 0:2048]), ch="V")
            ev_merged.append(ev_m)
            ev_par_free[es] = ev_m
            ev_num_free[es] = ev_m
            out_ev = ev_m
            if es == 1:
                evd = dma("sync", [ev_m], merged[64:128, pj * 2048:(pj + 1) * 2048],
                          stage[0:64, 0:2048], stream="sST")
                ev_merged.append(evd)
                ev_stage_free = evd
                out_ev = evd
            return out_ev

        def qs_front_all(h):
            st = qs_dash(h)
            qs_dash2(st, h)
            qs_scan(st, h)
            qs_ur(st, h)
            st_front[h] = st
            return st

        prev_psum = None
        ev_ag1s = dma("sync", ev_ctx_ev, ag1_in[:, :], ctx_sbb, stream="sAG1")
        ev_cc1 = op("gpsimd", [ev_ag1s], lambda e: e.collective_compute(
            "AllGather", OP.bypass, replica_groups=groups,
            ins=[ag1_in.ap()], outs=[ag1_out.ap()]), ch="cc")
        qs_front_all(0)
        qs_front_all(1)
        ctxTb = g16p(0, 65, RQ, 4256)
        ev_ctb = [dma("sync", [ev_cc1, ev_ctx_last], ctxTb[:, g * 2128:(g + 1) * 2128],
                      ag1_out[g * 65:(g + 1) * 65, :], stream="sL") for g in range(2)]

        Bcols = g32(3152, 65, 16)
        ev_B = op("vector", ev_ctb, lambda e: e.tensor_reduce(
            Bcols, ctxTb[:, :].rearrange("p (h m) -> p h m", m=M),
            AX.X, OP.add), ch="V")
        B_rows = g16p(0, 1, RQ + 14352, 1040)
        ev_Brt = []
        ev_Br = []
        for h in range(H):
            w = [ev_B, evc["idf"], ev_ctx_last]
            if h % 4 == 0 and ev_Br:
                w.append(ev_Br[-1])
            evt = op("tensor", w,
                     lambda e, h=h: e.transpose(
                         pB[0:1, 1536 + (h % 4) * 65:1536 + (h % 4) * 65 + 65],
                         Bcols[0:65, h:h + 1], c_idf[0:65, 0:65]), ch="T")
            ev_Brt.append(evt)
            if h % 4 == 3:
                ev_Br.append(op("vector", ev_Brt[-4:],
                                lambda e, h=h: e.tensor_copy(
                                    B_rows[0:1, (h - 3) * 65:(h + 1) * 65],
                                    pB[0:1, 1536:1536 + 260]), ch="V"))
        ev_B_done = ev_Br[-1]

        ctxL = g16(RM, 128, 2080)
        ctxL2 = g16p(0, 16, RM + 2080, 1040)
        ev_ctxL = {}
        prev_t = ev_Br[-1]
        for h in range(H):
            g = h // 8
            hh = h % 8
            evs = []
            for mi, (m0, mw) in enumerate(MCH):
                evt = op("tensor", [ev_ctb[g], evc["idb"], prev_t],
                         lambda e, g=g, hh=hh, m0=m0, mw=mw: e.transpose(
                             pBb[0:mw, 3072:3072 + 65],
                             ctxTb[0:65, g * 2128 + hh * M + m0:g * 2128 + hh * M + m0 + mw],
                             c_idb[0:65, 0:65]), ch="T")
                if mi < 2:
                    dst = ctxL[0:mw, h * 130 + mi * 65:h * 130 + mi * 65 + 65]
                else:
                    dst = ctxL2[0:10, h * 65:(h + 1) * 65]
                eve = op("vector", [evt], lambda e, dst=dst, mw=mw: e.tensor_copy(
                    dst, pBb[0:mw, 3072:3072 + 65]), ch="V")
                evs.append(eve)
                prev_t = eve
            ev_ctxL[h] = evs

        qs_am01(st_front[0], 0)
        qs_am01(st_front[1], 1)
        qs_am2(st_front[0], 0)
        qs_nev(st_front[0], 0)
        qs_dT(st_front[0], 0)
        qs_rcp(st_front[0], 0)
        qs_bc(st_front[0], 0)
        prev_psum = qs_mul(st_front[0], 0)
        for h in range(2, H):
            st = qs_dash(h)
            qs_dash2(st, h)
            qs_am2(st_front[h - 1], h - 1)
            qs_scan(st, h)
            qs_nev(st_front[h - 1], h - 1)
            qs_dT(st_front[h - 1], h - 1)
            qs_ur(st, h)
            qs_rcp(st_front[h - 1], h - 1)
            qs_bc(st_front[h - 1], h - 1)
            qs_am01(st, h)
            prev_psum = qs_mul(st_front[h - 1], h - 1)
            st_front[h] = st
        st = st_front[H - 1]
        qs_am2(st, H - 1)
        qs_nev(st, H - 1)
        qs_dT(st, H - 1)
        qs_rcp(st, H - 1)
        qs_bc(st, H - 1)
        prev_psum = qs_mul(st, H - 1)

        bw_t = [Wb[:, j * 1024:(j + 1) * 1024] for j in range(KC)]
        ev_bw = [dma("gpsimd", [ev_ctx_last], bw_t[j], d_bwT[j * 128:(j + 1) * 128, :], stream="gW3")
                 for j in range(KC)]

        attn = g16(RQ + 8192, 128, 16384)
        x1 = g16(ZB, 128, 16384)
        xf = g16(ZC, 128, 16384)
        ffo = g16(0, 128, 16384)
        h_ext = g16p(0, 65, RQ + 6144, 2048)
        M_ext = g16p(0, 65, ZD, 1024)
        xbh_s = [g16(RQ + s * 2048, 128, 2048) for s in range(2)]
        ln_m = g16(RM, 128, 2048)
        ln_r = g16(RM + 2048, 128, 2048)
        ln_t = g16(ZD + 1024, 128, 2048)
        ftmp = g32(0, 128, 2048)
        m12 = g32(2048, 128, 16)
        m12b = g32(2064, 128, 16)
        ivT = g32(2080, 128, 64)
        muiv = g32(2144, 128, 64)
        mui2 = g32(2208, 128, 64)
        csum = g32(2272, 128, 8)
        sq2 = g16(RQ + 4096, 128, 2048)
        rT32 = g32(3296, 128, 16)
        r_rb = g16p(0, 1, RM + 2048, 2048)
        ivb = g16p(0, 128, ZD + 3072, 64)
        muivb = g16p(0, 128, ZD + 3136, 64)
        csb = g16p(0, 128, ZD + 3200, 8)
        _of = (RQ + 8192) // 2
        outst = [G16f[0:128, _of + s * 2048:_of + (s + 1) * 2048] for s in range(4)]
        t_ffw = G16[0:128, RQ + 16384:RQ + 17536]
        t_ones512 = G16[0:1, RQ + 17536:RQ + 18048]

        ev_attn = [None] * KC
        ev_womm = None
        wo_free = {0: None, 1: None}
        for och in range(KC):
            ph = pA if och % 2 == 0 else pB
            ev_mm = None
            for g in range(4):
                for j in range(KC):
                    w = [ev_wo[j]]
                    if g == 0 and j == 0:
                        w.append(wo_free[och % 2])
                    if och == 0 and g == 0 and j == 0:
                        w += ev_merged + [prev_psum]
                    ev_mm = op("tensor", w,
                               lambda e, ph=ph, g=g, j=j, och=och: e.matmul(
                                   ph[:, g * 512:(g + 1) * 512],
                                   wo_t[j][:, och * 128:(och + 1) * 128],
                                   merged[:, j * 2048 + g * 512:j * 2048 + (g + 1) * 512],
                                   start=(j == 0), stop=(j == KC - 1)), ch="T")
            ev_womm = ev_mm
            ev_attn[och] = op("scalar", [ev_mm, evc["bo"]],
                              lambda e, ph=ph, och=och: e.activation(
                                  attn[:, och * 2048:(och + 1) * 2048], ph[:, :],
                                  AF.Identity, bias=c_bo[:, och:och + 1]), ch="S")
            wo_free[och % 2] = ev_attn[och]

        def ln_stats(src_fn, ev_src, sq_slot, ev_sq_pre, ev_psA_pre, ev_psB_pre):
            ev_mm = None
            for j in range(KC):
                for g in range(4):
                    w = [ev_src[j], evc["lnones"]]
                    if j == 0 and g == 0:
                        w += ev_psA_pre
                    ev_mm = op("tensor", w,
                               lambda e, j=j, g=g: e.matmul(
                                   pA[:, g * 512:(g + 1) * 512],
                                   c_lnones[:, :], src_fn(j)[:, g * 512:(g + 1) * 512],
                                   start=(j == 0), stop=(j == KC - 1)), ch="T")
            ev_mb = op("vector", [ev_mm], lambda e: e.tensor_copy(ln_m[:, :], pA[:, 0:2048]), ch="V")
            ev_smm = None
            prev = list(ev_sq_pre)
            for j in range(KC):
                ev_sq = op("scalar", [ev_src[j]] + prev,
                           lambda e, j=j: e.activation(sq_slot[:, :], src_fn(j)[:, :], AF.Square),
                           ch="S")
                for g in range(4):
                    w = [ev_sq]
                    if j == 0 and g == 0:
                        w += ev_psB_pre
                    ev_smm = op("tensor", w,
                                lambda e, j=j, g=g: e.matmul(
                                    pB[:, g * 512:(g + 1) * 512],
                                    c_lnones[:, :], sq_slot[:, g * 512:(g + 1) * 512],
                                    start=(j == 0), stop=(j == KC - 1)), ch="T")
                prev = [ev_smm]
            ev_t1 = op("vector", [ev_smm, ev_mb], lambda e: e.tensor_copy(ftmp, pB[:, 0:2048]), ch="V")
            ev_m2 = op("scalar", [ev_mb] + prev, lambda e: e.activation(ln_t[:, :], ln_m[:, :], AF.Square), ch="S")
            ev_var = op("vector", [ev_t1, ev_m2], lambda e: e.tensor_sub(ftmp, ftmp, ln_t[:, :]), ch="V")
            ev_sd = op("scalar", [ev_var], lambda e: e.activation(ftmp, ftmp, AF.Sqrt, bias=np.float32(1e-5)), ch="S")
            ev_vT = [op("tensor", [ev_sd, evc["idf"], ev_t1],
                       lambda e, cnk=cnk: e.transpose(
                           pB[0:128, 1536 + cnk:1536 + cnk + 1],
                           G32[0:1, cnk * 128:(cnk + 1) * 128],
                           c_idf[0:1, 0:1]), ch="T") for cnk in range(16)]
            ev_rc = op("vector", ev_vT, lowp(lambda e: e.reciprocal(
                rT32[0:128, 0:16], pB[0:128, 1536:1552])), ch="V")
            ev_rT = [op("tensor", [ev_rc, evc["idf"]],
                       lambda e, cnk=cnk: e.transpose(
                           pB[0:1, cnk * 128:(cnk + 1) * 128],
                           rT32[0:128, cnk:cnk + 1],
                           c_idf[:, :]), ch="T") for cnk in range(16)]
            ev_re = op("vector", ev_rT, lambda e: e.tensor_copy(r_rb[0:1, :], pB[0:1, 0:2048]), ch="V")
            ev_bcr = [op("tensor", [ev_re, evc["ones"]],
                        lambda e, g=g: e.matmul(
                            pB[0:128, g * 512:(g + 1) * 512],
                            c_ones[0:1, 0:128],
                            r_rb[0:1, g * 512:(g + 1) * 512],
                            start=True, stop=True), ch="T") for g in range(4)]
            ev_r = ev_bcr[-1]
            return ev_mb, ev_r, ev_smm, ev_t1, ev_sd

        xbh_ev = [dma("gpsimd", [ev_stage_free], xbh_s[0][:, :], d_xT[0:128, 0:2048], stream="gE2"),
                  dma("gpsimd", [ev_stage_free], xbh_s[1][:, :], d_xT[128:256, 0:2048], stream="gE3")]
        ev_m2b, ev_r2, ev_ss2, ev_t1_2, ev_sd2 = ln_stats(
            lambda j: attn[:, j * 2048:(j + 1) * 2048], ev_attn, sq2,
            [ev_ctxL[H - 1][2]], [ev_attn[KC - 2], ev_attn[KC - 1]],
            [ev_attn[KC - 2], ev_attn[KC - 1]])
        ev_x1 = [None] * KC
        ev_lnt_prev = ev_r2
        for j in range(KC):
            sj = j % 2
            ev_t = op("vector", [ev_attn[j], ev_m2b, ev_lnt_prev, ev_womm],
                      lambda e, j=j: e.tensor_sub(
                          ln_t[:, :], attn[:, j * 2048:(j + 1) * 2048], ln_m[:, :]), ch="V")
            ev_t2 = op("vector", [ev_t, ev_r2],
                       lambda e: e.tensor_mul(ln_t[:, :], ln_t[:, :], pB[:, 0:2048]), ch="V")
            ev_x1[j] = op("vector", [ev_t2, xbh_ev[sj]],
                          lambda e, j=j, sj=sj: e.tensor_add(
                              x1[:, j * 2048:(j + 1) * 2048], ln_t[:, :], xbh_s[sj][:, :]),
                          ch="V")
            ev_lnt_prev = ev_x1[j]
            if j + 2 < KC:
                xbh_ev[sj] = dma("gpsimd", [ev_x1[j]], xbh_s[sj][:, :],
                                 d_xT[(j + 2) * 128:(j + 3) * 128, 0:2048],
                                 stream=["gE2", "gE3"][sj])

        ev_m3b, ev_r3, ev_ss3, ev_t1_3, ev_sd3 = ln_stats(
            lambda j: x1[:, j * 2048:(j + 1) * 2048], ev_x1, sq2,
            [ev_ss2, ev_sd2, ev_x1[-1]], [ev_m2b, ev_t1_2], [ev_t1_2, ev_x1[-1]])
        ev_xf = [None] * KC
        ev_m2c = [None] * KC
        ev_lnt_prev = ev_r3
        for j in range(KC):
            ev_t = op("vector", [ev_x1[j], ev_m3b, ev_ss3, ev_lnt_prev],
                      lambda e, j=j: e.tensor_sub(
                          ln_t[:, :], x1[:, j * 2048:(j + 1) * 2048], ln_m[:, :]), ch="V")
            ev_xf[j] = op("vector", [ev_t, ev_r3, ev_ctxL[H - 1][2], ev_merged[-1]],
                          lambda e, j=j: e.scalar_tensor_tensor(
                              xf[:, j * 2048:(j + 1) * 2048], ln_t[:, :], 1.0, pB[:, 0:2048],
                              OP.mult, OP.mult,
                              accum_out=m12[:, j:j + 1]), ch="V")
            ev_lnt_prev = ev_xf[j]
            ev_m2c[j] = op("scalar", [ev_xf[j]] + ([ev_ss3, ev_sd3] if j == 0 else [ev_m2c[j - 1]]),
                           lambda e, j=j: e.activation(
                               sq2[:, :], xf[:, j * 2048:(j + 1) * 2048], AF.Square,
                               accum_out=m12[:, 8 + j:8 + j + 1]), ch="S")

        ev_mscl = op("vector", ev_xf + ev_m2c, lambda e: e.tensor_scalar_mul(m12, m12, 1.0 / NF), ch="V")
        mtb = m12b.tensor.bitcast(BF16)
        mtb_ap = lambda c0, c1: bass.AP(mtb, 2 * 2064 + c0, [[2 * 3328, 128], [1, c1 - c0]])
        ev_mcb = op("vector", [ev_mscl], lambda e: e.tensor_copy(mtb_ap(0, 16), m12), ch="V")
        ev_iv0 = op("scalar", [evc["smT"], ev_m2c[-1]], lambda e: e.activation(ivT, c_smT[:, :], AF.Square), ch="S")
        ev_iv = op("vector", [ev_iv0], lambda e: e.reciprocal(ivT, ivT), ch="V")
        ev_mv = op("vector", [ev_iv, evc["muT"]], lambda e: e.tensor_mul(muiv, c_muT[:, :], ivT), ch="V")
        ev_m2v = op("vector", [ev_mv], lambda e: e.tensor_mul(mui2, c_muT[:, :], muiv), ch="V")
        ev_cs8 = op("vector", [ev_m2v], lambda e: e.tensor_reduce(
            csum, mui2.rearrange("p (e c) -> p e c", c=8), AX.X, OP.add), ch="V")
        ev_ivb = op("vector", [ev_iv, ev_ctxL[H - 1][2]], lambda e: e.tensor_copy(ivb, ivT), ch="V")
        ev_muivb = op("vector", [ev_mv], lambda e: e.tensor_copy(muivb, muiv), ch="V")
        ev_csb = op("vector", [ev_cs8], lambda e: e.tensor_copy(csb, csum), ch="V")
        ev_lA = None
        ev_lB = None
        for c in range(8):
            w = [ev_mcb, ev_ivb]
            if c == 0:
                w += [ev_t1_3, ev_xf[-1]]
            ev_lA = op("tensor", w, lambda e, c=c: e.matmul(
                pB[0:1, 1536:1544], mtb_ap(8 + c, 9 + c), ivb[:, c:64:8],
                start=(c == 0), stop=(c == 7)), ch="T")
        for c in range(8):
            ev_lB = op("tensor", [ev_mcb, ev_muivb], lambda e, c=c: e.matmul(
                pB[0:1, 1544:1552], mtb_ap(c, c + 1), muivb[:, c:64:8],
                start=(c == 0), stop=(c == 7)), ch="T")
        ev_lC = op("tensor", [ev_csb, evc["ones"]], lambda e: e.matmul(
            pB[0:1, 1552:1560], c_ones[:, 0:1], csb[:, 0:8],
            start=True, stop=True), ch="T")
        ev_l0 = op("vector", [ev_lA, ev_lB, ev_lC], lambda e: e.tensor_copy(
            t_loss[0:1, 8:32], pB[0:1, 1536:1560]), ch="V")
        ev_l1 = op("vector", [ev_l0], lambda e: e.scalar_tensor_tensor(
            t_loss[0:1, 0:8], t_loss[0:1, 16:24], -2.0, t_loss[0:1, 8:16],
            OP.mult, OP.add), ch="V")
        ev_ag3s = dma("sync", [ev_l1], ag3_in[:, :], t_loss[0:1, 0:8], stream="sAG2")
        ev_cc3 = op("gpsimd", [ev_ag3s], lambda e: e.collective_compute(
            "AllGather", OP.bypass, replica_groups=groups,
            ins=[ag3_in.ap()], outs=[ag3_out.ap()]), ch="cc")
        ev_l3l = dma("gpsimd", [ev_cc3], t_loss[0:1, 8:24],
                     bass.AP(ag3_out, 0, [[16, 1], [1, 16]]), stream="gE")
        ev_l2a = op("vector", [ev_l3l], lambda e: e.tensor_add(
            t_loss[0:1, 0:8], t_loss[0:1, 8:16], t_loss[0:1, 16:24]), ch="V")
        ev_l2 = op("vector", [ev_l2a], lambda e: e.tensor_add(
            t_loss[0:1, 0:8], t_loss[0:1, 0:8], t_loss[0:1, 24:32]), ch="V")
        ev_lmin = op("vector", [ev_l2], lambda e: e.tensor_reduce(
            t_sc[0:1, 0:1], t_loss[0:1, 0:8], AX.X, OP.min), ch="V")
        ev_msk = op("vector", [ev_lmin], lambda e: e.tensor_tensor(
            t_sc[0:1, 2:10], t_loss[0:1, 0:8],
            t_sc[0:1, 0:1].broadcast_to((1, 8)), OP.is_equal), ch="V")
        ev_mio = op("vector", [ev_msk, evc["iota"]], lambda e: e.tensor_mul(
            t_sc[0:1, 2:10], t_sc[0:1, 2:10], c_iota[0:1, :]), ch="V")
        ev_idxf = op("vector", [ev_mio], lambda e: e.tensor_reduce(
            t_sc[0:1, 1:2], t_sc[0:1, 2:10], AX.X, OP.add), ch="V")
        ev_idx = op("vector", [ev_idxf], lambda e: e.tensor_copy(
            t_idx[0:1, 0:1], t_sc[0:1, 1:2]), ch="V")

        wc_t = [Wa[:, j * 1024:(j + 1) * 1024] for j in range(KC)]
        idx_cell = {}

        def _load_idx(e):
            idx_cell["v"] = e.value_load(t_idx[0:1, 0:1])
            return None
        op("gpsimd", [ev_idx, ev_womm], _load_idx)
        ev_w_dw = op("gpsimd", [("gE", CH["gE"][1])], lambda e: e.dma_start(
            out=bass.AP(G16, RQ + 16384, [[81032, 128], [64, 8], [1, 64]]),
            in_=bass.AP(d_dw, idx_cell["v"] * (D * RK), [[RK, 128], [128 * RK, 8], [1, RK]])),
            ch="gE", k=16)
        ev_w_db = op("gpsimd", [("gE2", CH["gE2"][1])], lambda e: e.dma_start(
            out=t_bdf, in_=bass.AP(d_db, idx_cell["v"] * RK, [[RK, 1], [1, RK]])),
            ch="gE2", k=16)
        ev_w_uw = op("gpsimd", [("gE3", CH["gE3"][1])], lambda e: e.dma_start(
            out=bass.AP(G16, RQ + 16384 + 512, [[81032, 128], [65, 8], [1, 64]]),
            in_=bass.AP(d_uwT, idx_cell["v"] * (D * RK), [[RK, 128], [128 * RK, 8], [1, RK]])),
            ch="gE3", k=16)
        ev_w_ub = op("gpsimd", [("gE2", CH["gE2"][1])], lambda e: e.dma_start(
            out=bass.AP(G16, RQ + 16384 + 1032, [[81032, 128], [1, 8]]),
            in_=bass.AP(d_ubT, idx_cell["v"], [[NE, 128], [128 * NE, 8]]),
            allow_slow_non_contiguous=True), ch="gE2", k=16)
        ev_w_cb = op("gpsimd", [("gE3", CH["gE3"][1])], lambda e: e.dma_start(
            out=t_bcrow, in_=bass.AP(d_cb, idx_cell["v"] * D, [[D, 1], [1, D]])),
            ch="gE3", k=16)
        ev_w_cw1 = op("gpsimd", [("gE", CH["gE"][1])], lambda e: e.dma_start(
            out=bass.AP(Wa, 0, [[8192, 128], [1024, 4], [1, 1024]]),
            in_=bass.AP(d_cwT, idx_cell["v"] * (D * D), [[D, 128], [128 * D, 4], [1, D]])),
            ch="gE", k=16)
        ev_w_cw2 = op("gpsimd", [("gE2", CH["gE2"][1])], lambda e: e.dma_start(
            out=bass.AP(Wa, 4096, [[8192, 128], [1024, 4], [1, 1024]]),
            in_=bass.AP(d_cwT, idx_cell["v"] * (D * D) + 4 * 128 * D, [[D, 128], [128 * D, 4], [1, D]])),
            ch="gE2", k=16)
        ev_ffw = [ev_w_cw1, ev_w_cw2]
        ev_bdc = op("vector", [ev_w_db], lambda e: e.tensor_copy(t_bdrow[0:1, :], t_bdf), ch="V")
        ev_bu = op("vector", [ev_w_ub], lambda e: e.tensor_copy(
            bass.AP(G16, RQ + 16384 + 512 + 64, [[81032, 128], [65, 8]]),
            bass.AP(G16, RQ + 16384 + 1032, [[81032, 128], [1, 8]])), ch="V")

        ev_base = [None] * KC
        bank_free = {0: [None] * 3, 1: [None] * 3}
        for och in range(KC):
            ph = pA if och % 2 == 0 else pB
            ev_ev = None
            for g in range(4):
                gb = g % 3
                ev_mm = None
                for j in range(KC):
                    w = [ev_bw[j], ev_xf[j]]
                    if j == 0:
                        w.append(bank_free[och % 2][gb])
                    if och == 1 and g == 0 and j == 0:
                        w.append(ev_xf[-1])
                    if och == 0 and g == 0 and j == 0:
                        w += [ev_m3b, ev_t1_3]
                    ev_mm = op("tensor", w,
                               lambda e, ph=ph, g=g, gb=gb, j=j, och=och: e.matmul(
                                   ph[:, gb * 512:(gb + 1) * 512],
                                   bw_t[j][:, och * 128:(och + 1) * 128],
                                   xf[:, j * 2048 + g * 512:j * 2048 + (g + 1) * 512],
                                   start=(j == 0), stop=(j == KC - 1)), ch="T")
                ev_ev = op("scalar", [ev_mm],
                           lambda e, ph=ph, g=g, gb=gb, och=och: e.activation(
                               ffo[:, och * 2048 + g * 512:och * 2048 + (g + 1) * 512],
                               ph[:, gb * 512:(gb + 1) * 512], AF.Copy), ch="S")
                bank_free[och % 2][gb] = ev_ev
            ev_base[och] = ev_ev

        ev_hones = op("vector", [ev_stage_free, ev_eq_free[0], ev_eq_free[1]],
                      lambda e: e.memset(h_ext[64:65, :], 1.0), ch="V")
        ev_or = op("vector", [ev_hones], lambda e: e.memset(t_ones512[0:1, :], 1.0), ch="V")
        ev_hmm = []
        for g in range(4):
            evh = None
            for j in range(KC):
                w = [ev_w_dw, ev_xf[j]]
                if g == 0 and j == 0:
                    w += [ev_lA, ev_m3b, ev_base[KC - 2]]
                evh = op("tensor", w,
                         lambda e, g=g, j=j: e.matmul(
                             pA[0:64, g * 512:(g + 1) * 512],
                             t_ffw[:, j * 64:(j + 1) * 64],
                             xf[:, j * 2048 + g * 512:j * 2048 + (g + 1) * 512],
                             start=(j == 0), stop=False), ch="T")
            evh = op("tensor", [ev_w_db, ev_or, ev_bdc], lambda e, g=g: e.matmul(
                pA[0:64, g * 512:(g + 1) * 512],
                t_bdrow[0:1, :], t_ones512[0:1, :],
                start=False, stop=True), ch="T")
            ev_hmm.append(evh)
        ev_hext = op("scalar", ev_hmm + [ev_m2c[-1]], lambda e: e.activation(
            h_ext[0:64, :], pA[0:64, 0:2048], AF.Relu), ch="S")

        ev_Mmm = None
        for j in range(KC):
            for g2 in range(2):
                w = [ev_w_uw, ev_w_cw1, ev_w_cw2, ev_bu]
                if j == 0 and g2 == 0:
                    w += [ev_l2, ev_lC, ev_l0, ev_base[KC - 1]]
                ev_Mmm = op("tensor", w,
                            lambda e, j=j, g2=g2: e.matmul(
                                pB[0:65, g2 * 512:(g2 + 1) * 512],
                                t_ffw[:, 512 + j * 65:512 + (j + 1) * 65],
                                wc_t[j][:, g2 * 512:(g2 + 1) * 512],
                                start=(j == 0), stop=(j == KC - 1)), ch="T")
        ev_Me1 = op("vector", [ev_Mmm, ev_hones], lambda e: e.tensor_copy(
            M_ext[0:64, :], pB[0:64, 0:1024]), ch="V")
        ev_Me2 = op("vector", [ev_Mmm, ev_w_cb], lambda e: e.scalar_tensor_tensor(
            M_ext[64:65, :], pB[64:65, 0:1024], 1.0, t_bcrow[0:1, :],
            OP.mult, OP.add), ch="V")

        ev_ffo = [None] * KC
        yb_free = {0: None, 1: None}
        for och in range(KC):
            ph = pA if och % 2 == 0 else pB
            ev_mm = None
            for g in range(4):
                w = [ev_hext, ev_Me1, ev_Me2]
                if g == 0:
                    w.append(yb_free[och % 2])
                if och == 0 and g == 0:
                    w += [ev_lA, ev_lB, ev_lC, ev_l0]
                ev_mm = op("tensor", w,
                           lambda e, ph=ph, g=g, och=och: e.matmul(
                               ph[:, g * 512:(g + 1) * 512],
                               M_ext[0:65, och * 128:(och + 1) * 128],
                               h_ext[0:65, g * 512:(g + 1) * 512],
                               start=(g >= 0), stop=True), ch="T")
            ev_ffo[och] = op("vector", [ev_mm, ev_base[och], evc["bb"]],
                             lambda e, ph=ph, och=och: e.scalar_tensor_tensor(
                                 ffo[:, och * 2048:(och + 1) * 2048], ph[:, :],
                                 c_bb[:, och:och + 1],
                                 ffo[:, och * 2048:(och + 1) * 2048],
                                 OP.add, OP.add), ch="V")
            yb_free[och % 2] = ev_ffo[och]

        sqf = g16(ZC, 128, 2048)
        ev_m4b, ev_r4, ev_ss4, ev_t1_4, ev_sd4 = ln_stats(
            lambda j: ffo[:, j * 2048:(j + 1) * 2048], ev_ffo, sqf,
            [ev_ffo[-1], ev_ss3, ev_sd3], [ev_ffo[KC - 2], ev_ffo[KC - 1]],
            [ev_ffo[KC - 2], ev_ffo[KC - 1]])
        ev_out = []
        ost_free = [None, None, None, None]
        ev_lnt_prev = ev_r4
        for j in range(KC):
            sj = j % 4
            ev_t = op("vector", [ev_ffo[j], ev_m4b, ev_ss4, ev_lnt_prev],
                      lambda e, j=j: e.tensor_sub(
                          ln_t[:, :], ffo[:, j * 2048:(j + 1) * 2048], ln_m[:, :]), ch="V")
            ev_t2 = op("vector", [ev_t, ev_r4],
                       lambda e: e.tensor_mul(ln_t[:, :], ln_t[:, :], pB[:, 0:2048]), ch="V")
            ev_o = op("vector", [ev_t2, ev_x1[j], ost_free[sj]],
                      lambda e, j=j, sj=sj: e.tensor_add(
                          outst[sj], ln_t[:, :], x1[:, j * 2048:(j + 1) * 2048]), ch="V")
            ev_lnt_prev = ev_o
            ev_d = dma("sync", [ev_o], d_out[j * 128:(j + 1) * 128, :], outst[sj],
                       stream=["sO0", "sO1", "sO2", "sO3"][j % 4])
            ost_free[sj] = ev_d
            ev_out.append(ev_d)
        op("sync", ev_out + [(nm, CH[nm][1]) for nm in
                             ("sQ", "sK", "sL", "sAG1", "sAG2", "sST",
                              "sO0", "sO1", "sO2", "sO3", "sC", "qd")],
           lambda e: None)
        op("gpsimd", [(nm, CH[nm][1]) for nm in
                      ("gC", "gW", "gW2", "gW3", "gX", "gX2", "gX3", "gX4",
                       "gE", "gE2", "gE3", "qg", "cc")],
           lambda e: None)
        op("scalar", [("qs", CH["qs"][1])], lambda e: None)

        import kernel as _K
        _K._PROG_SIZES = {k: len(v) for k, v in PROG.items()}
        if limit is not None:
            tgt, frac = limit
            for k in PROG:
                if tgt == "all" or k == tgt:
                    n = max(1, int(len(PROG[k]) * frac))
                    PROG[k] = PROG[k][:n]
        with nc.Block() as block:
            @block.sync
            def _(eng):
                for fn in PROG["sync"]:
                    fn(eng)

            @block.gpsimd
            def _(eng):
                for fn in PROG["gpsimd"]:
                    fn(eng)

            @block.tensor
            def _(eng):
                for fn in PROG["tensor"]:
                    fn(eng)

            @block.vector
            def _(eng):
                for fn in PROG["vector"]:
                    fn(eng)

            @block.scalar
            def _(eng):
                for fn in PROG["scalar"]:
                    fn(eng)

    return nc


def _consts():
    c = {}
    c["c_ones"] = np.ones((128, 128), np.float32)
    c["c_lnones"] = np.full((128, 128), 1.0 / 1024.0, np.float32)
    c["c_ident"] = np.eye(128, dtype=np.float32)
    selq = np.zeros((D, H), np.float32)
    selq[np.arange(D), np.arange(D) // 64] = 0.5
    c["c_selq"] = selq
    selk = np.zeros((512, HK), np.float32)
    selk[np.arange(512), np.arange(512) // 64] = 0.5
    c["c_selk"] = selk
    c["c_iota"] = np.arange(8, dtype=np.float32)[None, :].copy()
    return c


def _ec_layout(a):
    return np.ascontiguousarray(a.reshape(8, 8, 128).transpose(2, 0, 1).reshape(128, 64))


def kernel(**inputs):
    x = np.asarray(inputs["L0_attn_x"], np.float32)
    Wq = np.asarray(inputs["L0_attn_Wq"], np.float32)
    Wk = np.asarray(inputs["L0_attn_Wk"], np.float32)
    Wv = np.asarray(inputs["L0_attn_Wv"], np.float32)
    Wo = np.asarray(inputs["L0_attn_Wo"], np.float32)
    bo = np.asarray(inputs["L0_attn_bo"], np.float32)
    proj = np.asarray(inputs["L0_attn_proj"], np.float32)
    bW = np.asarray(inputs["L0_ffn_base_W"], np.float32)
    bb = np.asarray(inputs["L0_ffn_base_b"], np.float32)
    dmu = np.asarray(inputs["L0_ffn_disc_mu"], np.float32)
    dsg = np.asarray(inputs["L0_ffn_disc_sigma"], np.float32)
    dW = np.asarray(inputs["L0_ffn_down_W"], np.float32)
    db = np.asarray(inputs["L0_ffn_down_b"], np.float32)
    uW = np.asarray(inputs["L0_ffn_up_W"], np.float32)
    ub = np.asarray(inputs["L0_ffn_up_b"], np.float32)
    cW = np.asarray(inputs["L0_ffn_conv_W"], np.float32)
    cb = np.asarray(inputs["L0_ffn_conv_b"], np.float32)

    cst = _consts()
    projT2 = np.ascontiguousarray(np.vstack([proj.T, proj.T]))
    base_wT = np.ascontiguousarray(bW.T)
    up_wT = np.ascontiguousarray(uW.transpose(0, 2, 1))
    conv_wT = np.ascontiguousarray(cW.transpose(0, 2, 1))
    up_bT = np.ascontiguousarray(ub.T)
    bo_cols = np.ascontiguousarray(bo.reshape(8, 128).T)
    bb_cols = np.ascontiguousarray(bb.reshape(8, 128).T)
    smT = _ec_layout(dsg)
    muT = _ec_layout(dmu)

    nc = build_nc(4)
    in_maps = []
    for c in range(8):
        b, p = c // 2, c % 2
        xb = x[b]
        rolled = np.concatenate([xb[p * 2048:(p + 1) * 2048], xb[(1 - p) * 2048:(2 - p) * 2048]], 0)
        in_maps.append({
            "xT": np.ascontiguousarray(rolled.T),
            "wq": Wq,
            "wk": np.ascontiguousarray(Wk[:, p * 512:(p + 1) * 512]),
            "wv": np.ascontiguousarray(Wv[:, p * 512:(p + 1) * 512]),
            "wo": Wo,
            "bo_cols": bo_cols,
            "projT2": projT2,
            "base_wT": base_wT,
            "baseb_cols": bb_cols,
            "sigmaT_ec": smT,
            "muT_ec": muT,
            "down_w": dW,
            "down_b": db,
            "up_wT": up_wT,
            "up_bT": up_bT,
            "conv_wT": conv_wT,
            "conv_b": cb,
            "c_iota": cst["c_iota"],
            "c_ones": cst["c_ones"],
            "c_lnones": cst["c_lnones"],
            "c_ident": cst["c_ident"],
            "c_selq": cst["c_selq"],
            "c_selk": cst["c_selk"],
        })
    res = run_bass_kernel_spmd(nc, in_maps, core_ids=list(range(8)))
    out = np.empty((4, 4096, 1024), np.float32)
    for c in range(8):
        b, p = c // 2, c % 2
        out[b, p * 2048:(p + 1) * 2048] = np.ascontiguousarray(res.results[c]["outT"].T)
    return out


# version 29 (final)
# speedup vs baseline: 1.0482x; 1.0004x over previous
import numpy as np
from collections import defaultdict
from contextlib import ExitStack

import concourse.bass as bass
import concourse.mybir as mybir
from concourse.bass_utils import run_bass_kernel_spmd

F32 = mybir.dt.float32
BF16 = mybir.dt.bfloat16
I32 = mybir.dt.int32
AF = mybir.ActivationFunctionType
OP = mybir.AluOpType
AX = mybir.AxisListType

D = 1024
NF = 4096
NT = 2048
H = 16
HK = 8
M = 266
MCH = [(0, 128), (128, 128), (256, 10)]
EPS = 1e-4
DN = float(64.0 ** -0.25)
LNEPS = float(np.log(EPS))
NE = 8
RK = 64
KC = 8


def build_nc(npairs=4, limit=None):
    nc = bass.Bass("TRN2", target_bir_lowering=False)

    d_xT = nc.dram_tensor("xT", [D, NF], F32, kind="ExternalInput")
    d_wq = nc.dram_tensor("wq", [D, D], F32, kind="ExternalInput")
    d_wk = nc.dram_tensor("wk", [D, 512], F32, kind="ExternalInput")
    d_wv = nc.dram_tensor("wv", [D, 512], F32, kind="ExternalInput")
    d_wo = nc.dram_tensor("wo", [D, D], F32, kind="ExternalInput")
    d_bo = nc.dram_tensor("bo_cols", [128, 8], F32, kind="ExternalInput")
    d_projT = nc.dram_tensor("projT2", [128, M], F32, kind="ExternalInput")
    d_bwT = nc.dram_tensor("base_wT", [D, D], F32, kind="ExternalInput")
    d_bb = nc.dram_tensor("baseb_cols", [128, 8], F32, kind="ExternalInput")
    d_smT = nc.dram_tensor("sigmaT_ec", [128, 64], F32, kind="ExternalInput")
    d_muT = nc.dram_tensor("muT_ec", [128, 64], F32, kind="ExternalInput")
    d_dw = nc.dram_tensor("down_w", [NE, D, RK], F32, kind="ExternalInput")
    d_db = nc.dram_tensor("down_b", [NE, RK], F32, kind="ExternalInput")
    d_uwT = nc.dram_tensor("up_wT", [NE, D, RK], F32, kind="ExternalInput")
    d_ubT = nc.dram_tensor("up_bT", [D, NE], F32, kind="ExternalInput")
    d_cwT = nc.dram_tensor("conv_wT", [NE, D, D], F32, kind="ExternalInput")
    d_cb = nc.dram_tensor("conv_b", [NE, D], F32, kind="ExternalInput")
    d_ones = nc.dram_tensor("c_ones", [128, 128], F32, kind="ExternalInput")
    d_lnones = nc.dram_tensor("c_lnones", [128, 128], F32, kind="ExternalInput")
    d_ident = nc.dram_tensor("c_ident", [128, 128], F32, kind="ExternalInput")
    d_selq = nc.dram_tensor("c_selq", [D, H], F32, kind="ExternalInput")
    d_selk = nc.dram_tensor("c_selk", [512, HK], F32, kind="ExternalInput")
    d_iota = nc.dram_tensor("c_iota", [1, 8], F32, kind="ExternalInput")

    d_out = nc.dram_tensor("outT", [D, NT], F32, kind="ExternalOutput")
    ag1_in = nc.dram_tensor("ag1_in", [65, HK * M], BF16)
    ag1_out = nc.dram_tensor("ag1_out", [130, HK * M], BF16)
    ag2_in = nc.dram_tensor("ag2_in", [128, 16], F32)
    ag2_out = nc.dram_tensor("ag2_out", [256, 16], F32)
    ag3_in = nc.dram_tensor("ag3_in", [1, 8], F32)
    ag3_out = nc.dram_tensor("ag3_out", [2, 8], F32)
    groups = [[2 * i, 2 * i + 1] for i in range(npairs)]

    with ExitStack() as cx:
        en = cx.enter_context
        G16 = en(nc.sbuf_tensor("G16", [128, 81032], BF16))
        G32 = en(nc.sbuf_tensor("G32", [128, 3328], F32))
        Wa = en(nc.sbuf_tensor("W16a", [128, 8192], BF16))
        Wb = en(nc.sbuf_tensor("W16b", [128, 8192], BF16))
        c_ones = en(nc.sbuf_tensor("cb_ones", [128, 128], BF16))
        c_lnones = en(nc.sbuf_tensor("cb_lnones", [128, 128], BF16))
        c_idb = en(nc.sbuf_tensor("cb_id", [128, 128], BF16))
        c_idf = en(nc.sbuf_tensor("cf_id", [128, 128], F32))
        c_projb = en(nc.sbuf_tensor("cb_proj", [128, M], BF16))
        c_selq = en(nc.sbuf_tensor("cb_selq", [128, 128], BF16))
        c_selk = en(nc.sbuf_tensor("cb_selk", [128, 32], BF16))
        c_bo = en(nc.sbuf_tensor("cf_bo", [128, 8], F32))
        c_bb = en(nc.sbuf_tensor("cf_bb", [128, 8], F32))
        c_smT = en(nc.sbuf_tensor("cf_smT", [128, 64], F32))
        c_muT = en(nc.sbuf_tensor("cf_muT", [128, 64], F32))
        c_iota = en(nc.sbuf_tensor("cf_iota", [1, 8], F32))
        t_sc = en(nc.sbuf_tensor("t_sc", [128, 16], F32))
        t_scb = en(nc.sbuf_tensor("t_scb", [1, 16], BF16))
        t_loss = en(nc.sbuf_tensor("t_loss", [1, 32], F32))
        t_idx = en(nc.sbuf_tensor("t_idx", [1, 2], I32))
        t_bdrow = en(nc.sbuf_tensor("t_bdrow", [1, RK], BF16))
        pA = en(nc.psum_tensor("pA", [128, 2048], F32))
        pB = en(nc.psum_tensor("pB", [128, 2048], F32))
        pBb = pB.bitcast(BF16)
        pAb = pA.bitcast(BF16)
        G16f = G16.bitcast(F32)

        CH = {}
        for nm in ("T", "V", "S", "G", "qd", "qg", "qs", "cc",
                   "gC", "gW", "gW2", "gW3", "gX", "gX2", "gX3", "gX4",
                   "gE", "gE2", "gE3", "sO2", "sO3", "sQ", "sK", "sL",
                   "sAG1", "sAG2", "sST", "sO0", "sO1", "sC"):
            CH[nm] = [en(nc.semaphore("ch_" + nm)), 0]
        PROG = defaultdict(list)
        WAITED = defaultdict(dict)

        def op(eng, waits, build, ch=None, k=1):
            w2 = []
            for w in (waits or []):
                if w is None:
                    continue
                chn, v = w
                if v > WAITED[eng].get(chn, 0):
                    WAITED[eng][chn] = v
                    w2.append((chn, v))
            evout = None
            if ch is not None:
                c = CH[ch]
                c[1] += k
                evout = (ch, c[1])

            def thunk(engine, w2=w2, build=build, ch=ch, k=k):
                for chn, v in w2:
                    engine.wait_ge(CH[chn][0], v)
                ins = build(engine)
                if ch is not None and ins is not None:
                    ins.then_inc(CH[ch][0], k)
            PROG[eng].append(thunk)
            return evout

        QCH = {"sync": "qd", "gpsimd": "qg", "scalar": "qs"}

        def lowp(build):
            def b2(e, build=build):
                with nc.allow_low_precision(reason="bf16 operand rounding is intended"):
                    return build(e)
            return b2

        def dma(eng, waits, out, in_, stream=None):
            ch = stream or QCH[eng]
            pre = CH[ch][1]
            return op(eng, (waits or []) + ([(ch, pre)] if pre else []),
                      lambda e, o=out, i=in_: e.dma_start(out=o, in_=i),
                      ch=ch, k=16)

        def g16(off, parts, width):
            return G16[0:parts, off:off + width]

        def g16p(p0, p1, off, width):
            return G16[p0:p1, off:off + width]

        def g32(off, parts, width):
            return G32[0:parts, off:off + width]

        RS = 0
        RX = 19592
        RQ = 52360
        RM = 76936
        t_bcrow = g32(2280, 1, 1024)
        t_bdf = g32(1024, 1, RK)

        evc = {}
        evc["ones"] = dma("gpsimd", [], c_ones[:, :], d_ones[:, :], stream="gC")
        evc["lnones"] = dma("gpsimd", [], c_lnones[:, :], d_lnones[:, :], stream="gC")
        evc["idb"] = dma("gpsimd", [], c_idb[:, :], d_ident[:, :], stream="gC")
        evc["idf"] = dma("sync", [], c_idf[:, :], d_ident[:, :], stream="sC")
        evc["projb"] = dma("gpsimd", [], c_projb[:, :], d_projT[:, :], stream="gC")
        evc["selq"] = dma("gpsimd", [], c_selq[:, :],
                          bass.AP(d_selq, 0, [[H, 128], [128 * H, KC], [1, H]]), stream="gC")
        evc["selk"] = dma("gpsimd", [], c_selk[:, :],
                          bass.AP(d_selk, 0, [[HK, 128], [128 * HK, 4], [1, HK]]), stream="gC")
        evc["bo"] = dma("sync", [], c_bo[:, :], d_bo[:, :], stream="sC")
        evc["bb"] = dma("sync", [], c_bb[:, :], d_bb[:, :], stream="sC")
        evc["smT"] = dma("sync", [], c_smT[:, :], d_smT[:, :], stream="sC")
        evc["muT"] = dma("sync", [], c_muT[:, :], d_muT[:, :], stream="sC")
        evc["iota"] = dma("sync", [], c_iota[:, :], d_iota[:, :], stream="sC")

        wq_t = [Wa[:, j * 1024:(j + 1) * 1024] for j in range(KC)]
        wk_t = [Wb[:, j * 512:(j + 1) * 512] for j in range(KC)]
        wv_t = [Wb[:, 4096 + j * 512:4096 + (j + 1) * 512] for j in range(KC)]
        ev_wq = [dma("gpsimd", [], wq_t[j], d_wq[j * 128:(j + 1) * 128, :], stream="gW") for j in range(KC)]
        ev_wk = [dma("gpsimd", [], wk_t[j], d_wk[j * 128:(j + 1) * 128, :], stream="gW2") for j in range(KC)]
        ev_wv = [dma("gpsimd", [], wv_t[j], d_wv[j * 128:(j + 1) * 128, :], stream="gW3") for j in range(KC)]

        ZB = 16384
        ZC = 32768
        ZD = 49152
        xb_s = [g16(s * 8192, 128, 8192) for s in range(2)]
        xsq_s = [g16(ZD + s * 1024, 128, 1024) for s in range(2)]

        def xln_ap(j, c0, w):
            if c0 < 2048:
                return g16(ZC + j * 2048 + c0, 128, w)
            return g16(ZB + j * 2048 + (c0 - 2048), 128, w)

        m_b2 = [g16(RM + s_ * 1024, 128, 1024) for s_ in range(2)]
        msq_b2 = [g16(RM + 2048 + s_ * 1024, 128, 1024) for s_ in range(2)]
        r_rbp2 = [g16p(0, 1, RM + 2048 + s_ * 1024, 1024) for s_ in range(2)]
        tmp1_2 = [g32(s_ * 1024, 128, 1024) for s_ in range(2)]

        rT32p = g32(3296, 128, 16)
        r_rbp = g16p(0, 1, RM + 1024, 1024)
        ev_rT_p1 = None
        ev_stgf = [[], []]
        mr_free = [None, None]
        ev_xln = [[None] * 4 for _ in range(KC)]
        slot_free = [None, None]
        xsq_free = [None, None]
        mr_free = [None, None]
        ev_mb_last = None
        ev_t1_last = None
        for blk in range(4):
            s = blk % 2
            tb = blk * 1024
            stg0 = 8192 if blk % 2 == 0 else (RQ + 8192) // 2
            evxr = [dma("sync", list(ev_stgf[blk % 2]),
                        G16f[0:128, stg0 + j * 1024:stg0 + (j + 1) * 1024],
                        d_xT[j * 128:(j + 1) * 128, tb:tb + 1024],
                        stream=["sQ", "sK", "sO2", "sO3"][j % 4]) for j in range(KC)]
            evx = []
            for j in range(KC):
                src = G16f[0:128, stg0 + j * 1024:stg0 + (j + 1) * 1024]
                dst = xb_s[s][:, j * 1024:(j + 1) * 1024]
                if j % 2 == 0:
                    evx.append(op("scalar", [evxr[j], slot_free[s]],
                                  lambda e, src=src, dst=dst: e.activation(
                                      dst, src, AF.Copy), ch="S"))
                else:
                    evx.append(op("vector", [evxr[j], slot_free[s]],
                                  lambda e, src=src, dst=dst: e.tensor_copy(dst, src), ch="V"))
            ev_stgf[blk % 2] = [evx[-2], evx[-1]]
            pm = pA[:, s * 1024:(s + 1) * 1024]
            pq = pB[:, s * 1024:(s + 1) * 1024]
            m_b = m_b2[s]
            msq_b = msq_b2[s]
            r_rbp = r_rbp2[s]
            tmp1 = tmp1_2[s]
            ev_mm = None
            for j in range(KC):
                for g2 in range(2):
                    ev_mm = op("tensor", [evx[j], evc["lnones"]],
                               lambda e, pm=pm, s=s, j=j, g2=g2: e.matmul(
                                   pm[:, g2 * 512:(g2 + 1) * 512],
                                   c_lnones[:, :],
                                   xb_s[s][:, j * 1024 + g2 * 512:j * 1024 + (g2 + 1) * 512],
                                   start=(j == 0), stop=(j == KC - 1)), ch="T")
            ev_mb = op("vector", [ev_mm, mr_free[s]], lambda e, pm=pm, m_b=m_b: e.tensor_copy(m_b[:, :], pm), ch="V")
            ev_mb_last = ev_mb
            ev_smm = None
            for j in range(KC):
                sj = j % 2
                ev_sq = op("scalar", [evx[j], xsq_free[sj]],
                           lambda e, s=s, j=j, sj=sj: e.activation(
                               xsq_s[sj][:, :], xb_s[s][:, j * 1024:(j + 1) * 1024],
                               AF.Square), ch="S")
                for g2 in range(2):
                    ev_smm = op("tensor", [ev_sq, ev_mm if j == 0 else None],
                                lambda e, pq=pq, j=j, sj=sj, g2=g2: e.matmul(
                                    pq[:, g2 * 512:(g2 + 1) * 512],
                                    c_lnones[:, :],
                                    xsq_s[sj][:, g2 * 512:(g2 + 1) * 512],
                                    start=(j == 0), stop=(j == KC - 1)), ch="T")
                xsq_free[sj] = ev_smm
            ev_t1 = op("vector", [ev_smm], lambda e, pq=pq, tmp1=tmp1: e.tensor_copy(tmp1, pq), ch="V")
            ev_t1_last = ev_t1
            ev_m2 = op("scalar", [ev_mb, mr_free[s]], lambda e, msq_b=msq_b, m_b=m_b: e.activation(msq_b[:, :], m_b[:, :], AF.Square), ch="S")
            ev_var = op("vector", [ev_t1, ev_m2], lambda e, tmp1=tmp1, msq_b=msq_b: e.tensor_sub(tmp1, tmp1, msq_b[:, :]), ch="V")
            ev_sd = op("scalar", [ev_var], lambda e, tmp1=tmp1: e.activation(tmp1, tmp1, AF.Sqrt, bias=np.float32(1e-5)), ch="S")
            ev_vT = [op("tensor", [ev_sd, evc["idf"], ev_mb],
                        lambda e, cnk=cnk, s=s: e.transpose(
                            pA[0:128, s * 1024 + cnk:s * 1024 + cnk + 1],
                            G32[0:1, s * 1024 + cnk * 128:s * 1024 + (cnk + 1) * 128],
                            c_idf[0:1, 0:1]), ch="T") for cnk in range(8)]
            ev_rc = op("vector", ev_vT, lowp(lambda e, s=s: e.reciprocal(
                rT32p[0:128, s * 8:s * 8 + 8], pA[0:128, s * 1024:s * 1024 + 8])), ch="V")
            ev_rT = [op("tensor", [ev_rc, evc["idf"]],
                        lambda e, cnk=cnk, s=s: e.transpose(
                            pA[0:1, s * 1024 + cnk * 128:s * 1024 + (cnk + 1) * 128],
                            rT32p[0:128, s * 8 + cnk:s * 8 + cnk + 1],
                            c_idf[:, :]), ch="T") for cnk in range(8)]
            ev_re = op("vector", ev_rT + [ev_var], lambda e, s=s, r_rbp=r_rbp: e.tensor_copy(
                r_rbp[0:1, :], pA[0:1, s * 1024:s * 1024 + 1024]), ch="V")
            ev_bcr = [op("tensor", [ev_re, evc["ones"]],
                         lambda e, g=g, s=s, r_rbp=r_rbp: e.matmul(
                             pA[0:128, s * 1024 + g * 512:s * 1024 + (g + 1) * 512],
                             c_ones[0:1, 0:128],
                             r_rbp[0:1, g * 512:(g + 1) * 512],
                             start=True, stop=True), ch="T") for g in range(2)]
            ev_r = ev_bcr[-1]
            last_x = None
            for j in range(KC):
                sj = j % 2
                ev_t = op("vector", [ev_mb, evx[j], xsq_free[sj]],
                          lambda e, s=s, j=j, sj=sj, m_b=m_b: e.tensor_sub(
                              xsq_s[sj][:, :], xb_s[s][:, j * 1024:(j + 1) * 1024], m_b[:, :]),
                          ch="V")
                ev_x = op("vector", [ev_t, ev_r],
                          lambda e, j=j, tb=tb, sj=sj, s=s: e.tensor_mul(
                              xln_ap(j, tb, 1024), xsq_s[sj][:, :],
                              pA[:, s * 1024:(s + 1) * 1024]), ch="V")
                ev_xln[j][blk] = ev_x
                xsq_free[sj] = ev_x
                last_x = ev_x
            slot_free[s] = last_x
            mr_free[s] = last_x

        kpark = [g16(och * 4096, 128, 4096) for och in range(4)]
        ev_kpark = [None] * 4
        kh_free = {0: None, 1: None}
        for och in range(4):
            ev_ev = None
            for half in range(2):
                ph = pA if half == 0 else pB
                ev_mm = None
                for g in range(4):
                    for j in range(KC):
                        w = [ev_wk[j], ev_xln[j][(half * 2048 + g * 512) // 1024]]
                        if g == 0 and j == 0:
                            w.append(kh_free[half])
                        if och == 0 and g == 0 and j == 0:
                            w += [mr_free[0], mr_free[1], slot_free[0], slot_free[1], ev_t1_last]
                        ev_mm = op("tensor", w,
                                   lambda e, ph=ph, half=half, g=g, j=j, och=och: e.matmul(
                                       ph[:, g * 512:(g + 1) * 512],
                                       wk_t[j][:, och * 128:(och + 1) * 128],
                                       xln_ap(j, half * 2048 + g * 512, 512),
                                       start=(j == 0), stop=(j == KC - 1)), ch="T")
                ev_ev = op("scalar", [ev_mm, slot_free[och // 2] if half == 0 else None],
                           lambda e, ph=ph, och=och, half=half: e.activation(
                               kpark[och][:, half * 2048:(half + 1) * 2048], ph[:, :],
                               AF.Copy, scale=np.float32(DN)), ch="S")
                kh_free[half] = ev_ev
            ev_kpark[och] = ev_ev

        def v_t(cnk):
            return g16(RQ + 8192 + cnk * 512, 128, 512)

        ev_v = [None] * 32
        for blk in range(8):
            ph = pA if blk % 2 == 0 else pB
            evl = []
            for ci in range(4):
                cnk = blk * 4 + ci
                ev_mm = None
                for j in range(KC):
                    w = [ev_wv[j]]
                    if ci == 0 and j == 0:
                        if blk == 0:
                            w += [kh_free[0], kh_free[1]]
                        elif blk == 1:
                            w += [ev_v[3]]
                        else:
                            w += [ev_v[blk * 4 - 5]]
                    ev_mm = op("tensor", w,
                               lambda e, ph=ph, ci=ci, cnk=cnk, j=j: e.matmul(
                                   ph[:, ci * 512:(ci + 1) * 512],
                                   xln_ap(j, cnk * 128, 128),
                                   wv_t[j][:, :],
                                   start=(j == 0), stop=(j == KC - 1)), ch="T")
                evl.append(ev_mm)
            for ci in range(4):
                cnk = blk * 4 + ci
                ev_v[cnk] = op("scalar", [evl[ci]],
                               lambda e, ph=ph, ci=ci, cnk=cnk: e.activation(
                                   v_t(cnk)[:, :], ph[:, ci * 512:(ci + 1) * 512],
                                   AF.Copy), ch="S")

        qpark = [g16(ZB + pj * 2048, 128, 2048) for pj in range(KC)]
        ev_qpark = [None] * KC
        ph_free = {0: None, 1: None}
        ev_qmm_last = None
        for och in range(KC):
            ph = pA if och % 2 == 0 else pB
            ev_mm = None
            for g in range(4):
                for j in range(KC):
                    w = [ev_wq[j]]
                    if g == 0 and j == 0:
                        w.append(ph_free[och % 2])
                    if och == 0 and g == 0 and j == 0:
                        w += [ev_v[27], ev_v[31]]
                    if och == 1 and g == 0 and j == 0:
                        w += [ev_v[31]]
                    ev_mm = op("tensor", w,
                               lambda e, ph=ph, g=g, j=j, och=och: e.matmul(
                                   ph[:, g * 512:(g + 1) * 512],
                                   wq_t[j][:, och * 128:(och + 1) * 128],
                                   xln_ap(j, g * 512, 512),
                                   start=(j == 0), stop=(j == KC - 1)), ch="T")
            ev_qc = op("scalar", [ev_mm],
                       lambda e, ph=ph, och=och: e.activation(
                           qpark[och][:, :], ph[:, :], AF.Copy, scale=np.float32(DN)), ch="S")
            ev_qpark[och] = ev_qc
            ph_free[och % 2] = ev_qc
            ev_qmm_last = ev_mm

        dsc = [g16(RM + s_ * 2048, 128, 2048) for s_ in range(2)]
        ev_dq = None
        ev_dsq = [None, None]
        for pj in range(KC):
            sj = pj % 2
            w = [ev_qpark[pj], ev_dsq[sj]]
            if pj == 0:
                w += [slot_free[0], slot_free[1], mr_free[0], mr_free[1]]
            ev_sq2 = op("vector", w,
                        lambda e, pj=pj, sj=sj: e.tensor_mul(
                            dsc[sj][:, :], qpark[pj][:, :], qpark[pj][:, :]), ch="V")
            for g in range(4):
                ev_dq = op("tensor", [ev_sq2, evc["selq"]] +
                           ([ph_free[0], ph_free[1]] if (pj == 0 and g == 0) else []),
                           lambda e, pj=pj, g=g, sj=sj: e.matmul(
                               pA[0:16, g * 512:(g + 1) * 512],
                               c_selq[:, pj * 16:(pj + 1) * 16],
                               dsc[sj][:, g * 512:(g + 1) * 512],
                               start=(pj == 0), stop=(pj == KC - 1)), ch="T")
            ev_dsq[sj] = ev_dq
        dqrows = g32(1024, 16, 2048)
        ev_dqr = op("vector", [ev_dq], lambda e: e.tensor_copy(dqrows, pA[0:16, 0:2048]), ch="V")
        diagTq = g32(0, 128, 256)
        ev_tq_e = None
        for cnk in range(16):
            evt = op("tensor", [ev_dqr, evc["idf"], ev_tq_e],
                     lambda e, cnk=cnk: e.transpose(
                         pB[0:128, 1536:1552],
                         dqrows[0:16, cnk * 128:(cnk + 1) * 128],
                         c_idf[0:16, 0:16]), ch="T")
            ev_tq_e = op("vector", [evt], lambda e, cnk=cnk: e.tensor_copy(
                diagTq[:, cnk * 16:(cnk + 1) * 16], pB[0:128, 1536:1552]), ch="V")
        expdqe = g32(256, 128, 256)
        ev_expdq = op("scalar", [ev_tq_e], lambda e: e.activation(
            expdqe, diagTq, AF.Exp, bias=np.float32(LNEPS)), ch="S")

        ev_dk = None
        ev_dks = [None, None]
        for j in range(4):
            for half in range(2):
                sj = half
                w = [ev_kpark[j], ev_dks[sj]]
                if j == 0 and half == 0:
                    w.append(ev_dq)
                ev_sqk = op("vector", w,
                            lambda e, j=j, half=half, sj=sj: e.tensor_mul(
                                dsc[sj][:, :], kpark[j][:, half * 2048:(half + 1) * 2048],
                                kpark[j][:, half * 2048:(half + 1) * 2048]), ch="V")
                for gi in range(4):
                    g = half * 4 + gi
                    w2 = [ev_sqk, evc["selk"]]
                    if j == 0 and g == 0:
                        w2.append(ev_dqr)
                    if j == 0 and g == 4:
                        w2.append(ev_tq_e)
                    ev_dk = op("tensor", w2,
                               lambda e, j=j, g=g, gi=gi, sj=sj: e.matmul(
                                   (pA if g < 4 else pB)[0:8, (g % 4) * 512:(g % 4) * 512 + 512],
                                   c_selk[:, j * 8:(j + 1) * 8],
                                   dsc[sj][:, gi * 512:(gi + 1) * 512],
                                   start=(j == 0), stop=(j == 3)), ch="T")
                ev_dks[sj] = ev_dk
        dkrows = g32(1024, 8, 2048)
        diagTk = g32(512, 128, 256)
        ev_tk_e = None
        for half in range(2):
            ev_dkr = op("vector", [ev_dk] + ([ev_dqr] if half == 0 else [ev_tk_e]),
                        lambda e, half=half: e.tensor_copy(
                            dkrows, (pA if half == 0 else pB)[0:8, 0:2048]), ch="V")
            for c2 in range(16):
                cnk = half * 16 + c2
                evt = op("tensor", [ev_dkr, evc["idf"], ev_tk_e],
                         lambda e, c2=c2: e.transpose(
                             pA[0:128, 1536:1544],
                             dkrows[0:8, c2 * 128:(c2 + 1) * 128],
                             c_idf[0:8, 0:8]), ch="T")
                ev_tk_e = op("vector", [evt], lambda e, cnk=cnk: e.tensor_copy(
                    diagTk[:, cnk * 8:(cnk + 1) * 8], pA[0:128, 1536:1544]), ch="V")
        expndk = g32(768, 128, 256)
        ev_expdk = op("scalar", [ev_tk_e], lambda e: e.activation(
            expndk, diagTk, AF.Exp, scale=np.float32(-1.0)), ch="S")

        wo_t = [Wa[:, j * 1024:(j + 1) * 1024] for j in range(KC)]
        ev_wo = [dma("gpsimd", [ev_qmm_last], wo_t[j], d_wo[j * 128:(j + 1) * 128, :], stream="gW2")
                 for j in range(KC)]

        etok_s = [g16(ZC + s_ * 8512, 128, 8512) for s_ in range(2)]
        vts = g16(ZD + 640, 128, 2080)
        ctx_sbb = g16p(0, 65, RM, 2128)
        maxr2 = [g32(3152, 128, 34), g32(3186, 128, 34)]
        recB = g32(3220, 128, 1)
        epscol = g32(3222, 65, 1)
        skT = g32(3224, 128, 32)
        ev_ctx_ev = []
        ev_ctx_last = None
        ev_skT_free = [None]
        batches = [list(range(b, min(b + 3, 32))) for b in range(0, 32, 3)]
        ev_scan_of = {}
        etok_free = [None, None]
        ev_exp_of = {}

        ev_tr_of = {}
        ev_mx_of = {}
        ev_rbe_of = {}
        ev_ep_of = {}
        ev_ce_of = {}

        def ks_dash(hh):
            sl = hh % 2
            kch = hh // 2
            kb = (hh % 2) * 64
            ev_exp_all = []
            for bi, bat in enumerate(batches):
                ph = pA if bi % 2 == 0 else pB
                evm = []
                for ii, cnk in enumerate(bat):
                    w = [ev_kpark[kch], evc["projb"]]
                    if bi >= 2:
                        w.append(ev_exp_all[bi - 2])
                    if hh == 0 and bi <= 1:
                        w.append(ev_v[31])
                    if hh >= 1 and bi <= 1 and ii == 0:
                        w += ev_exp_of[hh - 1][-2:]
                    evm.append(op("tensor", w,
                                  lambda e, ph=ph, ii=ii, cnk=cnk, kch=kch, kb=kb: e.matmul(
                                      ph[0:128, ii * 512:ii * 512 + M],
                                      kpark[kch][kb:kb + 64, cnk * 128:(cnk + 1) * 128],
                                      c_projb[kb:kb + 64, :],
                                      start=True, stop=True), ch="T"))
                nb = len(bat)
                ev_exp = op("scalar", evm + [etok_free[sl] if bi == 0 else None],
                            lambda e, ph=ph, nb=nb, c0=bat[0], sl=sl: e.activation(
                                etok_s[sl][:, c0 * 266:(c0 + nb) * 266].rearrange(
                                    "p (b m) -> p b m", m=266),
                                bass.AP(ph, 0, [[2048, 128], [512, nb], [1, M]]),
                                AF.Exp), ch="S")
                ev_exp_all.append(ev_exp)
            ev_exp_of[hh] = ev_exp_all

        def ks_scan(hh):
            mx = maxr2[hh % 2]
            w = [ev_exp_of[hh][-1]]
            if hh >= 2:
                w.append(ev_tr_of[hh - 2])
            ev_red = op("vector", w,
                        lambda e, sl=hh % 2, mx=mx: e.tensor_reduce(
                            mx[:, 0:32],
                            etok_s[sl][:, :].rearrange("p (c m) -> p c m", m=266),
                            AX.X, OP.max), ch="V")
            ev_red2 = op("vector", [ev_red], lambda e, mx=mx: e.tensor_reduce(
                mx[:, 32:33], mx[:, 0:32], AX.X, OP.max), ch="V")
            ev_scan_of[hh] = ev_red2

        def ks_tailA(hh):
            mx = maxr2[hh % 2]
            w = [ev_scan_of[hh], evc["idf"]]
            if hh >= 2:
                w.append(ev_mx_of[hh - 2])
            ev_tr = op("tensor", w,
                       lambda e, mx=mx: e.transpose(pB[0:1, 1536:1664], mx[:, 32:33],
                                                    c_idf[:, :]), ch="T")
            ev_tr_of[hh] = ev_tr
            ev_mx = op("vector", [ev_tr], lambda e: e.tensor_reduce(
                t_sc[0:1, 0:1], pB[0:1, 1536:1664], AX.X, OP.max), ch="V")
            ev_mx_of[hh] = ev_mx
            ev_rc = op("vector", [ev_mx], lowp(lambda e: e.reciprocal(
                t_scb[0:1, 0:1], t_sc[0:1, 0:1])), ch="V")
            w = [ev_rc, evc["ones"]]
            if hh >= 2:
                w.append(ev_rbe_of[hh - 2])
            ev_rb = op("tensor", w,
                       lambda e: e.matmul(pB[0:128, 1700:1701],
                                          c_ones[0:1, 0:128], t_scb[0:1, 0:1],
                                          start=True, stop=True), ch="T")
            ev_rbe = op("vector", [ev_rb], lambda e: e.tensor_copy(recB, pB[0:128, 1700:1701]), ch="V")
            ev_rbe_of[hh] = ev_rbe
            ev_sk = op("vector", [ev_rbe, ev_expdk, ev_skT_free[0]],
                       lambda e, hh=hh: e.tensor_scalar_mul(
                           skT, G32[0:128, 768 + hh:1024:8], recB[:, 0:1]), ch="V")
            return dict(ev_sk=ev_sk, ev_rb=ev_rb)

        def ks_tailB(hh, stA):
            sl = hh % 2
            ev_vt1 = op("vector", [stA["ev_sk"]],
                        lambda e, hh=hh: e.tensor_tensor(
                            vts[:, :].rearrange("p (c w) -> p c w", w=65)[:, :, 0:64],
                            bass.AP(G16, RQ + 8192 + hh * 64, [[81032, 128], [512, 32], [1, 64]]),
                            skT.rearrange("p (c o) -> p c o", o=1).broadcast_to((128, 32, 64)),
                            OP.mult), ch="V")
            ev_vt2 = op("vector", [stA["ev_sk"]], lambda e: e.tensor_copy(
                vts[:, :].rearrange("p (c w) -> p c w", w=65)[:, :, 64:65],
                skT.rearrange("p (c o) -> p c o", o=1)), ch="V")
            ev_skT_free[0] = ev_vt2
            ev_cs = None
            for cnk in range(32):
                w = [ev_v[cnk], evc["ones"]]
                if cnk == 0:
                    w.append(stA["ev_rb"])
                    if hh >= 2:
                        w.append(ev_ep_of[hh - 2])
                ev_cs = op("tensor", w,
                           lambda e, cnk=cnk, hh=hh: e.matmul(
                               pB[0:64, 1792:1793],
                               v_t(cnk)[:, hh * 64:hh * 64 + 64],
                               c_ones[:, 0:1],
                               start=(cnk == 0), stop=(cnk == 31)), ch="T")
            ev_ep1 = op("vector", [ev_cs], lambda e: e.tensor_scalar_mul(
                epscol[0:64, 0:1], pB[0:64, 1792:1793], EPS), ch="V")
            ev_ep2 = op("vector", [ev_ep1], lambda e: e.memset(
                epscol[64:65, 0:1], EPS * float(NF)), ch="V")
            ev_ep_of[hh] = ev_ep1
            return dict(ev_vt1=ev_vt1, ev_vt2=ev_vt2, ev_ep2=ev_ep2, **stA)

        def ks_tailC(hh, stB):
            sl = hh % 2
            ev_exp_all = ev_exp_of[hh]
            ev_ctx = None
            for cnk in range(32):
                w = [stB["ev_vt1"], stB["ev_vt2"],
                     ev_exp_all[min(cnk // 3, len(ev_exp_all) - 1)]]
                if cnk == 0 and ev_ctx_ev:
                    w.append(ev_ctx_ev[-1])
                ev_ctx = op("tensor", w,
                            lambda e, cnk=cnk, sl=sl: e.matmul(
                                pA[0:65, 1536:1536 + M],
                                vts[:, cnk * 65:(cnk + 1) * 65],
                                etok_s[sl][:, cnk * 266:(cnk + 1) * 266],
                                start=(cnk == 0), stop=(cnk == 31)), ch="T")
            etok_free[sl] = ev_ctx
            ev_ce = op("vector", [ev_ctx, stB["ev_ep2"]],
                       lambda e, hh=hh: e.tensor_scalar(
                           ctx_sbb[:, hh * M:(hh + 1) * M], pA[0:65, 1536:1536 + M],
                           epscol[0:65, 0:1], None, OP.add), ch="V")
            ev_ctx_ev.append(ev_ce)
            return ev_ctx

        ev_ctx_last = None
        stA = {}
        stB = {}
        for hh in range(HK):
            if hh >= 1:
                stA[hh - 1] = ks_tailA(hh - 1)
            ks_dash(hh)
            if hh >= 1:
                stB[hh - 1] = ks_tailB(hh - 1, stA[hh - 1])
            ks_scan(hh)
            if hh >= 1:
                ev_ctx_last = ks_tailC(hh - 1, stB[hh - 1])
        stA[HK - 1] = ks_tailA(HK - 1)
        stB[HK - 1] = ks_tailB(HK - 1, stA[HK - 1])
        ev_ctx_last = ks_tailC(HK - 1, stB[HK - 1])

        eq0 = [g16(s_ * 6144, 128, 2048) for s_ in range(2)]
        eq1 = [g16(s_ * 6144 + 2048, 128, 2048) for s_ in range(2)]
        eq2 = [g16p(0, 16, s_ * 6144 + 4096, 2048) for s_ in range(2)]
        stage = g16(ZD + 640, 128, 2048)
        merged = g16(ZC, 128, 16384)
        u_row = [g16p(0, 1, 12288 + s_ * 2048, 2048) for s_ in range(2)]
        u_tr = g32(3168, 16, 128)
        maxEq_p = [g32(3296, 128, 16), g32(1024, 128, 16)]
        u_cols_p = [g32(3312, 128, 16), g32(1040, 128, 16)]
        rbT_b = g16p(0, 128, RQ + 14336, 16)
        scanb = g16(RQ + 8192, 128, 2048)
        num_sb = [g16p(0, 65, RQ + 10240 + s * 2048, 2048) for s in range(2)]

        ev_eq_free = [None, None]
        ev_u_free = [None, None]
        ev_stage_free = None
        ev_scan_prev = None
        ev_uc_pp = [None, None]
        ev_ut_pp = [None, None]
        ev_ur_prev = None
        ev_rT_prev = None
        ev_bc_prev = None
        ev_num_free = [None, None]
        ev_par_free = [ev_ctx_ev[-1], ev_ctx_ev[-1]]
        ev_merged = []
        st_front = {}

        def qs_dash(h):
            nonlocal ev_scan_prev
            pj = h // 2
            sl = pj % 2
            es = h % 2
            qb = es * 64
            P = pA if es == 0 else pB
            Pb = pAb if es == 0 else pBb

            def dmm(m0, mw, g, w):
                return op("tensor", w,
                          lambda e, g=g, m0=m0, mw=mw, qb=qb, pj=pj, P=P: e.matmul(
                              P[0:mw, g * 512:(g + 1) * 512],
                              c_projb[qb:qb + 64, m0:m0 + mw],
                              qpark[pj][qb:qb + 64, g * 512:(g + 1) * 512],
                              start=True, stop=True), ch="T")
            w0 = [ev_qpark[pj], evc["projb"], ev_par_free[es]]
            ev_c2a = [dmm(256, 10, g, w0) for g in (0, 1)]
            ev_c2b = [dmm(256, 10, g, w0) for g in (2, 3)]
            ev_e2a = op("scalar", ev_c2a + [ev_eq_free[es]],
                        lambda e, es=es, P=P: e.activation(
                            eq2[es][0:10, 0:1024], P[0:10, 0:1024], AF.Exp), ch="S")
            ev_e2b = op("scalar", ev_c2b + [ev_eq_free[es]],
                        lambda e, es=es, P=P: e.activation(
                            eq2[es][0:10, 1024:2048], P[0:10, 1024:2048], AF.Exp), ch="S")
            ev_c0a = [dmm(0, 128, g, [ev_e2a]) for g in (0, 1)]
            ev_c0b = [dmm(0, 128, g, [ev_e2b]) for g in (2, 3)]
            ev_e0a = op("scalar", ev_c0a + [ev_eq_free[es]],
                        lambda e, es=es, P=P: e.activation(
                            eq0[es][:, 0:1024], P[:, 0:1024], AF.Exp), ch="S")
            ev_e0b = op("scalar", ev_c0b + [ev_eq_free[es]],
                        lambda e, es=es, P=P: e.activation(
                            eq0[es][:, 1024:2048], P[:, 1024:2048], AF.Exp), ch="S")
            st = dict(es=es, sl=sl, pj=pj, qb=qb, P=P, Pb=Pb, dmm=dmm,
                      ev_e2a=ev_e2a, ev_e2b=ev_e2b, ev_e0a=ev_e0a, ev_e0b=ev_e0b)
            return st

        def qs_dash2(st, h):
            es = st["es"]; P = st["P"]; dmm = st["dmm"]
            ev_c1a = [dmm(128, 128, g, [st["ev_e0a"]]) for g in (0, 1)]
            ev_c1b = [dmm(128, 128, g, [st["ev_e0b"]]) for g in (2, 3)]
            ev_e1a = op("scalar", ev_c1a + [ev_eq_free[es]],
                        lambda e, es=es, P=P: e.activation(
                            eq1[es][:, 0:1024], P[:, 0:1024], AF.Exp), ch="S")
            ev_e1b = op("scalar", ev_c1b + [ev_eq_free[es]],
                        lambda e, es=es, P=P: e.activation(
                            eq1[es][:, 1024:2048], P[:, 1024:2048], AF.Exp), ch="S")
            st["ev_e1a"] = ev_e1a
            st["ev_e1b"] = ev_e1b
            st["ev_e0"] = st["ev_e0b"]
            st["ev_e1"] = ev_e1b
            st["ev_e2"] = st["ev_e2b"]
            return st

        def qs_scan(st, h):
            nonlocal ev_scan_prev
            es = st["es"]; P = st["P"]; Pb = st["Pb"]
            ev_cm = []
            for half, (ee0, ee1, ee2) in enumerate(
                    [(st["ev_e0a"], st["ev_e1a"], st["ev_e2a"]),
                     (st["ev_e0b"], st["ev_e1b"], st["ev_e2b"])]):
                c0 = half * 1024
                w_cm = [ee0, ee1]
                if ev_scan_prev is not None:
                    w_cm.append(ev_scan_prev)
                ev_cm1 = op("vector", w_cm, lambda e, es=es, c0=c0: e.tensor_max(
                    scanb[:, c0:c0 + 1024], eq0[es][:, c0:c0 + 1024],
                    eq1[es][:, c0:c0 + 1024]), ch="V")
                ev_cm2 = op("vector", [ev_cm1, ee2], lambda e, es=es, c0=c0: e.tensor_max(
                    scanb[0:10, c0:c0 + 1024], scanb[0:10, c0:c0 + 1024],
                    eq2[es][0:10, c0:c0 + 1024]), ch="V")
                ev_cm.append(ev_cm2)
            ev_sct = []
            ev_scr = []
            for cnk in range(16):
                w = [ev_cm[cnk // 8], evc["idb"], st["ev_e1b"]]
                if cnk == 8:
                    w.append(ev_scr[0])
                ev_sct.append(op("tensor", w,
                                 lambda e, cnk=cnk, Pb=Pb: e.transpose(
                                     Pb[:, (cnk % 8) * 128:(cnk % 8) * 128 + 128],
                                     scanb[:, cnk * 128:(cnk + 1) * 128],
                                     c_idb[:, :]), ch="T"))
                if cnk % 8 == 7:
                    wr = ev_sct[-8:]
                    if cnk == 7:
                        wr = wr + [ev_uc_pp[es]]
                    ev_scr.append(op("vector", wr,
                                     lambda e, cnk=cnk, Pb=Pb, es=es: e.tensor_reduce(
                                         maxEq_p[es][:, cnk - 7:cnk + 1],
                                         Pb[:, 0:1024].rearrange("p (c f) -> p c f", f=128),
                                         AX.X, OP.max), ch="V"))
            ev_scan_prev = ev_sct[-1]
            ev_uc = op("vector", [ev_scr[-1], ev_expdq, ev_ut_pp[es]],
                       lambda e, h=h, es=es: e.tensor_mul(
                           u_cols_p[es], G32[0:128, 256 + h:512:16], maxEq_p[es]), ch="V")
            ev_uc_pp[es] = ev_uc
            st.update(ev_cm1=ev_cm[1], ev_scr=ev_scr, ev_uc=ev_uc)
            return st

        def qs_ur(st, h):
            nonlocal ev_ur_prev
            es = st["es"]
            P = st["P"]
            ev_ut = op("tensor", [st["ev_uc"], evc["idf"], st["ev_e1b"]],
                       lambda e, P=P, es=es: e.transpose(
                           P[0:16, 1024:1152], u_cols_p[es], c_idf[:, :]), ch="T")
            ev_ut_pp[es] = ev_ut
            ev_ue = op("vector", [ev_ut, ev_ur_prev], lambda e, P=P: e.tensor_copy(
                u_tr, P[0:16, 1024:1152]), ch="V")
            st["ev_ue"] = ev_ue
            ev_ur = dma("gpsimd", [ev_ue, ev_u_free[es]],
                        u_row[es][0:1, :].rearrange("o (c f) -> o c f", c=16), u_tr,
                        stream="gE")
            ev_ur_prev = ev_ur
            st["ev_ur"] = ev_ur
            return st

        def qs_am01(st, h):
            es = st["es"]; P = st["P"]
            ev_am = []
            for g in (1, 2, 3, 0):
                hv = "a" if g < 2 else "b"
                for mi in (0, 1):
                    m0, mw = MCH[mi]
                    src = ctxL[0:mw, h * 130 + mi * 65:h * 130 + mi * 65 + 65]
                    esrc = (eq0[es] if mi == 0 else eq1[es])[0:mw, g * 512:(g + 1) * 512]
                    w = [ev_ctxL[h][mi], st["ev_e0" + hv] if mi == 0 else st["ev_e1" + hv],
                         st["ev_e1" + hv]]
                    if h <= 1 and g == 1 and mi == 0:
                        w.append(ev_ctxL[H - 1][2])
                    if mi == 0:
                        if g == 0:
                            w = w + st["ev_scr"]
                        elif g == 2:
                            w.append(st["ev_ue"])
                    ev_am.append(op("tensor", w,
                                    lambda e, g=g, src=src, esrc=esrc, mi=mi, P=P: e.matmul(
                                        P[0:65, g * 512:(g + 1) * 512],
                                        src, esrc,
                                        start=(mi == 0), stop=False), ch="T"))
            st["ev_am01"] = ev_am
            return st

        def qs_am2(st, h):
            es = st["es"]; P = st["P"]
            ev_am = []
            for g in (1, 2, 3, 0):
                ev_am.append(op("tensor", [ev_ctxL[h][2], st["ev_e2"]],
                                lambda e, g=g, h=h, es=es, P=P: e.matmul(
                                    P[0:65, g * 512:(g + 1) * 512],
                                    ctxL2[0:10, h * 65:(h + 1) * 65],
                                    eq2[es][0:10, g * 512:(g + 1) * 512],
                                    start=False, stop=False), ch="T"))
                ev_am.append(op("tensor", [st["ev_ur"], ev_B_done],
                                lambda e, g=g, h=h, es=es, P=P: e.matmul(
                                    P[0:65, g * 512:(g + 1) * 512],
                                    B_rows[0:1, h * 65:(h + 1) * 65],
                                    u_row[es][0:1, g * 512:(g + 1) * 512],
                                    start=False, stop=True), ch="T"))
            ev_am = st["ev_am01"] + ev_am
            ev_u_free[es] = ev_am[-1]
            ev_eq_free[es] = ev_am[-1]
            st["ev_am"] = ev_am
            return st

        def qs_nev(st, h):
            es = st["es"]; P = st["P"]
            st["ev_nev"] = op("scalar", st["ev_am"] + [ev_num_free[es]],
                              lambda e, es=es, P=P: e.activation(
                                  num_sb[es][0:65, :], P[0:65, :], AF.Copy), ch="S")
            return st

        def qs_dT(st, h):
            es = st["es"]; Pb = st["Pb"]
            st["ev_dT"] = [op("tensor", [st["ev_nev"], evc["idb"]],
                              lambda e, cnk=cnk, es=es, Pb=Pb: e.transpose(
                                  Pb[0:128, 2048 + 2 * cnk:2048 + 2 * cnk + 1],
                                  num_sb[es][64:65, cnk * 128:(cnk + 1) * 128],
                                  c_idb[64:65, 64:65]), ch="T") for cnk in range(16)]
            return st

        def qs_rcp(st, h):
            nonlocal ev_rT_prev
            Pb = st["Pb"]
            st["ev_rcp"] = op("vector", st["ev_dT"] + [ev_rT_prev],
                              lowp(lambda e, Pb=Pb: e.reciprocal(
                                  rbT_b[0:128, 0:16], Pb[0:128, 2048:2080:2])), ch="V")
            ev_rT_prev = st["ev_rcp"]
            return st

        def qs_bc(st, h):
            P = st["P"]
            st["ev_bc"] = [op("tensor", [st["ev_rcp"], evc["idb"]],
                              lambda e, cnk=cnk, P=P: e.matmul(
                                  P[0:64, cnk * 128:(cnk + 1) * 128],
                                  bass.AP(G16, RQ + 14336 + cnk, [[81032, 128], [0, 64]]),
                                  c_idb[:, :],
                                  start=True, stop=True), ch="T") for cnk in range(16)]
            return st

        def qs_mul(st, h):
            nonlocal ev_stage_free
            es = st["es"]; P = st["P"]; pj = st["pj"]
            if es == 0:
                dsth = merged[0:64, pj * 2048:(pj + 1) * 2048]
                wex = []
            else:
                dsth = stage[0:64, 0:2048]
                wex = [ev_stage_free]
            ev_m = op("vector", st["ev_bc"] + wex + [st["ev_nev"]],
                      lambda e, dsth=dsth, es=es, P=P: e.tensor_mul(
                          dsth, num_sb[es][0:64, 0:2048], P[0:64, 0:2048]), ch="V")
            ev_merged.append(ev_m)
            ev_par_free[es] = ev_m
            ev_num_free[es] = ev_m
            out_ev = ev_m
            if es == 1:
                evd = dma("sync", [ev_m], merged[64:128, pj * 2048:(pj + 1) * 2048],
                          stage[0:64, 0:2048], stream="sST")
                ev_merged.append(evd)
                ev_stage_free = evd
                out_ev = evd
            return out_ev

        def qs_front_all(h):
            st = qs_dash(h)
            qs_dash2(st, h)
            qs_scan(st, h)
            qs_ur(st, h)
            st_front[h] = st
            return st

        prev_psum = None
        ev_ag1s = dma("sync", ev_ctx_ev, ag1_in[:, :], ctx_sbb, stream="sAG1")
        ev_cc1 = op("gpsimd", [ev_ag1s], lambda e: e.collective_compute(
            "AllGather", OP.bypass, replica_groups=groups,
            ins=[ag1_in.ap()], outs=[ag1_out.ap()]), ch="cc")
        qs_front_all(0)
        qs_front_all(1)
        ctxTb = g16p(0, 65, RQ, 4256)
        ev_ctb = [dma("sync", [ev_cc1, ev_ctx_last], ctxTb[:, g * 2128:(g + 1) * 2128],
                      ag1_out[g * 65:(g + 1) * 65, :], stream="sL") for g in range(2)]

        Bcols = g32(3152, 65, 16)
        ev_B = op("vector", ev_ctb, lambda e: e.tensor_reduce(
            Bcols, ctxTb[:, :].rearrange("p (h m) -> p h m", m=M),
            AX.X, OP.add), ch="V")
        B_rows = g16p(0, 1, RQ + 14352, 1040)
        ev_Brt = []
        ev_Br = []
        for h in range(H):
            w = [ev_B, evc["idf"], ev_ctx_last]
            if h % 4 == 0 and ev_Br:
                w.append(ev_Br[-1])
            evt = op("tensor", w,
                     lambda e, h=h: e.transpose(
                         pB[0:1, 1536 + (h % 4) * 65:1536 + (h % 4) * 65 + 65],
                         Bcols[0:65, h:h + 1], c_idf[0:65, 0:65]), ch="T")
            ev_Brt.append(evt)
            if h % 4 == 3:
                ev_Br.append(op("vector", ev_Brt[-4:],
                                lambda e, h=h: e.tensor_copy(
                                    B_rows[0:1, (h - 3) * 65:(h + 1) * 65],
                                    pB[0:1, 1536:1536 + 260]), ch="V"))
        ev_B_done = ev_Br[-1]

        ctxL = g16(RM, 128, 2080)
        ctxL2 = g16p(0, 16, RM + 2080, 1040)
        ev_ctxL = {}
        prev_t = ev_Br[-1]
        for h in range(H):
            g = h // 8
            hh = h % 8
            evs = []
            for mi, (m0, mw) in enumerate(MCH):
                evt = op("tensor", [ev_ctb[g], evc["idb"], prev_t],
                         lambda e, g=g, hh=hh, m0=m0, mw=mw: e.transpose(
                             pBb[0:mw, 3072:3072 + 65],
                             ctxTb[0:65, g * 2128 + hh * M + m0:g * 2128 + hh * M + m0 + mw],
                             c_idb[0:65, 0:65]), ch="T")
                if mi < 2:
                    dst = ctxL[0:mw, h * 130 + mi * 65:h * 130 + mi * 65 + 65]
                else:
                    dst = ctxL2[0:10, h * 65:(h + 1) * 65]
                eve = op("vector", [evt], lambda e, dst=dst, mw=mw: e.tensor_copy(
                    dst, pBb[0:mw, 3072:3072 + 65]), ch="V")
                evs.append(eve)
                prev_t = eve
            ev_ctxL[h] = evs

        qs_am01(st_front[0], 0)
        qs_am01(st_front[1], 1)
        qs_am2(st_front[0], 0)
        qs_nev(st_front[0], 0)
        qs_dT(st_front[0], 0)
        qs_rcp(st_front[0], 0)
        qs_bc(st_front[0], 0)
        prev_psum = qs_mul(st_front[0], 0)
        for h in range(2, H):
            st = qs_dash(h)
            qs_dash2(st, h)
            qs_am2(st_front[h - 1], h - 1)
            qs_scan(st, h)
            qs_nev(st_front[h - 1], h - 1)
            qs_dT(st_front[h - 1], h - 1)
            qs_ur(st, h)
            qs_rcp(st_front[h - 1], h - 1)
            qs_bc(st_front[h - 1], h - 1)
            qs_am01(st, h)
            prev_psum = qs_mul(st_front[h - 1], h - 1)
            st_front[h] = st
        st = st_front[H - 1]
        qs_am2(st, H - 1)
        qs_nev(st, H - 1)
        qs_dT(st, H - 1)
        qs_rcp(st, H - 1)
        qs_bc(st, H - 1)
        prev_psum = qs_mul(st, H - 1)

        bw_t = [Wb[:, j * 1024:(j + 1) * 1024] for j in range(KC)]
        ev_bw = [dma("gpsimd", [ev_ctx_last], bw_t[j], d_bwT[j * 128:(j + 1) * 128, :], stream="gW3")
                 for j in range(KC)]

        attn = g16(RQ + 8192, 128, 16384)
        x1 = g16(ZB, 128, 16384)
        xf = g16(ZC, 128, 16384)
        ffo = g16(0, 128, 16384)
        h_ext = g16p(0, 65, RQ + 6144, 2048)
        M_ext = g16p(0, 65, ZD, 1024)
        xbh_s = [g16(RQ + s * 2048, 128, 2048) for s in range(2)]
        ln_m = g16(RM, 128, 2048)
        ln_r = g16(RM + 2048, 128, 2048)
        ln_t = g16(ZD + 1024, 128, 2048)
        ftmp = g32(0, 128, 2048)
        m12 = g32(2048, 128, 16)
        m12b = g32(2064, 128, 16)
        ivT = g32(2080, 128, 64)
        muiv = g32(2144, 128, 64)
        mui2 = g32(2208, 128, 64)
        csum = g32(2272, 128, 8)
        sq2 = g16(RQ + 4096, 128, 2048)
        rT32 = g32(3296, 128, 16)
        r_rb = g16p(0, 1, RM + 2048, 2048)
        ivb = g16p(0, 128, ZD + 3072, 64)
        muivb = g16p(0, 128, ZD + 3136, 64)
        csb = g16p(0, 128, ZD + 3200, 8)
        _of = (RQ + 8192) // 2
        outst = [G16f[0:128, _of + s * 2048:_of + (s + 1) * 2048] for s in range(4)]
        t_ffw = G16[0:128, RQ + 16384:RQ + 17536]
        t_ones512 = G16[0:1, RQ + 17536:RQ + 18048]

        ev_attn = [None] * KC
        ev_womm = None
        wo_free = {0: None, 1: None}
        for och in range(KC):
            ph = pA if och % 2 == 0 else pB
            ev_mm = None
            for g in range(4):
                for j in range(KC):
                    w = [ev_wo[j]]
                    if g == 0 and j == 0:
                        w.append(wo_free[och % 2])
                    if och == 0 and g == 0 and j == 0:
                        w += ev_merged + [prev_psum]
                    ev_mm = op("tensor", w,
                               lambda e, ph=ph, g=g, j=j, och=och: e.matmul(
                                   ph[:, g * 512:(g + 1) * 512],
                                   wo_t[j][:, och * 128:(och + 1) * 128],
                                   merged[:, j * 2048 + g * 512:j * 2048 + (g + 1) * 512],
                                   start=(j == 0), stop=(j == KC - 1)), ch="T")
            ev_womm = ev_mm
            ev_attn[och] = op("scalar", [ev_mm, evc["bo"]],
                              lambda e, ph=ph, och=och: e.activation(
                                  attn[:, och * 2048:(och + 1) * 2048], ph[:, :],
                                  AF.Identity, bias=c_bo[:, och:och + 1]), ch="S")
            wo_free[och % 2] = ev_attn[och]

        def ln_stats(src_fn, ev_src, sq_slot, ev_sq_pre, ev_psA_pre, ev_psB_pre):
            ev_mm = None
            for j in range(KC):
                for g in range(4):
                    w = [ev_src[j], evc["lnones"]]
                    if j == 0 and g == 0:
                        w += ev_psA_pre
                    ev_mm = op("tensor", w,
                               lambda e, j=j, g=g: e.matmul(
                                   pA[:, g * 512:(g + 1) * 512],
                                   c_lnones[:, :], src_fn(j)[:, g * 512:(g + 1) * 512],
                                   start=(j == 0), stop=(j == KC - 1)), ch="T")
            ev_mb = op("vector", [ev_mm], lambda e: e.tensor_copy(ln_m[:, :], pA[:, 0:2048]), ch="V")
            ev_smm = None
            prev = list(ev_sq_pre)
            for j in range(KC):
                ev_sq = op("scalar", [ev_src[j]] + prev,
                           lambda e, j=j: e.activation(sq_slot[:, :], src_fn(j)[:, :], AF.Square),
                           ch="S")
                for g in range(4):
                    w = [ev_sq]
                    if j == 0 and g == 0:
                        w += ev_psB_pre
                    ev_smm = op("tensor", w,
                                lambda e, j=j, g=g: e.matmul(
                                    pB[:, g * 512:(g + 1) * 512],
                                    c_lnones[:, :], sq_slot[:, g * 512:(g + 1) * 512],
                                    start=(j == 0), stop=(j == KC - 1)), ch="T")
                prev = [ev_smm]
            ev_t1 = op("vector", [ev_smm, ev_mb], lambda e: e.tensor_copy(ftmp, pB[:, 0:2048]), ch="V")
            ev_m2 = op("scalar", [ev_mb] + prev, lambda e: e.activation(ln_t[:, :], ln_m[:, :], AF.Square), ch="S")
            ev_var = op("vector", [ev_t1, ev_m2], lambda e: e.tensor_sub(ftmp, ftmp, ln_t[:, :]), ch="V")
            ev_sd = op("scalar", [ev_var], lambda e: e.activation(ftmp, ftmp, AF.Sqrt, bias=np.float32(1e-5)), ch="S")
            ev_vT = [op("tensor", [ev_sd, evc["idf"], ev_t1],
                       lambda e, cnk=cnk: e.transpose(
                           pB[0:128, 1536 + cnk:1536 + cnk + 1],
                           G32[0:1, cnk * 128:(cnk + 1) * 128],
                           c_idf[0:1, 0:1]), ch="T") for cnk in range(16)]
            ev_rc = op("vector", ev_vT, lowp(lambda e: e.reciprocal(
                rT32[0:128, 0:16], pB[0:128, 1536:1552])), ch="V")
            ev_rT = [op("tensor", [ev_rc, evc["idf"]],
                       lambda e, cnk=cnk: e.transpose(
                           pB[0:1, cnk * 128:(cnk + 1) * 128],
                           rT32[0:128, cnk:cnk + 1],
                           c_idf[:, :]), ch="T") for cnk in range(16)]
            ev_re = op("vector", ev_rT, lambda e: e.tensor_copy(r_rb[0:1, :], pB[0:1, 0:2048]), ch="V")
            ev_bcr = [op("tensor", [ev_re, evc["ones"]],
                        lambda e, g=g: e.matmul(
                            pB[0:128, g * 512:(g + 1) * 512],
                            c_ones[0:1, 0:128],
                            r_rb[0:1, g * 512:(g + 1) * 512],
                            start=True, stop=True), ch="T") for g in range(4)]
            ev_r = ev_bcr[-1]
            return ev_mb, ev_r, ev_smm, ev_t1, ev_sd

        xbh4 = [xbh_s[0], xbh_s[1], g16(ZC, 128, 2048), g16(ZC + 2048, 128, 2048)]
        xbh_ev = [dma("gpsimd", [ev_stage_free], xbh4[0][:, :], d_xT[0:128, 0:2048], stream="gE2"),
                  dma("gpsimd", [ev_stage_free], xbh4[1][:, :], d_xT[128:256, 0:2048], stream="gE3"),
                  dma("gpsimd", [ev_womm], xbh4[2][:, :], d_xT[256:384, 0:2048], stream="gE2"),
                  dma("gpsimd", [ev_womm], xbh4[3][:, :], d_xT[384:512, 0:2048], stream="gE3")]
        ev_m2b, ev_r2, ev_ss2, ev_t1_2, ev_sd2 = ln_stats(
            lambda j: attn[:, j * 2048:(j + 1) * 2048], ev_attn, sq2,
            [ev_ctxL[H - 1][2]], [ev_attn[KC - 2], ev_attn[KC - 1]],
            [ev_attn[KC - 2], ev_attn[KC - 1]])
        ev_x1 = [None] * KC
        ev_lnt_prev = ev_r2
        for j in range(KC):
            sj = j % 4
            ev_t = op("vector", [ev_attn[j], ev_m2b, ev_lnt_prev, ev_womm],
                      lambda e, j=j: e.tensor_sub(
                          ln_t[:, :], attn[:, j * 2048:(j + 1) * 2048], ln_m[:, :]), ch="V")
            ev_t2 = op("vector", [ev_t, ev_r2],
                       lambda e: e.tensor_mul(ln_t[:, :], ln_t[:, :], pB[:, 0:2048]), ch="V")
            ev_x1[j] = op("vector", [ev_t2, xbh_ev[sj]],
                          lambda e, j=j, sj=sj: e.tensor_add(
                              x1[:, j * 2048:(j + 1) * 2048], ln_t[:, :], xbh4[sj][:, :]),
                          ch="V")
            ev_lnt_prev = ev_x1[j]
            if j + 4 < KC:
                xbh_ev[sj] = dma("gpsimd", [ev_x1[j]], xbh4[sj][:, :],
                                 d_xT[(j + 4) * 128:(j + 5) * 128, 0:2048],
                                 stream=["gE2", "gE3"][sj % 2])

        ev_m3b, ev_r3, ev_ss3, ev_t1_3, ev_sd3 = ln_stats(
            lambda j: x1[:, j * 2048:(j + 1) * 2048], ev_x1, sq2,
            [ev_ss2, ev_sd2, ev_x1[-1]], [ev_m2b, ev_t1_2], [ev_t1_2, ev_x1[-1]])
        ev_xf = [None] * KC
        ev_m2c = [None] * KC
        ev_lnt_prev = ev_r3
        for j in range(KC):
            ev_t = op("vector", [ev_x1[j], ev_m3b, ev_ss3, ev_lnt_prev],
                      lambda e, j=j: e.tensor_sub(
                          ln_t[:, :], x1[:, j * 2048:(j + 1) * 2048], ln_m[:, :]), ch="V")
            ev_xf[j] = op("vector", [ev_t, ev_r3, ev_ctxL[H - 1][2], ev_merged[-1]],
                          lambda e, j=j: e.scalar_tensor_tensor(
                              xf[:, j * 2048:(j + 1) * 2048], ln_t[:, :], 1.0, pB[:, 0:2048],
                              OP.mult, OP.mult,
                              accum_out=m12[:, j:j + 1]), ch="V")
            ev_lnt_prev = ev_xf[j]
            ev_m2c[j] = op("scalar", [ev_xf[j]] + ([ev_ss3, ev_sd3] if j == 0 else [ev_m2c[j - 1]]),
                           lambda e, j=j: e.activation(
                               sq2[:, :], xf[:, j * 2048:(j + 1) * 2048], AF.Square,
                               accum_out=m12[:, 8 + j:8 + j + 1]), ch="S")

        ev_mscl = op("vector", ev_xf + ev_m2c, lambda e: e.tensor_scalar_mul(m12, m12, 1.0 / NF), ch="V")
        mtb = m12b.tensor.bitcast(BF16)
        mtb_ap = lambda c0, c1: bass.AP(mtb, 2 * 2064 + c0, [[2 * 3328, 128], [1, c1 - c0]])
        ev_mcb = op("vector", [ev_mscl], lambda e: e.tensor_copy(mtb_ap(0, 16), m12), ch="V")
        ev_iv0 = op("scalar", [evc["smT"], ev_m2c[-1]], lambda e: e.activation(ivT, c_smT[:, :], AF.Square), ch="S")
        ev_iv = op("vector", [ev_iv0], lambda e: e.reciprocal(ivT, ivT), ch="V")
        ev_mv = op("vector", [ev_iv, evc["muT"]], lambda e: e.tensor_mul(muiv, c_muT[:, :], ivT), ch="V")
        ev_m2v = op("vector", [ev_mv], lambda e: e.tensor_mul(mui2, c_muT[:, :], muiv), ch="V")
        ev_cs8 = op("vector", [ev_m2v], lambda e: e.tensor_reduce(
            csum, mui2.rearrange("p (e c) -> p e c", c=8), AX.X, OP.add), ch="V")
        ev_ivb = op("vector", [ev_iv, ev_ctxL[H - 1][2]], lambda e: e.tensor_copy(ivb, ivT), ch="V")
        ev_muivb = op("vector", [ev_mv], lambda e: e.tensor_copy(muivb, muiv), ch="V")
        ev_csb = op("vector", [ev_cs8], lambda e: e.tensor_copy(csb, csum), ch="V")
        ev_lA = None
        ev_lB = None
        for c in range(8):
            w = [ev_mcb, ev_ivb]
            if c == 0:
                w += [ev_t1_3, ev_xf[-1]]
            ev_lA = op("tensor", w, lambda e, c=c: e.matmul(
                pB[0:1, 1536:1544], mtb_ap(8 + c, 9 + c), ivb[:, c:64:8],
                start=(c == 0), stop=(c == 7)), ch="T")
        for c in range(8):
            ev_lB = op("tensor", [ev_mcb, ev_muivb], lambda e, c=c: e.matmul(
                pB[0:1, 1544:1552], mtb_ap(c, c + 1), muivb[:, c:64:8],
                start=(c == 0), stop=(c == 7)), ch="T")
        ev_lC = op("tensor", [ev_csb, evc["ones"]], lambda e: e.matmul(
            pB[0:1, 1552:1560], c_ones[:, 0:1], csb[:, 0:8],
            start=True, stop=True), ch="T")
        ev_l0 = op("vector", [ev_lA, ev_lB, ev_lC], lambda e: e.tensor_copy(
            t_loss[0:1, 8:32], pB[0:1, 1536:1560]), ch="V")
        ev_l1 = op("vector", [ev_l0], lambda e: e.scalar_tensor_tensor(
            t_loss[0:1, 0:8], t_loss[0:1, 16:24], -2.0, t_loss[0:1, 8:16],
            OP.mult, OP.add), ch="V")
        ev_ag3s = dma("sync", [ev_l1], ag3_in[:, :], t_loss[0:1, 0:8], stream="sAG2")
        ev_cc3 = op("gpsimd", [ev_ag3s], lambda e: e.collective_compute(
            "AllGather", OP.bypass, replica_groups=groups,
            ins=[ag3_in.ap()], outs=[ag3_out.ap()]), ch="cc")
        ev_l3l = dma("gpsimd", [ev_cc3], t_loss[0:1, 8:24],
                     bass.AP(ag3_out, 0, [[16, 1], [1, 16]]), stream="gE")
        ev_l2a = op("vector", [ev_l3l], lambda e: e.tensor_add(
            t_loss[0:1, 0:8], t_loss[0:1, 8:16], t_loss[0:1, 16:24]), ch="V")
        ev_l2 = op("vector", [ev_l2a], lambda e: e.tensor_add(
            t_loss[0:1, 0:8], t_loss[0:1, 0:8], t_loss[0:1, 24:32]), ch="V")
        ev_lmin = op("vector", [ev_l2], lambda e: e.tensor_reduce(
            t_sc[0:1, 0:1], t_loss[0:1, 0:8], AX.X, OP.min), ch="V")
        ev_msk = op("vector", [ev_lmin], lambda e: e.tensor_tensor(
            t_sc[0:1, 2:10], t_loss[0:1, 0:8],
            t_sc[0:1, 0:1].broadcast_to((1, 8)), OP.is_equal), ch="V")
        ev_mio = op("vector", [ev_msk, evc["iota"]], lambda e: e.tensor_mul(
            t_sc[0:1, 2:10], t_sc[0:1, 2:10], c_iota[0:1, :]), ch="V")
        ev_idxf = op("vector", [ev_mio], lambda e: e.tensor_reduce(
            t_sc[0:1, 1:2], t_sc[0:1, 2:10], AX.X, OP.add), ch="V")
        ev_idx = op("vector", [ev_idxf], lambda e: e.tensor_copy(
            t_idx[0:1, 0:1], t_sc[0:1, 1:2]), ch="V")

        wc_t = [Wa[:, j * 1024:(j + 1) * 1024] for j in range(KC)]
        idx_cell = {}

        def _load_idx(e):
            idx_cell["v"] = e.value_load(t_idx[0:1, 0:1])
            return None
        op("gpsimd", [ev_idx, ev_womm], _load_idx)
        ev_w_dw = op("gpsimd", [("gE", CH["gE"][1])], lambda e: e.dma_start(
            out=bass.AP(G16, RQ + 16384, [[81032, 128], [64, 8], [1, 64]]),
            in_=bass.AP(d_dw, idx_cell["v"] * (D * RK), [[RK, 128], [128 * RK, 8], [1, RK]])),
            ch="gE", k=16)
        ev_w_db = op("gpsimd", [("gE2", CH["gE2"][1])], lambda e: e.dma_start(
            out=t_bdf, in_=bass.AP(d_db, idx_cell["v"] * RK, [[RK, 1], [1, RK]])),
            ch="gE2", k=16)
        ev_w_uw = op("gpsimd", [("gE3", CH["gE3"][1])], lambda e: e.dma_start(
            out=bass.AP(G16, RQ + 16384 + 512, [[81032, 128], [65, 8], [1, 64]]),
            in_=bass.AP(d_uwT, idx_cell["v"] * (D * RK), [[RK, 128], [128 * RK, 8], [1, RK]])),
            ch="gE3", k=16)
        ev_w_ub = op("gpsimd", [("gE2", CH["gE2"][1])], lambda e: e.dma_start(
            out=bass.AP(G16, RQ + 16384 + 1032, [[81032, 128], [1, 8]]),
            in_=bass.AP(d_ubT, idx_cell["v"], [[NE, 128], [128 * NE, 8]]),
            allow_slow_non_contiguous=True), ch="gE2", k=16)
        ev_w_cb = op("gpsimd", [("gE3", CH["gE3"][1])], lambda e: e.dma_start(
            out=t_bcrow, in_=bass.AP(d_cb, idx_cell["v"] * D, [[D, 1], [1, D]])),
            ch="gE3", k=16)
        ev_w_cw1 = op("gpsimd", [("gE", CH["gE"][1])], lambda e: e.dma_start(
            out=bass.AP(Wa, 0, [[8192, 128], [1024, 4], [1, 1024]]),
            in_=bass.AP(d_cwT, idx_cell["v"] * (D * D), [[D, 128], [128 * D, 4], [1, D]])),
            ch="gE", k=16)
        ev_w_cw2 = op("gpsimd", [("gE2", CH["gE2"][1])], lambda e: e.dma_start(
            out=bass.AP(Wa, 4096, [[8192, 128], [1024, 4], [1, 1024]]),
            in_=bass.AP(d_cwT, idx_cell["v"] * (D * D) + 4 * 128 * D, [[D, 128], [128 * D, 4], [1, D]])),
            ch="gE2", k=16)
        ev_ffw = [ev_w_cw1, ev_w_cw2]
        ev_bdc = op("vector", [ev_w_db], lambda e: e.tensor_copy(t_bdrow[0:1, :], t_bdf), ch="V")
        ev_bu = op("vector", [ev_w_ub], lambda e: e.tensor_copy(
            bass.AP(G16, RQ + 16384 + 512 + 64, [[81032, 128], [65, 8]]),
            bass.AP(G16, RQ + 16384 + 1032, [[81032, 128], [1, 8]])), ch="V")

        ev_base = [None] * KC
        bank_free = {0: [None] * 3, 1: [None] * 3}
        for och in range(KC):
            ph = pA if och % 2 == 0 else pB
            ev_ev = None
            for g in range(4):
                gb = g % 3
                ev_mm = None
                for j in range(KC):
                    w = [ev_bw[j], ev_xf[j]]
                    if j == 0:
                        w.append(bank_free[och % 2][gb])
                    if och == 1 and g == 0 and j == 0:
                        w.append(ev_xf[-1])
                    if och == 0 and g == 0 and j == 0:
                        w += [ev_m3b, ev_t1_3]
                    ev_mm = op("tensor", w,
                               lambda e, ph=ph, g=g, gb=gb, j=j, och=och: e.matmul(
                                   ph[:, gb * 512:(gb + 1) * 512],
                                   bw_t[j][:, och * 128:(och + 1) * 128],
                                   xf[:, j * 2048 + g * 512:j * 2048 + (g + 1) * 512],
                                   start=(j == 0), stop=(j == KC - 1)), ch="T")
                ev_ev = op("scalar", [ev_mm],
                           lambda e, ph=ph, g=g, gb=gb, och=och: e.activation(
                               ffo[:, och * 2048 + g * 512:och * 2048 + (g + 1) * 512],
                               ph[:, gb * 512:(gb + 1) * 512], AF.Copy), ch="S")
                bank_free[och % 2][gb] = ev_ev
            ev_base[och] = ev_ev

        ev_hones = op("vector", [ev_stage_free, ev_eq_free[0], ev_eq_free[1]],
                      lambda e: e.memset(h_ext[64:65, :], 1.0), ch="V")
        ev_or = op("vector", [ev_hones], lambda e: e.memset(t_ones512[0:1, :], 1.0), ch="V")
        ev_hmm = []
        for g in range(4):
            evh = None
            for j in range(KC):
                w = [ev_w_dw, ev_xf[j]]
                if g == 0 and j == 0:
                    w += [ev_lA, ev_m3b, ev_base[KC - 2]]
                evh = op("tensor", w,
                         lambda e, g=g, j=j: e.matmul(
                             pA[0:64, g * 512:(g + 1) * 512],
                             t_ffw[:, j * 64:(j + 1) * 64],
                             xf[:, j * 2048 + g * 512:j * 2048 + (g + 1) * 512],
                             start=(j == 0), stop=False), ch="T")
            evh = op("tensor", [ev_w_db, ev_or, ev_bdc], lambda e, g=g: e.matmul(
                pA[0:64, g * 512:(g + 1) * 512],
                t_bdrow[0:1, :], t_ones512[0:1, :],
                start=False, stop=True), ch="T")
            ev_hmm.append(evh)
        ev_hext = op("scalar", ev_hmm + [ev_m2c[-1]], lambda e: e.activation(
            h_ext[0:64, :], pA[0:64, 0:2048], AF.Relu), ch="S")

        ev_Mmm = None
        for j in range(KC):
            for g2 in range(2):
                w = [ev_w_uw, ev_w_cw1, ev_w_cw2, ev_bu]
                if j == 0 and g2 == 0:
                    w += [ev_l2, ev_lC, ev_l0, ev_base[KC - 1]]
                ev_Mmm = op("tensor", w,
                            lambda e, j=j, g2=g2: e.matmul(
                                pB[0:65, g2 * 512:(g2 + 1) * 512],
                                t_ffw[:, 512 + j * 65:512 + (j + 1) * 65],
                                wc_t[j][:, g2 * 512:(g2 + 1) * 512],
                                start=(j == 0), stop=(j == KC - 1)), ch="T")
        ev_Me1 = op("vector", [ev_Mmm, ev_hones], lambda e: e.tensor_copy(
            M_ext[0:64, :], pB[0:64, 0:1024]), ch="V")
        ev_Me2 = op("vector", [ev_Mmm, ev_w_cb], lambda e: e.scalar_tensor_tensor(
            M_ext[64:65, :], pB[64:65, 0:1024], 1.0, t_bcrow[0:1, :],
            OP.mult, OP.add), ch="V")

        ev_ffo = [None] * KC
        yb_free = {0: None, 1: None}
        for och in range(KC):
            ph = pA if och % 2 == 0 else pB
            ev_mm = None
            for g in range(4):
                w = [ev_hext, ev_Me1, ev_Me2]
                if g == 0:
                    w.append(yb_free[och % 2])
                if och == 0 and g == 0:
                    w += [ev_lA, ev_lB, ev_lC, ev_l0]
                ev_mm = op("tensor", w,
                           lambda e, ph=ph, g=g, och=och: e.matmul(
                               ph[:, g * 512:(g + 1) * 512],
                               M_ext[0:65, och * 128:(och + 1) * 128],
                               h_ext[0:65, g * 512:(g + 1) * 512],
                               start=(g >= 0), stop=True), ch="T")
            ev_ffo[och] = op("vector", [ev_mm, ev_base[och], evc["bb"]],
                             lambda e, ph=ph, och=och: e.scalar_tensor_tensor(
                                 ffo[:, och * 2048:(och + 1) * 2048], ph[:, :],
                                 c_bb[:, och:och + 1],
                                 ffo[:, och * 2048:(och + 1) * 2048],
                                 OP.add, OP.add), ch="V")
            yb_free[och % 2] = ev_ffo[och]

        sqf = g16(ZC, 128, 2048)
        ev_m4b, ev_r4, ev_ss4, ev_t1_4, ev_sd4 = ln_stats(
            lambda j: ffo[:, j * 2048:(j + 1) * 2048], ev_ffo, sqf,
            [ev_ffo[-1], ev_ss3, ev_sd3], [ev_ffo[KC - 2], ev_ffo[KC - 1]],
            [ev_ffo[KC - 2], ev_ffo[KC - 1]])
        ev_out = []
        ost_free = [None, None, None, None]
        ev_lnt_prev = ev_r4
        for j in range(KC):
            sj = j % 4
            ev_t = op("vector", [ev_ffo[j], ev_m4b, ev_ss4, ev_lnt_prev],
                      lambda e, j=j: e.tensor_sub(
                          ln_t[:, :], ffo[:, j * 2048:(j + 1) * 2048], ln_m[:, :]), ch="V")
            ev_t2 = op("vector", [ev_t, ev_r4],
                       lambda e: e.tensor_mul(ln_t[:, :], ln_t[:, :], pB[:, 0:2048]), ch="V")
            ev_o = op("vector", [ev_t2, ev_x1[j], ost_free[sj]],
                      lambda e, j=j, sj=sj: e.tensor_add(
                          outst[sj], ln_t[:, :], x1[:, j * 2048:(j + 1) * 2048]), ch="V")
            ev_lnt_prev = ev_o
            ev_d = dma("sync", [ev_o], d_out[j * 128:(j + 1) * 128, :], outst[sj],
                       stream=["sO0", "sO1", "sO2", "sO3"][j % 4])
            ost_free[sj] = ev_d
            ev_out.append(ev_d)
        op("sync", ev_out + [(nm, CH[nm][1]) for nm in
                             ("sQ", "sK", "sL", "sAG1", "sAG2", "sST",
                              "sO0", "sO1", "sO2", "sO3", "sC", "qd")],
           lambda e: None)
        op("gpsimd", [(nm, CH[nm][1]) for nm in
                      ("gC", "gW", "gW2", "gW3", "gX", "gX2", "gX3", "gX4",
                       "gE", "gE2", "gE3", "qg", "cc")],
           lambda e: None)
        op("scalar", [("qs", CH["qs"][1])], lambda e: None)

        import kernel as _K
        _K._PROG_SIZES = {k: len(v) for k, v in PROG.items()}
        if limit is not None:
            tgt, frac = limit
            for k in PROG:
                if tgt == "all" or k == tgt:
                    n = max(1, int(len(PROG[k]) * frac))
                    PROG[k] = PROG[k][:n]
        with nc.Block() as block:
            @block.sync
            def _(eng):
                for fn in PROG["sync"]:
                    fn(eng)

            @block.gpsimd
            def _(eng):
                for fn in PROG["gpsimd"]:
                    fn(eng)

            @block.tensor
            def _(eng):
                for fn in PROG["tensor"]:
                    fn(eng)

            @block.vector
            def _(eng):
                for fn in PROG["vector"]:
                    fn(eng)

            @block.scalar
            def _(eng):
                for fn in PROG["scalar"]:
                    fn(eng)

    return nc


def _consts():
    c = {}
    c["c_ones"] = np.ones((128, 128), np.float32)
    c["c_lnones"] = np.full((128, 128), 1.0 / 1024.0, np.float32)
    c["c_ident"] = np.eye(128, dtype=np.float32)
    selq = np.zeros((D, H), np.float32)
    selq[np.arange(D), np.arange(D) // 64] = 0.5
    c["c_selq"] = selq
    selk = np.zeros((512, HK), np.float32)
    selk[np.arange(512), np.arange(512) // 64] = 0.5
    c["c_selk"] = selk
    c["c_iota"] = np.arange(8, dtype=np.float32)[None, :].copy()
    return c


def _ec_layout(a):
    return np.ascontiguousarray(a.reshape(8, 8, 128).transpose(2, 0, 1).reshape(128, 64))


def kernel(**inputs):
    x = np.asarray(inputs["L0_attn_x"], np.float32)
    Wq = np.asarray(inputs["L0_attn_Wq"], np.float32)
    Wk = np.asarray(inputs["L0_attn_Wk"], np.float32)
    Wv = np.asarray(inputs["L0_attn_Wv"], np.float32)
    Wo = np.asarray(inputs["L0_attn_Wo"], np.float32)
    bo = np.asarray(inputs["L0_attn_bo"], np.float32)
    proj = np.asarray(inputs["L0_attn_proj"], np.float32)
    bW = np.asarray(inputs["L0_ffn_base_W"], np.float32)
    bb = np.asarray(inputs["L0_ffn_base_b"], np.float32)
    dmu = np.asarray(inputs["L0_ffn_disc_mu"], np.float32)
    dsg = np.asarray(inputs["L0_ffn_disc_sigma"], np.float32)
    dW = np.asarray(inputs["L0_ffn_down_W"], np.float32)
    db = np.asarray(inputs["L0_ffn_down_b"], np.float32)
    uW = np.asarray(inputs["L0_ffn_up_W"], np.float32)
    ub = np.asarray(inputs["L0_ffn_up_b"], np.float32)
    cW = np.asarray(inputs["L0_ffn_conv_W"], np.float32)
    cb = np.asarray(inputs["L0_ffn_conv_b"], np.float32)

    cst = _consts()
    projT2 = np.ascontiguousarray(np.vstack([proj.T, proj.T]))
    base_wT = np.ascontiguousarray(bW.T)
    up_wT = np.ascontiguousarray(uW.transpose(0, 2, 1))
    conv_wT = np.ascontiguousarray(cW.transpose(0, 2, 1))
    up_bT = np.ascontiguousarray(ub.T)
    bo_cols = np.ascontiguousarray(bo.reshape(8, 128).T)
    bb_cols = np.ascontiguousarray(bb.reshape(8, 128).T)
    smT = _ec_layout(dsg)
    muT = _ec_layout(dmu)

    nc = build_nc(4)
    in_maps = []
    for c in range(8):
        b, p = c // 2, c % 2
        xb = x[b]
        rolled = np.concatenate([xb[p * 2048:(p + 1) * 2048], xb[(1 - p) * 2048:(2 - p) * 2048]], 0)
        in_maps.append({
            "xT": np.ascontiguousarray(rolled.T),
            "wq": Wq,
            "wk": np.ascontiguousarray(Wk[:, p * 512:(p + 1) * 512]),
            "wv": np.ascontiguousarray(Wv[:, p * 512:(p + 1) * 512]),
            "wo": Wo,
            "bo_cols": bo_cols,
            "projT2": projT2,
            "base_wT": base_wT,
            "baseb_cols": bb_cols,
            "sigmaT_ec": smT,
            "muT_ec": muT,
            "down_w": dW,
            "down_b": db,
            "up_wT": up_wT,
            "up_bT": up_bT,
            "conv_wT": conv_wT,
            "conv_b": cb,
            "c_iota": cst["c_iota"],
            "c_ones": cst["c_ones"],
            "c_lnones": cst["c_lnones"],
            "c_ident": cst["c_ident"],
            "c_selq": cst["c_selq"],
            "c_selk": cst["c_selk"],
        })
    res = run_bass_kernel_spmd(nc, in_maps, core_ids=list(range(8)))
    out = np.empty((4, 4096, 1024), np.float32)
    for c in range(8):
        b, p = c // 2, c % 2
        out[b, p * 2048:(p + 1) * 2048] = np.ascontiguousarray(res.results[c]["outT"].T)
    return out


# version 30
# speedup vs baseline: 1.0521x; 1.0037x over previous
import numpy as np
from collections import defaultdict
from contextlib import ExitStack

import concourse.bass as bass
import concourse.mybir as mybir
from concourse.bass_utils import run_bass_kernel_spmd

F32 = mybir.dt.float32
BF16 = mybir.dt.bfloat16
I32 = mybir.dt.int32
AF = mybir.ActivationFunctionType
OP = mybir.AluOpType
AX = mybir.AxisListType

D = 1024
NF = 4096
NT = 2048
H = 16
HK = 8
M = 266
MCH = [(0, 128), (128, 128), (256, 10)]
EPS = 1e-4
DN = float(64.0 ** -0.25)
LNEPS = float(np.log(EPS))
NE = 8
RK = 64
KC = 8


def build_nc(npairs=4, limit=None):
    nc = bass.Bass("TRN2", target_bir_lowering=False)

    d_xT = nc.dram_tensor("xT", [D, NF], F32, kind="ExternalInput")
    d_wq = nc.dram_tensor("wq", [D, D], F32, kind="ExternalInput")
    d_wk = nc.dram_tensor("wk", [D, 512], F32, kind="ExternalInput")
    d_wv = nc.dram_tensor("wv", [D, 512], F32, kind="ExternalInput")
    d_wo = nc.dram_tensor("wo", [D, D], F32, kind="ExternalInput")
    d_bo = nc.dram_tensor("bo_cols", [128, 8], F32, kind="ExternalInput")
    d_projT = nc.dram_tensor("projT2", [128, M], F32, kind="ExternalInput")
    d_bwT = nc.dram_tensor("base_wT", [D, D], F32, kind="ExternalInput")
    d_bb = nc.dram_tensor("baseb_cols", [128, 8], F32, kind="ExternalInput")
    d_smT = nc.dram_tensor("sigmaT_ec", [128, 64], F32, kind="ExternalInput")
    d_muT = nc.dram_tensor("muT_ec", [128, 64], F32, kind="ExternalInput")
    d_dw = nc.dram_tensor("down_w", [NE, D, RK], F32, kind="ExternalInput")
    d_db = nc.dram_tensor("down_b", [NE, RK], F32, kind="ExternalInput")
    d_uwT = nc.dram_tensor("up_wT", [NE, D, RK], F32, kind="ExternalInput")
    d_ubT = nc.dram_tensor("up_bT", [D, NE], F32, kind="ExternalInput")
    d_cwT = nc.dram_tensor("conv_wT", [NE, D, D], F32, kind="ExternalInput")
    d_cb = nc.dram_tensor("conv_b", [NE, D], F32, kind="ExternalInput")
    d_ones = nc.dram_tensor("c_ones", [128, 128], F32, kind="ExternalInput")
    d_lnones = nc.dram_tensor("c_lnones", [128, 128], F32, kind="ExternalInput")
    d_ident = nc.dram_tensor("c_ident", [128, 128], F32, kind="ExternalInput")
    d_selq = nc.dram_tensor("c_selq", [D, H], F32, kind="ExternalInput")
    d_selk = nc.dram_tensor("c_selk", [512, HK], F32, kind="ExternalInput")
    d_iota = nc.dram_tensor("c_iota", [1, 8], F32, kind="ExternalInput")

    d_out = nc.dram_tensor("outT", [D, NT], F32, kind="ExternalOutput")
    ag1_in = nc.dram_tensor("ag1_in", [65, HK * M], BF16)
    ag1_out = nc.dram_tensor("ag1_out", [130, HK * M], BF16)
    ag2_in = nc.dram_tensor("ag2_in", [128, 16], F32)
    ag2_out = nc.dram_tensor("ag2_out", [256, 16], F32)
    ag3_in = nc.dram_tensor("ag3_in", [1, 8], F32)
    ag3_out = nc.dram_tensor("ag3_out", [2, 8], F32)
    groups = [[2 * i, 2 * i + 1] for i in range(npairs)]

    with ExitStack() as cx:
        en = cx.enter_context
        G16 = en(nc.sbuf_tensor("G16", [128, 81032], BF16))
        G32 = en(nc.sbuf_tensor("G32", [128, 3328], F32))
        Wa = en(nc.sbuf_tensor("W16a", [128, 8192], BF16))
        Wb = en(nc.sbuf_tensor("W16b", [128, 8192], BF16))
        c_ones = en(nc.sbuf_tensor("cb_ones", [128, 128], BF16))
        c_lnones = en(nc.sbuf_tensor("cb_lnones", [128, 128], BF16))
        c_idb = en(nc.sbuf_tensor("cb_id", [128, 128], BF16))
        c_idf = en(nc.sbuf_tensor("cf_id", [128, 128], F32))
        c_projb = en(nc.sbuf_tensor("cb_proj", [128, M], BF16))
        c_selq = en(nc.sbuf_tensor("cb_selq", [128, 128], BF16))
        c_selk = en(nc.sbuf_tensor("cb_selk", [128, 32], BF16))
        c_bo = en(nc.sbuf_tensor("cf_bo", [128, 8], F32))
        c_bb = en(nc.sbuf_tensor("cf_bb", [128, 8], F32))
        c_smT = en(nc.sbuf_tensor("cf_smT", [128, 64], F32))
        c_muT = en(nc.sbuf_tensor("cf_muT", [128, 64], F32))
        c_iota = en(nc.sbuf_tensor("cf_iota", [1, 8], F32))
        t_sc = en(nc.sbuf_tensor("t_sc", [128, 16], F32))
        t_scb = en(nc.sbuf_tensor("t_scb", [1, 16], BF16))
        t_loss = en(nc.sbuf_tensor("t_loss", [1, 32], F32))
        t_idx = en(nc.sbuf_tensor("t_idx", [1, 2], I32))
        t_bdrow = en(nc.sbuf_tensor("t_bdrow", [1, RK], BF16))
        pA = en(nc.psum_tensor("pA", [128, 2048], F32))
        pB = en(nc.psum_tensor("pB", [128, 2048], F32))
        pBb = pB.bitcast(BF16)
        pAb = pA.bitcast(BF16)
        G16f = G16.bitcast(F32)

        CH = {}
        for nm in ("T", "V", "S", "G", "qd", "qg", "qs", "cc",
                   "gC", "gW", "gW2", "gW3", "gX", "gX2", "gX3", "gX4",
                   "gE", "gE2", "gE3", "gE4", "sO2", "sO3", "sQ", "sK", "sL",
                   "sAG1", "sAG2", "sST", "sO0", "sO1", "sC"):
            CH[nm] = [en(nc.semaphore("ch_" + nm)), 0]
        PROG = defaultdict(list)
        WAITED = defaultdict(dict)

        def op(eng, waits, build, ch=None, k=1):
            w2 = []
            for w in (waits or []):
                if w is None:
                    continue
                chn, v = w
                if v > WAITED[eng].get(chn, 0):
                    WAITED[eng][chn] = v
                    w2.append((chn, v))
            evout = None
            if ch is not None:
                c = CH[ch]
                c[1] += k
                evout = (ch, c[1])

            def thunk(engine, w2=w2, build=build, ch=ch, k=k):
                for chn, v in w2:
                    engine.wait_ge(CH[chn][0], v)
                ins = build(engine)
                if ch is not None and ins is not None:
                    ins.then_inc(CH[ch][0], k)
            PROG[eng].append(thunk)
            return evout

        QCH = {"sync": "qd", "gpsimd": "qg", "scalar": "qs"}

        def lowp(build):
            def b2(e, build=build):
                with nc.allow_low_precision(reason="bf16 operand rounding is intended"):
                    return build(e)
            return b2

        def dma(eng, waits, out, in_, stream=None):
            ch = stream or QCH[eng]
            pre = CH[ch][1]
            return op(eng, (waits or []) + ([(ch, pre)] if pre else []),
                      lambda e, o=out, i=in_: e.dma_start(out=o, in_=i),
                      ch=ch, k=16)

        def g16(off, parts, width):
            return G16[0:parts, off:off + width]

        def g16p(p0, p1, off, width):
            return G16[p0:p1, off:off + width]

        def g32(off, parts, width):
            return G32[0:parts, off:off + width]

        RS = 0
        RX = 19592
        RQ = 52360
        RM = 76936
        t_bcrow = g32(2280, 1, 1024)
        t_bdf = g32(1024, 1, RK)

        evc = {}
        evc["ones"] = dma("gpsimd", [], c_ones[:, :], d_ones[:, :], stream="gC")
        evc["lnones"] = dma("gpsimd", [], c_lnones[:, :], d_lnones[:, :], stream="gC")
        evc["idb"] = dma("gpsimd", [], c_idb[:, :], d_ident[:, :], stream="gC")
        evc["idf"] = dma("sync", [], c_idf[:, :], d_ident[:, :], stream="sC")
        evc["projb"] = dma("gpsimd", [], c_projb[:, :], d_projT[:, :], stream="gC")
        evc["selq"] = dma("gpsimd", [], c_selq[:, :],
                          bass.AP(d_selq, 0, [[H, 128], [128 * H, KC], [1, H]]), stream="gC")
        evc["selk"] = dma("gpsimd", [], c_selk[:, :],
                          bass.AP(d_selk, 0, [[HK, 128], [128 * HK, 4], [1, HK]]), stream="gC")
        evc["bo"] = dma("sync", [], c_bo[:, :], d_bo[:, :], stream="sC")
        evc["bb"] = dma("sync", [], c_bb[:, :], d_bb[:, :], stream="sC")
        evc["smT"] = dma("sync", [], c_smT[:, :], d_smT[:, :], stream="sC")
        evc["muT"] = dma("sync", [], c_muT[:, :], d_muT[:, :], stream="sC")
        evc["iota"] = dma("sync", [], c_iota[:, :], d_iota[:, :], stream="sC")

        wq_t = [Wa[:, j * 1024:(j + 1) * 1024] for j in range(KC)]
        wk_t = [Wb[:, j * 512:(j + 1) * 512] for j in range(KC)]
        wv_t = [Wb[:, 4096 + j * 512:4096 + (j + 1) * 512] for j in range(KC)]
        ev_wq = [dma("gpsimd", [], wq_t[j], d_wq[j * 128:(j + 1) * 128, :], stream="gW") for j in range(KC)]
        ev_wk = [dma("gpsimd", [], wk_t[j], d_wk[j * 128:(j + 1) * 128, :], stream="gW2") for j in range(KC)]
        ev_wv = [dma("gpsimd", [], wv_t[j], d_wv[j * 128:(j + 1) * 128, :], stream="gW3") for j in range(KC)]

        ZB = 16384
        ZC = 32768
        ZD = 49152
        xb_s = [g16(s * 8192, 128, 8192) for s in range(2)]
        xsq_s = [g16(ZD + s * 1024, 128, 1024) for s in range(2)]

        def xln_ap(j, c0, w):
            if c0 < 2048:
                return g16(ZC + j * 2048 + c0, 128, w)
            return g16(ZB + j * 2048 + (c0 - 2048), 128, w)

        m_b2 = [g16(RM + s_ * 1024, 128, 1024) for s_ in range(2)]
        msq_b2 = [g16(RM + 2048 + s_ * 1024, 128, 1024) for s_ in range(2)]
        r_rbp2 = [g16p(0, 1, RM + 2048 + s_ * 1024, 1024) for s_ in range(2)]
        tmp1_2 = [g32(s_ * 1024, 128, 1024) for s_ in range(2)]

        rT32p = g32(3296, 128, 16)
        r_rbp = g16p(0, 1, RM + 1024, 1024)
        ev_rT_p1 = None
        ev_stgf = [[], []]
        mr_free = [None, None]
        ev_xln = [[None] * 4 for _ in range(KC)]
        slot_free = [None, None]
        xsq_free = [None, None]
        mr_free = [None, None]
        ev_mb_last = None
        ev_t1_last = None
        for blk in range(4):
            s = blk % 2
            tb = blk * 1024
            stg0 = 8192 if blk % 2 == 0 else (RQ + 8192) // 2
            evxr = [dma("sync", list(ev_stgf[blk % 2]),
                        G16f[0:128, stg0 + j * 1024:stg0 + (j + 1) * 1024],
                        d_xT[j * 128:(j + 1) * 128, tb:tb + 1024],
                        stream=["sQ", "sK", "sO2", "sO3"][j % 4]) for j in range(KC)]
            evx = []
            for j in range(KC):
                src = G16f[0:128, stg0 + j * 1024:stg0 + (j + 1) * 1024]
                dst = xb_s[s][:, j * 1024:(j + 1) * 1024]
                if j % 2 == 0:
                    evx.append(op("scalar", [evxr[j], slot_free[s]],
                                  lambda e, src=src, dst=dst: e.activation(
                                      dst, src, AF.Copy), ch="S"))
                else:
                    evx.append(op("vector", [evxr[j], slot_free[s]],
                                  lambda e, src=src, dst=dst: e.tensor_copy(dst, src), ch="V"))
            ev_stgf[blk % 2] = [evx[-2], evx[-1]]
            pm = pA[:, s * 1024:(s + 1) * 1024]
            pq = pB[:, s * 1024:(s + 1) * 1024]
            m_b = m_b2[s]
            msq_b = msq_b2[s]
            r_rbp = r_rbp2[s]
            tmp1 = tmp1_2[s]
            ev_mm = None
            for j in range(KC):
                for g2 in range(2):
                    ev_mm = op("tensor", [evx[j], evc["lnones"]],
                               lambda e, pm=pm, s=s, j=j, g2=g2: e.matmul(
                                   pm[:, g2 * 512:(g2 + 1) * 512],
                                   c_lnones[:, :],
                                   xb_s[s][:, j * 1024 + g2 * 512:j * 1024 + (g2 + 1) * 512],
                                   start=(j == 0), stop=(j == KC - 1)), ch="T")
            ev_mb = op("vector", [ev_mm, mr_free[s]], lambda e, pm=pm, m_b=m_b: e.tensor_copy(m_b[:, :], pm), ch="V")
            ev_mb_last = ev_mb
            ev_smm = None
            for j in range(KC):
                sj = j % 2
                ev_sq = op("scalar", [evx[j], xsq_free[sj]],
                           lambda e, s=s, j=j, sj=sj: e.activation(
                               xsq_s[sj][:, :], xb_s[s][:, j * 1024:(j + 1) * 1024],
                               AF.Square), ch="S")
                for g2 in range(2):
                    ev_smm = op("tensor", [ev_sq, ev_mm if j == 0 else None],
                                lambda e, pq=pq, j=j, sj=sj, g2=g2: e.matmul(
                                    pq[:, g2 * 512:(g2 + 1) * 512],
                                    c_lnones[:, :],
                                    xsq_s[sj][:, g2 * 512:(g2 + 1) * 512],
                                    start=(j == 0), stop=(j == KC - 1)), ch="T")
                xsq_free[sj] = ev_smm
            ev_t1 = op("vector", [ev_smm], lambda e, pq=pq, tmp1=tmp1: e.tensor_copy(tmp1, pq), ch="V")
            ev_t1_last = ev_t1
            ev_m2 = op("scalar", [ev_mb, mr_free[s]], lambda e, msq_b=msq_b, m_b=m_b: e.activation(msq_b[:, :], m_b[:, :], AF.Square), ch="S")
            ev_var = op("vector", [ev_t1, ev_m2], lambda e, tmp1=tmp1, msq_b=msq_b: e.tensor_sub(tmp1, tmp1, msq_b[:, :]), ch="V")
            ev_sd = op("scalar", [ev_var], lambda e, tmp1=tmp1: e.activation(tmp1, tmp1, AF.Sqrt, bias=np.float32(1e-5)), ch="S")
            ev_vT = [op("tensor", [ev_sd, evc["idf"], ev_mb],
                        lambda e, cnk=cnk, s=s: e.transpose(
                            pA[0:128, s * 1024 + cnk:s * 1024 + cnk + 1],
                            G32[0:1, s * 1024 + cnk * 128:s * 1024 + (cnk + 1) * 128],
                            c_idf[0:1, 0:1]), ch="T") for cnk in range(8)]
            ev_rc = op("vector", ev_vT, lowp(lambda e, s=s: e.reciprocal(
                rT32p[0:128, s * 8:s * 8 + 8], pA[0:128, s * 1024:s * 1024 + 8])), ch="V")
            ev_rT = [op("tensor", [ev_rc, evc["idf"]],
                        lambda e, cnk=cnk, s=s: e.transpose(
                            pA[0:1, s * 1024 + cnk * 128:s * 1024 + (cnk + 1) * 128],
                            rT32p[0:128, s * 8 + cnk:s * 8 + cnk + 1],
                            c_idf[:, :]), ch="T") for cnk in range(8)]
            ev_re = op("vector", ev_rT + [ev_var], lambda e, s=s, r_rbp=r_rbp: e.tensor_copy(
                r_rbp[0:1, :], pA[0:1, s * 1024:s * 1024 + 1024]), ch="V")
            ev_bcr = [op("tensor", [ev_re, evc["ones"]],
                         lambda e, g=g, s=s, r_rbp=r_rbp: e.matmul(
                             pA[0:128, s * 1024 + g * 512:s * 1024 + (g + 1) * 512],
                             c_ones[0:1, 0:128],
                             r_rbp[0:1, g * 512:(g + 1) * 512],
                             start=True, stop=True), ch="T") for g in range(2)]
            ev_r = ev_bcr[-1]
            last_x = None
            for j in range(KC):
                sj = j % 2
                ev_t = op("vector", [ev_mb, evx[j], xsq_free[sj]],
                          lambda e, s=s, j=j, sj=sj, m_b=m_b: e.tensor_sub(
                              xsq_s[sj][:, :], xb_s[s][:, j * 1024:(j + 1) * 1024], m_b[:, :]),
                          ch="V")
                ev_x = op("vector", [ev_t, ev_r],
                          lambda e, j=j, tb=tb, sj=sj, s=s: e.tensor_mul(
                              xln_ap(j, tb, 1024), xsq_s[sj][:, :],
                              pA[:, s * 1024:(s + 1) * 1024]), ch="V")
                ev_xln[j][blk] = ev_x
                xsq_free[sj] = ev_x
                last_x = ev_x
            slot_free[s] = last_x
            mr_free[s] = last_x

        kpark = [g16(och * 4096, 128, 4096) for och in range(4)]
        ev_kpark = [None] * 4
        kh_free = {0: None, 1: None}
        for och in range(4):
            ev_ev = None
            for half in range(2):
                ph = pA if half == 0 else pB
                ev_mm = None
                for g in range(4):
                    for j in range(KC):
                        w = [ev_wk[j], ev_xln[j][(half * 2048 + g * 512) // 1024]]
                        if g == 0 and j == 0:
                            w.append(kh_free[half])
                        if och == 0 and g == 0 and j == 0:
                            w += [mr_free[0], mr_free[1], slot_free[0], slot_free[1], ev_t1_last]
                        ev_mm = op("tensor", w,
                                   lambda e, ph=ph, half=half, g=g, j=j, och=och: e.matmul(
                                       ph[:, g * 512:(g + 1) * 512],
                                       wk_t[j][:, och * 128:(och + 1) * 128],
                                       xln_ap(j, half * 2048 + g * 512, 512),
                                       start=(j == 0), stop=(j == KC - 1)), ch="T")
                ev_ev = op("scalar", [ev_mm, slot_free[och // 2] if half == 0 else None],
                           lambda e, ph=ph, och=och, half=half: e.activation(
                               kpark[och][:, half * 2048:(half + 1) * 2048], ph[:, :],
                               AF.Copy, scale=np.float32(DN)), ch="S")
                kh_free[half] = ev_ev
            ev_kpark[och] = ev_ev

        def v_t(cnk):
            return g16(RQ + 8192 + cnk * 512, 128, 512)

        ev_v = [None] * 32
        for blk in range(8):
            ph = pA if blk % 2 == 0 else pB
            evl = []
            for ci in range(4):
                cnk = blk * 4 + ci
                ev_mm = None
                for j in range(KC):
                    w = [ev_wv[j]]
                    if ci == 0 and j == 0:
                        if blk == 0:
                            w += [kh_free[0], kh_free[1]]
                        elif blk == 1:
                            w += [ev_v[3]]
                        else:
                            w += [ev_v[blk * 4 - 5]]
                    ev_mm = op("tensor", w,
                               lambda e, ph=ph, ci=ci, cnk=cnk, j=j: e.matmul(
                                   ph[:, ci * 512:(ci + 1) * 512],
                                   xln_ap(j, cnk * 128, 128),
                                   wv_t[j][:, :],
                                   start=(j == 0), stop=(j == KC - 1)), ch="T")
                evl.append(ev_mm)
            for ci in range(4):
                cnk = blk * 4 + ci
                ev_v[cnk] = op("scalar", [evl[ci]],
                               lambda e, ph=ph, ci=ci, cnk=cnk: e.activation(
                                   v_t(cnk)[:, :], ph[:, ci * 512:(ci + 1) * 512],
                                   AF.Copy), ch="S")

        qpark = [g16(ZB + pj * 2048, 128, 2048) for pj in range(KC)]
        ev_qpark = [None] * KC
        ph_free = {0: None, 1: None}
        ev_qmm_last = None
        for och in range(KC):
            ph = pA if och % 2 == 0 else pB
            ev_mm = None
            for g in range(4):
                for j in range(KC):
                    w = [ev_wq[j]]
                    if g == 0 and j == 0:
                        w.append(ph_free[och % 2])
                    if och == 0 and g == 0 and j == 0:
                        w += [ev_v[27], ev_v[31]]
                    if och == 1 and g == 0 and j == 0:
                        w += [ev_v[31]]
                    ev_mm = op("tensor", w,
                               lambda e, ph=ph, g=g, j=j, och=och: e.matmul(
                                   ph[:, g * 512:(g + 1) * 512],
                                   wq_t[j][:, och * 128:(och + 1) * 128],
                                   xln_ap(j, g * 512, 512),
                                   start=(j == 0), stop=(j == KC - 1)), ch="T")
            ev_qc = op("scalar", [ev_mm],
                       lambda e, ph=ph, och=och: e.activation(
                           qpark[och][:, :], ph[:, :], AF.Copy, scale=np.float32(DN)), ch="S")
            ev_qpark[och] = ev_qc
            ph_free[och % 2] = ev_qc
            ev_qmm_last = ev_mm

        dsc = [g16(RM + s_ * 2048, 128, 2048) for s_ in range(2)]
        ev_dq = None
        ev_dsq = [None, None]
        for pj in range(KC):
            sj = pj % 2
            w = [ev_qpark[pj], ev_dsq[sj]]
            if pj == 0:
                w += [slot_free[0], slot_free[1], mr_free[0], mr_free[1]]
            ev_sq2 = op("vector", w,
                        lambda e, pj=pj, sj=sj: e.tensor_mul(
                            dsc[sj][:, :], qpark[pj][:, :], qpark[pj][:, :]), ch="V")
            for g in range(4):
                ev_dq = op("tensor", [ev_sq2, evc["selq"]] +
                           ([ph_free[0], ph_free[1]] if (pj == 0 and g == 0) else []),
                           lambda e, pj=pj, g=g, sj=sj: e.matmul(
                               pA[0:16, g * 512:(g + 1) * 512],
                               c_selq[:, pj * 16:(pj + 1) * 16],
                               dsc[sj][:, g * 512:(g + 1) * 512],
                               start=(pj == 0), stop=(pj == KC - 1)), ch="T")
            ev_dsq[sj] = ev_dq
        dqrows = g32(1024, 16, 2048)
        ev_dqr = op("vector", [ev_dq], lambda e: e.tensor_copy(dqrows, pA[0:16, 0:2048]), ch="V")
        diagTq = g32(0, 128, 256)
        ev_tq_e = None
        for cnk in range(16):
            evt = op("tensor", [ev_dqr, evc["idf"], ev_tq_e],
                     lambda e, cnk=cnk: e.transpose(
                         pB[0:128, 1536:1552],
                         dqrows[0:16, cnk * 128:(cnk + 1) * 128],
                         c_idf[0:16, 0:16]), ch="T")
            ev_tq_e = op("vector", [evt], lambda e, cnk=cnk: e.tensor_copy(
                diagTq[:, cnk * 16:(cnk + 1) * 16], pB[0:128, 1536:1552]), ch="V")
        expdqe = g32(256, 128, 256)
        ev_expdq = op("scalar", [ev_tq_e], lambda e: e.activation(
            expdqe, diagTq, AF.Exp, bias=np.float32(LNEPS)), ch="S")

        ev_dk = None
        ev_dks = [None, None]
        for j in range(4):
            for half in range(2):
                sj = half
                w = [ev_kpark[j], ev_dks[sj]]
                if j == 0 and half == 0:
                    w.append(ev_dq)
                ev_sqk = op("vector", w,
                            lambda e, j=j, half=half, sj=sj: e.tensor_mul(
                                dsc[sj][:, :], kpark[j][:, half * 2048:(half + 1) * 2048],
                                kpark[j][:, half * 2048:(half + 1) * 2048]), ch="V")
                for gi in range(4):
                    g = half * 4 + gi
                    w2 = [ev_sqk, evc["selk"]]
                    if j == 0 and g == 0:
                        w2.append(ev_dqr)
                    if j == 0 and g == 4:
                        w2.append(ev_tq_e)
                    ev_dk = op("tensor", w2,
                               lambda e, j=j, g=g, gi=gi, sj=sj: e.matmul(
                                   (pA if g < 4 else pB)[0:8, (g % 4) * 512:(g % 4) * 512 + 512],
                                   c_selk[:, j * 8:(j + 1) * 8],
                                   dsc[sj][:, gi * 512:(gi + 1) * 512],
                                   start=(j == 0), stop=(j == 3)), ch="T")
                ev_dks[sj] = ev_dk
        dkrows = g32(1024, 8, 2048)
        diagTk = g32(512, 128, 256)
        ev_tk_e = None
        for half in range(2):
            ev_dkr = op("vector", [ev_dk] + ([ev_dqr] if half == 0 else [ev_tk_e]),
                        lambda e, half=half: e.tensor_copy(
                            dkrows, (pA if half == 0 else pB)[0:8, 0:2048]), ch="V")
            for c2 in range(16):
                cnk = half * 16 + c2
                evt = op("tensor", [ev_dkr, evc["idf"], ev_tk_e],
                         lambda e, c2=c2: e.transpose(
                             pA[0:128, 1536:1544],
                             dkrows[0:8, c2 * 128:(c2 + 1) * 128],
                             c_idf[0:8, 0:8]), ch="T")
                ev_tk_e = op("vector", [evt], lambda e, cnk=cnk: e.tensor_copy(
                    diagTk[:, cnk * 8:(cnk + 1) * 8], pA[0:128, 1536:1544]), ch="V")
        expndk = g32(768, 128, 256)
        ev_expdk = op("scalar", [ev_tk_e], lambda e: e.activation(
            expndk, diagTk, AF.Exp, scale=np.float32(-1.0)), ch="S")

        wo_t = [Wa[:, j * 1024:(j + 1) * 1024] for j in range(KC)]
        ev_wo = [dma("gpsimd", [ev_qmm_last], wo_t[j], d_wo[j * 128:(j + 1) * 128, :], stream="gW2")
                 for j in range(KC)]

        etok_s = [g16(ZC + s_ * 8512, 128, 8512) for s_ in range(2)]
        vts = g16(ZD + 640, 128, 2080)
        ctx_sbb = g16p(0, 65, RM, 2128)
        maxr2 = [g32(3152, 128, 34), g32(3186, 128, 34)]
        recB = g32(3220, 128, 1)
        epscol = g32(3222, 65, 1)
        skT = g32(3224, 128, 32)
        ev_ctx_ev = []
        ev_ctx_last = None
        ev_skT_free = [None]
        batches = [list(range(b, min(b + 3, 32))) for b in range(0, 32, 3)]
        ev_scan_of = {}
        etok_free = [None, None]
        ev_exp_of = {}

        ev_tr_of = {}
        ev_mx_of = {}
        ev_rbe_of = {}
        ev_ep_of = {}
        ev_ce_of = {}

        def ks_dash(hh):
            sl = hh % 2
            kch = hh // 2
            kb = (hh % 2) * 64
            ev_exp_all = []
            for bi, bat in enumerate(batches):
                ph = pA if bi % 2 == 0 else pB
                evm = []
                for ii, cnk in enumerate(bat):
                    w = [ev_kpark[kch], evc["projb"]]
                    if bi >= 2:
                        w.append(ev_exp_all[bi - 2])
                    if hh == 0 and bi <= 1:
                        w.append(ev_v[31])
                    if hh >= 1 and bi <= 1 and ii == 0:
                        w += ev_exp_of[hh - 1][-2:]
                    evm.append(op("tensor", w,
                                  lambda e, ph=ph, ii=ii, cnk=cnk, kch=kch, kb=kb: e.matmul(
                                      ph[0:128, ii * 512:ii * 512 + M],
                                      kpark[kch][kb:kb + 64, cnk * 128:(cnk + 1) * 128],
                                      c_projb[kb:kb + 64, :],
                                      start=True, stop=True), ch="T"))
                nb = len(bat)
                ev_exp = op("scalar", evm + [etok_free[sl] if bi == 0 else None],
                            lambda e, ph=ph, nb=nb, c0=bat[0], sl=sl: e.activation(
                                etok_s[sl][:, c0 * 266:(c0 + nb) * 266].rearrange(
                                    "p (b m) -> p b m", m=266),
                                bass.AP(ph, 0, [[2048, 128], [512, nb], [1, M]]),
                                AF.Exp), ch="S")
                ev_exp_all.append(ev_exp)
            ev_exp_of[hh] = ev_exp_all

        def ks_scan(hh):
            mx = maxr2[hh % 2]
            w = [ev_exp_of[hh][-1]]
            if hh >= 2:
                w.append(ev_tr_of[hh - 2])
            ev_red = op("vector", w,
                        lambda e, sl=hh % 2, mx=mx: e.tensor_reduce(
                            mx[:, 0:32],
                            etok_s[sl][:, :].rearrange("p (c m) -> p c m", m=266),
                            AX.X, OP.max), ch="V")
            ev_red2 = op("vector", [ev_red], lambda e, mx=mx: e.tensor_reduce(
                mx[:, 32:33], mx[:, 0:32], AX.X, OP.max), ch="V")
            ev_scan_of[hh] = ev_red2

        def ks_tailA(hh):
            mx = maxr2[hh % 2]
            w = [ev_scan_of[hh], evc["idf"]]
            if hh >= 2:
                w.append(ev_mx_of[hh - 2])
            ev_tr = op("tensor", w,
                       lambda e, mx=mx: e.transpose(pB[0:1, 1536:1664], mx[:, 32:33],
                                                    c_idf[:, :]), ch="T")
            ev_tr_of[hh] = ev_tr
            ev_mx = op("vector", [ev_tr], lambda e: e.tensor_reduce(
                t_sc[0:1, 0:1], pB[0:1, 1536:1664], AX.X, OP.max), ch="V")
            ev_mx_of[hh] = ev_mx
            ev_rc = op("vector", [ev_mx], lowp(lambda e: e.reciprocal(
                t_scb[0:1, 0:1], t_sc[0:1, 0:1])), ch="V")
            w = [ev_rc, evc["ones"]]
            if hh >= 2:
                w.append(ev_rbe_of[hh - 2])
            ev_rb = op("tensor", w,
                       lambda e: e.matmul(pB[0:128, 1700:1701],
                                          c_ones[0:1, 0:128], t_scb[0:1, 0:1],
                                          start=True, stop=True), ch="T")
            ev_rbe = op("vector", [ev_rb], lambda e: e.tensor_copy(recB, pB[0:128, 1700:1701]), ch="V")
            ev_rbe_of[hh] = ev_rbe
            ev_sk = op("vector", [ev_rbe, ev_expdk, ev_skT_free[0]],
                       lambda e, hh=hh: e.tensor_scalar_mul(
                           skT, G32[0:128, 768 + hh:1024:8], recB[:, 0:1]), ch="V")
            return dict(ev_sk=ev_sk, ev_rb=ev_rb)

        def ks_tailB(hh, stA):
            sl = hh % 2
            ev_vt1 = op("vector", [stA["ev_sk"]],
                        lambda e, hh=hh: e.tensor_tensor(
                            vts[:, :].rearrange("p (c w) -> p c w", w=65)[:, :, 0:64],
                            bass.AP(G16, RQ + 8192 + hh * 64, [[81032, 128], [512, 32], [1, 64]]),
                            skT.rearrange("p (c o) -> p c o", o=1).broadcast_to((128, 32, 64)),
                            OP.mult), ch="V")
            ev_vt2 = op("vector", [stA["ev_sk"]], lambda e: e.tensor_copy(
                vts[:, :].rearrange("p (c w) -> p c w", w=65)[:, :, 64:65],
                skT.rearrange("p (c o) -> p c o", o=1)), ch="V")
            ev_skT_free[0] = ev_vt2
            ev_cs = None
            for cnk in range(32):
                w = [ev_v[cnk], evc["ones"]]
                if cnk == 0:
                    w.append(stA["ev_rb"])
                    if hh >= 2:
                        w.append(ev_ep_of[hh - 2])
                ev_cs = op("tensor", w,
                           lambda e, cnk=cnk, hh=hh: e.matmul(
                               pB[0:64, 1792:1793],
                               v_t(cnk)[:, hh * 64:hh * 64 + 64],
                               c_ones[:, 0:1],
                               start=(cnk == 0), stop=(cnk == 31)), ch="T")
            ev_ep1 = op("vector", [ev_cs], lambda e: e.tensor_scalar_mul(
                epscol[0:64, 0:1], pB[0:64, 1792:1793], EPS), ch="V")
            ev_ep2 = op("vector", [ev_ep1], lambda e: e.memset(
                epscol[64:65, 0:1], EPS * float(NF)), ch="V")
            ev_ep_of[hh] = ev_ep1
            return dict(ev_vt1=ev_vt1, ev_vt2=ev_vt2, ev_ep2=ev_ep2, **stA)

        def ks_tailC(hh, stB):
            sl = hh % 2
            ev_exp_all = ev_exp_of[hh]
            ev_ctx = None
            for cnk in range(32):
                w = [stB["ev_vt1"], stB["ev_vt2"],
                     ev_exp_all[min(cnk // 3, len(ev_exp_all) - 1)]]
                if cnk == 0 and ev_ctx_ev:
                    w.append(ev_ctx_ev[-1])
                ev_ctx = op("tensor", w,
                            lambda e, cnk=cnk, sl=sl: e.matmul(
                                pA[0:65, 1536:1536 + M],
                                vts[:, cnk * 65:(cnk + 1) * 65],
                                etok_s[sl][:, cnk * 266:(cnk + 1) * 266],
                                start=(cnk == 0), stop=(cnk == 31)), ch="T")
            etok_free[sl] = ev_ctx
            ev_ce = op("vector", [ev_ctx, stB["ev_ep2"]],
                       lambda e, hh=hh: e.tensor_scalar(
                           ctx_sbb[:, hh * M:(hh + 1) * M], pA[0:65, 1536:1536 + M],
                           epscol[0:65, 0:1], None, OP.add), ch="V")
            ev_ctx_ev.append(ev_ce)
            return ev_ctx

        ev_ctx_last = None
        stA = {}
        stB = {}
        for hh in range(HK):
            if hh >= 1:
                stA[hh - 1] = ks_tailA(hh - 1)
            ks_dash(hh)
            if hh >= 1:
                stB[hh - 1] = ks_tailB(hh - 1, stA[hh - 1])
            ks_scan(hh)
            if hh >= 1:
                ev_ctx_last = ks_tailC(hh - 1, stB[hh - 1])
        stA[HK - 1] = ks_tailA(HK - 1)
        stB[HK - 1] = ks_tailB(HK - 1, stA[HK - 1])
        ev_ctx_last = ks_tailC(HK - 1, stB[HK - 1])

        eq0 = [g16(s_ * 6144, 128, 2048) for s_ in range(2)]
        eq1 = [g16(s_ * 6144 + 2048, 128, 2048) for s_ in range(2)]
        eq2 = [g16p(0, 16, s_ * 6144 + 4096, 2048) for s_ in range(2)]
        stage = g16(ZD + 640, 128, 2048)
        merged = g16(ZC, 128, 16384)
        u_row = [g16p(0, 1, 12288 + s_ * 2048, 2048) for s_ in range(2)]
        u_tr = g32(3168, 16, 128)
        maxEq_p = [g32(3296, 128, 16), g32(1024, 128, 16)]
        u_cols_p = [g32(3312, 128, 16), g32(1040, 128, 16)]
        rbT_b = g16p(0, 128, RQ + 14336, 16)
        scanb = g16(RQ + 8192, 128, 2048)
        num_sb = [g16p(0, 65, RQ + 10240 + s * 2048, 2048) for s in range(2)]

        ev_eq_free = [None, None]
        ev_u_free = [None, None]
        ev_stage_free = None
        ev_scan_prev = None
        ev_uc_pp = [None, None]
        ev_ut_pp = [None, None]
        ev_ur_prev = None
        ev_rT_prev = None
        ev_bc_prev = None
        ev_num_free = [None, None]
        ev_par_free = [ev_ctx_ev[-1], ev_ctx_ev[-1]]
        ev_merged = []
        st_front = {}

        def qs_dash(h):
            nonlocal ev_scan_prev
            pj = h // 2
            sl = pj % 2
            es = h % 2
            qb = es * 64
            P = pA if es == 0 else pB
            Pb = pAb if es == 0 else pBb

            def dmm(m0, mw, g, w):
                return op("tensor", w,
                          lambda e, g=g, m0=m0, mw=mw, qb=qb, pj=pj, P=P: e.matmul(
                              P[0:mw, g * 512:(g + 1) * 512],
                              c_projb[qb:qb + 64, m0:m0 + mw],
                              qpark[pj][qb:qb + 64, g * 512:(g + 1) * 512],
                              start=True, stop=True), ch="T")
            w0 = [ev_qpark[pj], evc["projb"], ev_par_free[es]]
            ev_c2a = [dmm(256, 10, g, w0) for g in (0, 1)]
            ev_c2b = [dmm(256, 10, g, w0) for g in (2, 3)]
            ev_e2a = op("scalar", ev_c2a + [ev_eq_free[es]],
                        lambda e, es=es, P=P: e.activation(
                            eq2[es][0:10, 0:1024], P[0:10, 0:1024], AF.Exp), ch="S")
            ev_e2b = op("scalar", ev_c2b + [ev_eq_free[es]],
                        lambda e, es=es, P=P: e.activation(
                            eq2[es][0:10, 1024:2048], P[0:10, 1024:2048], AF.Exp), ch="S")
            ev_c0a = [dmm(0, 128, g, [ev_e2a]) for g in (0, 1)]
            ev_c0b = [dmm(0, 128, g, [ev_e2b]) for g in (2, 3)]
            ev_e0a = op("scalar", ev_c0a + [ev_eq_free[es]],
                        lambda e, es=es, P=P: e.activation(
                            eq0[es][:, 0:1024], P[:, 0:1024], AF.Exp), ch="S")
            ev_e0b = op("scalar", ev_c0b + [ev_eq_free[es]],
                        lambda e, es=es, P=P: e.activation(
                            eq0[es][:, 1024:2048], P[:, 1024:2048], AF.Exp), ch="S")
            st = dict(es=es, sl=sl, pj=pj, qb=qb, P=P, Pb=Pb, dmm=dmm,
                      ev_e2a=ev_e2a, ev_e2b=ev_e2b, ev_e0a=ev_e0a, ev_e0b=ev_e0b)
            return st

        def qs_dash2(st, h):
            es = st["es"]; P = st["P"]; dmm = st["dmm"]
            ev_c1a = [dmm(128, 128, g, [st["ev_e0a"]]) for g in (0, 1)]
            ev_c1b = [dmm(128, 128, g, [st["ev_e0b"]]) for g in (2, 3)]
            ev_e1a = op("scalar", ev_c1a + [ev_eq_free[es]],
                        lambda e, es=es, P=P: e.activation(
                            eq1[es][:, 0:1024], P[:, 0:1024], AF.Exp), ch="S")
            ev_e1b = op("scalar", ev_c1b + [ev_eq_free[es]],
                        lambda e, es=es, P=P: e.activation(
                            eq1[es][:, 1024:2048], P[:, 1024:2048], AF.Exp), ch="S")
            st["ev_e1a"] = ev_e1a
            st["ev_e1b"] = ev_e1b
            st["ev_e0"] = st["ev_e0b"]
            st["ev_e1"] = ev_e1b
            st["ev_e2"] = st["ev_e2b"]
            return st

        def qs_scan(st, h):
            nonlocal ev_scan_prev
            es = st["es"]; P = st["P"]; Pb = st["Pb"]
            ev_cm = []
            for half, (ee0, ee1, ee2) in enumerate(
                    [(st["ev_e0a"], st["ev_e1a"], st["ev_e2a"]),
                     (st["ev_e0b"], st["ev_e1b"], st["ev_e2b"])]):
                c0 = half * 1024
                w_cm = [ee0, ee1]
                if ev_scan_prev is not None:
                    w_cm.append(ev_scan_prev)
                ev_cm1 = op("vector", w_cm, lambda e, es=es, c0=c0: e.tensor_max(
                    scanb[:, c0:c0 + 1024], eq0[es][:, c0:c0 + 1024],
                    eq1[es][:, c0:c0 + 1024]), ch="V")
                ev_cm2 = op("vector", [ev_cm1, ee2], lambda e, es=es, c0=c0: e.tensor_max(
                    scanb[0:10, c0:c0 + 1024], scanb[0:10, c0:c0 + 1024],
                    eq2[es][0:10, c0:c0 + 1024]), ch="V")
                ev_cm.append(ev_cm2)
            ev_sct = []
            ev_scr = []
            for cnk in range(16):
                w = [ev_cm[cnk // 8], evc["idb"], st["ev_e1b"]]
                if cnk == 8:
                    w.append(ev_scr[0])
                ev_sct.append(op("tensor", w,
                                 lambda e, cnk=cnk, Pb=Pb: e.transpose(
                                     Pb[:, (cnk % 8) * 128:(cnk % 8) * 128 + 128],
                                     scanb[:, cnk * 128:(cnk + 1) * 128],
                                     c_idb[:, :]), ch="T"))
                if cnk % 8 == 7:
                    wr = ev_sct[-8:]
                    if cnk == 7:
                        wr = wr + [ev_uc_pp[es]]
                    ev_scr.append(op("vector", wr,
                                     lambda e, cnk=cnk, Pb=Pb, es=es: e.tensor_reduce(
                                         maxEq_p[es][:, cnk - 7:cnk + 1],
                                         Pb[:, 0:1024].rearrange("p (c f) -> p c f", f=128),
                                         AX.X, OP.max), ch="V"))
            ev_scan_prev = ev_sct[-1]
            ev_uc = op("vector", [ev_scr[-1], ev_expdq, ev_ut_pp[es]],
                       lambda e, h=h, es=es: e.tensor_mul(
                           u_cols_p[es], G32[0:128, 256 + h:512:16], maxEq_p[es]), ch="V")
            ev_uc_pp[es] = ev_uc
            st.update(ev_cm1=ev_cm[1], ev_scr=ev_scr, ev_uc=ev_uc)
            return st

        def qs_ur(st, h):
            nonlocal ev_ur_prev
            es = st["es"]
            P = st["P"]
            ev_ut = op("tensor", [st["ev_uc"], evc["idf"], st["ev_e1b"]],
                       lambda e, P=P, es=es: e.transpose(
                           P[0:16, 1024:1152], u_cols_p[es], c_idf[:, :]), ch="T")
            ev_ut_pp[es] = ev_ut
            ev_ue = op("vector", [ev_ut, ev_ur_prev], lambda e, P=P: e.tensor_copy(
                u_tr, P[0:16, 1024:1152]), ch="V")
            st["ev_ue"] = ev_ue
            ev_ur = dma("gpsimd", [ev_ue, ev_u_free[es]],
                        u_row[es][0:1, :].rearrange("o (c f) -> o c f", c=16), u_tr,
                        stream="gE")
            ev_ur_prev = ev_ur
            st["ev_ur"] = ev_ur
            return st

        def qs_am01(st, h):
            es = st["es"]; P = st["P"]
            ev_am = []
            for g in (1, 2, 3, 0):
                hv = "a" if g < 2 else "b"
                for mi in (0, 1):
                    m0, mw = MCH[mi]
                    src = ctxL[0:mw, h * 130 + mi * 65:h * 130 + mi * 65 + 65]
                    esrc = (eq0[es] if mi == 0 else eq1[es])[0:mw, g * 512:(g + 1) * 512]
                    w = [ev_ctxL[h][mi], st["ev_e0" + hv] if mi == 0 else st["ev_e1" + hv],
                         st["ev_e1" + hv]]
                    if h <= 1 and g == 1 and mi == 0:
                        w.append(ev_ctxL[H - 1][2])
                    if mi == 0:
                        if g == 0:
                            w = w + st["ev_scr"]
                        elif g == 2:
                            w.append(st["ev_ue"])
                    ev_am.append(op("tensor", w,
                                    lambda e, g=g, src=src, esrc=esrc, mi=mi, P=P: e.matmul(
                                        P[0:65, g * 512:(g + 1) * 512],
                                        src, esrc,
                                        start=(mi == 0), stop=False), ch="T"))
            st["ev_am01"] = ev_am
            return st

        def qs_am2(st, h):
            es = st["es"]; P = st["P"]
            ev_am = []
            for g in (1, 2, 3, 0):
                ev_am.append(op("tensor", [ev_ctxL[h][2], st["ev_e2"]],
                                lambda e, g=g, h=h, es=es, P=P: e.matmul(
                                    P[0:65, g * 512:(g + 1) * 512],
                                    ctxL2[0:10, h * 65:(h + 1) * 65],
                                    eq2[es][0:10, g * 512:(g + 1) * 512],
                                    start=False, stop=False), ch="T"))
                ev_am.append(op("tensor", [st["ev_ur"], ev_B_done],
                                lambda e, g=g, h=h, es=es, P=P: e.matmul(
                                    P[0:65, g * 512:(g + 1) * 512],
                                    B_rows[0:1, h * 65:(h + 1) * 65],
                                    u_row[es][0:1, g * 512:(g + 1) * 512],
                                    start=False, stop=True), ch="T"))
            ev_am = st["ev_am01"] + ev_am
            ev_u_free[es] = ev_am[-1]
            ev_eq_free[es] = ev_am[-1]
            st["ev_am"] = ev_am
            return st

        def qs_nev(st, h):
            es = st["es"]; P = st["P"]
            st["ev_nev"] = op("scalar", st["ev_am"] + [ev_num_free[es]],
                              lambda e, es=es, P=P: e.activation(
                                  num_sb[es][0:65, :], P[0:65, :], AF.Copy), ch="S")
            return st

        def qs_dT(st, h):
            es = st["es"]; Pb = st["Pb"]
            st["ev_dT"] = [op("tensor", [st["ev_nev"], evc["idb"]],
                              lambda e, cnk=cnk, es=es, Pb=Pb: e.transpose(
                                  Pb[0:128, 2048 + 2 * cnk:2048 + 2 * cnk + 1],
                                  num_sb[es][64:65, cnk * 128:(cnk + 1) * 128],
                                  c_idb[64:65, 64:65]), ch="T") for cnk in range(16)]
            return st

        def qs_rcp(st, h):
            nonlocal ev_rT_prev
            Pb = st["Pb"]
            st["ev_rcp"] = op("vector", st["ev_dT"] + [ev_rT_prev],
                              lowp(lambda e, Pb=Pb: e.reciprocal(
                                  rbT_b[0:128, 0:16], Pb[0:128, 2048:2080:2])), ch="V")
            ev_rT_prev = st["ev_rcp"]
            return st

        def qs_bc(st, h):
            P = st["P"]
            st["ev_bc"] = [op("tensor", [st["ev_rcp"], evc["idb"]],
                              lambda e, cnk=cnk, P=P: e.matmul(
                                  P[0:64, cnk * 128:(cnk + 1) * 128],
                                  bass.AP(G16, RQ + 14336 + cnk, [[81032, 128], [0, 64]]),
                                  c_idb[:, :],
                                  start=True, stop=True), ch="T") for cnk in range(16)]
            return st

        def qs_mul(st, h):
            nonlocal ev_stage_free
            es = st["es"]; P = st["P"]; pj = st["pj"]
            if es == 0:
                dsth = merged[0:64, pj * 2048:(pj + 1) * 2048]
                wex = []
            else:
                dsth = stage[0:64, 0:2048]
                wex = [ev_stage_free]
            ev_m = op("vector", st["ev_bc"] + wex + [st["ev_nev"]],
                      lambda e, dsth=dsth, es=es, P=P: e.tensor_mul(
                          dsth, num_sb[es][0:64, 0:2048], P[0:64, 0:2048]), ch="V")
            ev_merged.append(ev_m)
            ev_par_free[es] = ev_m
            ev_num_free[es] = ev_m
            out_ev = ev_m
            if es == 1:
                evd = dma("sync", [ev_m], merged[64:128, pj * 2048:(pj + 1) * 2048],
                          stage[0:64, 0:2048], stream="sST")
                ev_merged.append(evd)
                ev_stage_free = evd
                out_ev = evd
            return out_ev

        def qs_front_all(h):
            st = qs_dash(h)
            qs_dash2(st, h)
            qs_scan(st, h)
            qs_ur(st, h)
            st_front[h] = st
            return st

        prev_psum = None
        ev_ag1s = dma("sync", ev_ctx_ev, ag1_in[:, :], ctx_sbb, stream="sAG1")
        ev_cc1 = op("gpsimd", [ev_ag1s], lambda e: e.collective_compute(
            "AllGather", OP.bypass, replica_groups=groups,
            ins=[ag1_in.ap()], outs=[ag1_out.ap()]), ch="cc")
        qs_front_all(0)
        qs_front_all(1)
        ctxTb = g16p(0, 65, RQ, 4256)
        ev_ctb = [dma("sync", [ev_cc1, ev_ctx_last], ctxTb[:, g * 2128:(g + 1) * 2128],
                      ag1_out[g * 65:(g + 1) * 65, :], stream="sL") for g in range(2)]

        Bcols = g32(3152, 65, 16)
        ev_B = op("vector", ev_ctb, lambda e: e.tensor_reduce(
            Bcols, ctxTb[:, :].rearrange("p (h m) -> p h m", m=M),
            AX.X, OP.add), ch="V")
        B_rows = g16p(0, 1, RQ + 14352, 1040)
        ev_Brt = []
        ev_Br = []
        for h in range(H):
            w = [ev_B, evc["idf"], ev_ctx_last]
            if h % 4 == 0 and ev_Br:
                w.append(ev_Br[-1])
            evt = op("tensor", w,
                     lambda e, h=h: e.transpose(
                         pB[0:1, 1536 + (h % 4) * 65:1536 + (h % 4) * 65 + 65],
                         Bcols[0:65, h:h + 1], c_idf[0:65, 0:65]), ch="T")
            ev_Brt.append(evt)
            if h % 4 == 3:
                ev_Br.append(op("vector", ev_Brt[-4:],
                                lambda e, h=h: e.tensor_copy(
                                    B_rows[0:1, (h - 3) * 65:(h + 1) * 65],
                                    pB[0:1, 1536:1536 + 260]), ch="V"))
        ev_B_done = ev_Br[-1]

        ctxL = g16(RM, 128, 2080)
        ctxL2 = g16p(0, 16, RM + 2080, 1040)
        ev_ctxL = {}
        prev_t = ev_Br[-1]
        for h in range(H):
            g = h // 8
            hh = h % 8
            evs = []
            for mi, (m0, mw) in enumerate(MCH):
                evt = op("tensor", [ev_ctb[g], evc["idb"], prev_t],
                         lambda e, g=g, hh=hh, m0=m0, mw=mw: e.transpose(
                             pBb[0:mw, 3072:3072 + 65],
                             ctxTb[0:65, g * 2128 + hh * M + m0:g * 2128 + hh * M + m0 + mw],
                             c_idb[0:65, 0:65]), ch="T")
                if mi < 2:
                    dst = ctxL[0:mw, h * 130 + mi * 65:h * 130 + mi * 65 + 65]
                else:
                    dst = ctxL2[0:10, h * 65:(h + 1) * 65]
                eve = op("vector", [evt], lambda e, dst=dst, mw=mw: e.tensor_copy(
                    dst, pBb[0:mw, 3072:3072 + 65]), ch="V")
                evs.append(eve)
                prev_t = eve
            ev_ctxL[h] = evs

        qs_am01(st_front[0], 0)
        qs_am01(st_front[1], 1)
        qs_am2(st_front[0], 0)
        qs_nev(st_front[0], 0)
        qs_dT(st_front[0], 0)
        qs_rcp(st_front[0], 0)
        qs_bc(st_front[0], 0)
        prev_psum = qs_mul(st_front[0], 0)
        for h in range(2, H):
            st = qs_dash(h)
            qs_dash2(st, h)
            qs_am2(st_front[h - 1], h - 1)
            qs_scan(st, h)
            qs_nev(st_front[h - 1], h - 1)
            qs_dT(st_front[h - 1], h - 1)
            qs_ur(st, h)
            qs_rcp(st_front[h - 1], h - 1)
            qs_bc(st_front[h - 1], h - 1)
            qs_am01(st, h)
            prev_psum = qs_mul(st_front[h - 1], h - 1)
            st_front[h] = st
        st = st_front[H - 1]
        qs_am2(st, H - 1)
        qs_nev(st, H - 1)
        qs_dT(st, H - 1)
        qs_rcp(st, H - 1)
        qs_bc(st, H - 1)
        prev_psum = qs_mul(st, H - 1)

        bw_t = [Wb[:, j * 1024:(j + 1) * 1024] for j in range(KC)]
        ev_bw = [dma("gpsimd", [ev_ctx_last], bw_t[j], d_bwT[j * 128:(j + 1) * 128, :], stream="gW3")
                 for j in range(KC)]

        attn = g16(RQ + 8192, 128, 16384)
        x1 = g16(ZB, 128, 16384)
        xf = g16(ZC, 128, 16384)
        ffo = g16(0, 128, 16384)
        h_ext = g16p(0, 65, RQ + 6144, 2048)
        M_ext = g16p(0, 65, ZD, 1024)
        xbh_s = [g16(RQ + s * 2048, 128, 2048) for s in range(2)]
        ln_m = g16(RM, 128, 2048)
        ln_r = g16(RM + 2048, 128, 2048)
        ln_t = g16(ZD + 1024, 128, 2048)
        ftmp = g32(0, 128, 2048)
        m12 = g32(2048, 128, 16)
        m12b = g32(2064, 128, 16)
        ivT = g32(2080, 128, 64)
        muiv = g32(2144, 128, 64)
        mui2 = g32(2208, 128, 64)
        csum = g32(2272, 128, 8)
        sq2 = g16(RQ + 4096, 128, 2048)
        rT32 = g32(3296, 128, 16)
        r_rb = g16p(0, 1, RM + 2048, 2048)
        ivb = g16p(0, 128, ZD + 3072, 64)
        muivb = g16p(0, 128, ZD + 3136, 64)
        csb = g16p(0, 128, ZD + 3200, 8)
        _of = (RQ + 8192) // 2
        outst = [G16f[0:128, _of + s * 2048:_of + (s + 1) * 2048] for s in range(4)]
        t_ffw = G16[0:128, RQ + 16384:RQ + 17536]
        t_ones512 = G16[0:1, RQ + 17536:RQ + 18048]

        ev_attn = [None] * KC
        ev_womm = None
        wo_free = {0: None, 1: None}
        for och in range(KC):
            ph = pA if och % 2 == 0 else pB
            ev_mm = None
            for g in range(4):
                for j in range(KC):
                    w = [ev_wo[j]]
                    if g == 0 and j == 0:
                        w.append(wo_free[och % 2])
                    if och == 0 and g == 0 and j == 0:
                        w += ev_merged + [prev_psum]
                    ev_mm = op("tensor", w,
                               lambda e, ph=ph, g=g, j=j, och=och: e.matmul(
                                   ph[:, g * 512:(g + 1) * 512],
                                   wo_t[j][:, och * 128:(och + 1) * 128],
                                   merged[:, j * 2048 + g * 512:j * 2048 + (g + 1) * 512],
                                   start=(j == 0), stop=(j == KC - 1)), ch="T")
            ev_womm = ev_mm
            ev_attn[och] = op("scalar", [ev_mm, evc["bo"]],
                              lambda e, ph=ph, och=och: e.activation(
                                  attn[:, och * 2048:(och + 1) * 2048], ph[:, :],
                                  AF.Identity, bias=c_bo[:, och:och + 1]), ch="S")
            wo_free[och % 2] = ev_attn[och]

        def ln_stats(src_fn, ev_src, sq_slot, ev_sq_pre, ev_psA_pre, ev_psB_pre):
            ev_mm = None
            for j in range(KC):
                for g in range(4):
                    w = [ev_src[j], evc["lnones"]]
                    if j == 0 and g == 0:
                        w += ev_psA_pre
                    ev_mm = op("tensor", w,
                               lambda e, j=j, g=g: e.matmul(
                                   pA[:, g * 512:(g + 1) * 512],
                                   c_lnones[:, :], src_fn(j)[:, g * 512:(g + 1) * 512],
                                   start=(j == 0), stop=(j == KC - 1)), ch="T")
            ev_mb = op("vector", [ev_mm], lambda e: e.tensor_copy(ln_m[:, :], pA[:, 0:2048]), ch="V")
            ev_smm = None
            prev = list(ev_sq_pre)
            for j in range(KC):
                ev_sq = op("scalar", [ev_src[j]] + prev,
                           lambda e, j=j: e.activation(sq_slot[:, :], src_fn(j)[:, :], AF.Square),
                           ch="S")
                for g in range(4):
                    w = [ev_sq]
                    if j == 0 and g == 0:
                        w += ev_psB_pre
                    ev_smm = op("tensor", w,
                                lambda e, j=j, g=g: e.matmul(
                                    pB[:, g * 512:(g + 1) * 512],
                                    c_lnones[:, :], sq_slot[:, g * 512:(g + 1) * 512],
                                    start=(j == 0), stop=(j == KC - 1)), ch="T")
                prev = [ev_smm]
            ev_t1 = op("vector", [ev_smm, ev_mb], lambda e: e.tensor_copy(ftmp, pB[:, 0:2048]), ch="V")
            ev_m2 = op("scalar", [ev_mb] + prev, lambda e: e.activation(ln_t[:, :], ln_m[:, :], AF.Square), ch="S")
            ev_var = op("vector", [ev_t1, ev_m2], lambda e: e.tensor_sub(ftmp, ftmp, ln_t[:, :]), ch="V")
            ev_sd = op("scalar", [ev_var], lambda e: e.activation(ftmp, ftmp, AF.Sqrt, bias=np.float32(1e-5)), ch="S")
            ev_vT = [op("tensor", [ev_sd, evc["idf"], ev_t1],
                       lambda e, cnk=cnk: e.transpose(
                           pB[0:128, 1536 + cnk:1536 + cnk + 1],
                           G32[0:1, cnk * 128:(cnk + 1) * 128],
                           c_idf[0:1, 0:1]), ch="T") for cnk in range(16)]
            ev_rc = op("vector", ev_vT, lowp(lambda e: e.reciprocal(
                rT32[0:128, 0:16], pB[0:128, 1536:1552])), ch="V")
            ev_rT = [op("tensor", [ev_rc, evc["idf"]],
                       lambda e, cnk=cnk: e.transpose(
                           pB[0:1, cnk * 128:(cnk + 1) * 128],
                           rT32[0:128, cnk:cnk + 1],
                           c_idf[:, :]), ch="T") for cnk in range(16)]
            ev_re = op("vector", ev_rT, lambda e: e.tensor_copy(r_rb[0:1, :], pB[0:1, 0:2048]), ch="V")
            ev_bcr = [op("tensor", [ev_re, evc["ones"]],
                        lambda e, g=g: e.matmul(
                            pB[0:128, g * 512:(g + 1) * 512],
                            c_ones[0:1, 0:128],
                            r_rb[0:1, g * 512:(g + 1) * 512],
                            start=True, stop=True), ch="T") for g in range(4)]
            ev_r = ev_bcr[-1]
            return ev_mb, ev_r, ev_smm, ev_t1, ev_sd

        xbh4 = [xbh_s[0], xbh_s[1], g16(ZC, 128, 2048), g16(ZC + 2048, 128, 2048)]
        xbh_ev = [dma("gpsimd", [ev_stage_free], xbh4[0][:, :], d_xT[0:128, 0:2048], stream="gE2"),
                  dma("gpsimd", [ev_stage_free], xbh4[1][:, :], d_xT[128:256, 0:2048], stream="gE3"),
                  dma("gpsimd", [ev_womm], xbh4[2][:, :], d_xT[256:384, 0:2048], stream="gE2"),
                  dma("gpsimd", [ev_womm], xbh4[3][:, :], d_xT[384:512, 0:2048], stream="gE3")]
        ev_m2b, ev_r2, ev_ss2, ev_t1_2, ev_sd2 = ln_stats(
            lambda j: attn[:, j * 2048:(j + 1) * 2048], ev_attn, sq2,
            [ev_ctxL[H - 1][2]], [ev_attn[KC - 2], ev_attn[KC - 1]],
            [ev_attn[KC - 2], ev_attn[KC - 1]])
        ev_x1 = [None] * KC
        ev_lnt_prev = ev_r2
        for j in range(KC):
            sj = j % 4
            ev_t = op("vector", [ev_attn[j], ev_m2b, ev_lnt_prev, ev_womm],
                      lambda e, j=j: e.tensor_sub(
                          ln_t[:, :], attn[:, j * 2048:(j + 1) * 2048], ln_m[:, :]), ch="V")
            ev_t2 = op("vector", [ev_t, ev_r2],
                       lambda e: e.tensor_mul(ln_t[:, :], ln_t[:, :], pB[:, 0:2048]), ch="V")
            ev_x1[j] = op("vector", [ev_t2, xbh_ev[sj]],
                          lambda e, j=j, sj=sj: e.tensor_add(
                              x1[:, j * 2048:(j + 1) * 2048], ln_t[:, :], xbh4[sj][:, :]),
                          ch="V")
            ev_lnt_prev = ev_x1[j]
            if j + 4 < KC:
                xbh_ev[sj] = dma("gpsimd", [ev_x1[j]], xbh4[sj][:, :],
                                 d_xT[(j + 4) * 128:(j + 5) * 128, 0:2048],
                                 stream=["gE2", "gE3"][sj % 2])

        ev_m3b, ev_r3, ev_ss3, ev_t1_3, ev_sd3 = ln_stats(
            lambda j: x1[:, j * 2048:(j + 1) * 2048], ev_x1, sq2,
            [ev_ss2, ev_sd2, ev_x1[-1]], [ev_m2b, ev_t1_2], [ev_t1_2, ev_x1[-1]])
        ev_xf = [None] * KC
        ev_m2c = [None] * KC
        ev_lnt_prev = ev_r3
        for j in range(KC):
            ev_t = op("vector", [ev_x1[j], ev_m3b, ev_ss3, ev_lnt_prev],
                      lambda e, j=j: e.tensor_sub(
                          ln_t[:, :], x1[:, j * 2048:(j + 1) * 2048], ln_m[:, :]), ch="V")
            ev_xf[j] = op("vector", [ev_t, ev_r3, ev_ctxL[H - 1][2], ev_merged[-1]],
                          lambda e, j=j: e.scalar_tensor_tensor(
                              xf[:, j * 2048:(j + 1) * 2048], ln_t[:, :], 1.0, pB[:, 0:2048],
                              OP.mult, OP.mult,
                              accum_out=m12[:, j:j + 1]), ch="V")
            ev_lnt_prev = ev_xf[j]
            ev_m2c[j] = op("scalar", [ev_xf[j]] + ([ev_ss3, ev_sd3] if j == 0 else [ev_m2c[j - 1]]),
                           lambda e, j=j: e.activation(
                               sq2[:, :], xf[:, j * 2048:(j + 1) * 2048], AF.Square,
                               accum_out=m12[:, 8 + j:8 + j + 1]), ch="S")

        ev_mscl = op("vector", ev_xf + ev_m2c, lambda e: e.tensor_scalar_mul(m12, m12, 1.0 / NF), ch="V")
        mtb = m12b.tensor.bitcast(BF16)
        mtb_ap = lambda c0, c1: bass.AP(mtb, 2 * 2064 + c0, [[2 * 3328, 128], [1, c1 - c0]])
        ev_mcb = op("vector", [ev_mscl], lambda e: e.tensor_copy(mtb_ap(0, 16), m12), ch="V")
        ev_iv0 = op("scalar", [evc["smT"], ev_m2c[-1]], lambda e: e.activation(ivT, c_smT[:, :], AF.Square), ch="S")
        ev_iv = op("vector", [ev_iv0], lambda e: e.reciprocal(ivT, ivT), ch="V")
        ev_mv = op("vector", [ev_iv, evc["muT"]], lambda e: e.tensor_mul(muiv, c_muT[:, :], ivT), ch="V")
        ev_m2v = op("vector", [ev_mv], lambda e: e.tensor_mul(mui2, c_muT[:, :], muiv), ch="V")
        ev_cs8 = op("vector", [ev_m2v], lambda e: e.tensor_reduce(
            csum, mui2.rearrange("p (e c) -> p e c", c=8), AX.X, OP.add), ch="V")
        ev_ivb = op("vector", [ev_iv, ev_ctxL[H - 1][2]], lambda e: e.tensor_copy(ivb, ivT), ch="V")
        ev_muivb = op("vector", [ev_mv], lambda e: e.tensor_copy(muivb, muiv), ch="V")
        ev_csb = op("vector", [ev_cs8], lambda e: e.tensor_copy(csb, csum), ch="V")
        ev_lA = None
        ev_lB = None
        for c in range(8):
            w = [ev_mcb, ev_ivb]
            if c == 0:
                w += [ev_t1_3, ev_xf[-1]]
            ev_lA = op("tensor", w, lambda e, c=c: e.matmul(
                pB[0:1, 1536:1544], mtb_ap(8 + c, 9 + c), ivb[:, c:64:8],
                start=(c == 0), stop=(c == 7)), ch="T")
        for c in range(8):
            ev_lB = op("tensor", [ev_mcb, ev_muivb], lambda e, c=c: e.matmul(
                pB[0:1, 1544:1552], mtb_ap(c, c + 1), muivb[:, c:64:8],
                start=(c == 0), stop=(c == 7)), ch="T")
        ev_lC = op("tensor", [ev_csb, evc["ones"]], lambda e: e.matmul(
            pB[0:1, 1552:1560], c_ones[:, 0:1], csb[:, 0:8],
            start=True, stop=True), ch="T")
        ev_l0 = op("vector", [ev_lA, ev_lB, ev_lC], lambda e: e.tensor_copy(
            t_loss[0:1, 8:32], pB[0:1, 1536:1560]), ch="V")
        ev_l1 = op("vector", [ev_l0], lambda e: e.scalar_tensor_tensor(
            t_loss[0:1, 0:8], t_loss[0:1, 16:24], -2.0, t_loss[0:1, 8:16],
            OP.mult, OP.add), ch="V")
        ev_ag3s = dma("sync", [ev_l1], ag3_in[:, :], t_loss[0:1, 0:8], stream="sAG2")
        ev_cc3 = op("gpsimd", [ev_ag3s], lambda e: e.collective_compute(
            "AllGather", OP.bypass, replica_groups=groups,
            ins=[ag3_in.ap()], outs=[ag3_out.ap()]), ch="cc")
        ev_l3l = dma("gpsimd", [ev_cc3], t_loss[0:1, 8:24],
                     bass.AP(ag3_out, 0, [[16, 1], [1, 16]]), stream="gE")
        ev_l2a = op("vector", [ev_l3l], lambda e: e.tensor_add(
            t_loss[0:1, 0:8], t_loss[0:1, 8:16], t_loss[0:1, 16:24]), ch="V")
        ev_l2 = op("vector", [ev_l2a], lambda e: e.tensor_add(
            t_loss[0:1, 0:8], t_loss[0:1, 0:8], t_loss[0:1, 24:32]), ch="V")
        ev_lmin = op("vector", [ev_l2], lambda e: e.tensor_reduce(
            t_sc[0:1, 0:1], t_loss[0:1, 0:8], AX.X, OP.min), ch="V")
        ev_msk = op("vector", [ev_lmin], lambda e: e.tensor_tensor(
            t_sc[0:1, 2:10], t_loss[0:1, 0:8],
            t_sc[0:1, 0:1].broadcast_to((1, 8)), OP.is_equal), ch="V")
        ev_mio = op("vector", [ev_msk, evc["iota"]], lambda e: e.tensor_mul(
            t_sc[0:1, 2:10], t_sc[0:1, 2:10], c_iota[0:1, :]), ch="V")
        ev_idxf = op("vector", [ev_mio], lambda e: e.tensor_reduce(
            t_sc[0:1, 1:2], t_sc[0:1, 2:10], AX.X, OP.add), ch="V")
        ev_idx = op("vector", [ev_idxf], lambda e: e.tensor_copy(
            t_idx[0:1, 0:1], t_sc[0:1, 1:2]), ch="V")

        wc_t = [Wa[:, j * 1024:(j + 1) * 1024] for j in range(KC)]
        idx_cell = {}

        def _load_idx(e):
            idx_cell["v"] = e.value_load(t_idx[0:1, 0:1])
            return None
        op("gpsimd", [ev_idx, ev_womm], _load_idx)
        ev_w_dw = op("gpsimd", [("gE", CH["gE"][1])], lambda e: e.dma_start(
            out=bass.AP(G16, RQ + 16384, [[81032, 128], [64, 8], [1, 64]]),
            in_=bass.AP(d_dw, idx_cell["v"] * (D * RK), [[RK, 128], [128 * RK, 8], [1, RK]])),
            ch="gE", k=16)
        ev_w_db = op("gpsimd", [("gE2", CH["gE2"][1])], lambda e: e.dma_start(
            out=t_bdf, in_=bass.AP(d_db, idx_cell["v"] * RK, [[RK, 1], [1, RK]])),
            ch="gE2", k=16)
        ev_w_uw = op("gpsimd", [("gE3", CH["gE3"][1])], lambda e: e.dma_start(
            out=bass.AP(G16, RQ + 16384 + 512, [[81032, 128], [65, 8], [1, 64]]),
            in_=bass.AP(d_uwT, idx_cell["v"] * (D * RK), [[RK, 128], [128 * RK, 8], [1, RK]])),
            ch="gE3", k=16)
        ev_w_ub = op("gpsimd", [("gE2", CH["gE2"][1])], lambda e: e.dma_start(
            out=bass.AP(G16, RQ + 16384 + 1032, [[81032, 128], [1, 8]]),
            in_=bass.AP(d_ubT, idx_cell["v"], [[NE, 128], [128 * NE, 8]]),
            allow_slow_non_contiguous=True), ch="gE2", k=16)
        ev_w_cb = op("gpsimd", [("gE3", CH["gE3"][1])], lambda e: e.dma_start(
            out=t_bcrow, in_=bass.AP(d_cb, idx_cell["v"] * D, [[D, 1], [1, D]])),
            ch="gE3", k=16)
        ev_cwq = []
        for q in range(4):
            st = ["gE", "gE2", "gE3", "gE4"][q]
            ev_cwq.append(op("gpsimd", [(st, CH[st][1])], lambda e, q=q: e.dma_start(
                out=bass.AP(Wa, q * 2048, [[8192, 128], [1024, 2], [1, 1024]]),
                in_=bass.AP(d_cwT, idx_cell["v"] * (D * D) + q * 2 * 128 * D,
                            [[D, 128], [128 * D, 2], [1, D]])),
                ch=st, k=16))
        ev_w_cw1 = ev_cwq[0]
        ev_w_cw2 = ev_cwq[-1]
        ev_ffw = list(ev_cwq)
        ev_bdc = op("vector", [ev_w_db], lambda e: e.tensor_copy(t_bdrow[0:1, :], t_bdf), ch="V")
        ev_bu = op("vector", [ev_w_ub], lambda e: e.tensor_copy(
            bass.AP(G16, RQ + 16384 + 512 + 64, [[81032, 128], [65, 8]]),
            bass.AP(G16, RQ + 16384 + 1032, [[81032, 128], [1, 8]])), ch="V")

        ev_base = [None] * KC
        bank_free = {0: [None] * 3, 1: [None] * 3}
        for och in range(KC):
            ph = pA if och % 2 == 0 else pB
            ev_ev = None
            for g in range(4):
                gb = g % 3
                ev_mm = None
                for j in range(KC):
                    w = [ev_bw[j], ev_xf[j]]
                    if j == 0:
                        w.append(bank_free[och % 2][gb])
                    if och == 1 and g == 0 and j == 0:
                        w.append(ev_xf[-1])
                    if och == 0 and g == 0 and j == 0:
                        w += [ev_m3b, ev_t1_3]
                    ev_mm = op("tensor", w,
                               lambda e, ph=ph, g=g, gb=gb, j=j, och=och: e.matmul(
                                   ph[:, gb * 512:(gb + 1) * 512],
                                   bw_t[j][:, och * 128:(och + 1) * 128],
                                   xf[:, j * 2048 + g * 512:j * 2048 + (g + 1) * 512],
                                   start=(j == 0), stop=(j == KC - 1)), ch="T")
                ev_ev = op("scalar", [ev_mm],
                           lambda e, ph=ph, g=g, gb=gb, och=och: e.activation(
                               ffo[:, och * 2048 + g * 512:och * 2048 + (g + 1) * 512],
                               ph[:, gb * 512:(gb + 1) * 512], AF.Copy), ch="S")
                bank_free[och % 2][gb] = ev_ev
            ev_base[och] = ev_ev

        ev_hones = op("vector", [ev_stage_free, ev_eq_free[0], ev_eq_free[1]],
                      lambda e: e.memset(h_ext[64:65, :], 1.0), ch="V")
        ev_or = op("vector", [ev_hones], lambda e: e.memset(t_ones512[0:1, :], 1.0), ch="V")
        ev_hmm = []
        for g in range(4):
            evh = None
            for j in range(KC):
                w = [ev_w_dw, ev_xf[j]]
                if g == 0 and j == 0:
                    w += [ev_lA, ev_m3b, ev_base[KC - 2]]
                evh = op("tensor", w,
                         lambda e, g=g, j=j: e.matmul(
                             pA[0:64, g * 512:(g + 1) * 512],
                             t_ffw[:, j * 64:(j + 1) * 64],
                             xf[:, j * 2048 + g * 512:j * 2048 + (g + 1) * 512],
                             start=(j == 0), stop=False), ch="T")
            evh = op("tensor", [ev_w_db, ev_or, ev_bdc], lambda e, g=g: e.matmul(
                pA[0:64, g * 512:(g + 1) * 512],
                t_bdrow[0:1, :], t_ones512[0:1, :],
                start=False, stop=True), ch="T")
            ev_hmm.append(evh)
        ev_hext = op("scalar", ev_hmm + [ev_m2c[-1]], lambda e: e.activation(
            h_ext[0:64, :], pA[0:64, 0:2048], AF.Relu), ch="S")

        ev_Mmm = None
        for j in range(KC):
            for g2 in range(2):
                w = [ev_w_uw, ev_bu] + ev_cwq
                if j == 0 and g2 == 0:
                    w += [ev_l2, ev_lC, ev_l0, ev_base[KC - 1]]
                ev_Mmm = op("tensor", w,
                            lambda e, j=j, g2=g2: e.matmul(
                                pB[0:65, g2 * 512:(g2 + 1) * 512],
                                t_ffw[:, 512 + j * 65:512 + (j + 1) * 65],
                                wc_t[j][:, g2 * 512:(g2 + 1) * 512],
                                start=(j == 0), stop=(j == KC - 1)), ch="T")
        ev_Me1 = op("vector", [ev_Mmm, ev_hones], lambda e: e.tensor_copy(
            M_ext[0:64, :], pB[0:64, 0:1024]), ch="V")
        ev_Me2 = op("vector", [ev_Mmm, ev_w_cb], lambda e: e.scalar_tensor_tensor(
            M_ext[64:65, :], pB[64:65, 0:1024], 1.0, t_bcrow[0:1, :],
            OP.mult, OP.add), ch="V")

        ev_ffo = [None] * KC
        yb_free = {0: None, 1: None}
        for och in range(KC):
            ph = pA if och % 2 == 0 else pB
            ev_mm = None
            for g in range(4):
                w = [ev_hext, ev_Me1, ev_Me2]
                if g == 0:
                    w.append(yb_free[och % 2])
                if och == 0 and g == 0:
                    w += [ev_lA, ev_lB, ev_lC, ev_l0]
                ev_mm = op("tensor", w,
                           lambda e, ph=ph, g=g, och=och: e.matmul(
                               ph[:, g * 512:(g + 1) * 512],
                               M_ext[0:65, och * 128:(och + 1) * 128],
                               h_ext[0:65, g * 512:(g + 1) * 512],
                               start=(g >= 0), stop=True), ch="T")
            ev_ffo[och] = op("vector", [ev_mm, ev_base[och], evc["bb"]],
                             lambda e, ph=ph, och=och: e.scalar_tensor_tensor(
                                 ffo[:, och * 2048:(och + 1) * 2048], ph[:, :],
                                 c_bb[:, och:och + 1],
                                 ffo[:, och * 2048:(och + 1) * 2048],
                                 OP.add, OP.add), ch="V")
            yb_free[och % 2] = ev_ffo[och]

        sqf = g16(ZC, 128, 2048)
        ev_m4b, ev_r4, ev_ss4, ev_t1_4, ev_sd4 = ln_stats(
            lambda j: ffo[:, j * 2048:(j + 1) * 2048], ev_ffo, sqf,
            [ev_ffo[-1], ev_ss3, ev_sd3], [ev_ffo[KC - 2], ev_ffo[KC - 1]],
            [ev_ffo[KC - 2], ev_ffo[KC - 1]])
        ev_out = []
        ost_free = [None, None, None, None]
        ev_lnt_prev = ev_r4
        for j in range(KC):
            sj = j % 4
            ev_t = op("vector", [ev_ffo[j], ev_m4b, ev_ss4, ev_lnt_prev],
                      lambda e, j=j: e.tensor_sub(
                          ln_t[:, :], ffo[:, j * 2048:(j + 1) * 2048], ln_m[:, :]), ch="V")
            ev_t2 = op("vector", [ev_t, ev_r4],
                       lambda e: e.tensor_mul(ln_t[:, :], ln_t[:, :], pB[:, 0:2048]), ch="V")
            ev_o = op("vector", [ev_t2, ev_x1[j], ost_free[sj]],
                      lambda e, j=j, sj=sj: e.tensor_add(
                          outst[sj], ln_t[:, :], x1[:, j * 2048:(j + 1) * 2048]), ch="V")
            ev_lnt_prev = ev_o
            ev_d = dma("sync", [ev_o], d_out[j * 128:(j + 1) * 128, :], outst[sj],
                       stream=["sO0", "sO1", "sO2", "sO3"][j % 4])
            ost_free[sj] = ev_d
            ev_out.append(ev_d)
        op("sync", ev_out + [(nm, CH[nm][1]) for nm in
                             ("sQ", "sK", "sL", "sAG1", "sAG2", "sST",
                              "sO0", "sO1", "sO2", "sO3", "sC", "qd")],
           lambda e: None)
        op("gpsimd", [(nm, CH[nm][1]) for nm in
                      ("gC", "gW", "gW2", "gW3", "gX", "gX2", "gX3", "gX4",
                       "gE", "gE2", "gE3", "gE4", "qg", "cc")],
           lambda e: None)
        op("scalar", [("qs", CH["qs"][1])], lambda e: None)

        import kernel as _K
        _K._PROG_SIZES = {k: len(v) for k, v in PROG.items()}
        if limit is not None:
            tgt, frac = limit
            for k in PROG:
                if tgt == "all" or k == tgt:
                    n = max(1, int(len(PROG[k]) * frac))
                    PROG[k] = PROG[k][:n]
        with nc.Block() as block:
            @block.sync
            def _(eng):
                for fn in PROG["sync"]:
                    fn(eng)

            @block.gpsimd
            def _(eng):
                for fn in PROG["gpsimd"]:
                    fn(eng)

            @block.tensor
            def _(eng):
                for fn in PROG["tensor"]:
                    fn(eng)

            @block.vector
            def _(eng):
                for fn in PROG["vector"]:
                    fn(eng)

            @block.scalar
            def _(eng):
                for fn in PROG["scalar"]:
                    fn(eng)

    return nc


def _consts():
    c = {}
    c["c_ones"] = np.ones((128, 128), np.float32)
    c["c_lnones"] = np.full((128, 128), 1.0 / 1024.0, np.float32)
    c["c_ident"] = np.eye(128, dtype=np.float32)
    selq = np.zeros((D, H), np.float32)
    selq[np.arange(D), np.arange(D) // 64] = 0.5
    c["c_selq"] = selq
    selk = np.zeros((512, HK), np.float32)
    selk[np.arange(512), np.arange(512) // 64] = 0.5
    c["c_selk"] = selk
    c["c_iota"] = np.arange(8, dtype=np.float32)[None, :].copy()
    return c


def _ec_layout(a):
    return np.ascontiguousarray(a.reshape(8, 8, 128).transpose(2, 0, 1).reshape(128, 64))


def kernel(**inputs):
    x = np.asarray(inputs["L0_attn_x"], np.float32)
    Wq = np.asarray(inputs["L0_attn_Wq"], np.float32)
    Wk = np.asarray(inputs["L0_attn_Wk"], np.float32)
    Wv = np.asarray(inputs["L0_attn_Wv"], np.float32)
    Wo = np.asarray(inputs["L0_attn_Wo"], np.float32)
    bo = np.asarray(inputs["L0_attn_bo"], np.float32)
    proj = np.asarray(inputs["L0_attn_proj"], np.float32)
    bW = np.asarray(inputs["L0_ffn_base_W"], np.float32)
    bb = np.asarray(inputs["L0_ffn_base_b"], np.float32)
    dmu = np.asarray(inputs["L0_ffn_disc_mu"], np.float32)
    dsg = np.asarray(inputs["L0_ffn_disc_sigma"], np.float32)
    dW = np.asarray(inputs["L0_ffn_down_W"], np.float32)
    db = np.asarray(inputs["L0_ffn_down_b"], np.float32)
    uW = np.asarray(inputs["L0_ffn_up_W"], np.float32)
    ub = np.asarray(inputs["L0_ffn_up_b"], np.float32)
    cW = np.asarray(inputs["L0_ffn_conv_W"], np.float32)
    cb = np.asarray(inputs["L0_ffn_conv_b"], np.float32)

    cst = _consts()
    projT2 = np.ascontiguousarray(np.vstack([proj.T, proj.T]))
    base_wT = np.ascontiguousarray(bW.T)
    up_wT = np.ascontiguousarray(uW.transpose(0, 2, 1))
    conv_wT = np.ascontiguousarray(cW.transpose(0, 2, 1))
    up_bT = np.ascontiguousarray(ub.T)
    bo_cols = np.ascontiguousarray(bo.reshape(8, 128).T)
    bb_cols = np.ascontiguousarray(bb.reshape(8, 128).T)
    smT = _ec_layout(dsg)
    muT = _ec_layout(dmu)

    nc = build_nc(4)
    in_maps = []
    for c in range(8):
        b, p = c // 2, c % 2
        xb = x[b]
        rolled = np.concatenate([xb[p * 2048:(p + 1) * 2048], xb[(1 - p) * 2048:(2 - p) * 2048]], 0)
        in_maps.append({
            "xT": np.ascontiguousarray(rolled.T),
            "wq": Wq,
            "wk": np.ascontiguousarray(Wk[:, p * 512:(p + 1) * 512]),
            "wv": np.ascontiguousarray(Wv[:, p * 512:(p + 1) * 512]),
            "wo": Wo,
            "bo_cols": bo_cols,
            "projT2": projT2,
            "base_wT": base_wT,
            "baseb_cols": bb_cols,
            "sigmaT_ec": smT,
            "muT_ec": muT,
            "down_w": dW,
            "down_b": db,
            "up_wT": up_wT,
            "up_bT": up_bT,
            "conv_wT": conv_wT,
            "conv_b": cb,
            "c_iota": cst["c_iota"],
            "c_ones": cst["c_ones"],
            "c_lnones": cst["c_lnones"],
            "c_ident": cst["c_ident"],
            "c_selq": cst["c_selq"],
            "c_selk": cst["c_selk"],
        })
    res = run_bass_kernel_spmd(nc, in_maps, core_ids=list(range(8)))
    out = np.empty((4, 4096, 1024), np.float32)
    for c in range(8):
        b, p = c // 2, c % 2
        out[b, p * 2048:(p + 1) * 2048] = np.ascontiguousarray(res.results[c]["outT"].T)
    return out
